# version 04
# baseline (speedup 1.0000x reference)
import numpy as np
import ml_dtypes
from contextlib import ExitStack

import concourse.bass as bass
import concourse.mybir as mybir
from concourse.bass_utils import run_bass_kernel_spmd

F32 = mybir.dt.float32
BF16 = mybir.dt.bfloat16
AF = mybir.ActivationFunctionType
ALU = mybir.AluOpType
AX = mybir.AxisListType

NEGBIG = -1.0e30
NCORE = 8
NCH = 16


def make_cfg(T=8192, HID=8192):
    c = {}
    c["T"], c["D"], c["H"] = T, 2048, 16
    c["CH"] = T // NCH
    c["KS"] = c["CH"] // 128
    c["NT"] = 2 * c["CH"]
    c["NH"] = c["NT"] // 512
    c["DC"], c["HID"], c["JC"] = 16, HID, HID // 128
    c["LNEPS"], c["RMSEPS"] = 1e-5, 1e-6
    c["stage"] = 4
    assert c["NT"] % 512 == 0 and c["CH"] % 128 == 0
    return c


def gloc(j):
    return (j, 0) if j < NCORE else (NCH - 1 - j, 1)


def cidx(j):
    return j if j < NCORE else 8 + (NCH - 1 - j)



def prep_blocks(W):
    K, M = W.shape
    Kc, Mc = K // 128, M // 128
    return np.ascontiguousarray(
        W.reshape(Kc, 128, Mc, 128).transpose(2, 1, 0, 3).reshape(Mc, 128, Kc * 128))


def host_prep(cfg, inputs):
    T, D, H, CH = cfg["T"], cfg["D"], cfg["H"], cfg["CH"]
    JC, DC = cfg["JC"], cfg["DC"]
    x = np.asarray(inputs["L0_attn_a_x"], np.float32).reshape(T, D)
    xT = np.ascontiguousarray(x.T)

    wq = np.asarray(inputs["L0_attn_b_wq"], np.float32)
    nope_cols = np.concatenate([wq[:, 192 * h:192 * h + 128] for h in range(H)], 1)
    rope_cols = np.concatenate([wq[:, 192 * h + 128:192 * h + 192] for h in range(H)], 1)
    wq_blk = prep_blocks(np.concatenate([nope_cols, rope_cols], 1))

    wkva = np.asarray(inputs["L0_attn_b_wkv_a"], np.float32)
    wkva_main = prep_blocks(wkva[:, :512])
    wkva_rope = np.ascontiguousarray(
        wkva[:, 512:].reshape(16, 128, 64).transpose(1, 0, 2).reshape(128, 1024))

    wkvb = np.asarray(inputs["L0_attn_b_wkv_b"], np.float32)
    nope_c = np.concatenate([wkvb[:, 256 * h:256 * h + 128] for h in range(H)], 1)
    v_c = np.concatenate([wkvb[:, 256 * h + 128:256 * h + 256] for h in range(H)], 1)
    wkvb_blk = prep_blocks(np.concatenate([nope_c, v_c], 1))

    wo_blk = prep_blocks(np.asarray(inputs["L0_attn_b_wo"], np.float32))

    up_blk = prep_blocks(np.ascontiguousarray(
        np.asarray(inputs["L0_ffn_up_weight"], np.float32).T))
    down_blk = prep_blocks(np.ascontiguousarray(
        np.asarray(inputs["L0_ffn_down_weight"], np.float32).T))

    usc = np.ascontiguousarray(
        np.asarray(inputs["L0_ffn_up_scale"], np.float32).reshape(JC, 128).T)
    dsc = np.ascontiguousarray(
        np.asarray(inputs["L0_ffn_down_scale"], np.float32).reshape(DC, 128).T)
    knw = np.ascontiguousarray(
        np.asarray(inputs["L0_attn_b_kv_norm_w"], np.float32).reshape(4, 128).T)
    osc = np.asarray(inputs["L0_ffn_output_scale"], np.float32).reshape(1, 1)

    ones_col = np.ones((128, 1), np.float32)
    ones_row = np.ones((1, 128), np.float32)
    idf = np.eye(128, dtype=np.float32)
    idb = np.eye(128, dtype=ml_dtypes.bfloat16)
    tri = np.where(np.arange(128)[None, :] >= np.arange(128)[:, None],
                   0.0, NEGBIG).astype(np.float32)

    inv = 1.0 / (10000.0 ** (np.arange(0, 64, 2, dtype=np.float32) / 64.0))
    maps = []
    for me in range(NCORE):
        c0, c1 = me, NCH - 1 - me
        cols = np.concatenate([np.arange(c0 * CH, (c0 + 1) * CH),
                               np.arange(c1 * CH, (c1 + 1) * CH)])
        f = np.outer(inv, cols.astype(np.float32))
        emb = np.concatenate([f, f], 0)
        maps.append({
            "x0T_own": np.ascontiguousarray(xT[:, cols]),
            "wq_blk": wq_blk, "wkva_main": wkva_main, "wkva_rope": wkva_rope,
            "wkvb_blk": wkvb_blk, "wo_blk": wo_blk,
            "up_blk": up_blk, "down_blk": down_blk,
            "usc": usc, "dsc": dsc, "knw": knw, "osc": osc,
            "ones_col": ones_col, "ones_row": ones_row,
            "idf": idf, "idb": idb, "tri": tri,
            "cosq": np.cos(emb).astype(np.float32),
            "sinq": np.sin(emb).astype(np.float32),
        })
    return maps



ENGS = ("pe", "act", "dve", "sp", "pool")


class Op:
    __slots__ = ("eng", "fn", "reads", "writes", "sem_key", "inc", "idx",
                 "edges", "need_inc", "val", "clock", "self_inc")

    def __init__(self, eng, fn, reads, writes, sem_key, inc, idx, self_inc):
        self.eng, self.fn = eng, fn
        self.reads, self.writes = tuple(reads), tuple(writes)
        self.sem_key, self.inc, self.idx = sem_key, inc, idx
        self.self_inc = self_inc
        self.edges = []
        self.need_inc = False
        self.val = None
        self.clock = None


class Prog:
    def __init__(self):
        self.ops = []
        self.last_w = {}
        self.readers = {}
        self.pending = {e: [] for e in ENGS}

    def op(self, eng, fn, r=(), w=(), sem_key=None, inc=1, self_inc=False):
        o = Op(eng, fn, r, w, sem_key, inc, len(self.ops), self_inc)
        seen = {}
        for k in o.reads:
            p = self.last_w.get(k)
            if p is not None:
                seen[p.idx] = p
        for k in o.writes:
            p = self.last_w.get(k)
            if p is not None:
                seen[p.idx] = p
            for rd in self.readers.get(k, ()):
                seen[rd.idx] = rd
        for p in self.pending[eng]:
            seen[p.idx] = p
        self.pending[eng] = []
        seen.pop(o.idx, None)
        o.edges = list(seen.values())
        for k in o.writes:
            self.last_w[k] = o
            self.readers[k] = []
        for k in o.reads:
            self.readers.setdefault(k, []).append(o)
        self.ops.append(o)
        return o

    def barrier(self):
        lasts = {}
        for o in self.ops:
            key = ("g", o.sem_key) if o.sem_key is not None else ("eng", o.eng)
            lasts[key] = o
        alll = list(lasts.values())
        for e in ENGS:
            self.pending[e] = list(alll)

    def finalize(self, nc, stack):
        for o in self.ops:
            for p in o.edges:
                p.need_inc = True
            if o.sem_key is not None:
                o.need_inc = True
        sems = {}
        for e in ENGS:
            sems[("eng", e)] = stack.enter_context(nc.semaphore(f"s_{e}"))
        for o in self.ops:
            if o.sem_key is not None and ("g", o.sem_key) not in sems:
                sems[("g", o.sem_key)] = stack.enter_context(
                    nc.semaphore(f"s_{o.sem_key}"))
        counts = {}
        for o in self.ops:
            if not o.need_inc:
                continue
            key = ("g", o.sem_key) if o.sem_key is not None else ("eng", o.eng)
            counts[key] = counts.get(key, 0) + o.inc
            o.val = counts[key]
        eng_clock = {e: {} for e in ENGS}
        emit = {e: [] for e in ENGS}
        for o in self.ops:
            clk = eng_clock[o.eng]
            waits = []
            for p in sorted(o.edges, key=lambda q: q.idx):
                pkey = ("g", p.sem_key) if p.sem_key is not None else ("eng", p.eng)
                if clk.get(pkey, 0) < p.val:
                    waits.append((sems[pkey], p.val))
                    clk[pkey] = p.val
                if p.clock:
                    for k2, v2 in p.clock.items():
                        if clk.get(k2, 0) < v2:
                            clk[k2] = v2
            okey = ("g", o.sem_key) if o.sem_key is not None else ("eng", o.eng)
            if o.need_inc:
                c2 = dict(clk)
                c2[okey] = max(c2.get(okey, 0), o.val)
                o.clock = c2
            else:
                o.clock = dict(clk)
            sem = sems[okey] if (o.need_inc and not o.self_inc) else None
            selfsem = sems[okey] if (o.need_inc and o.self_inc) else None
            emit[o.eng].append((waits, o.fn, sem, o.inc, selfsem))
        return emit



def build(cfg):
    nc = bass.Bass("TRN2", target_bir_lowering=False)
    P = Prog()
    stack = ExitStack()
    T, D, H, CH, NT, NH = (cfg["T"], cfg["D"], cfg["H"], cfg["CH"], cfg["NT"],
                           cfg["NH"])
    KS, DC, JC, HID = cfg["KS"], cfg["DC"], cfg["JC"], cfg["HID"]
    TTn = T // 128
    NTT = NT // 128

    din = lambda n, s, dt=F32: nc.dram_tensor(n, list(s), dt, kind="ExternalInput").ap()
    dint = lambda n, s, dt: nc.dram_tensor(n, list(s), dt, kind="Internal").ap()

    x0T = din("x0T_own", [D, NT])
    WQ = din("wq_blk", [24, 128, D])
    WKA = din("wkva_main", [4, 128, D])
    WKAR = din("wkva_rope", [128, 1024])
    WKB = din("wkvb_blk", [32, 128, 512])
    WOd = din("wo_blk", [16, 128, D])
    UPB = din("up_blk", [JC, 128, D])
    DNB = din("down_blk", [16, 128, HID])
    cons_in = [("USC", din("usc", [128, JC])), ("DSC", din("dsc", [128, DC])),
               ("KNW", din("knw", [128, 4])), ("OSC", din("osc", [1, 1])),
               ("ONESC", din("ones_col", [128, 1])),
               ("ONESR", din("ones_row", [1, 128])),
               ("IDF", din("idf", [128, 128])),
               ("IDB", din("idb", [128, 128], BF16)),
               ("TRI", din("tri", [128, 128])),
               ("COS", din("cosq", [64, NT])), ("SIN", din("sinq", [64, NT])),
               ("WKARs", WKAR)]
    OUT = nc.dram_tensor("out_own", [NT, D], F32, kind="ExternalOutput").ap()

    XST = dint("XST", [128, DC * NT], F32)
    GA_in = dint("GA_in", [D + 1, NT], BF16)
    GA_out = dint("GA_out", [NCORE * (D + 1), NT], BF16)
    GVA_in = dint("GVA_in", [NT, H * 129], BF16)
    GVA_out = dint("GVA_out", [NCORE * NT, H * 129], BF16)
    GYA_in = dint("GYA_in", [T, 256], BF16)
    GYA_out = dint("GYA_out", [NCORE * T, 256], BF16)
    NRB = 2 * D + H * 64 + 64 + 2
    GB_in = dint("GB_in", [NRB, NT], BF16)
    GB_out = dint("GB_out", [NCORE * NRB, NT], BF16)
    GVB_in = dint("GVB_in", [NT, H * 129], BF16)
    GVB_out = dint("GVB_out", [NCORE * NT, H * 129], BF16)
    GYB_in = dint("GYB_in", [T, 256], BF16)
    GYB_out = dint("GYB_out", [NCORE * T, 256], BF16)
    UWS = dint("UWS", [JC, 128, D], BF16)
    DWS = dint("DWS", [16, 128, HID], BF16)

    spec_pers = [
        ("ONESC", [128, 1], F32), ("ONESR", [1, 128], F32),
        ("IDF", [128, 128], F32), ("IDB", [128, 128], BF16),
        ("TRI", [128, 128], F32), ("USC", [128, JC], F32),
        ("DSC", [128, DC], F32), ("CSC", [128, DC], F32),
        ("KNW", [128, 4], F32), ("OSC", [1, 1], F32),
        ("COS", [64, NT], F32), ("SIN", [64, NT], F32),
        ("WKARs", [128, 1024], F32), ("WKARb", [128, 1024], BF16),
        ("BIAS", [128, 64], F32), ("CROW", [1, 32], BF16),
        ("CROWF", [1, 32], F32), ("QN2", [1, 32], F32),
        ("ROWA", [1, 512], F32), ("ROWB", [1, 512], F32),
        ("ROWC", [1, 512], F32), ("ROWD", [1, 512], F32),
        ("KMN", [1, 128], F32), ("KMX", [1, 16], F32), ("RL", [128, 1], F32),
        ("ROWB16", [1, 64], BF16), ("QROW", [1, 32], BF16), ("KMROW", [1, 16], BF16),
        ("CSTG", [1, NT], BF16), ("CSTG2", [1, NT], BF16),
        ("KH", [1, 2], F32),
        ("EPSA", [1, 1], F32), ("EPSB", [1, 1], F32),
        ("XW0", [128, NT], F32), ("XW1", [128, NT], F32),
        ("YTA", [128, 128], F32), ("YTB", [128, 128], BF16),
        ("SQ0", [128, 512], F32), ("SQ1", [128, 512], F32),
    ]
    spec_a = [
        ("N", [128, DC * NT], BF16),
        ("VSTG", [128, NTT * H * 129], BF16),
        ("PSB0", [128, 2 * CH], BF16), ("PSB1", [128, 2 * CH], BF16),
    ]
    spec_a2 = [
        ("KQA", [128, 2 * T], BF16),
        ("VAA", [128, TTn * 258], BF16),
        ("YIN", [128, NCORE * NTT * 256], BF16),
        ("PSB0b", [128, 2 * CH], BF16), ("PSB1b", [128, 2 * CH], BF16),
    ]
    spec_f = [
        ("NF", [128, DC * NT], BF16),
        ("HSB", [128, JC * 512], BF16),
        ("UPIN0", [128, D], F32), ("UPIN1", [128, D], F32),
        ("UWSB0", [128, D], BF16), ("UWSB1", [128, D], BF16),
        ("DWSB0", [128, HID], BF16), ("DWSB1", [128, HID], BF16),
        ("DNIN", [128, HID // 2], F32, "UPIN0"),
    ]
    spec_p = [
        ("NP", [128, DC * NT], BF16),
        ("WBLK0", [128, D], F32), ("WBLK1", [128, D], F32),
        ("WBB0", [128, D], BF16), ("WBB1", [128, D], BF16),
        ("RQ", [64, H * NT], BF16),
        ("RH", [64, NT], F32), ("RT1", [64, NT], F32), ("RT2", [64, NT], F32),
        ("LATN", [128, 4 * NT], BF16),
        ("KRO", [64, NT], BF16),
        ("STG", [128, NT], BF16),
        ("VSTG2", [128, NTT * H * 129], BF16),
    ]
    spec_b = [
        ("KNB", [128, 2 * T], BF16),
        ("QNB", [128, 2 * T], BF16),
        ("QRB", [64, 2 * T], BF16),
        ("KRB", [64, T], BF16),
        ("VBB", [128, TTn * 258], BF16),
        ("PSB2", [128, 2 * CH], BF16), ("PSB3", [128, 2 * CH], BF16),
    ]
    spec_w = [
        ("YFM", [128, H * NT], BF16),
        ("YIN2", [128, NCORE * NTT * 256], BF16),
        ("WOB0", [128, D], F32), ("WOB1", [128, D], F32),
        ("WOBB", [128, D], BF16),
    ]
    spec_o = [("OS0", [128, 128], F32), ("OS1", [128, 128], F32)]

    def size_of(spec):
        off, offs = 0, {}
        for ent in spec:
            nm, sh, dt = ent[0], ent[1], ent[2]
            if len(ent) == 4:
                offs[nm] = offs[ent[3]]
                continue
            nb = int(np.prod(sh[1:])) * mybir.dt.size(dt)
            offs[nm] = off
            off += ((nb + 31) // 32) * 32
        return off, offs

    pers_sz, pers_off = size_of(spec_pers)
    arena_specs = [spec_a, spec_a2, spec_f, spec_p, spec_b, spec_w, spec_o]
    arena_sz = max(size_of(s)[0] for s in arena_specs)
    total = pers_sz + arena_sz

    backing = stack.enter_context(
        nc.sbuf_tensor("arena", [128, total], mybir.dt.uint8))
    base = nc.lookup_mloc(backing).addr

    V = {}

    def place(spec, offs, origin):
        for ent in spec:
            nm, sh, dt = ent[0], ent[1], ent[2]
            hnd = nc.virtual_sbuf_tensor("sb_" + nm, list(sh), dt)
            nc._tensor(hnd.name, list(hnd.shape), hnd.dtype, type="SB")
            mloc = nc.lookup_mloc(hnd)
            mloc.allocated = True
            mloc.addr = base + origin + offs[nm]
            V[nm] = hnd.ap()

    place(spec_pers, pers_off, 0)
    for sp_ in arena_specs:
        _, offs = size_of(sp_)
        place(sp_, offs, pers_sz)

    PA = stack.enter_context(nc.psum_tensor("PA", [128, 1024], F32)).ap()
    PB = stack.enter_context(nc.psum_tensor("PB", [128, 1024], F32)).ap()
    PC = stack.enter_context(nc.psum_tensor("PC", [128, 1024], F32)).ap()
    PD = stack.enter_context(nc.psum_tensor("PD", [128, 512], F32)).ap()
    PTB = stack.enter_context(nc.psum_tensor("PTB", [128, 512], BF16)).ap()

    op = P.op

    def MM(out, lhsT, rhs, start, stop, r, w):
        op("pe", (lambda o, l, rr, st, sp2: lambda e: e.matmul(
            o, l, rr, start=st, stop=sp2, skip_group_check=True))(
                out, lhsT, rhs, start, stop), r, w)

    def TRANS(out, in_, ident, r, w):
        op("pe", (lambda o, i_, id_: lambda e: e.transpose(o, i_, id_))(
            out, in_, ident), r, w)

    def ACTV(out, in_, func, r, w, bias=0.0, scale=1.0):
        op("act", (lambda o, i_, f, bb, s: lambda e: e.activation(
            o, i_, f, bias=bb, scale=s))(out, in_, func, bias, scale), r, w)

    def SIGN(out, in_, r, w):
        op("act", (lambda o, i_: lambda e: e.sign(o, i_))(out, in_), r, w)

    def TT(out, a, b, alu, r, w):
        op("dve", (lambda o, a_, b_, u: lambda e: e.tensor_tensor(o, a_, b_, u))(
            out, a, b, alu), r, w)

    def TS(out, a, s1, alu, r, w):
        op("dve", (lambda o, a_, s_, u: lambda e: e.tensor_single_scalar(
            o, a_, s_, u))(out, a, s1, alu), r, w)

    def TSM(out, a, s1, r, w):
        op("dve", (lambda o, a_, s_: lambda e: e.tensor_scalar_mul(o, a_, s_))(
            out, a, s1), r, w)

    def STT(out, in0, scal, in1, op0, op1, r, w):
        op("dve", (lambda o, i0, s_, i1, u0, u1: lambda e: e.scalar_tensor_tensor(
            o, i0, s_, i1, u0, u1))(out, in0, scal, in1, op0, op1), r, w)

    def RED(out, in_, alu, r, w):
        op("dve", (lambda o, i_, u: lambda e: e.tensor_reduce(o, i_, AX.X, u))(
            out, in_, alu), r, w)

    def RECIP(out, in_, r, w):
        op("dve", (lambda o, i_: lambda e: e.reciprocal(o, i_))(out, in_), r, w)

    def CPY(out, in_, r, w, eng="dve"):
        if eng == "act":
            op(eng, (lambda o, i_: lambda e: e.copy(o, i_))(out, in_), r, w)
        else:
            op(eng, (lambda o, i_: lambda e: e.tensor_copy(o, i_))(out, in_), r, w)

    def MEMSET(ap_, val, r, w):
        op("dve", (lambda a_, v_: lambda e: e.memset(a_, v_))(ap_, val), r, w)

    def DMA(eng, out, in_, slot, r, w):
        op(eng, (lambda o, i_: lambda e: e.dma_start(out=o, in_=i_))(out, in_),
           r, w, sem_key=slot, inc=16)

    def SWDMA(bodies, slot, r, w):
        n = len(bodies[0])
        assert all(len(bb) == n for bb in bodies)

        def fn(e, sem):
            if not hasattr(e, "_pid_cached"):
                e._pid_cached = e.compute_val(e.partition_id())
            idx = e._pid_cached
            for i in nc.Switch(engines=[e], index=[idx], n=NCORE):
                for (o_, i_) in bodies[i]:
                    e.dma_start(out=o_, in_=i_).then_inc(sem, 16)
        op("pool", fn, r, w, sem_key=slot, inc=16 * n, self_inc=True)

    def COLL(ins, outs, slot, r, w):
        op("pool", (lambda i_, o_: lambda e: e.collective_compute(
            "AllGather", ALU.bypass, replica_groups=[list(range(NCORE))],
            ins=[i_], outs=[o_]))(ins, outs), r, w, sem_key=slot, inc=1)

    def row(half):
        return (PD[0:1, 0:512], "PD") if half == 0 else (PC[0:1, 0:512], "PC0")

    for nm, src in cons_in:
        DMA("sp", V[nm][:], src, "kc_" + nm, r=[], w=[nm])
    MM(PD[:, 0:1], V["ONESR"][:], V["OSC"][:], True, True,
       r=["ONESR", "OSC"], w=["PD"])
    TSM(V["CSC"][:], V["DSC"][:], PD[:, 0:1], r=["DSC", "PD"], w=["CSC"])
    ACTV(V["WKARb"][:], V["WKARs"][:], AF.Copy, r=["WKARs"], w=["WKARb"])
    MEMSET(V["EPSA"][:], float(cfg["LNEPS"]), r=[], w=["EPSA"])
    MEMSET(V["EPSB"][:], float(cfg["RMSEPS"]), r=[], w=["EPSB"])
    DMA("sp", XST.rearrange("p (c t) -> p c t", c=DC),
        x0T.rearrange("(c p) t -> p c t", p=128), "x0", r=[], w=["XST"])

    def emit_ln(nbuf, nkey):
        for half in range(NH):
            hs = half * 512
            for kc in range(DC):
                xw = V["XW0"] if kc % 2 == 0 else V["XW1"]
                xk = "XW0" if kc % 2 == 0 else "XW1"
                DMA("sp", xw[:, 0:512], XST[:, kc * NT + hs: kc * NT + hs + 512],
                    "dm" + xk, r=["XST"], w=[xk])
                MM(PD[0:1, 0:512], V["ONESC"][:], xw[:, 0:512], kc == 0,
                   kc == DC - 1, r=[xk, "ONESC"], w=["PD"])
                sq = V["SQ0"] if kc % 2 == 0 else V["SQ1"]
                sqk = "SQ0" if kc % 2 == 0 else "SQ1"
                ACTV(sq[:], xw[:, 0:512], AF.Square, r=[xk], w=[sqk])
                MM(PC[0:1, 0:512], V["ONESC"][:], sq[:], kc == 0, kc == DC - 1,
                   r=[sqk, "ONESC"], w=["PC0"])
            TS(V["ROWA"][:], PD[0:1, 0:512], 1.0 / D, ALU.mult, r=["PD"], w=["ROWA"])
            TS(V["ROWB"][:], PC[0:1, 0:512], 1.0 / D, ALU.mult, r=["PC0"], w=["ROWB"])
            TT(V["ROWC"][:], V["ROWA"][:], V["ROWA"][:], ALU.mult,
               r=["ROWA"], w=["ROWC"])
            TT(V["ROWB"][:], V["ROWB"][:], V["ROWC"][:], ALU.subtract,
               r=["ROWB", "ROWC"], w=["ROWB"])
            ACTV(V["ROWB"][:], V["ROWB"][:], AF.Sqrt, bias=V["EPSA"][:],
                 r=["ROWB", "EPSA"], w=["ROWB"])
            RECIP(V["ROWB"][:], V["ROWB"][:], r=["ROWB"], w=["ROWB"])
            TT(V["ROWC"][:], V["ROWA"][:], V["ROWB"][:], ALU.mult,
               r=["ROWA", "ROWB"], w=["ROWC"])
            MM(PC[:, 0:512], V["ONESR"][:], V["ROWB"][:], True, True,
               r=["ROWB", "ONESR"], w=["PC0"])
            MM(PC[:, 512:1024], V["ONESR"][:], V["ROWC"][:], True, True,
               r=["ROWC", "ONESR"], w=["PC1"])
            for kc in range(DC):
                xw = V["XW0"] if kc % 2 == 0 else V["XW1"]
                xk = "XW0" if kc % 2 == 0 else "XW1"
                DMA("sp", xw[:, 0:512], XST[:, kc * NT + hs: kc * NT + hs + 512],
                    "dm" + xk, r=["XST"], w=[xk])
                sq = V["SQ0"] if kc % 2 == 0 else V["SQ1"]
                sqk = "SQ0" if kc % 2 == 0 else "SQ1"
                TT(sq[:], xw[:, 0:512], PC[:, 0:512], ALU.mult,
                   r=[xk, "PC0"], w=[sqk])
                TT(nbuf[:, kc * NT + hs: kc * NT + hs + 512], sq[:],
                   PC[:, 512:1024], ALU.subtract, r=[sqk, "PC1"], w=[nkey])

    def emit_vstage(vsrc, vsrck, vstg, vstgk):
        for i in range(NTT):
            base_c = i * H * 129
            MEMSET(vstg[:, base_c: base_c + H * 129].rearrange(
                "p (h c) -> p h c", h=H)[:, :, 128:129].rearrange(
                    "p h c -> p (h c)"), 1.0, r=[], w=[vstgk])
        for h in range(H):
            for i in range(NTT):
                TRANS(PTB[:, 0:128],
                      vsrc[:, h * NT + i * 128: h * NT + (i + 1) * 128],
                      V["IDB"][:], r=[vsrck, "IDB"], w=["PTB"])
                CPY(vstg[:, i * H * 129 + h * 129: i * H * 129 + h * 129 + 128],
                    PTB[:, 0:128], r=["PTB"], w=[vstgk], eng="act")

    STAGE = cfg.get("stage", 4)

    if STAGE >= 1:
      emit_ln(V["N"], "N")
      DMA("sp", GA_in[0:D, :].rearrange("(c p) t -> p c t", p=128),
          V["N"].rearrange("p (c t) -> p c t", c=DC), "gai", r=["N"], w=["GA_in"])
      emit_vstage(V["N"], "N", V["VSTG"], "VSTG")
      DMA("sp", GVA_in.rearrange("(i p) c -> p i c", p=128),
          V["VSTG"].rearrange("p (i c) -> p i c", i=NTT), "gvi",
          r=["VSTG"], w=["GVA_in"])
      for h in range(H):
          ACTV(V["SQ0"][:, 0:512], V["N"][:, h * NT: h * NT + 512], AF.Square,
               r=["N"], w=["SQ0"])
          MM(PD[0:1, 0:512], V["ONESC"][:], V["SQ0"][:], True, True,
             r=["SQ0", "ONESC"], w=["PD"])
          if NH == 2:
              ACTV(V["SQ1"][:, 0:512], V["N"][:, h * NT + 512: h * NT + 1024],
                   AF.Square, r=["N"], w=["SQ1"])
              MM(PC[0:1, 0:512], V["ONESC"][:], V["SQ1"][:], True, True,
                 r=["SQ1", "ONESC"], w=["PC0"])
          for qc in range(2):
              hh = (qc * CH) // 512
              c0 = qc * CH - hh * 512
              src, srck = (PD[0:1, c0:c0 + CH], "PD") if hh == 0 else \
                  (PC[0:1, c0:c0 + CH], "PC0")
              RED(V["ROWA"][0:1, 0:1], src, ALU.max, r=[srck], w=["ROWA"])
              ACTV(V["ROWA"][0:1, 0:1], V["ROWA"][0:1, 0:1], AF.Sqrt,
                   r=["ROWA"], w=["ROWA"])
              TS(V["ROWD"][0:1, qc * 16 + h: qc * 16 + h + 1],
                 V["ROWA"][0:1, 0:1], float(np.sqrt(D)), ALU.mult,
                 r=["ROWA"], w=["ROWD"])
      MEMSET(V["CSTG"][:], 0.0, r=[], w=["CSTG"])
      ACTV(V["CSTG"][0:1, 0:32], V["ROWD"][0:1, 0:32], AF.Copy,
           r=["ROWD"], w=["CSTG"])
      DMA("sp", GA_in[D:D + 1, :], V["CSTG"][:], "cai",
          r=["CSTG"], w=["GA_in"])
      COLL(GA_in, GA_out, "cc_ga", r=["GA_in"], w=["GA_out"])
      COLL(GVA_in, GVA_out, "cc_gva", r=["GVA_in"], w=["GVA_out"])

      P.barrier()

      def switched_heads_bodies(dst, gout, rowblk, rowoff=0):
          bodies = []
          for me in range(NCORE):
              pairs = []
              for s in range(NCORE):
                  for hl in range(2):
                      h = 2 * me + hl
                      for qc in range(2):
                          j = s if qc == 0 else NCH - 1 - s
                          src = gout[s * rowblk + rowoff + h * 128:
                                     s * rowblk + rowoff + (h + 1) * 128,
                                     qc * CH:(qc + 1) * CH]
                          d = dst[:, hl * T + j * CH: hl * T + (j + 1) * CH]
                          pairs.append((d, src))
              bodies.append(pairs)
          return bodies

      def switched_V_bodies(dst, gout):
          bodies = []
          for me in range(NCORE):
              pairs = []
              for s in range(NCORE):
                  for qc in range(2):
                      j = s if qc == 0 else NCH - 1 - s
                      src = gout[s * NT + qc * CH: s * NT + (qc + 1) * CH,
                                 me * 258:(me + 1) * 258].rearrange(
                                     "(k p) c -> p k c", p=128)
                      d = dst[:, j * KS * 258:(j + 1) * KS * 258].rearrange(
                          "p (k c) -> p k c", k=KS)
                      pairs.append((d, src))
              bodies.append(pairs)
          return bodies

      def switched_C_bodies(gout, rowblk, rowoff, dsttile):
          bodies = []
          for me in range(NCORE):
              rows = gout.rearrange("(r x) t -> r x t", x=rowblk)[:, rowoff, 0:32]
              src = rows.rearrange("r (q h) -> q r h", q=2)[
                  :, :, 2 * me: 2 * me + 2].unsqueeze(0)
              dstc = dsttile[0:1, 0:32].rearrange(
                  "a (q r h) -> a q r h", q=2, r=NCORE)
              bodies.append([(dstc, src)])
          return bodies

      def switched_KM_bodies(gout, rowblk, rowoff):
          bodies = []
          for me in range(NCORE):
              rows = gout.rearrange("(r x) t -> r x t", x=rowblk)[
                  :, rowoff, 2 * me: 2 * me + 2].unsqueeze(0)
              dstc = V["KMROW"][0:1, 0:16].rearrange("a (r h) -> a r h", r=NCORE)
              bodies.append([(dstc, rows)])
          return bodies

      def merge_bodies(*lists):
          out = []
          for me in range(NCORE):
              pairs = []
              for L in lists:
                  pairs.extend(L[me])
              out.append(pairs)
          return out

      def bias_prep(attn):
          scale = float(192 ** -0.5) if attn else float(128 ** -0.5)
          ACTV(V["CROWF"][:], V["CROW"][0:1, 0:32], AF.Copy,
               r=["CROW"], w=["CROWF"])
          for hl in range(2):
              for j in range(NCH):
                  r_, qc = gloc(j)
                  src = V["CROWF"][0:1, 16 * qc + 2 * r_ + hl:
                                   16 * qc + 2 * r_ + hl + 1]
                  MM(PD[:, 0:1], V["ONESR"][:], src, True, True,
                     r=["CROWF", "ONESR"], w=["PD"])
                  col = attn * 32 + hl * 16 + cidx(j)
                  ACTV(V["BIAS"][:, col:col + 1], PD[:, 0:1], AF.Copy,
                       scale=-scale, r=["PD"], w=["BIAS"])

      SWDMA(merge_bodies(switched_heads_bodies(V["KQA"], GA_out, D + 1),
                         switched_C_bodies(GA_out, D + 1, D, V["CROW"])),
            "swa", r=["GA_out"], w=["KQA", "CROW"])
      SWDMA(switched_V_bodies(V["VAA"], GVA_out), "swav",
            r=["GVA_out"], w=["VAA"])
      bias_prep(0)

      def bias_prep_b():
          scale = float(192 ** -0.5)
          ACTV(V["CROWF"][:], V["CROW"][0:1, 0:32], AF.Copy,
               r=["CROW"], w=["CROWF"])
          ACTV(V["KMX"][0:1, 0:16], V["KMROW"][0:1, 0:16], AF.Copy,
               r=["KMROW"], w=["KMX"])
          for hl in range(2):
              RED(V["KH"][0:1, hl:hl + 1],
                  V["KMX"].rearrange("a (r h) -> a h r", h=2)[0:1, hl, 0:NCORE],
                  ALU.max, r=["KMX"], w=["KH"])
          for hl in range(2):
              for j in range(NCH):
                  r_, qc = gloc(j)
                  TT(V["ROWA"][0:1, 0:1],
                     V["CROWF"][0:1, 16 * qc + 2 * r_ + hl:
                                16 * qc + 2 * r_ + hl + 1],
                     V["KH"][0:1, hl:hl + 1], ALU.mult,
                     r=["CROWF", "KH"], w=["ROWA"])
                  ACTV(V["ROWA"][0:1, 0:1], V["ROWA"][0:1, 0:1], AF.Sqrt,
                       r=["ROWA"], w=["ROWA"])
                  MM(PD[:, 0:1], V["ONESR"][:], V["ROWA"][0:1, 0:1], True, True,
                     r=["ROWA", "ONESR"], w=["PD"])
                  col = 32 + hl * 16 + cidx(j)
                  ACTV(V["BIAS"][:, col:col + 1], PD[:, 0:1], AF.Copy,
                       scale=-scale, r=["PD"], w=["BIAS"])

      def emit_attn(b):
          KQ = V["KNB"] if b else V["KQA"]
          QN = V["QNB"] if b else V["KQA"]
          VA = V["VBB"] if b else V["VAA"]
          kqk = "KNB" if b else "KQA"
          qnk = "QNB" if b else "KQA"
          vak = "VBB" if b else "VAA"
          GY = GYB_in if b else GYA_in
          gyk = "GYB_in" if b else "GYA_in"
          scale = float(192 ** -0.5) if b else float(128 ** -0.5)
          psb = [V["PSB2"], V["PSB3"]] if b else [V["PSB0b"], V["PSB1b"]]
          psbk = ["PSB2", "PSB3"] if b else ["PSB0b", "PSB1b"]
          prc = [0]
          for hl in range(2):
              for j in range(NCH):
                  def emit_pv(st):
                      (PSBt_, psbkt_, pr_, kc_, nsub_) = st
                      for qs in range(KS):
                          for s2 in range(nsub_):
                              ks = 2 * pr_ + s2
                              if kc_ == j and ks > qs:
                                  continue
                              first = (kc_ == 0 and ks == 0)
                              last = (kc_ == j and ks == min(qs, KS - 1))
                              ao = (qs % 2) * 132 + (qs // 2) * 512
                              ack = "PC" + str(qs // 2)
                              lt = PSBt_[:, s2 * CH + qs * 128:
                                         s2 * CH + qs * 128 + 128]
                              rv = VA[:, (kc_ * KS + ks) * 258 + hl * 129:
                                      (kc_ * KS + ks) * 258 + hl * 129 + 129]
                              MM(PC[:, ao: ao + 129], lt, rv, first, last,
                                 r=[psbkt_, vak], w=[ack])
                  pending = None
                  for kc in range(j + 1):
                      for pr in range((KS + 1) // 2):
                          par = prc[0] % 2
                          prc[0] += 1
                          SP_ = PA if par == 0 else PB
                          spk = "PA" if par == 0 else "PB"
                          PSBt, psbkt = psb[par], psbk[par]
                          nsub = min(2, KS - 2 * pr)
                          for s2 in range(nsub):
                              ks = 2 * pr + s2
                              lt = KQ[:, hl * T + kc * CH + ks * 128:
                                      hl * T + kc * CH + ks * 128 + 128]
                              rh_ = QN[:, hl * T + j * CH: hl * T + (j + 1) * CH]
                              bnk = spk + str(s2)
                              MM(SP_[:, s2 * CH:(s2 + 1) * CH], lt, rh_,
                                 True, not b, r=[kqk, qnk], w=[bnk])
                              if b:
                                  ltr = V["KRB"][0:64, kc * CH + ks * 128:
                                                 kc * CH + ks * 128 + 128]
                                  rhr = V["QRB"][0:64, hl * T + j * CH:
                                                 hl * T + (j + 1) * CH]
                                  MM(SP_[:, s2 * CH:(s2 + 1) * CH], ltr, rhr,
                                     False, True, r=["KRB", "QRB"], w=[bnk])
                              if kc == j:
                                  sl = SP_[:, s2 * CH + ks * 128:
                                           s2 * CH + ks * 128 + 128]
                                  TT(sl, sl, V["TRI"][:], ALU.add,
                                     r=[bnk, "TRI"], w=[bnk])
                          bcol = (32 if b else 0) + hl * 16 + cidx(j)
                          rk = [spk + "0"] + ([spk + "1"] if nsub == 2 else [])
                          ACTV(PSBt[:, 0:nsub * CH], SP_[:, 0:nsub * CH], AF.Exp,
                               bias=V["BIAS"][:, bcol:bcol + 1], scale=scale,
                               r=rk + ["BIAS"], w=[psbkt])
                          if pending is not None:
                              emit_pv(pending)
                          pending = (PSBt, psbkt, pr, kc, nsub)
                  if pending is not None:
                      emit_pv(pending)
                  for qs in range(KS):
                      ao = (qs % 2) * 132 + (qs // 2) * 512
                      ack = "PC" + str(qs // 2)
                      RECIP(V["RL"][:], PC[:, ao + 128: ao + 129], r=[ack], w=["RL"])
                      op("dve", (lambda ao_: lambda e: e.tensor_scalar_mul(
                          V["YTA"][:], PC[:, ao_: ao_ + 128], V["RL"][:]))(ao),
                         r=[ack, "RL"], w=["YTA"])
                      ACTV(V["YTB"][:], V["YTA"][:], AF.Copy, r=["YTA"], w=["YTB"])
                      row0 = j * CH + qs * 128
                      DMA("pool", GY[row0:row0 + 128, hl * 128:(hl + 1) * 128],
                          V["YTB"][:], "gy" + ("b" if b else "a"),
                          r=["YTB"], w=[gyk])

      emit_attn(0)
      COLL(GYA_in, GYA_out, "cc_gya", r=["GYA_in"], w=["GYA_out"])

      def yret_bodies(gyout, yin):
          bodies = []
          for me in range(NCORE):
              pairs = []
              for s in range(NCORE):
                  for qc in range(2):
                      ch = me if qc == 0 else NCH - 1 - me
                      src = gyout[s * T + ch * CH: s * T + (ch + 1) * CH, :]\
                          .rearrange("(i p) c -> p i c", p=128)
                      d = yin[:, (s * NTT + qc * KS) * 256:
                              (s * NTT + (qc + 1) * KS) * 256].rearrange(
                          "p (i c) -> p i c", i=KS)
                      pairs.append((d, src))
              bodies.append(pairs)
          return bodies

      SWDMA(yret_bodies(GYA_out, V["YIN"]), "swya", r=["GYA_out"], w=["YIN"])
      for s in range(NCORE):
          for hl in range(2):
              dc = 2 * s + hl
              xw = V["XW0"] if dc % 2 == 0 else V["XW1"]
              xk = "XW0" if dc % 2 == 0 else "XW1"
              DMA("sp", xw[:], XST[:, dc * NT:(dc + 1) * NT], "dm" + xk,
                  r=["XST"], w=[xk])
              for i in range(NTT):
                  c0 = (s * NTT + i) * 256 + hl * 128
                  TRANS(PTB[:, 0:128], V["YIN"][:, c0: c0 + 128],
                        V["IDB"][:], r=["YIN", "IDB"], w=["PTB"])
                  TT(xw[:, i * 128:(i + 1) * 128], xw[:, i * 128:(i + 1) * 128],
                     PTB[:, 0:128], ALU.add, r=[xk, "PTB"], w=[xk])
              DMA("sp", XST[:, dc * NT:(dc + 1) * NT], xw[:], "st" + xk,
                  r=[xk], w=["XST"])

    P.barrier()

    def emit_ffn(first):
        emit_ln(V["NF"], "NF")
        for half in range(NH):
            hs = half * 512
            for jc in range(JC):
                upin = V["UPIN0"] if jc % 2 == 0 else V["UPIN1"]
                upk = "UPIN0" if jc % 2 == 0 else "UPIN1"
                uws = V["UWSB0"] if jc % 2 == 0 else V["UWSB1"]
                uwk = "UWSB0" if jc % 2 == 0 else "UWSB1"
                if first and half == 0:
                    DMA("sp", upin[:], UPB[jc], "up" + upk, r=[], w=[upk])
                    SIGN(uws[:], upin[:], r=[upk], w=[uwk])
                    DMA("pool", UWS[jc], uws[:], "uwsw", r=[uwk], w=["UWS"])
                else:
                    DMA("sp", uws[:], UWS[jc], "up" + uwk, r=["UWS"], w=[uwk])
                PU, puk = (PD, "PD") if jc % 2 == 0 else (PA[:, 0:512], "PA0")
                for kc in range(DC):
                    MM(PU[:, 0:512] if jc % 2 == 0 else PU,
                       uws[:, kc * 128:(kc + 1) * 128],
                       V["NF"][:, kc * NT + hs: kc * NT + hs + 512],
                       kc == 0, kc == DC - 1, r=[uwk, "NF"], w=[puk])
                op("act", (lambda jc_, PU_: lambda e: e.activation(
                    V["HSB"][:, jc_ * 512:(jc_ + 1) * 512], PU_, AF.Relu,
                    scale=V["USC"][:, jc_:jc_ + 1]))(
                        jc, PU[:, 0:512] if jc % 2 == 0 else PU),
                   r=[puk, "USC"], w=["HSB"])
            for dg in range(8):
                for dci in range(2):
                    dc = dg * 2 + dci
                    dws = V["DWSB0"] if dci == 0 else V["DWSB1"]
                    dwk = "DWSB0" if dci == 0 else "DWSB1"
                    if first and half == 0:
                        for hh2 in range(2):
                            DMA("sp", V["DNIN"][:],
                                DNB[dc][:, hh2 * (HID // 2):(hh2 + 1) * (HID // 2)],
                                "dnin", r=[], w=["DNIN", "UPIN0", "UPIN1"])
                            SIGN(dws[:, hh2 * (HID // 2):(hh2 + 1) * (HID // 2)],
                                 V["DNIN"][:], r=["DNIN"], w=[dwk])
                        DMA("pool", DWS[dc], dws[:], "dwsw", r=[dwk], w=["DWS"])
                    else:
                        DMA("sp", dws[:], DWS[dc], "dn" + dwk, r=["DWS"], w=[dwk])
                    pbk = "PB" + str(dci)
                    for jc in range(JC):
                        MM(PB[:, dci * 512: dci * 512 + 512],
                           dws[:, jc * 128:(jc + 1) * 128],
                           V["HSB"][:, jc * 512:(jc + 1) * 512],
                           jc == 0, jc == JC - 1, r=[dwk, "HSB"], w=[pbk])
                for dci in range(2):
                    dc = dg * 2 + dci
                    pbk = "PB" + str(dci)
                    xw = V["XW0"] if dci == 0 else V["XW1"]
                    xk = "XW0" if dci == 0 else "XW1"
                    DMA("sp", xw[:, 0:512],
                        XST[:, dc * NT + hs: dc * NT + hs + 512],
                        "dm" + xk, r=["XST"], w=[xk])
                    op("dve", (lambda dc_, dci_, xw_: lambda e:
                        e.scalar_tensor_tensor(
                            xw_[:, 0:512], PB[:, dci_ * 512: dci_ * 512 + 512],
                            V["CSC"][:, dc_:dc_ + 1], xw_[:, 0:512],
                            ALU.mult, ALU.add))(dc, dci, xw),
                       r=[pbk, "CSC", xk], w=[xk])
                    DMA("sp", XST[:, dc * NT + hs: dc * NT + hs + 512],
                        xw[:, 0:512], "st" + xk, r=[xk], w=["XST"])

    if STAGE >= 2:
        emit_ffn(True)
    P.barrier()

    if STAGE >= 2.5:
      emit_ln(V["NP"], "NP")

      def proj_mm(blk_dram, m, out_psum, pk, nsrc, nsrck, slot_extra=""):
          wb = V["WBLK0"] if m % 2 == 0 else V["WBLK1"]
          wk = "WBLK0" if m % 2 == 0 else "WBLK1"
          wbb = V["WBB0"] if m % 2 == 0 else V["WBB1"]
          wbbk = "WBB0" if m % 2 == 0 else "WBB1"
          nb = blk_dram.shape[1]
          DMA("sp", wb[:, 0:nb], blk_dram, "pr" + wk + slot_extra,
              r=[], w=[wk])
          ACTV(wbb[:, 0:nb], wb[:, 0:nb], AF.Copy, r=[wk], w=[wbbk])
          kcn = nb // 128
          for hh in range(NH):
              for kc in range(kcn):
                  MM(out_psum[:, hh * 512: hh * 512 + 512],
                     wbb[:, kc * 128:(kc + 1) * 128],
                     nsrc[:, kc * NT + hh * 512: kc * NT + hh * 512 + 512],
                     kc == 0, kc == kcn - 1, r=[wbbk, nsrck], w=[pk + str(hh)])

      for m in range(24):
          PQ, pk = (PA, "PA") if m % 2 == 0 else (PB, "PB")
          proj_mm(WQ[m], m, PQ, pk, V["NP"], "NP")
          pks = [pk + str(hh) for hh in range(NH)]
          if m < 16:
              ACTV(V["STG"][:], PQ[:, 0:NT], AF.Copy, r=pks, w=["STG"])
              DMA("sp", GB_in[m * 128:(m + 1) * 128, :], V["STG"][:],
                  "gqn", r=["STG"], w=["GB_in"])
              for hh in range(NH):
                  sq = V["SQ0"] if hh == 0 else V["SQ1"]
                  sqk = "SQ0" if hh == 0 else "SQ1"
                  ACTV(sq[:], PQ[:, hh * 512: hh * 512 + 512], AF.Square,
                       r=[pk + str(hh)], w=[sqk])
                  rw, rwk = row(hh)
                  MM(rw, V["ONESC"][:], sq[:], True, True, r=[sqk, "ONESC"], w=[rwk])
              for qc in range(2):
                  hh = (qc * CH) // 512
                  c0 = qc * CH - hh * 512
                  rw, rwk = row(hh)
                  RED(V["QN2"][0:1, qc * 16 + m: qc * 16 + m + 1],
                      rw[0:1, c0:c0 + CH], ALU.max, r=[rwk], w=["QN2"])
          else:
              h1, h2 = 2 * (m - 16), 2 * (m - 16) + 1
              ACTV(V["RQ"][0:64, h1 * NT:(h1 + 1) * NT], PQ[0:64, 0:NT], AF.Copy,
                   r=pks, w=["RQ"])
              ACTV(V["RQ"][0:64, h2 * NT:(h2 + 1) * NT], PQ[64:128, 0:NT], AF.Copy,
                   r=pks, w=["RQ"])

      def rope_apply(src, srck, dst, dstk):
          TS(V["RH"][0:32, :], src[32:64, :], -1.0, ALU.mult, r=[srck], w=["RH"])
          CPY(V["RH"][32:64, :], src[0:32, :], r=[srck], w=["RH"])
          TT(V["RT1"][0:64, :], src, V["COS"][0:64, :], ALU.mult,
             r=[srck, "COS"], w=["RT1"])
          TT(V["RT2"][0:64, :], V["RH"][0:64, :], V["SIN"][0:64, :], ALU.mult,
             r=["RH", "SIN"], w=["RT2"])
          TT(dst, V["RT1"][0:64, :], V["RT2"][0:64, :], ALU.add,
             r=["RT1", "RT2"], w=[dstk])

      for h in range(H):
          rope_apply(V["RQ"][0:64, h * NT:(h + 1) * NT], "RQ",
                     V["RQ"][0:64, h * NT:(h + 1) * NT], "RQ")
          ACTV(V["RT1"][0:64, :], V["RQ"][0:64, h * NT:(h + 1) * NT], AF.Square,
               r=["RQ"], w=["RT1"])
          for hh in range(NH):
              rw, rwk = row(hh)
              MM(rw, V["ONESC"][0:64, :], V["RT1"][0:64, hh * 512: hh * 512 + 512],
                 True, True, r=["RT1", "ONESC"], w=[rwk])
          for qc in range(2):
              hh = (qc * CH) // 512
              c0 = qc * CH - hh * 512
              rw, rwk = row(hh)
              RED(V["ROWA"][0:1, 0:1], rw[0:1, c0:c0 + CH], ALU.max,
                  r=[rwk], w=["ROWA"])
              TT(V["QN2"][0:1, qc * 16 + h: qc * 16 + h + 1],
                 V["QN2"][0:1, qc * 16 + h: qc * 16 + h + 1],
                 V["ROWA"][0:1, 0:1], ALU.add, r=["QN2", "ROWA"], w=["QN2"])
      DMA("sp", GB_in[2 * D: 2 * D + H * 64, :].rearrange(
              "(h p) t -> p h t", p=64),
          V["RQ"][0:64, :].rearrange("p (h t) -> p h t", h=H), "gqr",
          r=["RQ"], w=["GB_in"])

      for m in range(4):
          PQ, pk = (PA, "PA") if m % 2 == 0 else (PB, "PB")
          proj_mm(WKA[m], m, PQ, pk, V["NP"], "NP")
          pks = [pk + str(hh) for hh in range(NH)]
          ACTV(V["LATN"][:, m * NT:(m + 1) * NT], PQ[:, 0:NT], AF.Copy,
               r=pks, w=["LATN"])
          for hh in range(NH):
              sq = V["SQ0"] if hh == 0 else V["SQ1"]
              sqk = "SQ0" if hh == 0 else "SQ1"
              ACTV(sq[:], PQ[:, hh * 512: hh * 512 + 512], AF.Square,
                   r=[pk + str(hh)], w=[sqk])
              rw, rwk = row(hh)
              MM(rw, V["ONESC"][:], sq[:], m == 0, m == 3, r=[sqk, "ONESC"], w=[rwk])
      for hh in range(NH):
          rw, rwk = row(hh)
          dstrow = V["ROWA"] if hh == 0 else V["ROWB"]
          dk = "ROWA" if hh == 0 else "ROWB"
          TS(dstrow[:], rw, 1.0 / 512.0, ALU.mult, r=[rwk], w=[dk])
          ACTV(dstrow[:], dstrow[:], AF.Sqrt, bias=V["EPSB"][:],
               r=[dk, "EPSB"], w=[dk])
          RECIP(dstrow[:], dstrow[:], r=[dk], w=[dk])
      for hh in range(NH):
          dstrow = V["ROWA"] if hh == 0 else V["ROWB"]
          dk = "ROWA" if hh == 0 else "ROWB"
          MM(PC[:, hh * 512: hh * 512 + 512], V["ONESR"][:], dstrow[:], True, True,
             r=[dk, "ONESR"], w=["PC" + str(hh)])
      for m in range(4):
          for hh in range(NH):
              STT(V["LATN"][:, m * NT + hh * 512: m * NT + hh * 512 + 512],
                  V["LATN"][:, m * NT + hh * 512: m * NT + hh * 512 + 512],
                  V["KNW"][:, m:m + 1], PC[:, hh * 512: hh * 512 + 512],
                  ALU.mult, ALU.mult, r=["LATN", "KNW", "PC" + str(hh)], w=["LATN"])

      for hh in range(NH):
          for kc in range(DC):
              MM(PA[0:64, hh * 512: hh * 512 + 512],
                 V["WKARb"][:, kc * 64:(kc + 1) * 64],
                 V["NP"][:, kc * NT + hh * 512: kc * NT + hh * 512 + 512],
                 kc == 0, kc == DC - 1, r=["WKARb", "NP"], w=["PA" + str(hh)])
      ACTV(V["KRO"][0:64, :], PA[0:64, 0:NT], AF.Copy,
           r=["PA" + str(hh) for hh in range(NH)], w=["KRO"])
      rope_apply(V["KRO"][0:64, :], "KRO", V["KRO"][0:64, :], "KRO")
      DMA("sp", GB_in[2 * D + H * 64: 2 * D + H * 64 + 64, :],
          V["KRO"][0:64, :], "gkr", r=["KRO"], w=["GB_in"])
      ACTV(V["RT1"][0:64, :], V["KRO"][0:64, :], AF.Square, r=["KRO"], w=["RT1"])
      for hh in range(NH):
          rw, rwk = row(hh)
          MM(rw, V["ONESC"][0:64, :], V["RT1"][0:64, hh * 512: hh * 512 + 512],
             True, True, r=["RT1", "ONESC"], w=[rwk])
          dst = V["ROWC"] if hh == 0 else V["ROWD"]
          dk = "ROWC" if hh == 0 else "ROWD"
          CPY(dst[:], rw, r=[rwk], w=[dk])

      emit_vstage_pending = []
      for m in range(32):
          PQ, pk = (PA, "PA") if m % 2 == 0 else (PB, "PB")
          proj_mm(WKB[m], m, PQ, pk, V["LATN"], "LATN")
          pks = [pk + str(hh) for hh in range(NH)]
          if m < 16:
              ACTV(V["STG"][:], PQ[:, 0:NT], AF.Copy, r=pks, w=["STG"])
              DMA("sp", GB_in[D + m * 128: D + (m + 1) * 128, :], V["STG"][:],
                  "gkn", r=["STG"], w=["GB_in"])
              first_red = True
              for hh in range(NH):
                  sq = V["SQ0"] if hh == 0 else V["SQ1"]
                  sqk = "SQ0" if hh == 0 else "SQ1"
                  ACTV(sq[:], PQ[:, hh * 512: hh * 512 + 512], AF.Square,
                       r=[pk + str(hh)], w=[sqk])
                  rw, rwk = row(hh)
                  MM(rw, V["ONESC"][:], sq[:], True, True, r=[sqk, "ONESC"], w=[rwk])
                  ro = V["ROWC"] if hh == 0 else V["ROWD"]
                  rok = "ROWC" if hh == 0 else "ROWD"
                  sq2 = V["SQ0"] if hh == 0 else V["SQ1"]
                  TT(sq2[0:1, 0:512], rw, ro[:], ALU.add, r=[rwk, rok], w=[sqk])
                  RED(V["ROWB"][0:1, hh:hh + 1], sq2[0:1, 0:512], ALU.max,
                      r=[sqk], w=["ROWB"])
              if NH == 2:
                  TT(V["ROWB"][0:1, 0:1], V["ROWB"][0:1, 0:1], V["ROWB"][0:1, 1:2],
                     ALU.max, r=["ROWB"], w=["ROWB"])
              CPY(V["KMN"][0:1, m:m + 1], V["ROWB"][0:1, 0:1], r=["ROWB"], w=["KMN"])
          else:
              h = m - 16
              ACTV(V["STG"][:], PQ[:, 0:NT], AF.Copy, r=pks, w=["STG"])
              for i in range(NTT):
                  TRANS(PTB[:, 0:128], V["STG"][:, i * 128:(i + 1) * 128],
                        V["IDB"][:], r=["STG", "IDB"], w=["PTB"])
                  CPY(V["VSTG2"][:, i * H * 129 + h * 129:
                                 i * H * 129 + h * 129 + 128],
                      PTB[:, 0:128], r=["PTB"], w=["VSTG2"], eng="act")
      for i in range(NTT):
          MEMSET(V["VSTG2"][:, i * H * 129: (i + 1) * H * 129].rearrange(
              "p (h c) -> p h c", h=H)[:, :, 128:129].rearrange("p h c -> p (h c)"),
              1.0, r=[], w=["VSTG2"])
      DMA("sp", GVB_in.rearrange("(i p) c -> p i c", p=128),
          V["VSTG2"].rearrange("p (i c) -> p i c", i=NTT), "gvb",
          r=["VSTG2"], w=["GVB_in"])
      MEMSET(V["CSTG"][:], 0.0, r=[], w=["CSTG"])
      ACTV(V["CSTG"][0:1, 0:16], V["KMN"][0:1, 0:16], AF.Copy,
           r=["KMN"], w=["CSTG"])
      DMA("sp", GB_in[NRB - 1: NRB, :], V["CSTG"][:], "gkm",
          r=["CSTG"], w=["GB_in"])
      MEMSET(V["CSTG2"][:], 0.0, r=[], w=["CSTG2"])
      ACTV(V["CSTG2"][0:1, 0:32], V["QN2"][0:1, 0:32], AF.Copy,
           r=["QN2"], w=["CSTG2"])
      DMA("sp", GB_in[NRB - 2: NRB - 1, :], V["CSTG2"][:], "gqn2",
          r=["CSTG2"], w=["GB_in"])
      COLL(GB_in, GB_out, "cc_gb", r=["GB_in"], w=["GB_out"])
      COLL(GVB_in, GVB_out, "cc_gvb", r=["GVB_in"], w=["GVB_out"])

      P.barrier()

      pass
    if STAGE >= 2.8:
      bodies_b = [switched_heads_bodies(V["QNB"], GB_out, NRB, 0),
                  switched_heads_bodies(V["KNB"], GB_out, NRB, D)]
      qr_bodies = []
      for me in range(NCORE):
          pairs = []
          for s in range(NCORE):
              for hl in range(2):
                  for qc in range(2):
                      j = s if qc == 0 else NCH - 1 - s
                      r0 = s * NRB + 2 * D + (2 * me + hl) * 64
                      src = GB_out[r0: r0 + 64, qc * CH:(qc + 1) * CH]
                      d = V["QRB"][0:64, hl * T + j * CH: hl * T + (j + 1) * CH]
                      pairs.append((d, src))
          qr_bodies.append(pairs)
      bodies_b.append(qr_bodies)
      bodies_b.append(switched_C_bodies(GB_out, NRB, NRB - 2, V["CROW"]))
      bodies_b.append(switched_KM_bodies(GB_out, NRB, NRB - 1))
      for s in range(NCORE):
          for qc in range(2):
              j = s if qc == 0 else NCH - 1 - s
              r0 = s * NRB + 2 * D + H * 64
              DMA("pool", V["KRB"][0:64, j * CH:(j + 1) * CH],
                  GB_out[r0: r0 + 64, qc * CH:(qc + 1) * CH],
                  "krl", r=["GB_out"], w=["KRB"])
      SWDMA(merge_bodies(*bodies_b), "swb",
            r=["GB_out"], w=["KNB", "QNB", "QRB", "CROW", "KMROW"])
      SWDMA(switched_V_bodies(V["VBB"], GVB_out), "swbv",
            r=["GVB_out"], w=["VBB"])
      bias_prep_b()
      if cfg.get("dump"):
          DBG1 = nc.dram_tensor("dbg_bias", [128, 64], F32, kind="ExternalOutput").ap()
          DBG2 = nc.dram_tensor("dbg_crow", [1, 32], F32, kind="ExternalOutput").ap()
          DBG3 = nc.dram_tensor("dbg_qn2", [1, 32], F32, kind="ExternalOutput").ap()
          DBG4 = nc.dram_tensor("dbg_kmx", [1, 16], F32, kind="ExternalOutput").ap()
          DBG5 = nc.dram_tensor("dbg_kqs", [128, 2048], BF16, kind="ExternalOutput").ap()
          DBG6 = nc.dram_tensor("dbg_qns", [128, 2048], BF16, kind="ExternalOutput").ap()
          DBG7 = nc.dram_tensor("dbg_vbs", [128, 2048], BF16, kind="ExternalOutput").ap()
          DBG8 = nc.dram_tensor("dbg_qrs", [64, 2048], BF16, kind="ExternalOutput").ap()
          DBG9 = nc.dram_tensor("dbg_krs", [64, 2048], BF16, kind="ExternalOutput").ap()
          DMA("pool", DBG1, V["BIAS"][:], "dbgw", r=["BIAS"], w=["DBG"])
          DMA("pool", DBG2, V["CROW"][0:1, 0:32], "dbgw", r=["CROW"], w=["DBG"])
          DMA("pool", DBG3, V["QN2"][0:1, 0:32], "dbgw", r=["QN2"], w=["DBG"])
          DMA("pool", DBG4, V["KMX"][0:1, 0:16], "dbgw", r=["KMX"], w=["DBG"])
          DMA("pool", DBG5, V["KNB"][:, 0:2048], "dbgw", r=["KNB"], w=["DBG"])
          DMA("pool", DBG6, V["QNB"][:, 0:2048], "dbgw", r=["QNB"], w=["DBG"])
          DMA("pool", DBG7, V["VBB"][:, 0:2048], "dbgw", r=["VBB"], w=["DBG"])
          DMA("pool", DBG8, V["QRB"][0:64, 0:2048], "dbgw", r=["QRB"], w=["DBG"])
          DMA("pool", DBG9, V["KRB"][0:64, 0:2048], "dbgw", r=["KRB"], w=["DBG"])
          DBG10 = nc.dram_tensor("dbg_gkm", [NCORE, 16], F32, kind="ExternalOutput").ap()
          DBG11 = nc.dram_tensor("dbg_kmn", [1, 128], F32, kind="ExternalOutput").ap()
          DBG12 = nc.dram_tensor("dbg_rowcd", [1, 1024], F32, kind="ExternalOutput").ap()
          DMA("pool", DBG10, GKM_out, "dbgw", r=["GKM_out"], w=["DBG"])
          DMA("pool", DBG11, V["KMN"][0:1, 0:128], "dbgw", r=["KMN"], w=["DBG"])
          DMA("pool", DBG12[0:1, 0:512], V["ROWC"][0:1, 0:512], "dbgw", r=["ROWC"], w=["DBG"])
          DMA("pool", DBG12[0:1, 512:1024], V["ROWD"][0:1, 0:512], "dbgw", r=["ROWD"], w=["DBG"])
          DBG13 = nc.dram_tensor("dbg_kro", [64, NT], BF16, kind="ExternalOutput").ap()
          DBG14 = nc.dram_tensor("dbg_latn", [128, NT], BF16, kind="ExternalOutput").ap()
          DBG15 = nc.dram_tensor("dbg_np", [128, NT], BF16, kind="ExternalOutput").ap()
          DMA("pool", DBG13, V["KRO"][0:64, :], "dbgw", r=["KRO"], w=["DBG"])
          DMA("pool", DBG14, V["LATN"][:, 0:NT], "dbgw", r=["LATN"], w=["DBG"])
          DMA("pool", DBG15, V["NP"][:, 0:NT], "dbgw", r=["NP"], w=["DBG"])
          op("sp", lambda e: e.nop(), r=["DBG"], w=[])

      pass
    if STAGE >= 3:
      emit_attn(1)
      COLL(GYB_in, GYB_out, "cc_gyb", r=["GYB_in"], w=["GYB_out"])

      P.barrier()

      SWDMA(yret_bodies(GYB_out, V["YIN2"]), "swyb", r=["GYB_out"], w=["YIN2"])
      for s in range(NCORE):
          for hl in range(2):
              h = 2 * s + hl
              for i in range(NTT):
                  c0 = (s * NTT + i) * 256 + hl * 128
                  TRANS(PTB[:, 0:128], V["YIN2"][:, c0: c0 + 128],
                        V["IDB"][:], r=["YIN2", "IDB"], w=["PTB"])
                  CPY(V["YFM"][:, h * NT + i * 128: h * NT + (i + 1) * 128],
                      PTB[:, 0:128], r=["PTB"], w=["YFM"], eng="act")
      for dc in range(DC):
          wob = V["WOB0"] if dc % 2 == 0 else V["WOB1"]
          wbk = "WOB0" if dc % 2 == 0 else "WOB1"
          DMA("sp", wob[:], WOd[dc], "wo" + wbk, r=[], w=[wbk])
          ACTV(V["WOBB"][:], wob[:], AF.Copy, r=[wbk], w=["WOBB"])
          for hh in range(NH):
              for hv in range(H):
                  MM(PA[:, hh * 512: hh * 512 + 512],
                     V["WOBB"][:, hv * 128:(hv + 1) * 128],
                     V["YFM"][:, hv * NT + hh * 512: hv * NT + hh * 512 + 512],
                     hv == 0, hv == H - 1, r=["WOBB", "YFM"], w=["PA" + str(hh)])
          xw = V["XW0"] if dc % 2 == 0 else V["XW1"]
          xk = "XW0" if dc % 2 == 0 else "XW1"
          DMA("sp", xw[:], XST[:, dc * NT:(dc + 1) * NT], "dm" + xk,
              r=["XST"], w=[xk])
          op("dve", (lambda xw_: lambda e: e.tensor_add(
              xw_[:, 0:NT], xw_[:, 0:NT], PA[:, 0:NT]))(xw),
             r=[xk] + ["PA" + str(hh) for hh in range(NH)], w=[xk])
          DMA("sp", XST[:, dc * NT:(dc + 1) * NT], xw[:], "st" + xk,
              r=[xk], w=["XST"])

    P.barrier()

    if STAGE >= 4:
        emit_ffn(False)
    P.barrier()

    for dc in range(DC):
        xw = V["XW0"] if dc % 2 == 0 else V["XW1"]
        xk = "XW0" if dc % 2 == 0 else "XW1"
        DMA("sp", xw[:], XST[:, dc * NT:(dc + 1) * NT], "dm" + xk,
            r=["XST"], w=[xk])
        for i in range(NTT):
            os_ = V["OS0"] if i % 2 == 0 else V["OS1"]
            osk = "OS0" if i % 2 == 0 else "OS1"
            TRANS(PA[:, 0:128], xw[:, i * 128:(i + 1) * 128], V["IDF"][:],
                  r=[xk, "IDF"], w=["PA0"])
            CPY(os_[:], PA[:, 0:128], r=["PA0"], w=[osk])
            DMA("pool", OUT[i * 128:(i + 1) * 128, dc * 128:(dc + 1) * 128],
                os_[:], "outw", r=[osk], w=["OUTW"])
    op("sp", lambda e: e.nop(), r=["OUTW"], w=[])

    emit = P.finalize(nc, stack)

    with nc.Block() as block:
        def runner(items):
            def go(eng):
                for waits, fn, sem, inc, selfsem in items:
                    for s_, v_ in waits:
                        eng.wait_ge(s_, v_)
                    if selfsem is not None:
                        fn(eng, selfsem)
                    else:
                        inst = fn(eng)
                        if sem is not None:
                            inst.then_inc(sem, inc)
            return go
        block.tensor(runner(emit["pe"]))
        block.scalar(runner(emit["act"]))
        block.vector(runner(emit["dve"]))
        block.sync(runner(emit["sp"]))
        block.gpsimd(runner(emit["pool"]))

    stack.pop_all()
    return nc


def kernel(**inputs):
    cfg = make_cfg()
    maps = host_prep(cfg, inputs)
    nc = build(cfg)
    res = run_bass_kernel_spmd(nc, maps, core_ids=list(range(NCORE)))
    T, D, CH = cfg["T"], cfg["D"], cfg["CH"]
    out = np.zeros((T, D), np.float32)
    for me in range(NCORE):
        y = res.results[me]["out_own"]
        out[me * CH:(me + 1) * CH] = y[:CH]
        out[(NCH - 1 - me) * CH:(NCH - me) * CH] = y[CH:]
    return out.reshape(1, T, D)


# version 05
# speedup vs baseline: 1.0109x; 1.0109x over previous
import numpy as np
import ml_dtypes
from contextlib import ExitStack

import concourse.bass as bass
import concourse.mybir as mybir
from concourse.bass_utils import run_bass_kernel_spmd

F32 = mybir.dt.float32
BF16 = mybir.dt.bfloat16
AF = mybir.ActivationFunctionType
ALU = mybir.AluOpType
AX = mybir.AxisListType

NEGBIG = -1.0e30
NCORE = 8
NCH = 16


def make_cfg(T=8192, HID=8192):
    c = {}
    c["T"], c["D"], c["H"] = T, 2048, 16
    c["CH"] = T // NCH
    c["KS"] = c["CH"] // 128
    c["NT"] = 2 * c["CH"]
    c["NH"] = c["NT"] // 512
    c["DC"], c["HID"], c["JC"] = 16, HID, HID // 128
    c["LNEPS"], c["RMSEPS"] = 1e-5, 1e-6
    c["stage"] = 4
    assert c["NT"] % 512 == 0 and c["CH"] % 128 == 0
    return c


def gloc(j):
    return (j, 0) if j < NCORE else (NCH - 1 - j, 1)


def cidx(j):
    return j if j < NCORE else 8 + (NCH - 1 - j)



def prep_blocks(W):
    K, M = W.shape
    Kc, Mc = K // 128, M // 128
    return np.ascontiguousarray(
        W.reshape(Kc, 128, Mc, 128).transpose(2, 1, 0, 3).reshape(Mc, 128, Kc * 128))


def host_prep(cfg, inputs):
    T, D, H, CH = cfg["T"], cfg["D"], cfg["H"], cfg["CH"]
    JC, DC = cfg["JC"], cfg["DC"]
    x = np.asarray(inputs["L0_attn_a_x"], np.float32).reshape(T, D)
    xT = np.ascontiguousarray(x.T)

    wq = np.asarray(inputs["L0_attn_b_wq"], np.float32)
    nope_cols = np.concatenate([wq[:, 192 * h:192 * h + 128] for h in range(H)], 1)
    rope_cols = np.concatenate([wq[:, 192 * h + 128:192 * h + 192] for h in range(H)], 1)
    wq_blk = prep_blocks(np.concatenate([nope_cols, rope_cols], 1))

    wkva = np.asarray(inputs["L0_attn_b_wkv_a"], np.float32)
    wkva_main = prep_blocks(wkva[:, :512])
    wkva_rope = np.ascontiguousarray(
        wkva[:, 512:].reshape(16, 128, 64).transpose(1, 0, 2).reshape(128, 1024))

    wkvb = np.asarray(inputs["L0_attn_b_wkv_b"], np.float32)
    nope_c = np.concatenate([wkvb[:, 256 * h:256 * h + 128] for h in range(H)], 1)
    v_c = np.concatenate([wkvb[:, 256 * h + 128:256 * h + 256] for h in range(H)], 1)
    wkvb_blk = prep_blocks(np.concatenate([nope_c, v_c], 1))

    wo_blk = prep_blocks(np.asarray(inputs["L0_attn_b_wo"], np.float32))

    up_blk = prep_blocks(np.ascontiguousarray(
        np.asarray(inputs["L0_ffn_up_weight"], np.float32).T))
    down_blk = prep_blocks(np.ascontiguousarray(
        np.asarray(inputs["L0_ffn_down_weight"], np.float32).T))

    usc = np.ascontiguousarray(
        np.asarray(inputs["L0_ffn_up_scale"], np.float32).reshape(JC, 128).T)
    dsc = np.ascontiguousarray(
        np.asarray(inputs["L0_ffn_down_scale"], np.float32).reshape(DC, 128).T)
    knw = np.ascontiguousarray(
        np.asarray(inputs["L0_attn_b_kv_norm_w"], np.float32).reshape(4, 128).T)
    osc = np.asarray(inputs["L0_ffn_output_scale"], np.float32).reshape(1, 1)

    ones_col = np.ones((128, 1), np.float32)
    ones_row = np.ones((1, 128), np.float32)
    idf = np.eye(128, dtype=np.float32)
    idb = np.eye(128, dtype=ml_dtypes.bfloat16)
    tri = np.where(np.arange(128)[None, :] >= np.arange(128)[:, None],
                   0.0, NEGBIG).astype(np.float32)

    inv = 1.0 / (10000.0 ** (np.arange(0, 64, 2, dtype=np.float32) / 64.0))
    maps = []
    for me in range(NCORE):
        c0, c1 = me, NCH - 1 - me
        cols = np.concatenate([np.arange(c0 * CH, (c0 + 1) * CH),
                               np.arange(c1 * CH, (c1 + 1) * CH)])
        f = np.outer(inv, cols.astype(np.float32))
        emb = np.concatenate([f, f], 0)
        maps.append({
            "x0T_own": np.ascontiguousarray(xT[:, cols]),
            "wq_blk": wq_blk, "wkva_main": wkva_main, "wkva_rope": wkva_rope,
            "wkvb_blk": wkvb_blk, "wo_blk": wo_blk,
            "up_blk": up_blk, "down_blk": down_blk,
            "usc": usc, "dsc": dsc, "knw": knw, "osc": osc,
            "ones_col": ones_col, "ones_row": ones_row,
            "idf": idf, "idb": idb, "tri": tri,
            "cosq": np.cos(emb).astype(np.float32),
            "sinq": np.sin(emb).astype(np.float32),
        })
    return maps



ENGS = ("pe", "act", "dve", "sp", "pool")


class Op:
    __slots__ = ("eng", "fn", "reads", "writes", "sem_key", "inc", "idx",
                 "edges", "need_inc", "val", "clock", "self_inc")

    def __init__(self, eng, fn, reads, writes, sem_key, inc, idx, self_inc):
        self.eng, self.fn = eng, fn
        self.reads, self.writes = tuple(reads), tuple(writes)
        self.sem_key, self.inc, self.idx = sem_key, inc, idx
        self.self_inc = self_inc
        self.edges = []
        self.need_inc = False
        self.val = None
        self.clock = None


class Prog:
    def __init__(self):
        self.ops = []
        self.last_w = {}
        self.readers = {}
        self.pending = {e: [] for e in ENGS}

    def op(self, eng, fn, r=(), w=(), sem_key=None, inc=1, self_inc=False):
        o = Op(eng, fn, r, w, sem_key, inc, len(self.ops), self_inc)
        seen = {}
        for k in o.reads:
            p = self.last_w.get(k)
            if p is not None:
                seen[p.idx] = p
        for k in o.writes:
            p = self.last_w.get(k)
            if p is not None:
                seen[p.idx] = p
            for rd in self.readers.get(k, ()):
                seen[rd.idx] = rd
        for p in self.pending[eng]:
            seen[p.idx] = p
        self.pending[eng] = []
        seen.pop(o.idx, None)
        o.edges = list(seen.values())
        for k in o.writes:
            self.last_w[k] = o
            self.readers[k] = []
        for k in o.reads:
            self.readers.setdefault(k, []).append(o)
        self.ops.append(o)
        return o

    def barrier(self):
        lasts = {}
        for o in self.ops:
            key = ("g", o.sem_key) if o.sem_key is not None else ("eng", o.eng)
            lasts[key] = o
        alll = list(lasts.values())
        for e in ENGS:
            self.pending[e] = list(alll)

    def finalize(self, nc, stack):
        for o in self.ops:
            for p in o.edges:
                p.need_inc = True
            if o.sem_key is not None:
                o.need_inc = True
        sems = {}
        for e in ENGS:
            sems[("eng", e)] = stack.enter_context(nc.semaphore(f"s_{e}"))
        for o in self.ops:
            if o.sem_key is not None and ("g", o.sem_key) not in sems:
                sems[("g", o.sem_key)] = stack.enter_context(
                    nc.semaphore(f"s_{o.sem_key}"))
        counts = {}
        for o in self.ops:
            if not o.need_inc:
                continue
            key = ("g", o.sem_key) if o.sem_key is not None else ("eng", o.eng)
            counts[key] = counts.get(key, 0) + o.inc
            o.val = counts[key]
        eng_clock = {e: {} for e in ENGS}
        emit = {e: [] for e in ENGS}
        for o in self.ops:
            clk = eng_clock[o.eng]
            waits = []
            for p in sorted(o.edges, key=lambda q: q.idx):
                pkey = ("g", p.sem_key) if p.sem_key is not None else ("eng", p.eng)
                if clk.get(pkey, 0) < p.val:
                    waits.append((sems[pkey], p.val))
                    clk[pkey] = p.val
                if p.clock:
                    for k2, v2 in p.clock.items():
                        if clk.get(k2, 0) < v2:
                            clk[k2] = v2
            okey = ("g", o.sem_key) if o.sem_key is not None else ("eng", o.eng)
            if o.need_inc:
                c2 = dict(clk)
                c2[okey] = max(c2.get(okey, 0), o.val)
                o.clock = c2
            else:
                o.clock = dict(clk)
            sem = sems[okey] if (o.need_inc and not o.self_inc) else None
            selfsem = sems[okey] if (o.need_inc and o.self_inc) else None
            emit[o.eng].append((waits, o.fn, sem, o.inc, selfsem))
        return emit



def build(cfg):
    nc = bass.Bass("TRN2", target_bir_lowering=False)
    P = Prog()
    stack = ExitStack()
    T, D, H, CH, NT, NH = (cfg["T"], cfg["D"], cfg["H"], cfg["CH"], cfg["NT"],
                           cfg["NH"])
    KS, DC, JC, HID = cfg["KS"], cfg["DC"], cfg["JC"], cfg["HID"]
    TTn = T // 128
    NTT = NT // 128

    din = lambda n, s, dt=F32: nc.dram_tensor(n, list(s), dt, kind="ExternalInput").ap()
    dint = lambda n, s, dt: nc.dram_tensor(n, list(s), dt, kind="Internal").ap()

    x0T = din("x0T_own", [D, NT])
    WQ = din("wq_blk", [24, 128, D])
    WKA = din("wkva_main", [4, 128, D])
    WKAR = din("wkva_rope", [128, 1024])
    WKB = din("wkvb_blk", [32, 128, 512])
    WOd = din("wo_blk", [16, 128, D])
    UPB = din("up_blk", [JC, 128, D])
    DNB = din("down_blk", [16, 128, HID])
    cons_in = [("USC", din("usc", [128, JC])), ("DSC", din("dsc", [128, DC])),
               ("KNW", din("knw", [128, 4])), ("OSC", din("osc", [1, 1])),
               ("ONESC", din("ones_col", [128, 1])),
               ("ONESR", din("ones_row", [1, 128])),
               ("IDF", din("idf", [128, 128])),
               ("IDB", din("idb", [128, 128], BF16)),
               ("TRI", din("tri", [128, 128])),
               ("COS", din("cosq", [64, NT])), ("SIN", din("sinq", [64, NT])),
               ("WKARs", WKAR)]
    OUT = nc.dram_tensor("out_own", [NT, D], F32, kind="ExternalOutput").ap()

    XST = dint("XST", [128, DC * NT], F32)
    GA_inA = dint("GA_inA", [D + 1, CH], BF16)
    GA_inB = dint("GA_inB", [D, CH], BF16)
    GA_outA = dint("GA_outA", [NCORE * (D + 1), CH], BF16)
    GA_outB = dint("GA_outB", [NCORE * D, CH], BF16)
    GVA_inA = dint("GVA_inA", [CH, H * 129], BF16)
    GVA_inB = dint("GVA_inB", [CH, H * 129], BF16)
    GVA_outA = dint("GVA_outA", [NCORE * CH, H * 129], BF16)
    GVA_outB = dint("GVA_outB", [NCORE * CH, H * 129], BF16)
    GYA_in0 = dint("GYA_in0", [T, 128], BF16)
    GYA_in1 = dint("GYA_in1", [T, 128], BF16)
    GYA_out0 = dint("GYA_out0", [NCORE * T, 128], BF16)
    GYA_out1 = dint("GYA_out1", [NCORE * T, 128], BF16)
    NRB = 2 * D + H * 64 + 64 + 2
    GB_inA = dint("GB_inA", [NRB, CH], BF16)
    GB_inB = dint("GB_inB", [NRB - 2, CH], BF16)
    GB_outA = dint("GB_outA", [NCORE * NRB, CH], BF16)
    GB_outB = dint("GB_outB", [NCORE * (NRB - 2), CH], BF16)
    GVB_inA = dint("GVB_inA", [CH, H * 129], BF16)
    GVB_inB = dint("GVB_inB", [CH, H * 129], BF16)
    GVB_outA = dint("GVB_outA", [NCORE * CH, H * 129], BF16)
    GVB_outB = dint("GVB_outB", [NCORE * CH, H * 129], BF16)
    GYB_in0 = dint("GYB_in0", [T, 128], BF16)
    GYB_in1 = dint("GYB_in1", [T, 128], BF16)
    GYB_out0 = dint("GYB_out0", [NCORE * T, 128], BF16)
    GYB_out1 = dint("GYB_out1", [NCORE * T, 128], BF16)
    UWS = dint("UWS", [JC, 128, D], BF16)
    DWS = dint("DWS", [16, 128, HID], BF16)

    spec_pers = [
        ("ONESC", [128, 1], F32), ("ONESR", [1, 128], F32),
        ("IDF", [128, 128], F32), ("IDB", [128, 128], BF16),
        ("TRI", [128, 128], F32), ("USC", [128, JC], F32),
        ("DSC", [128, DC], F32), ("CSC", [128, DC], F32),
        ("KNW", [128, 4], F32), ("OSC", [1, 1], F32),
        ("COS", [64, NT], F32), ("SIN", [64, NT], F32),
        ("WKARs", [128, 1024], F32), ("WKARb", [128, 1024], BF16),
        ("BIAS", [128, 64], F32), ("CROW", [1, 32], BF16),
        ("CROWF", [1, 32], F32), ("QN2", [1, 32], F32),
        ("ROWA", [1, 512], F32), ("ROWB", [1, 512], F32),
        ("ROWC", [1, 512], F32), ("ROWD", [1, 512], F32),
        ("KMN", [1, 128], F32), ("KMX", [1, 16], F32), ("RL", [128, 1], F32),
        ("ROWB16", [1, 64], BF16), ("QROW", [1, 32], BF16), ("KMROW", [1, 16], BF16),
        ("CSTG", [1, NT], BF16), ("CSTG2", [1, NT], BF16),
        ("KH", [1, 2], F32),
        ("EPSA", [1, 1], F32), ("EPSB", [1, 1], F32),
        ("XW0", [128, NT], F32), ("XW1", [128, NT], F32),
        ("YTA", [128, 128], F32), ("YTB", [128, 128], BF16),
        ("SQ0", [128, 512], F32), ("SQ1", [128, 512], F32),
    ]
    spec_a = [
        ("N", [128, DC * NT], BF16),
        ("VSTG", [128, NTT * H * 129], BF16),
        ("PSB0", [128, 2 * CH], BF16), ("PSB1", [128, 2 * CH], BF16),
    ]
    spec_a2 = [
        ("KQA", [128, 2 * T], BF16),
        ("VAA", [128, TTn * 258], BF16),
        ("YIN", [128, NCORE * NTT * 256], BF16),
        ("PSB0b", [128, 2 * CH], BF16), ("PSB1b", [128, 2 * CH], BF16),
    ]
    spec_f = [
        ("NF", [128, DC * NT], BF16),
        ("HSB", [128, JC * 512], BF16),
        ("UPIN0", [128, D], F32), ("UPIN1", [128, D], F32),
        ("UWSB0", [128, D], BF16), ("UWSB1", [128, D], BF16),
        ("DWSB0", [128, HID], BF16), ("DWSB1", [128, HID], BF16),
        ("DNIN", [128, HID // 2], F32, "UPIN0"),
    ]
    spec_p = [
        ("NP", [128, DC * NT], BF16),
        ("WBLK0", [128, D], F32), ("WBLK1", [128, D], F32),
        ("WBB0", [128, D], BF16), ("WBB1", [128, D], BF16),
        ("RQ", [64, H * NT], BF16),
        ("RH", [64, NT], F32), ("RT1", [64, NT], F32), ("RT2", [64, NT], F32),
        ("LATN", [128, 4 * NT], BF16),
        ("KRO", [64, NT], BF16),
        ("STG", [128, NT], BF16),
        ("VSTG2", [128, NTT * H * 129], BF16),
    ]
    spec_b = [
        ("KNB", [128, 2 * T], BF16),
        ("QNB", [128, 2 * T], BF16),
        ("QRB", [64, 2 * T], BF16),
        ("KRB", [64, T], BF16),
        ("VBB", [128, TTn * 258], BF16),
        ("PSB2", [128, 2 * CH], BF16), ("PSB3", [128, 2 * CH], BF16),
    ]
    spec_w = [
        ("YFM", [128, H * NT], BF16),
        ("YIN2", [128, NCORE * NTT * 256], BF16),
        ("WOB0", [128, D], F32), ("WOB1", [128, D], F32),
        ("WOBB", [128, D], BF16),
    ]
    spec_o = [("OS0", [128, 128], F32), ("OS1", [128, 128], F32)]

    def size_of(spec):
        off, offs = 0, {}
        for ent in spec:
            nm, sh, dt = ent[0], ent[1], ent[2]
            if len(ent) == 4:
                offs[nm] = offs[ent[3]]
                continue
            nb = int(np.prod(sh[1:])) * mybir.dt.size(dt)
            offs[nm] = off
            off += ((nb + 31) // 32) * 32
        return off, offs

    pers_sz, pers_off = size_of(spec_pers)
    arena_specs = [spec_a, spec_a2, spec_f, spec_p, spec_b, spec_w, spec_o]
    arena_sz = max(size_of(s)[0] for s in arena_specs)
    total = pers_sz + arena_sz

    backing = stack.enter_context(
        nc.sbuf_tensor("arena", [128, total], mybir.dt.uint8))
    base = nc.lookup_mloc(backing).addr

    V = {}

    def place(spec, offs, origin):
        for ent in spec:
            nm, sh, dt = ent[0], ent[1], ent[2]
            hnd = nc.virtual_sbuf_tensor("sb_" + nm, list(sh), dt)
            nc._tensor(hnd.name, list(hnd.shape), hnd.dtype, type="SB")
            mloc = nc.lookup_mloc(hnd)
            mloc.allocated = True
            mloc.addr = base + origin + offs[nm]
            V[nm] = hnd.ap()

    place(spec_pers, pers_off, 0)
    for sp_ in arena_specs:
        _, offs = size_of(sp_)
        place(sp_, offs, pers_sz)

    PA = stack.enter_context(nc.psum_tensor("PA", [128, 1024], F32)).ap()
    PB = stack.enter_context(nc.psum_tensor("PB", [128, 1024], F32)).ap()
    PC = stack.enter_context(nc.psum_tensor("PC", [128, 1024], F32)).ap()
    PD = stack.enter_context(nc.psum_tensor("PD", [128, 512], F32)).ap()
    PTB = stack.enter_context(nc.psum_tensor("PTB", [128, 512], BF16)).ap()

    op = P.op

    def MM(out, lhsT, rhs, start, stop, r, w):
        op("pe", (lambda o, l, rr, st, sp2: lambda e: e.matmul(
            o, l, rr, start=st, stop=sp2, skip_group_check=True))(
                out, lhsT, rhs, start, stop), r, w)

    def TRANS(out, in_, ident, r, w):
        op("pe", (lambda o, i_, id_: lambda e: e.transpose(o, i_, id_))(
            out, in_, ident), r, w)

    def ACTV(out, in_, func, r, w, bias=0.0, scale=1.0):
        op("act", (lambda o, i_, f, bb, s: lambda e: e.activation(
            o, i_, f, bias=bb, scale=s))(out, in_, func, bias, scale), r, w)

    def SIGN(out, in_, r, w):
        op("act", (lambda o, i_: lambda e: e.sign(o, i_))(out, in_), r, w)

    def TT(out, a, b, alu, r, w):
        op("dve", (lambda o, a_, b_, u: lambda e: e.tensor_tensor(o, a_, b_, u))(
            out, a, b, alu), r, w)

    def TS(out, a, s1, alu, r, w):
        op("dve", (lambda o, a_, s_, u: lambda e: e.tensor_single_scalar(
            o, a_, s_, u))(out, a, s1, alu), r, w)

    def TSM(out, a, s1, r, w):
        op("dve", (lambda o, a_, s_: lambda e: e.tensor_scalar_mul(o, a_, s_))(
            out, a, s1), r, w)

    def STT(out, in0, scal, in1, op0, op1, r, w):
        op("dve", (lambda o, i0, s_, i1, u0, u1: lambda e: e.scalar_tensor_tensor(
            o, i0, s_, i1, u0, u1))(out, in0, scal, in1, op0, op1), r, w)

    def RED(out, in_, alu, r, w):
        op("dve", (lambda o, i_, u: lambda e: e.tensor_reduce(o, i_, AX.X, u))(
            out, in_, alu), r, w)

    def RECIP(out, in_, r, w):
        op("dve", (lambda o, i_: lambda e: e.reciprocal(o, i_))(out, in_), r, w)

    def CPY(out, in_, r, w, eng="dve"):
        if eng == "act":
            op(eng, (lambda o, i_: lambda e: e.copy(o, i_))(out, in_), r, w)
        else:
            op(eng, (lambda o, i_: lambda e: e.tensor_copy(o, i_))(out, in_), r, w)

    def MEMSET(ap_, val, r, w):
        op("dve", (lambda a_, v_: lambda e: e.memset(a_, v_))(ap_, val), r, w)

    def DMA(eng, out, in_, slot, r, w):
        op(eng, (lambda o, i_: lambda e: e.dma_start(out=o, in_=i_))(out, in_),
           r, w, sem_key=slot, inc=16)

    def SWDMA(bodies, slot, r, w):
        n = len(bodies[0])
        assert all(len(bb) == n for bb in bodies)

        def fn(e, sem):
            if not hasattr(e, "_pid_cached"):
                e._pid_cached = e.compute_val(e.partition_id())
            idx = e._pid_cached
            for i in nc.Switch(engines=[e], index=[idx], n=NCORE):
                for (o_, i_) in bodies[i]:
                    e.dma_start(out=o_, in_=i_).then_inc(sem, 16)
        op("pool", fn, r, w, sem_key=slot, inc=16 * n, self_inc=True)

    def COLL(ins, outs, slot, r, w):
        op("pool", (lambda i_, o_: lambda e: e.collective_compute(
            "AllGather", ALU.bypass, replica_groups=[list(range(NCORE))],
            ins=[i_], outs=[o_]))(ins, outs), r, w, sem_key=slot, inc=1)

    def row(half):
        return (PD[0:1, 0:512], "PD") if half == 0 else (PC[0:1, 0:512], "PC0")

    for nm, src in cons_in:
        DMA("sp", V[nm][:], src, "kc_" + nm, r=[], w=[nm])
    MM(PD[:, 0:1], V["ONESR"][:], V["OSC"][:], True, True,
       r=["ONESR", "OSC"], w=["PD"])
    TSM(V["CSC"][:], V["DSC"][:], PD[:, 0:1], r=["DSC", "PD"], w=["CSC"])
    ACTV(V["WKARb"][:], V["WKARs"][:], AF.Copy, r=["WKARs"], w=["WKARb"])
    MEMSET(V["EPSA"][:], float(cfg["LNEPS"]), r=[], w=["EPSA"])
    MEMSET(V["EPSB"][:], float(cfg["RMSEPS"]), r=[], w=["EPSB"])
    DMA("sp", XST.rearrange("p (c t) -> p c t", c=DC),
        x0T.rearrange("(c p) t -> p c t", p=128), "x0", r=[], w=["XST"])

    def emit_ln(nbuf, nkey):
        for half in range(NH):
            hs = half * 512
            for kc in range(DC):
                xw = V["XW0"] if kc % 2 == 0 else V["XW1"]
                xk = "XW0" if kc % 2 == 0 else "XW1"
                DMA("sp", xw[:, 0:512], XST[:, kc * NT + hs: kc * NT + hs + 512],
                    "dm" + xk, r=["XST"], w=[xk])
                MM(PD[0:1, 0:512], V["ONESC"][:], xw[:, 0:512], kc == 0,
                   kc == DC - 1, r=[xk, "ONESC"], w=["PD"])
                sq = V["SQ0"] if kc % 2 == 0 else V["SQ1"]
                sqk = "SQ0" if kc % 2 == 0 else "SQ1"
                ACTV(sq[:], xw[:, 0:512], AF.Square, r=[xk], w=[sqk])
                MM(PC[0:1, 0:512], V["ONESC"][:], sq[:], kc == 0, kc == DC - 1,
                   r=[sqk, "ONESC"], w=["PC0"])
            TS(V["ROWA"][:], PD[0:1, 0:512], 1.0 / D, ALU.mult, r=["PD"], w=["ROWA"])
            TS(V["ROWB"][:], PC[0:1, 0:512], 1.0 / D, ALU.mult, r=["PC0"], w=["ROWB"])
            TT(V["ROWC"][:], V["ROWA"][:], V["ROWA"][:], ALU.mult,
               r=["ROWA"], w=["ROWC"])
            TT(V["ROWB"][:], V["ROWB"][:], V["ROWC"][:], ALU.subtract,
               r=["ROWB", "ROWC"], w=["ROWB"])
            ACTV(V["ROWB"][:], V["ROWB"][:], AF.Sqrt, bias=V["EPSA"][:],
                 r=["ROWB", "EPSA"], w=["ROWB"])
            RECIP(V["ROWB"][:], V["ROWB"][:], r=["ROWB"], w=["ROWB"])
            TT(V["ROWC"][:], V["ROWA"][:], V["ROWB"][:], ALU.mult,
               r=["ROWA", "ROWB"], w=["ROWC"])
            MM(PC[:, 0:512], V["ONESR"][:], V["ROWB"][:], True, True,
               r=["ROWB", "ONESR"], w=["PC0"])
            MM(PC[:, 512:1024], V["ONESR"][:], V["ROWC"][:], True, True,
               r=["ROWC", "ONESR"], w=["PC1"])
            for kc in range(DC):
                xw = V["XW0"] if kc % 2 == 0 else V["XW1"]
                xk = "XW0" if kc % 2 == 0 else "XW1"
                DMA("sp", xw[:, 0:512], XST[:, kc * NT + hs: kc * NT + hs + 512],
                    "dm" + xk, r=["XST"], w=[xk])
                sq = V["SQ0"] if kc % 2 == 0 else V["SQ1"]
                sqk = "SQ0" if kc % 2 == 0 else "SQ1"
                TT(sq[:], xw[:, 0:512], PC[:, 0:512], ALU.mult,
                   r=[xk, "PC0"], w=[sqk])
                TT(nbuf[:, kc * NT + hs: kc * NT + hs + 512], sq[:],
                   PC[:, 512:1024], ALU.subtract, r=[sqk, "PC1"], w=[nkey])

    def emit_vstage(vsrc, vsrck, vstg, vstgk, i0=0, i1=None):
        if i1 is None:
            i1 = NTT
        for i in range(i0, i1):
            base_c = i * H * 129
            MEMSET(vstg[:, base_c: base_c + H * 129].rearrange(
                "p (h c) -> p h c", h=H)[:, :, 128:129].rearrange(
                    "p h c -> p (h c)"), 1.0, r=[], w=[vstgk])
        for h in range(H):
            for i in range(i0, i1):
                TRANS(PTB[:, 0:128],
                      vsrc[:, h * NT + i * 128: h * NT + (i + 1) * 128],
                      V["IDB"][:], r=[vsrck, "IDB"], w=["PTB"])
                CPY(vstg[:, i * H * 129 + h * 129: i * H * 129 + h * 129 + 128],
                    PTB[:, 0:128], r=["PTB"], w=[vstgk], eng="act")

    STAGE = cfg.get("stage", 4)

    if STAGE >= 1:
      emit_ln(V["N"], "N")
      for h in range(H):
          ACTV(V["SQ0"][:, 0:512], V["N"][:, h * NT: h * NT + 512], AF.Square,
               r=["N"], w=["SQ0"])
          MM(PD[0:1, 0:512], V["ONESC"][:], V["SQ0"][:], True, True,
             r=["SQ0", "ONESC"], w=["PD"])
          if NH == 2:
              ACTV(V["SQ1"][:, 0:512], V["N"][:, h * NT + 512: h * NT + 1024],
                   AF.Square, r=["N"], w=["SQ1"])
              MM(PC[0:1, 0:512], V["ONESC"][:], V["SQ1"][:], True, True,
                 r=["SQ1", "ONESC"], w=["PC0"])
          for qc in range(2):
              hh = (qc * CH) // 512
              c0 = qc * CH - hh * 512
              srcp, srck = (PD[0:1, c0:c0 + CH], "PD") if hh == 0 else                   (PC[0:1, c0:c0 + CH], "PC0")
              RED(V["ROWA"][0:1, 0:1], srcp, ALU.max, r=[srck], w=["ROWA"])
              ACTV(V["ROWA"][0:1, 0:1], V["ROWA"][0:1, 0:1], AF.Sqrt,
                   r=["ROWA"], w=["ROWA"])
              TS(V["ROWD"][0:1, qc * 16 + h: qc * 16 + h + 1],
                 V["ROWA"][0:1, 0:1], float(np.sqrt(D)), ALU.mult,
                 r=["ROWA"], w=["ROWD"])
      MEMSET(V["CSTG"][:], 0.0, r=[], w=["CSTG"])
      ACTV(V["CSTG"][0:1, 0:32], V["ROWD"][0:1, 0:32], AF.Copy,
           r=["ROWD"], w=["CSTG"])
      DMA("sp", GA_inA[D:D + 1, :], V["CSTG"][0:1, 0:CH], "cai",
          r=["CSTG"], w=["GA_inA"])
      DMA("sp", GA_inA[0:D, :].rearrange("(c p) t -> p c t", p=128),
          V["N"].rearrange("p (c t) -> p c t", c=DC)[:, :, 0:CH],
          "gai", r=["N"], w=["GA_inA"])
      DMA("sp", GA_inB.rearrange("(c p) t -> p c t", p=128),
          V["N"].rearrange("p (c t) -> p c t", c=DC)[:, :, CH:NT],
          "gaib", r=["N"], w=["GA_inB"])
      COLL(GA_inA, GA_outA, "cc_gaa", r=["GA_inA"], w=["GA_outA"])
      emit_vstage(V["N"], "N", V["VSTG"], "VSTG", 0, NTT // 2)
      DMA("sp", GVA_inA.rearrange("(i p) c -> p i c", p=128),
          V["VSTG"].rearrange("p (i c) -> p i c", i=NTT)[:, 0:NTT // 2, :],
          "gvia", r=["VSTG"], w=["GVA_inA"])
      COLL(GVA_inA, GVA_outA, "cc_gvaa", r=["GVA_inA"], w=["GVA_outA"])
      COLL(GA_inB, GA_outB, "cc_gab", r=["GA_inB"], w=["GA_outB"])
      emit_vstage(V["N"], "N", V["VSTG"], "VSTGB", NTT // 2, NTT)
      DMA("sp", GVA_inB.rearrange("(i p) c -> p i c", p=128),
          V["VSTG"].rearrange("p (i c) -> p i c", i=NTT)[:, NTT // 2:NTT, :],
          "gvib", r=["VSTGB"], w=["GVA_inB"])
      COLL(GVA_inB, GVA_outB, "cc_gvab", r=["GVA_inB"], w=["GVA_outB"])

      P.barrier()

      def switched_heads_bodies(dst, gout, rowblk, rowoff, qc):
          bodies = []
          for me in range(NCORE):
              pairs = []
              for s in range(NCORE):
                  for hl in range(2):
                      h = 2 * me + hl
                      j = s if qc == 0 else NCH - 1 - s
                      src = gout[s * rowblk + rowoff + h * 128:
                                 s * rowblk + rowoff + (h + 1) * 128, :]
                      d = dst[:, hl * T + j * CH: hl * T + (j + 1) * CH]
                      pairs.append((d, src))
              bodies.append(pairs)
          return bodies

      def switched_V_bodies(dst, gout, qc):
          bodies = []
          for me in range(NCORE):
              pairs = []
              for s in range(NCORE):
                  j = s if qc == 0 else NCH - 1 - s
                  src = gout[s * CH: (s + 1) * CH,
                             me * 258:(me + 1) * 258].rearrange(
                                 "(k p) c -> p k c", p=128)
                  d = dst[:, j * KS * 258:(j + 1) * KS * 258].rearrange(
                      "p (k c) -> p k c", k=KS)
                  pairs.append((d, src))
              bodies.append(pairs)
          return bodies

      def switched_C_bodies(gout, rowblk, rowoff, dsttile):
          bodies = []
          for me in range(NCORE):
              rows = gout.rearrange("(r x) t -> r x t", x=rowblk)[:, rowoff, 0:32]
              src = rows.rearrange("r (q h) -> q r h", q=2)[
                  :, :, 2 * me: 2 * me + 2].unsqueeze(0)
              dstc = dsttile[0:1, 0:32].rearrange(
                  "a (q r h) -> a q r h", q=2, r=NCORE)
              bodies.append([(dstc, src)])
          return bodies

      def switched_KM_bodies(gout, rowblk, rowoff):
          bodies = []
          for me in range(NCORE):
              rows = gout.rearrange("(r x) t -> r x t", x=rowblk)[
                  :, rowoff, 2 * me: 2 * me + 2].unsqueeze(0)
              dstc = V["KMROW"][0:1, 0:16].rearrange("a (r h) -> a r h", r=NCORE)
              bodies.append([(dstc, rows)])
          return bodies

      def merge_bodies(*lists):
          out = []
          for me in range(NCORE):
              pairs = []
              for L in lists:
                  pairs.extend(L[me])
              out.append(pairs)
          return out

      def bias_prep(attn):
          scale = float(192 ** -0.5) if attn else float(128 ** -0.5)
          ACTV(V["CROWF"][:], V["CROW"][0:1, 0:32], AF.Copy,
               r=["CROW"], w=["CROWF"])
          for hl in range(2):
              for j in range(NCH):
                  r_, qc = gloc(j)
                  src = V["CROWF"][0:1, 16 * qc + 2 * r_ + hl:
                                   16 * qc + 2 * r_ + hl + 1]
                  MM(PD[:, 0:1], V["ONESR"][:], src, True, True,
                     r=["CROWF", "ONESR"], w=["PD"])
                  col = attn * 32 + hl * 16 + cidx(j)
                  ACTV(V["BIAS"][:, col:col + 1], PD[:, 0:1], AF.Copy,
                       scale=-scale, r=["PD"], w=["BIAS"])

      SWDMA(merge_bodies(switched_heads_bodies(V["KQA"], GA_outA, D + 1, 0, 0),
                         switched_V_bodies(V["VAA"], GVA_outA, 0),
                         switched_C_bodies(GA_outA, D + 1, D, V["CROW"])),
            "swa", r=["GA_outA", "GVA_outA"], w=["KQA_A", "VAA_A", "CROW"])
      SWDMA(merge_bodies(switched_heads_bodies(V["KQA"], GA_outB, D, 0, 1),
                         switched_V_bodies(V["VAA"], GVA_outB, 1)),
            "swab", r=["GA_outB", "GVA_outB"], w=["KQA_B", "VAA_B"])
      bias_prep(0)

      def bias_prep_b():
          scale = float(192 ** -0.5)
          ACTV(V["CROWF"][:], V["CROW"][0:1, 0:32], AF.Copy,
               r=["CROW"], w=["CROWF"])
          ACTV(V["KMX"][0:1, 0:16], V["KMROW"][0:1, 0:16], AF.Copy,
               r=["KMROW"], w=["KMX"])
          for hl in range(2):
              RED(V["KH"][0:1, hl:hl + 1],
                  V["KMX"].rearrange("a (r h) -> a h r", h=2)[0:1, hl, 0:NCORE],
                  ALU.max, r=["KMX"], w=["KH"])
          for hl in range(2):
              for j in range(NCH):
                  r_, qc = gloc(j)
                  TT(V["ROWA"][0:1, 0:1],
                     V["CROWF"][0:1, 16 * qc + 2 * r_ + hl:
                                16 * qc + 2 * r_ + hl + 1],
                     V["KH"][0:1, hl:hl + 1], ALU.mult,
                     r=["CROWF", "KH"], w=["ROWA"])
                  ACTV(V["ROWA"][0:1, 0:1], V["ROWA"][0:1, 0:1], AF.Sqrt,
                       r=["ROWA"], w=["ROWA"])
                  MM(PD[:, 0:1], V["ONESR"][:], V["ROWA"][0:1, 0:1], True, True,
                     r=["ROWA", "ONESR"], w=["PD"])
                  col = 32 + hl * 16 + cidx(j)
                  ACTV(V["BIAS"][:, col:col + 1], PD[:, 0:1], AF.Copy,
                       scale=-scale, r=["PD"], w=["BIAS"])

      def emit_attn(b, on_hl=None):
          KQ = V["KNB"] if b else V["KQA"]
          QN = V["QNB"] if b else V["KQA"]
          VA = V["VBB"] if b else V["VAA"]
          def kqk(c):
              base = "KNB" if b else "KQA"
              return base + ("_A" if c < NCORE else "_B")
          def qnk(c):
              base = "QNB" if b else "KQA"
              return base + ("_A" if c < NCORE else "_B")
          def vak(c):
              base = "VBB" if b else "VAA"
              return base + ("_A" if c < NCORE else "_B")
          def krk(c):
              return "KRB_A" if c < NCORE else "KRB_B"
          def qrk(c):
              return "QRB_A" if c < NCORE else "QRB_B"
          GY = (GYB_in0, GYB_in1) if b else (GYA_in0, GYA_in1)
          gyk = ("GYB0", "GYB1") if b else ("GYA0", "GYA1")
          scale = float(192 ** -0.5) if b else float(128 ** -0.5)
          psb = [V["PSB2"], V["PSB3"]] if b else [V["PSB0b"], V["PSB1b"]]
          psbk = ["PSB2", "PSB3"] if b else ["PSB0b", "PSB1b"]
          prc = [0]
          for hl in range(2):
              for j in range(NCH):
                  def emit_pv(st):
                      (PSBt_, psbkt_, pr_, kc_, nsub_) = st
                      for qs in range(KS):
                          for s2 in range(nsub_):
                              ks = 2 * pr_ + s2
                              if kc_ == j and ks > qs:
                                  continue
                              first = (kc_ == 0 and ks == 0)
                              last = (kc_ == j and ks == min(qs, KS - 1))
                              ao = (qs % 2) * 132 + (qs // 2) * 512
                              ack = "PC" + str(qs // 2)
                              lt = PSBt_[:, s2 * CH + qs * 128:
                                         s2 * CH + qs * 128 + 128]
                              rv = VA[:, (kc_ * KS + ks) * 258 + hl * 129:
                                      (kc_ * KS + ks) * 258 + hl * 129 + 129]
                              MM(PC[:, ao: ao + 129], lt, rv, first, last,
                                 r=[psbkt_, vak(kc_)], w=[ack])
                  pending = None
                  for kc in range(j + 1):
                      for pr in range((KS + 1) // 2):
                          par = prc[0] % 2
                          prc[0] += 1
                          SP_ = PA if par == 0 else PB
                          spk = "PA" if par == 0 else "PB"
                          PSBt, psbkt = psb[par], psbk[par]
                          nsub = min(2, KS - 2 * pr)
                          for s2 in range(nsub):
                              ks = 2 * pr + s2
                              lt = KQ[:, hl * T + kc * CH + ks * 128:
                                      hl * T + kc * CH + ks * 128 + 128]
                              rh_ = QN[:, hl * T + j * CH: hl * T + (j + 1) * CH]
                              bnk = spk + str(s2)
                              MM(SP_[:, s2 * CH:(s2 + 1) * CH], lt, rh_,
                                 True, not b, r=[kqk(kc), qnk(j)], w=[bnk])
                              if b:
                                  ltr = V["KRB"][0:64, kc * CH + ks * 128:
                                                 kc * CH + ks * 128 + 128]
                                  rhr = V["QRB"][0:64, hl * T + j * CH:
                                                 hl * T + (j + 1) * CH]
                                  MM(SP_[:, s2 * CH:(s2 + 1) * CH], ltr, rhr,
                                     False, True, r=[krk(kc), qrk(j)], w=[bnk])
                              if kc == j:
                                  sl = SP_[:, s2 * CH + ks * 128:
                                           s2 * CH + ks * 128 + 128]
                                  TT(sl, sl, V["TRI"][:], ALU.add,
                                     r=[bnk, "TRI"], w=[bnk])
                          bcol = (32 if b else 0) + hl * 16 + cidx(j)
                          rk = [spk + "0"] + ([spk + "1"] if nsub == 2 else [])
                          ACTV(PSBt[:, 0:nsub * CH], SP_[:, 0:nsub * CH], AF.Exp,
                               bias=V["BIAS"][:, bcol:bcol + 1], scale=scale,
                               r=rk + ["BIAS"], w=[psbkt])
                          if pending is not None:
                              emit_pv(pending)
                          pending = (PSBt, psbkt, pr, kc, nsub)
                  if pending is not None:
                      emit_pv(pending)
                  for qs in range(KS):
                      ao = (qs % 2) * 132 + (qs // 2) * 512
                      ack = "PC" + str(qs // 2)
                      RECIP(V["RL"][:], PC[:, ao + 128: ao + 129], r=[ack], w=["RL"])
                      op("dve", (lambda ao_: lambda e: e.tensor_scalar_mul(
                          V["YTA"][:], PC[:, ao_: ao_ + 128], V["RL"][:]))(ao),
                         r=[ack, "RL"], w=["YTA"])
                      ACTV(V["YTB"][:], V["YTA"][:], AF.Copy, r=["YTA"], w=["YTB"])
                      row0 = j * CH + qs * 128
                      DMA("pool", GY[hl][row0:row0 + 128, :],
                          V["YTB"][:], "gy" + ("b" if b else "a") + str(hl),
                          r=["YTB"], w=[gyk[hl]])
                  if on_hl is not None and j == NCH - 1:
                      on_hl(hl)

      def ya_coll(hl):
          if hl == 0:
              COLL(GYA_in0, GYA_out0, "cc_gya0", r=["GYA0"], w=["GYA_out0"])
          else:
              COLL(GYA_in1, GYA_out1, "cc_gya1", r=["GYA1"], w=["GYA_out1"])
      emit_attn(0, ya_coll)

      def yret_bodies(gyout, yin, hl):
          bodies = []
          for me in range(NCORE):
              pairs = []
              for s in range(NCORE):
                  for qc in range(2):
                      ch = me if qc == 0 else NCH - 1 - me
                      src = gyout[s * T + ch * CH: s * T + (ch + 1) * CH, :]\
                          .rearrange("(i p) c -> p i c", p=128)
                      d = yin.rearrange("p (i c) -> p i c", c=256)[
                          :, s * NTT + qc * KS: s * NTT + (qc + 1) * KS,
                          hl * 128:(hl + 1) * 128]
                      pairs.append((d, src))
              bodies.append(pairs)
          return bodies

      for hl in range(2):
          gyo = GYA_out0 if hl == 0 else GYA_out1
          gyok = "GYA_out0" if hl == 0 else "GYA_out1"
          yk = "YIN" + str(hl)
          SWDMA(yret_bodies(gyo, V["YIN"], hl), "swya" + str(hl),
                r=[gyok], w=[yk])
          for s in range(NCORE):
              dc = 2 * s + hl
              xw = V["XW0"] if dc % 2 == 0 else V["XW1"]
              xk = "XW0" if dc % 2 == 0 else "XW1"
              DMA("sp", xw[:], XST[:, dc * NT:(dc + 1) * NT], "dm" + xk,
                  r=["XST"], w=[xk])
              for i in range(NTT):
                  c0 = (s * NTT + i) * 256 + hl * 128
                  TRANS(PTB[:, 0:128], V["YIN"][:, c0: c0 + 128],
                        V["IDB"][:], r=[yk, "IDB"], w=["PTB"])
                  TT(xw[:, i * 128:(i + 1) * 128], xw[:, i * 128:(i + 1) * 128],
                     PTB[:, 0:128], ALU.add, r=[xk, "PTB"], w=[xk])
              DMA("sp", XST[:, dc * NT:(dc + 1) * NT], xw[:], "st" + xk,
                  r=[xk], w=["XST"])

    P.barrier()

    def emit_ffn(first):
        emit_ln(V["NF"], "NF")
        for half in range(NH):
            hs = half * 512
            for jc in range(JC):
                upin = V["UPIN0"] if jc % 2 == 0 else V["UPIN1"]
                upk = "UPIN0" if jc % 2 == 0 else "UPIN1"
                uws = V["UWSB0"] if jc % 2 == 0 else V["UWSB1"]
                uwk = "UWSB0" if jc % 2 == 0 else "UWSB1"
                if first and half == 0:
                    DMA("sp", upin[:], UPB[jc], "up" + upk, r=[], w=[upk])
                    SIGN(uws[:], upin[:], r=[upk], w=[uwk])
                    DMA("pool", UWS[jc], uws[:], "uwsw", r=[uwk], w=["UWS"])
                else:
                    DMA("sp", uws[:], UWS[jc], "up" + uwk, r=["UWS"], w=[uwk])
                PU, puk = (PD, "PD") if jc % 2 == 0 else (PA[:, 0:512], "PA0")
                for kc in range(DC):
                    MM(PU[:, 0:512] if jc % 2 == 0 else PU,
                       uws[:, kc * 128:(kc + 1) * 128],
                       V["NF"][:, kc * NT + hs: kc * NT + hs + 512],
                       kc == 0, kc == DC - 1, r=[uwk, "NF"], w=[puk])
                op("act", (lambda jc_, PU_: lambda e: e.activation(
                    V["HSB"][:, jc_ * 512:(jc_ + 1) * 512], PU_, AF.Relu,
                    scale=V["USC"][:, jc_:jc_ + 1]))(
                        jc, PU[:, 0:512] if jc % 2 == 0 else PU),
                   r=[puk, "USC"], w=["HSB"])
            for dg in range(8):
                for dci in range(2):
                    dc = dg * 2 + dci
                    dws = V["DWSB0"] if dci == 0 else V["DWSB1"]
                    dwk = "DWSB0" if dci == 0 else "DWSB1"
                    if first and half == 0:
                        for hh2 in range(2):
                            DMA("sp", V["DNIN"][:],
                                DNB[dc][:, hh2 * (HID // 2):(hh2 + 1) * (HID // 2)],
                                "dnin", r=[], w=["DNIN", "UPIN0", "UPIN1"])
                            SIGN(dws[:, hh2 * (HID // 2):(hh2 + 1) * (HID // 2)],
                                 V["DNIN"][:], r=["DNIN"], w=[dwk])
                        DMA("pool", DWS[dc], dws[:], "dwsw", r=[dwk], w=["DWS"])
                    else:
                        DMA("sp", dws[:], DWS[dc], "dn" + dwk, r=["DWS"], w=[dwk])
                    pbk = "PB" + str(dci)
                    for jc in range(JC):
                        MM(PB[:, dci * 512: dci * 512 + 512],
                           dws[:, jc * 128:(jc + 1) * 128],
                           V["HSB"][:, jc * 512:(jc + 1) * 512],
                           jc == 0, jc == JC - 1, r=[dwk, "HSB"], w=[pbk])
                for dci in range(2):
                    dc = dg * 2 + dci
                    pbk = "PB" + str(dci)
                    xw = V["XW0"] if dci == 0 else V["XW1"]
                    xk = "XW0" if dci == 0 else "XW1"
                    DMA("sp", xw[:, 0:512],
                        XST[:, dc * NT + hs: dc * NT + hs + 512],
                        "dm" + xk, r=["XST"], w=[xk])
                    op("dve", (lambda dc_, dci_, xw_: lambda e:
                        e.scalar_tensor_tensor(
                            xw_[:, 0:512], PB[:, dci_ * 512: dci_ * 512 + 512],
                            V["CSC"][:, dc_:dc_ + 1], xw_[:, 0:512],
                            ALU.mult, ALU.add))(dc, dci, xw),
                       r=[pbk, "CSC", xk], w=[xk])
                    DMA("sp", XST[:, dc * NT + hs: dc * NT + hs + 512],
                        xw[:, 0:512], "st" + xk, r=[xk], w=["XST"])

    if STAGE >= 2:
        emit_ffn(True)
    P.barrier()

    if STAGE >= 2.5:
      emit_ln(V["NP"], "NP")

      def proj_mm(blk_dram, m, out_psum, pk, nsrc, nsrck, slot_extra=""):
          wb = V["WBLK0"] if m % 2 == 0 else V["WBLK1"]
          wk = "WBLK0" if m % 2 == 0 else "WBLK1"
          wbb = V["WBB0"] if m % 2 == 0 else V["WBB1"]
          wbbk = "WBB0" if m % 2 == 0 else "WBB1"
          nb = blk_dram.shape[1]
          DMA("sp", wb[:, 0:nb], blk_dram, "pr" + wk + slot_extra,
              r=[], w=[wk])
          ACTV(wbb[:, 0:nb], wb[:, 0:nb], AF.Copy, r=[wk], w=[wbbk])
          kcn = nb // 128
          for hh in range(NH):
              for kc in range(kcn):
                  MM(out_psum[:, hh * 512: hh * 512 + 512],
                     wbb[:, kc * 128:(kc + 1) * 128],
                     nsrc[:, kc * NT + hh * 512: kc * NT + hh * 512 + 512],
                     kc == 0, kc == kcn - 1, r=[wbbk, nsrck], w=[pk + str(hh)])

      for m in range(24):
          PQ, pk = (PA, "PA") if m % 2 == 0 else (PB, "PB")
          proj_mm(WQ[m], m, PQ, pk, V["NP"], "NP")
          pks = [pk + str(hh) for hh in range(NH)]
          if m < 16:
              ACTV(V["STG"][:], PQ[:, 0:NT], AF.Copy, r=pks, w=["STG"])
              DMA("sp", GB_inA[m * 128:(m + 1) * 128, :], V["STG"][:, 0:CH],
                  "gqn", r=["STG"], w=["GB_inA"])
              DMA("sp", GB_inB[m * 128:(m + 1) * 128, :], V["STG"][:, CH:NT],
                  "gqnb", r=["STG"], w=["GB_inB"])
              for hh in range(NH):
                  sq = V["SQ0"] if hh == 0 else V["SQ1"]
                  sqk = "SQ0" if hh == 0 else "SQ1"
                  ACTV(sq[:], PQ[:, hh * 512: hh * 512 + 512], AF.Square,
                       r=[pk + str(hh)], w=[sqk])
                  rw, rwk = row(hh)
                  MM(rw, V["ONESC"][:], sq[:], True, True, r=[sqk, "ONESC"], w=[rwk])
              for qc in range(2):
                  hh = (qc * CH) // 512
                  c0 = qc * CH - hh * 512
                  rw, rwk = row(hh)
                  RED(V["QN2"][0:1, qc * 16 + m: qc * 16 + m + 1],
                      rw[0:1, c0:c0 + CH], ALU.max, r=[rwk], w=["QN2"])
          else:
              h1, h2 = 2 * (m - 16), 2 * (m - 16) + 1
              ACTV(V["RQ"][0:64, h1 * NT:(h1 + 1) * NT], PQ[0:64, 0:NT], AF.Copy,
                   r=pks, w=["RQ"])
              ACTV(V["RQ"][0:64, h2 * NT:(h2 + 1) * NT], PQ[64:128, 0:NT], AF.Copy,
                   r=pks, w=["RQ"])

      def rope_apply(src, srck, dst, dstk):
          TS(V["RH"][0:32, :], src[32:64, :], -1.0, ALU.mult, r=[srck], w=["RH"])
          CPY(V["RH"][32:64, :], src[0:32, :], r=[srck], w=["RH"])
          TT(V["RT1"][0:64, :], src, V["COS"][0:64, :], ALU.mult,
             r=[srck, "COS"], w=["RT1"])
          TT(V["RT2"][0:64, :], V["RH"][0:64, :], V["SIN"][0:64, :], ALU.mult,
             r=["RH", "SIN"], w=["RT2"])
          TT(dst, V["RT1"][0:64, :], V["RT2"][0:64, :], ALU.add,
             r=["RT1", "RT2"], w=[dstk])

      for h in range(H):
          rope_apply(V["RQ"][0:64, h * NT:(h + 1) * NT], "RQ",
                     V["RQ"][0:64, h * NT:(h + 1) * NT], "RQ")
          ACTV(V["RT1"][0:64, :], V["RQ"][0:64, h * NT:(h + 1) * NT], AF.Square,
               r=["RQ"], w=["RT1"])
          for hh in range(NH):
              rw, rwk = row(hh)
              MM(rw, V["ONESC"][0:64, :], V["RT1"][0:64, hh * 512: hh * 512 + 512],
                 True, True, r=["RT1", "ONESC"], w=[rwk])
          for qc in range(2):
              hh = (qc * CH) // 512
              c0 = qc * CH - hh * 512
              rw, rwk = row(hh)
              RED(V["ROWA"][0:1, 0:1], rw[0:1, c0:c0 + CH], ALU.max,
                  r=[rwk], w=["ROWA"])
              TT(V["QN2"][0:1, qc * 16 + h: qc * 16 + h + 1],
                 V["QN2"][0:1, qc * 16 + h: qc * 16 + h + 1],
                 V["ROWA"][0:1, 0:1], ALU.add, r=["QN2", "ROWA"], w=["QN2"])
      DMA("sp", GB_inA[2 * D: 2 * D + H * 64, :].rearrange(
              "(h p) t -> p h t", p=64),
          V["RQ"][0:64, :].rearrange("p (h t) -> p h t", h=H)[:, :, 0:CH],
          "gqr", r=["RQ"], w=["GB_inA"])
      DMA("sp", GB_inB[2 * D: 2 * D + H * 64, :].rearrange(
              "(h p) t -> p h t", p=64),
          V["RQ"][0:64, :].rearrange("p (h t) -> p h t", h=H)[:, :, CH:NT],
          "gqrb", r=["RQ"], w=["GB_inB"])

      for m in range(4):
          PQ, pk = (PA, "PA") if m % 2 == 0 else (PB, "PB")
          proj_mm(WKA[m], m, PQ, pk, V["NP"], "NP")
          pks = [pk + str(hh) for hh in range(NH)]
          ACTV(V["LATN"][:, m * NT:(m + 1) * NT], PQ[:, 0:NT], AF.Copy,
               r=pks, w=["LATN"])
          for hh in range(NH):
              sq = V["SQ0"] if hh == 0 else V["SQ1"]
              sqk = "SQ0" if hh == 0 else "SQ1"
              ACTV(sq[:], PQ[:, hh * 512: hh * 512 + 512], AF.Square,
                   r=[pk + str(hh)], w=[sqk])
              rw, rwk = row(hh)
              MM(rw, V["ONESC"][:], sq[:], m == 0, m == 3, r=[sqk, "ONESC"], w=[rwk])
      for hh in range(NH):
          rw, rwk = row(hh)
          dstrow = V["ROWA"] if hh == 0 else V["ROWB"]
          dk = "ROWA" if hh == 0 else "ROWB"
          TS(dstrow[:], rw, 1.0 / 512.0, ALU.mult, r=[rwk], w=[dk])
          ACTV(dstrow[:], dstrow[:], AF.Sqrt, bias=V["EPSB"][:],
               r=[dk, "EPSB"], w=[dk])
          RECIP(dstrow[:], dstrow[:], r=[dk], w=[dk])
      for hh in range(NH):
          dstrow = V["ROWA"] if hh == 0 else V["ROWB"]
          dk = "ROWA" if hh == 0 else "ROWB"
          MM(PC[:, hh * 512: hh * 512 + 512], V["ONESR"][:], dstrow[:], True, True,
             r=[dk, "ONESR"], w=["PC" + str(hh)])
      for m in range(4):
          for hh in range(NH):
              STT(V["LATN"][:, m * NT + hh * 512: m * NT + hh * 512 + 512],
                  V["LATN"][:, m * NT + hh * 512: m * NT + hh * 512 + 512],
                  V["KNW"][:, m:m + 1], PC[:, hh * 512: hh * 512 + 512],
                  ALU.mult, ALU.mult, r=["LATN", "KNW", "PC" + str(hh)], w=["LATN"])

      for hh in range(NH):
          for kc in range(DC):
              MM(PA[0:64, hh * 512: hh * 512 + 512],
                 V["WKARb"][:, kc * 64:(kc + 1) * 64],
                 V["NP"][:, kc * NT + hh * 512: kc * NT + hh * 512 + 512],
                 kc == 0, kc == DC - 1, r=["WKARb", "NP"], w=["PA" + str(hh)])
      ACTV(V["KRO"][0:64, :], PA[0:64, 0:NT], AF.Copy,
           r=["PA" + str(hh) for hh in range(NH)], w=["KRO"])
      rope_apply(V["KRO"][0:64, :], "KRO", V["KRO"][0:64, :], "KRO")
      DMA("sp", GB_inA[2 * D + H * 64: 2 * D + H * 64 + 64, :],
          V["KRO"][0:64, 0:CH], "gkr", r=["KRO"], w=["GB_inA"])
      DMA("sp", GB_inB[2 * D + H * 64: 2 * D + H * 64 + 64, :],
          V["KRO"][0:64, CH:NT], "gkrb", r=["KRO"], w=["GB_inB"])
      ACTV(V["RT1"][0:64, :], V["KRO"][0:64, :], AF.Square, r=["KRO"], w=["RT1"])
      for hh in range(NH):
          rw, rwk = row(hh)
          MM(rw, V["ONESC"][0:64, :], V["RT1"][0:64, hh * 512: hh * 512 + 512],
             True, True, r=["RT1", "ONESC"], w=[rwk])
          dst = V["ROWC"] if hh == 0 else V["ROWD"]
          dk = "ROWC" if hh == 0 else "ROWD"
          CPY(dst[:], rw, r=[rwk], w=[dk])

      emit_vstage_pending = []
      for m in range(32):
          PQ, pk = (PA, "PA") if m % 2 == 0 else (PB, "PB")
          proj_mm(WKB[m], m, PQ, pk, V["LATN"], "LATN")
          pks = [pk + str(hh) for hh in range(NH)]
          if m < 16:
              ACTV(V["STG"][:], PQ[:, 0:NT], AF.Copy, r=pks, w=["STG"])
              DMA("sp", GB_inA[D + m * 128: D + (m + 1) * 128, :],
                  V["STG"][:, 0:CH], "gkn", r=["STG"], w=["GB_inA"])
              DMA("sp", GB_inB[D + m * 128: D + (m + 1) * 128, :],
                  V["STG"][:, CH:NT], "gknb", r=["STG"], w=["GB_inB"])
              first_red = True
              for hh in range(NH):
                  sq = V["SQ0"] if hh == 0 else V["SQ1"]
                  sqk = "SQ0" if hh == 0 else "SQ1"
                  ACTV(sq[:], PQ[:, hh * 512: hh * 512 + 512], AF.Square,
                       r=[pk + str(hh)], w=[sqk])
                  rw, rwk = row(hh)
                  MM(rw, V["ONESC"][:], sq[:], True, True, r=[sqk, "ONESC"], w=[rwk])
                  ro = V["ROWC"] if hh == 0 else V["ROWD"]
                  rok = "ROWC" if hh == 0 else "ROWD"
                  sq2 = V["SQ0"] if hh == 0 else V["SQ1"]
                  TT(sq2[0:1, 0:512], rw, ro[:], ALU.add, r=[rwk, rok], w=[sqk])
                  RED(V["ROWB"][0:1, hh:hh + 1], sq2[0:1, 0:512], ALU.max,
                      r=[sqk], w=["ROWB"])
              if NH == 2:
                  TT(V["ROWB"][0:1, 0:1], V["ROWB"][0:1, 0:1], V["ROWB"][0:1, 1:2],
                     ALU.max, r=["ROWB"], w=["ROWB"])
              CPY(V["KMN"][0:1, m:m + 1], V["ROWB"][0:1, 0:1], r=["ROWB"], w=["KMN"])
          else:
              h = m - 16
              ACTV(V["STG"][:], PQ[:, 0:NT], AF.Copy, r=pks, w=["STG"])
              for i in range(NTT):
                  TRANS(PTB[:, 0:128], V["STG"][:, i * 128:(i + 1) * 128],
                        V["IDB"][:], r=["STG", "IDB"], w=["PTB"])
                  CPY(V["VSTG2"][:, i * H * 129 + h * 129:
                                 i * H * 129 + h * 129 + 128],
                      PTB[:, 0:128], r=["PTB"], w=["VSTG2"], eng="act")
      for i in range(NTT):
          MEMSET(V["VSTG2"][:, i * H * 129: (i + 1) * H * 129].rearrange(
              "p (h c) -> p h c", h=H)[:, :, 128:129].rearrange("p h c -> p (h c)"),
              1.0, r=[], w=["VSTG2"])
      DMA("sp", GVB_inA.rearrange("(i p) c -> p i c", p=128),
          V["VSTG2"].rearrange("p (i c) -> p i c", i=NTT)[:, 0:NTT // 2, :],
          "gvb", r=["VSTG2"], w=["GVB_inA"])
      DMA("sp", GVB_inB.rearrange("(i p) c -> p i c", p=128),
          V["VSTG2"].rearrange("p (i c) -> p i c", i=NTT)[:, NTT // 2:NTT, :],
          "gvbb", r=["VSTG2"], w=["GVB_inB"])
      MEMSET(V["CSTG"][:], 0.0, r=[], w=["CSTG"])
      ACTV(V["CSTG"][0:1, 0:16], V["KMN"][0:1, 0:16], AF.Copy,
           r=["KMN"], w=["CSTG"])
      DMA("sp", GB_inA[NRB - 1: NRB, :], V["CSTG"][0:1, 0:CH], "gkm",
          r=["CSTG"], w=["GB_inA"])
      MEMSET(V["CSTG2"][:], 0.0, r=[], w=["CSTG2"])
      ACTV(V["CSTG2"][0:1, 0:32], V["QN2"][0:1, 0:32], AF.Copy,
           r=["QN2"], w=["CSTG2"])
      DMA("sp", GB_inA[NRB - 2: NRB - 1, :], V["CSTG2"][0:1, 0:CH], "gqn2",
          r=["CSTG2"], w=["GB_inA"])
      COLL(GB_inA, GB_outA, "cc_gba", r=["GB_inA"], w=["GB_outA"])
      COLL(GVB_inA, GVB_outA, "cc_gvba", r=["GVB_inA"], w=["GVB_outA"])
      COLL(GB_inB, GB_outB, "cc_gbb", r=["GB_inB"], w=["GB_outB"])
      COLL(GVB_inB, GVB_outB, "cc_gvbb", r=["GVB_inB"], w=["GVB_outB"])

      P.barrier()

      pass
    if STAGE >= 2.8:
      def qr_bodies_f(gout, rowblk, qc):
          out = []
          for me in range(NCORE):
              pairs = []
              for s in range(NCORE):
                  for hl in range(2):
                      j = s if qc == 0 else NCH - 1 - s
                      r0 = s * rowblk + 2 * D + (2 * me + hl) * 64
                      src = gout[r0: r0 + 64, :]
                      d = V["QRB"][0:64, hl * T + j * CH: hl * T + (j + 1) * CH]
                      pairs.append((d, src))
              out.append(pairs)
          return out
      for s in range(NCORE):
          r0 = s * NRB + 2 * D + H * 64
          DMA("pool", V["KRB"][0:64, s * CH:(s + 1) * CH],
              GB_outA[r0: r0 + 64, :], "krl", r=["GB_outA"], w=["KRB_A"])
          r0 = s * (NRB - 2) + 2 * D + H * 64
          j = NCH - 1 - s
          DMA("pool", V["KRB"][0:64, j * CH:(j + 1) * CH],
              GB_outB[r0: r0 + 64, :], "krlb", r=["GB_outB"], w=["KRB_B"])
      SWDMA(merge_bodies(switched_heads_bodies(V["QNB"], GB_outA, NRB, 0, 0),
                         switched_heads_bodies(V["KNB"], GB_outA, NRB, D, 0),
                         qr_bodies_f(GB_outA, NRB, 0),
                         switched_V_bodies(V["VBB"], GVB_outA, 0),
                         switched_C_bodies(GB_outA, NRB, NRB - 2, V["CROW"]),
                         switched_KM_bodies(GB_outA, NRB, NRB - 1)),
            "swb", r=["GB_outA", "GVB_outA"],
            w=["KNB_A", "QNB_A", "QRB_A", "VBB_A", "CROW", "KMROW"])
      SWDMA(merge_bodies(switched_heads_bodies(V["QNB"], GB_outB, NRB - 2, 0, 1),
                         switched_heads_bodies(V["KNB"], GB_outB, NRB - 2, D, 1),
                         qr_bodies_f(GB_outB, NRB - 2, 1),
                         switched_V_bodies(V["VBB"], GVB_outB, 1)),
            "swbb", r=["GB_outB", "GVB_outB"],
            w=["KNB_B", "QNB_B", "QRB_B", "VBB_B"])
      bias_prep_b()
      if cfg.get("dump"):
          DBG1 = nc.dram_tensor("dbg_bias", [128, 64], F32, kind="ExternalOutput").ap()
          DBG2 = nc.dram_tensor("dbg_crow", [1, 32], F32, kind="ExternalOutput").ap()
          DBG3 = nc.dram_tensor("dbg_qn2", [1, 32], F32, kind="ExternalOutput").ap()
          DBG4 = nc.dram_tensor("dbg_kmx", [1, 16], F32, kind="ExternalOutput").ap()
          DBG5 = nc.dram_tensor("dbg_kqs", [128, 2048], BF16, kind="ExternalOutput").ap()
          DBG6 = nc.dram_tensor("dbg_qns", [128, 2048], BF16, kind="ExternalOutput").ap()
          DBG7 = nc.dram_tensor("dbg_vbs", [128, 2048], BF16, kind="ExternalOutput").ap()
          DBG8 = nc.dram_tensor("dbg_qrs", [64, 2048], BF16, kind="ExternalOutput").ap()
          DBG9 = nc.dram_tensor("dbg_krs", [64, 2048], BF16, kind="ExternalOutput").ap()
          DMA("pool", DBG1, V["BIAS"][:], "dbgw", r=["BIAS"], w=["DBG"])
          DMA("pool", DBG2, V["CROW"][0:1, 0:32], "dbgw", r=["CROW"], w=["DBG"])
          DMA("pool", DBG3, V["QN2"][0:1, 0:32], "dbgw", r=["QN2"], w=["DBG"])
          DMA("pool", DBG4, V["KMX"][0:1, 0:16], "dbgw", r=["KMX"], w=["DBG"])
          DMA("pool", DBG5, V["KNB"][:, 0:2048], "dbgw", r=["KNB"], w=["DBG"])
          DMA("pool", DBG6, V["QNB"][:, 0:2048], "dbgw", r=["QNB"], w=["DBG"])
          DMA("pool", DBG7, V["VBB"][:, 0:2048], "dbgw", r=["VBB"], w=["DBG"])
          DMA("pool", DBG8, V["QRB"][0:64, 0:2048], "dbgw", r=["QRB"], w=["DBG"])
          DMA("pool", DBG9, V["KRB"][0:64, 0:2048], "dbgw", r=["KRB"], w=["DBG"])
          DBG10 = nc.dram_tensor("dbg_gkm", [NCORE, 16], F32, kind="ExternalOutput").ap()
          DBG11 = nc.dram_tensor("dbg_kmn", [1, 128], F32, kind="ExternalOutput").ap()
          DBG12 = nc.dram_tensor("dbg_rowcd", [1, 1024], F32, kind="ExternalOutput").ap()
          DMA("pool", DBG10, GKM_out, "dbgw", r=["GKM_out"], w=["DBG"])
          DMA("pool", DBG11, V["KMN"][0:1, 0:128], "dbgw", r=["KMN"], w=["DBG"])
          DMA("pool", DBG12[0:1, 0:512], V["ROWC"][0:1, 0:512], "dbgw", r=["ROWC"], w=["DBG"])
          DMA("pool", DBG12[0:1, 512:1024], V["ROWD"][0:1, 0:512], "dbgw", r=["ROWD"], w=["DBG"])
          DBG13 = nc.dram_tensor("dbg_kro", [64, NT], BF16, kind="ExternalOutput").ap()
          DBG14 = nc.dram_tensor("dbg_latn", [128, NT], BF16, kind="ExternalOutput").ap()
          DBG15 = nc.dram_tensor("dbg_np", [128, NT], BF16, kind="ExternalOutput").ap()
          DMA("pool", DBG13, V["KRO"][0:64, :], "dbgw", r=["KRO"], w=["DBG"])
          DMA("pool", DBG14, V["LATN"][:, 0:NT], "dbgw", r=["LATN"], w=["DBG"])
          DMA("pool", DBG15, V["NP"][:, 0:NT], "dbgw", r=["NP"], w=["DBG"])
          op("sp", lambda e: e.nop(), r=["DBG"], w=[])

      pass
    if STAGE >= 3:
      def yb_coll(hl):
          if hl == 0:
              COLL(GYB_in0, GYB_out0, "cc_gyb0", r=["GYB0"], w=["GYB_out0"])
          else:
              COLL(GYB_in1, GYB_out1, "cc_gyb1", r=["GYB1"], w=["GYB_out1"])
      emit_attn(1, yb_coll)

      P.barrier()

      for hl in range(2):
          gyo = GYB_out0 if hl == 0 else GYB_out1
          gyok = "GYB_out0" if hl == 0 else "GYB_out1"
          yk = "YIN2" + str(hl)
          SWDMA(yret_bodies(gyo, V["YIN2"], hl), "swyb" + str(hl),
                r=[gyok], w=[yk])
          for s in range(NCORE):
              h = 2 * s + hl
              for i in range(NTT):
                  c0 = (s * NTT + i) * 256 + hl * 128
                  TRANS(PTB[:, 0:128], V["YIN2"][:, c0: c0 + 128],
                        V["IDB"][:], r=[yk, "IDB"], w=["PTB"])
                  CPY(V["YFM"][:, h * NT + i * 128: h * NT + (i + 1) * 128],
                      PTB[:, 0:128], r=["PTB"], w=["YFM"], eng="act")
      for dc in range(DC):
          wob = V["WOB0"] if dc % 2 == 0 else V["WOB1"]
          wbk = "WOB0" if dc % 2 == 0 else "WOB1"
          DMA("sp", wob[:], WOd[dc], "wo" + wbk, r=[], w=[wbk])
          ACTV(V["WOBB"][:], wob[:], AF.Copy, r=[wbk], w=["WOBB"])
          for hh in range(NH):
              for hv in range(H):
                  MM(PA[:, hh * 512: hh * 512 + 512],
                     V["WOBB"][:, hv * 128:(hv + 1) * 128],
                     V["YFM"][:, hv * NT + hh * 512: hv * NT + hh * 512 + 512],
                     hv == 0, hv == H - 1, r=["WOBB", "YFM"], w=["PA" + str(hh)])
          xw = V["XW0"] if dc % 2 == 0 else V["XW1"]
          xk = "XW0" if dc % 2 == 0 else "XW1"
          DMA("sp", xw[:], XST[:, dc * NT:(dc + 1) * NT], "dm" + xk,
              r=["XST"], w=[xk])
          op("dve", (lambda xw_: lambda e: e.tensor_add(
              xw_[:, 0:NT], xw_[:, 0:NT], PA[:, 0:NT]))(xw),
             r=[xk] + ["PA" + str(hh) for hh in range(NH)], w=[xk])
          DMA("sp", XST[:, dc * NT:(dc + 1) * NT], xw[:], "st" + xk,
              r=[xk], w=["XST"])

    P.barrier()

    if STAGE >= 4:
        emit_ffn(False)
    P.barrier()

    for dc in range(DC):
        xw = V["XW0"] if dc % 2 == 0 else V["XW1"]
        xk = "XW0" if dc % 2 == 0 else "XW1"
        DMA("sp", xw[:], XST[:, dc * NT:(dc + 1) * NT], "dm" + xk,
            r=["XST"], w=[xk])
        for i in range(NTT):
            os_ = V["OS0"] if i % 2 == 0 else V["OS1"]
            osk = "OS0" if i % 2 == 0 else "OS1"
            TRANS(PA[:, 0:128], xw[:, i * 128:(i + 1) * 128], V["IDF"][:],
                  r=[xk, "IDF"], w=["PA0"])
            CPY(os_[:], PA[:, 0:128], r=["PA0"], w=[osk])
            DMA("pool", OUT[i * 128:(i + 1) * 128, dc * 128:(dc + 1) * 128],
                os_[:], "outw", r=[osk], w=["OUTW"])
    op("sp", lambda e: e.nop(), r=["OUTW"], w=[])

    emit = P.finalize(nc, stack)

    with nc.Block() as block:
        def runner(items):
            def go(eng):
                for waits, fn, sem, inc, selfsem in items:
                    for s_, v_ in waits:
                        eng.wait_ge(s_, v_)
                    if selfsem is not None:
                        fn(eng, selfsem)
                    else:
                        inst = fn(eng)
                        if sem is not None:
                            inst.then_inc(sem, inc)
            return go
        block.tensor(runner(emit["pe"]))
        block.scalar(runner(emit["act"]))
        block.vector(runner(emit["dve"]))
        block.sync(runner(emit["sp"]))
        block.gpsimd(runner(emit["pool"]))

    stack.pop_all()
    return nc


def kernel(**inputs):
    cfg = make_cfg()
    maps = host_prep(cfg, inputs)
    nc = build(cfg)
    res = run_bass_kernel_spmd(nc, maps, core_ids=list(range(NCORE)))
    T, D, CH = cfg["T"], cfg["D"], cfg["CH"]
    out = np.zeros((T, D), np.float32)
    for me in range(NCORE):
        y = res.results[me]["out_own"]
        out[me * CH:(me + 1) * CH] = y[:CH]
        out[(NCH - 1 - me) * CH:(NCH - me) * CH] = y[CH:]
    return out.reshape(1, T, D)


# version 06
# speedup vs baseline: 1.1015x; 1.0897x over previous
import numpy as np
import ml_dtypes
from contextlib import ExitStack

import concourse.bass as bass
import concourse.mybir as mybir
from concourse.bass_utils import run_bass_kernel_spmd

F32 = mybir.dt.float32
BF16 = mybir.dt.bfloat16
AF = mybir.ActivationFunctionType
ALU = mybir.AluOpType
AX = mybir.AxisListType

NEGBIG = -1.0e30
NCORE = 8
NCH = 16


def make_cfg(T=8192, HID=8192):
    c = {}
    c["T"], c["D"], c["H"] = T, 2048, 16
    c["CH"] = T // NCH
    c["KS"] = c["CH"] // 128
    c["NT"] = 2 * c["CH"]
    c["NH"] = c["NT"] // 512
    c["DC"], c["HID"], c["JC"] = 16, HID, HID // 128
    c["LNEPS"], c["RMSEPS"] = 1e-5, 1e-6
    c["stage"] = 4
    assert c["NT"] % 512 == 0 and c["CH"] % 128 == 0
    return c


def gloc(j):
    return (j, 0) if j < NCORE else (NCH - 1 - j, 1)


def cidx(j):
    return j if j < NCORE else 8 + (NCH - 1 - j)



def prep_blocks(W):
    K, M = W.shape
    Kc, Mc = K // 128, M // 128
    return np.ascontiguousarray(
        W.reshape(Kc, 128, Mc, 128).transpose(2, 1, 0, 3).reshape(Mc, 128, Kc * 128))


def host_prep(cfg, inputs):
    T, D, H, CH = cfg["T"], cfg["D"], cfg["H"], cfg["CH"]
    JC, DC = cfg["JC"], cfg["DC"]
    x = np.asarray(inputs["L0_attn_a_x"], np.float32).reshape(T, D)
    xT = np.ascontiguousarray(x.T)

    wq = np.asarray(inputs["L0_attn_b_wq"], np.float32)
    nope_cols = np.concatenate([wq[:, 192 * h:192 * h + 128] for h in range(H)], 1)
    rope_cols = np.concatenate([wq[:, 192 * h + 128:192 * h + 192] for h in range(H)], 1)
    wq_blk = prep_blocks(np.concatenate([nope_cols, rope_cols], 1))

    wkva = np.asarray(inputs["L0_attn_b_wkv_a"], np.float32)
    wkva_main = prep_blocks(wkva[:, :512])
    wkva_rope = np.ascontiguousarray(
        wkva[:, 512:].reshape(16, 128, 64).transpose(1, 0, 2).reshape(128, 1024))

    wkvb = np.asarray(inputs["L0_attn_b_wkv_b"], np.float32)
    nope_c = np.concatenate([wkvb[:, 256 * h:256 * h + 128] for h in range(H)], 1)
    v_c = np.concatenate([wkvb[:, 256 * h + 128:256 * h + 256] for h in range(H)], 1)
    wkvb_blk = prep_blocks(np.concatenate([nope_c, v_c], 1))

    wo_blk = prep_blocks(np.asarray(inputs["L0_attn_b_wo"], np.float32))

    up_blk = prep_blocks(np.ascontiguousarray(
        np.asarray(inputs["L0_ffn_up_weight"], np.float32).T))
    down_blk = prep_blocks(np.ascontiguousarray(
        np.asarray(inputs["L0_ffn_down_weight"], np.float32).T))

    usc = np.ascontiguousarray(
        np.asarray(inputs["L0_ffn_up_scale"], np.float32).reshape(JC, 128).T)
    dsc = np.ascontiguousarray(
        np.asarray(inputs["L0_ffn_down_scale"], np.float32).reshape(DC, 128).T)
    knw = np.ascontiguousarray(
        np.asarray(inputs["L0_attn_b_kv_norm_w"], np.float32).reshape(4, 128).T)
    osc = np.asarray(inputs["L0_ffn_output_scale"], np.float32).reshape(1, 1)

    ones_col = np.ones((128, 1), np.float32)
    ones_row = np.ones((1, 128), np.float32)
    idf = np.eye(128, dtype=np.float32)
    idb = np.eye(128, dtype=ml_dtypes.bfloat16)
    tri = np.where(np.arange(128)[None, :] >= np.arange(128)[:, None],
                   0.0, NEGBIG).astype(np.float32)

    inv = 1.0 / (10000.0 ** (np.arange(0, 64, 2, dtype=np.float32) / 64.0))
    maps = []
    for me in range(NCORE):
        c0, c1 = me, NCH - 1 - me
        cols = np.concatenate([np.arange(c0 * CH, (c0 + 1) * CH),
                               np.arange(c1 * CH, (c1 + 1) * CH)])
        f = np.outer(inv, cols.astype(np.float32))
        emb = np.concatenate([f, f], 0)
        maps.append({
            "x0T_own": np.ascontiguousarray(xT[:, cols]),
            "wq_blk": wq_blk, "wkva_main": wkva_main, "wkva_rope": wkva_rope,
            "wkvb_blk": wkvb_blk, "wo_blk": wo_blk,
            "up_blk": up_blk, "down_blk": down_blk,
            "usc": usc, "dsc": dsc, "knw": knw, "osc": osc,
            "ones_col": ones_col, "ones_row": ones_row,
            "idf": idf, "idb": idb, "tri": tri,
            "cosq": np.cos(emb).astype(np.float32),
            "sinq": np.sin(emb).astype(np.float32),
        })
    return maps



ENGS = ("pe", "act", "dve", "sp", "pool")


class Op:
    __slots__ = ("eng", "fn", "reads", "writes", "sem_key", "inc", "idx",
                 "edges", "need_inc", "val", "clock", "self_inc")

    def __init__(self, eng, fn, reads, writes, sem_key, inc, idx, self_inc):
        self.eng, self.fn = eng, fn
        self.reads, self.writes = tuple(reads), tuple(writes)
        self.sem_key, self.inc, self.idx = sem_key, inc, idx
        self.self_inc = self_inc
        self.edges = []
        self.need_inc = False
        self.val = None
        self.clock = None


class Prog:
    def __init__(self):
        self.ops = []
        self.last_w = {}
        self.readers = {}
        self.pending = {e: [] for e in ENGS}

    def op(self, eng, fn, r=(), w=(), sem_key=None, inc=1, self_inc=False):
        o = Op(eng, fn, r, w, sem_key, inc, len(self.ops), self_inc)
        seen = {}
        for k in o.reads:
            p = self.last_w.get(k)
            if p is not None:
                seen[p.idx] = p
        for k in o.writes:
            p = self.last_w.get(k)
            if p is not None:
                seen[p.idx] = p
            for rd in self.readers.get(k, ()):
                seen[rd.idx] = rd
        for p in self.pending[eng]:
            seen[p.idx] = p
        self.pending[eng] = []
        seen.pop(o.idx, None)
        o.edges = list(seen.values())
        for k in o.writes:
            self.last_w[k] = o
            self.readers[k] = []
        for k in o.reads:
            self.readers.setdefault(k, []).append(o)
        self.ops.append(o)
        return o

    def barrier(self):
        lasts = {}
        for o in self.ops:
            key = ("g", o.sem_key) if o.sem_key is not None else ("eng", o.eng)
            lasts[key] = o
        alll = list(lasts.values())
        for e in ENGS:
            self.pending[e] = list(alll)

    def finalize(self, nc, stack):
        for o in self.ops:
            for p in o.edges:
                p.need_inc = True
            if o.sem_key is not None:
                o.need_inc = True
        sems = {}
        for e in ENGS:
            sems[("eng", e)] = stack.enter_context(nc.semaphore(f"s_{e}"))
        for o in self.ops:
            if o.sem_key is not None and ("g", o.sem_key) not in sems:
                sems[("g", o.sem_key)] = stack.enter_context(
                    nc.semaphore(f"s_{o.sem_key}"))
        counts = {}
        for o in self.ops:
            if not o.need_inc:
                continue
            key = ("g", o.sem_key) if o.sem_key is not None else ("eng", o.eng)
            counts[key] = counts.get(key, 0) + o.inc
            o.val = counts[key]
        eng_clock = {e: {} for e in ENGS}
        emit = {e: [] for e in ENGS}
        for o in self.ops:
            clk = eng_clock[o.eng]
            waits = []
            for p in sorted(o.edges, key=lambda q: q.idx):
                pkey = ("g", p.sem_key) if p.sem_key is not None else ("eng", p.eng)
                if clk.get(pkey, 0) < p.val:
                    waits.append((sems[pkey], p.val))
                    clk[pkey] = p.val
                if p.clock:
                    for k2, v2 in p.clock.items():
                        if clk.get(k2, 0) < v2:
                            clk[k2] = v2
            okey = ("g", o.sem_key) if o.sem_key is not None else ("eng", o.eng)
            if o.need_inc:
                c2 = dict(clk)
                c2[okey] = max(c2.get(okey, 0), o.val)
                o.clock = c2
            else:
                o.clock = dict(clk)
            sem = sems[okey] if (o.need_inc and not o.self_inc) else None
            selfsem = sems[okey] if (o.need_inc and o.self_inc) else None
            emit[o.eng].append((waits, o.fn, sem, o.inc, selfsem))
        return emit



def build(cfg):
    nc = bass.Bass("TRN2", target_bir_lowering=False)
    P = Prog()
    stack = ExitStack()
    T, D, H, CH, NT, NH = (cfg["T"], cfg["D"], cfg["H"], cfg["CH"], cfg["NT"],
                           cfg["NH"])
    KS, DC, JC, HID = cfg["KS"], cfg["DC"], cfg["JC"], cfg["HID"]
    TTn = T // 128
    NTT = NT // 128

    din = lambda n, s, dt=F32: nc.dram_tensor(n, list(s), dt, kind="ExternalInput").ap()
    dint = lambda n, s, dt: nc.dram_tensor(n, list(s), dt, kind="Internal").ap()

    x0T = din("x0T_own", [D, NT])
    WQ = din("wq_blk", [24, 128, D])
    WKA = din("wkva_main", [4, 128, D])
    WKAR = din("wkva_rope", [128, 1024])
    WKB = din("wkvb_blk", [32, 128, 512])
    WOd = din("wo_blk", [16, 128, D])
    UPB = din("up_blk", [JC, 128, D])
    DNB = din("down_blk", [16, 128, HID])
    cons_in = [("USC", din("usc", [128, JC])), ("DSC", din("dsc", [128, DC])),
               ("KNW", din("knw", [128, 4])), ("OSC", din("osc", [1, 1])),
               ("ONESC", din("ones_col", [128, 1])),
               ("ONESR", din("ones_row", [1, 128])),
               ("IDF", din("idf", [128, 128])),
               ("IDB", din("idb", [128, 128], BF16)),
               ("TRI", din("tri", [128, 128])),
               ("COS", din("cosq", [64, NT])), ("SIN", din("sinq", [64, NT])),
               ("WKARs", WKAR)]
    OUT = nc.dram_tensor("out_own", [NT, D], F32, kind="ExternalOutput").ap()

    XST = dint("XST", [128, DC * NT], F32)
    GA_inA = dint("GA_inA", [D + 1, CH], BF16)
    GA_inB = dint("GA_inB", [D, CH], BF16)
    GA_outA = dint("GA_outA", [NCORE * (D + 1), CH], BF16)
    GA_outB = dint("GA_outB", [NCORE * D, CH], BF16)
    GVA_inA = dint("GVA_inA", [CH, H * 128], BF16)
    GVA_inB = dint("GVA_inB", [CH, H * 128], BF16)
    GVA_outA = dint("GVA_outA", [NCORE * CH, H * 128], BF16)
    GVA_outB = dint("GVA_outB", [NCORE * CH, H * 128], BF16)
    GYA_in0 = dint("GYA_in0", [128, T], BF16)
    GYA_in1 = dint("GYA_in1", [128, T], BF16)
    GYA_out0 = dint("GYA_out0", [NCORE * 128, T], BF16)
    GYA_out1 = dint("GYA_out1", [NCORE * 128, T], BF16)
    NRB = 2 * D + H * 64 + 64 + 2
    GB_inA = dint("GB_inA", [NRB, CH], BF16)
    GB_inB = dint("GB_inB", [NRB - 2, CH], BF16)
    GB_outA = dint("GB_outA", [NCORE * NRB, CH], BF16)
    GB_outB = dint("GB_outB", [NCORE * (NRB - 2), CH], BF16)
    GVB_inA = dint("GVB_inA", [CH, H * 128], BF16)
    GVB_inB = dint("GVB_inB", [CH, H * 128], BF16)
    GVB_outA = dint("GVB_outA", [NCORE * CH, H * 128], BF16)
    GVB_outB = dint("GVB_outB", [NCORE * CH, H * 128], BF16)
    GYB_in0 = dint("GYB_in0", [128, T], BF16)
    GYB_in1 = dint("GYB_in1", [128, T], BF16)
    GYB_out0 = dint("GYB_out0", [NCORE * 128, T], BF16)
    GYB_out1 = dint("GYB_out1", [NCORE * 128, T], BF16)
    UWS = dint("UWS", [JC, 128, D], BF16)
    DWS = dint("DWS", [16, 128, HID], BF16)

    spec_pers = [
        ("ONESC", [128, 1], F32), ("ONESR", [1, 128], F32),
        ("IDF", [128, 128], F32), ("IDB", [128, 128], BF16),
        ("TRI", [128, 128], F32), ("USC", [128, JC], F32),
        ("DSC", [128, DC], F32), ("CSC", [128, DC], F32),
        ("KNW", [128, 4], F32), ("OSC", [1, 1], F32),
        ("COS", [64, NT], F32), ("SIN", [64, NT], F32),
        ("WKARs", [128, 1024], F32), ("WKARb", [128, 1024], BF16),
        ("BIAS", [128, 64], F32), ("CROW", [1, 32], BF16),
        ("CROWF", [1, 32], F32), ("QN2", [1, 32], F32),
        ("ROWA", [1, 512], F32), ("ROWB", [1, 512], F32),
        ("ROWC", [1, 512], F32), ("ROWD", [1, 512], F32),
        ("KMN", [1, 128], F32), ("KMX", [1, 16], F32), ("RL", [128, 1], F32),
        ("ROWB16", [1, 64], BF16), ("QROW", [1, 32], BF16), ("KMROW", [1, 16], BF16),
        ("CSTG", [1, NT], BF16), ("CSTG2", [1, NT], BF16),
        ("KH", [1, 2], F32),
        ("EPSA", [1, 1], F32), ("EPSB", [1, 1], F32),
        ("YS", [128, 512], F32), ("YB16", [128, 512], BF16),
        ("ONESB", [128, 1], BF16),
        ("XW0", [128, NT], F32), ("XW1", [128, NT], F32),
        ("YTA", [128, 128], F32), ("YTB", [128, 128], BF16),
        ("SQ0", [128, 512], F32), ("SQ1", [128, 512], F32),
    ]
    spec_a = [
        ("N", [128, DC * NT], BF16),
        ("VSTG", [128, NTT * H * 128], BF16),
        ("PSB0", [128, 2 * CH], BF16), ("PSB1", [128, 2 * CH], BF16),
    ]
    spec_a2 = [
        ("KQA", [128, 2 * T], BF16),
        ("VAA", [128, TTn * 256], BF16),
        ("YIN", [128, NCORE * 2 * NT], BF16),
        ("PSB0b", [128, 2 * CH], BF16), ("PSB1b", [128, 2 * CH], BF16),
    ]
    spec_f = [
        ("NF", [128, DC * NT], BF16),
        ("HSB", [128, JC * 512], BF16),
        ("UPIN0", [128, D], F32), ("UPIN1", [128, D], F32),
        ("UWSB0", [128, D], BF16), ("UWSB1", [128, D], BF16),
        ("DWSB0", [128, HID], BF16), ("DWSB1", [128, HID], BF16),
        ("DNIN", [128, HID // 2], F32, "UPIN0"),
    ]
    spec_p = [
        ("NP", [128, DC * NT], BF16),
        ("WBLK0", [128, D], F32), ("WBLK1", [128, D], F32),
        ("WBB0", [128, D], BF16), ("WBB1", [128, D], BF16),
        ("RQ", [64, H * NT], BF16),
        ("RH", [64, NT], F32), ("RT1", [64, NT], F32), ("RT2", [64, NT], F32),
        ("LATN", [128, 4 * NT], BF16),
        ("KRO", [64, NT], BF16),
        ("STG", [128, NT], BF16),
        ("VSTG2", [128, NTT * H * 128], BF16),
    ]
    spec_b = [
        ("KNB", [128, 2 * T], BF16),
        ("QNB", [128, 2 * T], BF16),
        ("QRB", [64, 2 * T], BF16),
        ("KRB", [64, T], BF16),
        ("VBB", [128, TTn * 256], BF16),
        ("PSB2", [128, 2 * CH], BF16), ("PSB3", [128, 2 * CH], BF16),
    ]
    spec_w = [
        ("YFM", [128, H * NT], BF16),
        ("WOB0", [128, D], F32), ("WOB1", [128, D], F32),
        ("WOBB", [128, D], BF16),
    ]
    spec_o = [("OS0", [128, 128], F32), ("OS1", [128, 128], F32)]

    def size_of(spec):
        off, offs = 0, {}
        for ent in spec:
            nm, sh, dt = ent[0], ent[1], ent[2]
            if len(ent) == 4:
                offs[nm] = offs[ent[3]]
                continue
            nb = int(np.prod(sh[1:])) * mybir.dt.size(dt)
            offs[nm] = off
            off += ((nb + 31) // 32) * 32
        return off, offs

    pers_sz, pers_off = size_of(spec_pers)
    arena_specs = [spec_a, spec_a2, spec_f, spec_p, spec_b, spec_w, spec_o]
    arena_sz = max(size_of(s)[0] for s in arena_specs)
    total = pers_sz + arena_sz

    backing = stack.enter_context(
        nc.sbuf_tensor("arena", [128, total], mybir.dt.uint8))
    base = nc.lookup_mloc(backing).addr

    V = {}

    def place(spec, offs, origin):
        for ent in spec:
            nm, sh, dt = ent[0], ent[1], ent[2]
            hnd = nc.virtual_sbuf_tensor("sb_" + nm, list(sh), dt)
            nc._tensor(hnd.name, list(hnd.shape), hnd.dtype, type="SB")
            mloc = nc.lookup_mloc(hnd)
            mloc.allocated = True
            mloc.addr = base + origin + offs[nm]
            V[nm] = hnd.ap()

    place(spec_pers, pers_off, 0)
    for sp_ in arena_specs:
        _, offs = size_of(sp_)
        place(sp_, offs, pers_sz)

    PA = stack.enter_context(nc.psum_tensor("PA", [128, 1024], F32)).ap()
    PB = stack.enter_context(nc.psum_tensor("PB", [128, 1024], F32)).ap()
    PC = stack.enter_context(nc.psum_tensor("PC", [128, 1024], F32)).ap()
    PD = stack.enter_context(nc.psum_tensor("PD", [128, 512], F32)).ap()
    PTB = stack.enter_context(nc.psum_tensor("PTB", [128, 512], BF16)).ap()

    op = P.op

    def MM(out, lhsT, rhs, start, stop, r, w):
        op("pe", (lambda o, l, rr, st, sp2: lambda e: e.matmul(
            o, l, rr, start=st, stop=sp2, skip_group_check=True))(
                out, lhsT, rhs, start, stop), r, w)

    def TRANS(out, in_, ident, r, w):
        op("pe", (lambda o, i_, id_: lambda e: e.transpose(o, i_, id_))(
            out, in_, ident), r, w)

    def ACTV(out, in_, func, r, w, bias=0.0, scale=1.0):
        op("act", (lambda o, i_, f, bb, s: lambda e: e.activation(
            o, i_, f, bias=bb, scale=s))(out, in_, func, bias, scale), r, w)

    def SIGN(out, in_, r, w):
        op("act", (lambda o, i_: lambda e: e.sign(o, i_))(out, in_), r, w)

    def TT(out, a, b, alu, r, w):
        op("dve", (lambda o, a_, b_, u: lambda e: e.tensor_tensor(o, a_, b_, u))(
            out, a, b, alu), r, w)

    def TS(out, a, s1, alu, r, w):
        op("dve", (lambda o, a_, s_, u: lambda e: e.tensor_single_scalar(
            o, a_, s_, u))(out, a, s1, alu), r, w)

    def TSM(out, a, s1, r, w):
        op("dve", (lambda o, a_, s_: lambda e: e.tensor_scalar_mul(o, a_, s_))(
            out, a, s1), r, w)

    def STT(out, in0, scal, in1, op0, op1, r, w):
        op("dve", (lambda o, i0, s_, i1, u0, u1: lambda e: e.scalar_tensor_tensor(
            o, i0, s_, i1, u0, u1))(out, in0, scal, in1, op0, op1), r, w)

    def RED(out, in_, alu, r, w):
        op("dve", (lambda o, i_, u: lambda e: e.tensor_reduce(o, i_, AX.X, u))(
            out, in_, alu), r, w)

    def RECIP(out, in_, r, w):
        op("dve", (lambda o, i_: lambda e: e.reciprocal(o, i_))(out, in_), r, w)

    def CPY(out, in_, r, w, eng="dve"):
        if eng == "act":
            op(eng, (lambda o, i_: lambda e: e.copy(o, i_))(out, in_), r, w)
        else:
            op(eng, (lambda o, i_: lambda e: e.tensor_copy(o, i_))(out, in_), r, w)

    def MEMSET(ap_, val, r, w):
        op("dve", (lambda a_, v_: lambda e: e.memset(a_, v_))(ap_, val), r, w)

    def DMA(eng, out, in_, slot, r, w):
        op(eng, (lambda o, i_: lambda e: e.dma_start(out=o, in_=i_))(out, in_),
           r, w, sem_key=slot, inc=16)

    def SWDMA(bodies, slot, r, w):
        n = len(bodies[0])
        assert all(len(bb) == n for bb in bodies)

        def fn(e, sem):
            if not hasattr(e, "_pid_cached"):
                e._pid_cached = e.compute_val(e.partition_id())
            idx = e._pid_cached
            for i in nc.Switch(engines=[e], index=[idx], n=NCORE):
                for (o_, i_) in bodies[i]:
                    e.dma_start(out=o_, in_=i_).then_inc(sem, 16)
        op("pool", fn, r, w, sem_key=slot, inc=16 * n, self_inc=True)

    def COLL(ins, outs, slot, r, w):
        op("pool", (lambda i_, o_: lambda e: e.collective_compute(
            "AllGather", ALU.bypass, replica_groups=[list(range(NCORE))],
            ins=[i_], outs=[o_]))(ins, outs), r, w, sem_key=slot, inc=1)

    def row(half):
        return (PD[0:1, 0:512], "PD") if half == 0 else (PC[0:1, 0:512], "PC0")

    for nm, src in cons_in:
        DMA("sp", V[nm][:], src, "kc_" + nm, r=[], w=[nm])
    MM(PD[:, 0:1], V["ONESR"][:], V["OSC"][:], True, True,
       r=["ONESR", "OSC"], w=["PD"])
    TSM(V["CSC"][:], V["DSC"][:], PD[:, 0:1], r=["DSC", "PD"], w=["CSC"])
    ACTV(V["WKARb"][:], V["WKARs"][:], AF.Copy, r=["WKARs"], w=["WKARb"])
    ACTV(V["ONESB"][:], V["ONESC"][:], AF.Copy, r=["ONESC"], w=["ONESB"])
    MEMSET(V["EPSA"][:], float(cfg["LNEPS"]), r=[], w=["EPSA"])
    MEMSET(V["EPSB"][:], float(cfg["RMSEPS"]), r=[], w=["EPSB"])
    DMA("sp", XST.rearrange("p (c t) -> p c t", c=DC),
        x0T.rearrange("(c p) t -> p c t", p=128), "x0", r=[], w=["XST"])

    def emit_ln(nbuf, nkey):
        for half in range(NH):
            hs = half * 512
            for kc in range(DC):
                xw = V["XW0"] if kc % 2 == 0 else V["XW1"]
                xk = "XW0" if kc % 2 == 0 else "XW1"
                DMA("sp", xw[:, 0:512], XST[:, kc * NT + hs: kc * NT + hs + 512],
                    "dm" + xk, r=["XST"], w=[xk])
                MM(PD[0:1, 0:512], V["ONESC"][:], xw[:, 0:512], kc == 0,
                   kc == DC - 1, r=[xk, "ONESC"], w=["PD"])
                sq = V["SQ0"] if kc % 2 == 0 else V["SQ1"]
                sqk = "SQ0" if kc % 2 == 0 else "SQ1"
                ACTV(sq[:], xw[:, 0:512], AF.Square, r=[xk], w=[sqk])
                MM(PC[0:1, 0:512], V["ONESC"][:], sq[:], kc == 0, kc == DC - 1,
                   r=[sqk, "ONESC"], w=["PC0"])
            TS(V["ROWA"][:], PD[0:1, 0:512], 1.0 / D, ALU.mult, r=["PD"], w=["ROWA"])
            TS(V["ROWB"][:], PC[0:1, 0:512], 1.0 / D, ALU.mult, r=["PC0"], w=["ROWB"])
            TT(V["ROWC"][:], V["ROWA"][:], V["ROWA"][:], ALU.mult,
               r=["ROWA"], w=["ROWC"])
            TT(V["ROWB"][:], V["ROWB"][:], V["ROWC"][:], ALU.subtract,
               r=["ROWB", "ROWC"], w=["ROWB"])
            ACTV(V["ROWB"][:], V["ROWB"][:], AF.Sqrt, bias=V["EPSA"][:],
                 r=["ROWB", "EPSA"], w=["ROWB"])
            RECIP(V["ROWB"][:], V["ROWB"][:], r=["ROWB"], w=["ROWB"])
            TT(V["ROWC"][:], V["ROWA"][:], V["ROWB"][:], ALU.mult,
               r=["ROWA", "ROWB"], w=["ROWC"])
            MM(PC[:, 0:512], V["ONESR"][:], V["ROWB"][:], True, True,
               r=["ROWB", "ONESR"], w=["PC0"])
            MM(PC[:, 512:1024], V["ONESR"][:], V["ROWC"][:], True, True,
               r=["ROWC", "ONESR"], w=["PC1"])
            for kc in range(DC):
                xw = V["XW0"] if kc % 2 == 0 else V["XW1"]
                xk = "XW0" if kc % 2 == 0 else "XW1"
                DMA("sp", xw[:, 0:512], XST[:, kc * NT + hs: kc * NT + hs + 512],
                    "dm" + xk, r=["XST"], w=[xk])
                sq = V["SQ0"] if kc % 2 == 0 else V["SQ1"]
                sqk = "SQ0" if kc % 2 == 0 else "SQ1"
                TT(sq[:], xw[:, 0:512], PC[:, 0:512], ALU.mult,
                   r=[xk, "PC0"], w=[sqk])
                TT(nbuf[:, kc * NT + hs: kc * NT + hs + 512], sq[:],
                   PC[:, 512:1024], ALU.subtract, r=[sqk, "PC1"], w=[nkey])

    def emit_vstage(vsrc, vsrck, vstg, vstgk, i0=0, i1=None):
        if i1 is None:
            i1 = NTT
        for h in range(H):
            for i in range(i0, i1):
                TRANS(PTB[:, 0:128],
                      vsrc[:, h * NT + i * 128: h * NT + (i + 1) * 128],
                      V["IDB"][:], r=[vsrck, "IDB"], w=["PTB"])
                CPY(vstg[:, (i * H + h) * 128: (i * H + h) * 128 + 128],
                    PTB[:, 0:128], r=["PTB"], w=[vstgk], eng="act")

    STAGE = cfg.get("stage", 4)

    if STAGE >= 1:
      emit_ln(V["N"], "N")
      for h in range(H):
          ACTV(V["SQ0"][:, 0:512], V["N"][:, h * NT: h * NT + 512], AF.Square,
               r=["N"], w=["SQ0"])
          MM(PD[0:1, 0:512], V["ONESC"][:], V["SQ0"][:], True, True,
             r=["SQ0", "ONESC"], w=["PD"])
          if NH == 2:
              ACTV(V["SQ1"][:, 0:512], V["N"][:, h * NT + 512: h * NT + 1024],
                   AF.Square, r=["N"], w=["SQ1"])
              MM(PC[0:1, 0:512], V["ONESC"][:], V["SQ1"][:], True, True,
                 r=["SQ1", "ONESC"], w=["PC0"])
          for qc in range(2):
              hh = (qc * CH) // 512
              c0 = qc * CH - hh * 512
              srcp, srck = (PD[0:1, c0:c0 + CH], "PD") if hh == 0 else                   (PC[0:1, c0:c0 + CH], "PC0")
              RED(V["ROWA"][0:1, 0:1], srcp, ALU.max, r=[srck], w=["ROWA"])
              ACTV(V["ROWA"][0:1, 0:1], V["ROWA"][0:1, 0:1], AF.Sqrt,
                   r=["ROWA"], w=["ROWA"])
              TS(V["ROWD"][0:1, qc * 16 + h: qc * 16 + h + 1],
                 V["ROWA"][0:1, 0:1], float(np.sqrt(D)), ALU.mult,
                 r=["ROWA"], w=["ROWD"])
      MEMSET(V["CSTG"][:], 0.0, r=[], w=["CSTG"])
      ACTV(V["CSTG"][0:1, 0:32], V["ROWD"][0:1, 0:32], AF.Copy,
           r=["ROWD"], w=["CSTG"])
      DMA("sp", GA_inA[D:D + 1, :], V["CSTG"][0:1, 0:CH], "cai",
          r=["CSTG"], w=["GA_inA"])
      DMA("sp", GA_inA[0:D, :].rearrange("(c p) t -> p c t", p=128),
          V["N"].rearrange("p (c t) -> p c t", c=DC)[:, :, 0:CH],
          "gai", r=["N"], w=["GA_inA"])
      DMA("sp", GA_inB.rearrange("(c p) t -> p c t", p=128),
          V["N"].rearrange("p (c t) -> p c t", c=DC)[:, :, CH:NT],
          "gaib", r=["N"], w=["GA_inB"])
      COLL(GA_inA, GA_outA, "cc_gaa", r=["GA_inA"], w=["GA_outA"])
      emit_vstage(V["N"], "N", V["VSTG"], "VSTG", 0, NTT // 2)
      DMA("sp", GVA_inA.rearrange("(i p) c -> p i c", p=128),
          V["VSTG"].rearrange("p (i c) -> p i c", i=NTT)[:, 0:NTT // 2, :],
          "gvia", r=["VSTG"], w=["GVA_inA"])
      COLL(GVA_inA, GVA_outA, "cc_gvaa", r=["GVA_inA"], w=["GVA_outA"])
      COLL(GA_inB, GA_outB, "cc_gab", r=["GA_inB"], w=["GA_outB"])
      emit_vstage(V["N"], "N", V["VSTG"], "VSTGB", NTT // 2, NTT)
      DMA("sp", GVA_inB.rearrange("(i p) c -> p i c", p=128),
          V["VSTG"].rearrange("p (i c) -> p i c", i=NTT)[:, NTT // 2:NTT, :],
          "gvib", r=["VSTGB"], w=["GVA_inB"])
      COLL(GVA_inB, GVA_outB, "cc_gvab", r=["GVA_inB"], w=["GVA_outB"])

      P.barrier()

      def switched_heads_bodies(dst, gout, rowblk, rowoff, qc):
          bodies = []
          for me in range(NCORE):
              pairs = []
              for s in range(NCORE):
                  for hl in range(2):
                      h = 2 * me + hl
                      j = s if qc == 0 else NCH - 1 - s
                      src = gout[s * rowblk + rowoff + h * 128:
                                 s * rowblk + rowoff + (h + 1) * 128, :]
                      d = dst[:, hl * T + j * CH: hl * T + (j + 1) * CH]
                      pairs.append((d, src))
              bodies.append(pairs)
          return bodies

      def switched_V_bodies(dst, gout, qc):
          bodies = []
          for me in range(NCORE):
              pairs = []
              for s in range(NCORE):
                  j = s if qc == 0 else NCH - 1 - s
                  src = gout[s * CH: (s + 1) * CH,
                             me * 256:(me + 1) * 256].rearrange(
                                 "(k p) c -> p k c", p=128)
                  d = dst[:, j * KS * 256:(j + 1) * KS * 256].rearrange(
                      "p (k c) -> p k c", k=KS)
                  pairs.append((d, src))
              bodies.append(pairs)
          return bodies

      def switched_C_bodies(gout, rowblk, rowoff, dsttile):
          bodies = []
          for me in range(NCORE):
              rows = gout.rearrange("(r x) t -> r x t", x=rowblk)[:, rowoff, 0:32]
              src = rows.rearrange("r (q h) -> q r h", q=2)[
                  :, :, 2 * me: 2 * me + 2].unsqueeze(0)
              dstc = dsttile[0:1, 0:32].rearrange(
                  "a (q r h) -> a q r h", q=2, r=NCORE)
              bodies.append([(dstc, src)])
          return bodies

      def switched_KM_bodies(gout, rowblk, rowoff):
          bodies = []
          for me in range(NCORE):
              rows = gout.rearrange("(r x) t -> r x t", x=rowblk)[
                  :, rowoff, 2 * me: 2 * me + 2].unsqueeze(0)
              dstc = V["KMROW"][0:1, 0:16].rearrange("a (r h) -> a r h", r=NCORE)
              bodies.append([(dstc, rows)])
          return bodies

      def merge_bodies(*lists):
          out = []
          for me in range(NCORE):
              pairs = []
              for L in lists:
                  pairs.extend(L[me])
              out.append(pairs)
          return out

      def bias_prep(attn):
          scale = float(192 ** -0.5) if attn else float(128 ** -0.5)
          ACTV(V["CROWF"][:], V["CROW"][0:1, 0:32], AF.Copy,
               r=["CROW"], w=["CROWF"])
          for hl in range(2):
              for j in range(NCH):
                  r_, qc = gloc(j)
                  src = V["CROWF"][0:1, 16 * qc + 2 * r_ + hl:
                                   16 * qc + 2 * r_ + hl + 1]
                  MM(PD[:, 0:1], V["ONESR"][:], src, True, True,
                     r=["CROWF", "ONESR"], w=["PD"])
                  col = attn * 32 + hl * 16 + cidx(j)
                  ACTV(V["BIAS"][:, col:col + 1], PD[:, 0:1], AF.Copy,
                       scale=-scale, r=["PD"], w=["BIAS"])

      SWDMA(merge_bodies(switched_heads_bodies(V["KQA"], GA_outA, D + 1, 0, 0),
                         switched_V_bodies(V["VAA"], GVA_outA, 0),
                         switched_C_bodies(GA_outA, D + 1, D, V["CROW"])),
            "swa", r=["GA_outA", "GVA_outA"], w=["KQA_A", "VAA_A", "CROW"])
      SWDMA(merge_bodies(switched_heads_bodies(V["KQA"], GA_outB, D, 0, 1),
                         switched_V_bodies(V["VAA"], GVA_outB, 1)),
            "swab", r=["GA_outB", "GVA_outB"], w=["KQA_B", "VAA_B"])
      bias_prep(0)

      def bias_prep_b():
          scale = float(192 ** -0.5)
          ACTV(V["CROWF"][:], V["CROW"][0:1, 0:32], AF.Copy,
               r=["CROW"], w=["CROWF"])
          ACTV(V["KMX"][0:1, 0:16], V["KMROW"][0:1, 0:16], AF.Copy,
               r=["KMROW"], w=["KMX"])
          for hl in range(2):
              RED(V["KH"][0:1, hl:hl + 1],
                  V["KMX"].rearrange("a (r h) -> a h r", h=2)[0:1, hl, 0:NCORE],
                  ALU.max, r=["KMX"], w=["KH"])
          for hl in range(2):
              for j in range(NCH):
                  r_, qc = gloc(j)
                  TT(V["ROWA"][0:1, 0:1],
                     V["CROWF"][0:1, 16 * qc + 2 * r_ + hl:
                                16 * qc + 2 * r_ + hl + 1],
                     V["KH"][0:1, hl:hl + 1], ALU.mult,
                     r=["CROWF", "KH"], w=["ROWA"])
                  ACTV(V["ROWA"][0:1, 0:1], V["ROWA"][0:1, 0:1], AF.Sqrt,
                       r=["ROWA"], w=["ROWA"])
                  MM(PD[:, 0:1], V["ONESR"][:], V["ROWA"][0:1, 0:1], True, True,
                     r=["ROWA", "ONESR"], w=["PD"])
                  col = 32 + hl * 16 + cidx(j)
                  ACTV(V["BIAS"][:, col:col + 1], PD[:, 0:1], AF.Copy,
                       scale=-scale, r=["PD"], w=["BIAS"])

      def emit_attn(b, on_hl=None):
          KQ = V["KNB"] if b else V["KQA"]
          QN = V["QNB"] if b else V["KQA"]
          VA = V["VBB"] if b else V["VAA"]
          def kqk(c):
              base = "KNB" if b else "KQA"
              return base + ("_A" if c < NCORE else "_B")
          def qnk(c):
              base = "QNB" if b else "KQA"
              return base + ("_A" if c < NCORE else "_B")
          def vak(c):
              base = "VBB" if b else "VAA"
              return base + ("_A" if c < NCORE else "_B")
          def krk(c):
              return "KRB_A" if c < NCORE else "KRB_B"
          def qrk(c):
              return "QRB_A" if c < NCORE else "QRB_B"
          GY = (GYB_in0, GYB_in1) if b else (GYA_in0, GYA_in1)
          gyk = ("GYB0", "GYB1") if b else ("GYA0", "GYA1")
          scale = float(192 ** -0.5) if b else float(128 ** -0.5)
          psb = [V["PSB2"], V["PSB3"]] if b else [V["PSB0b"], V["PSB1b"]]
          psbk = ["PSB2", "PSB3"] if b else ["PSB0b", "PSB1b"]
          prc = [0]
          for hl in range(2):
              for j in range(NCH):
                  yb = j % 2
                  ybk = "PC" + str(yb)
                  def emit_pv(st):
                      (PSBt_, psbkt_, pr_, kc_, nsub_) = st
                      for s2 in range(nsub_):
                          ks = 2 * pr_ + s2
                          first = (kc_ == 0 and ks == 0)
                          last = (kc_ == j and ks == KS - 1)
                          lv = VA[:, (kc_ * KS + ks) * 256 + hl * 128:
                                  (kc_ * KS + ks) * 256 + hl * 128 + 128]
                          rp = PSBt_[:, s2 * CH: s2 * CH + CH]
                          MM(PC[:, yb * 512: yb * 512 + CH], lv, rp,
                             first, last, r=[psbkt_, vak(kc_)], w=[ybk])
                          MM(PD[0:1, 0:CH], V["ONESB"][:], rp, first, last,
                             r=[psbkt_, "ONESB"], w=["PD"])
                  pending = None
                  for kc in range(j + 1):
                      for pr in range((KS + 1) // 2):
                          par = prc[0] % 2
                          prc[0] += 1
                          SP_ = PA if par == 0 else PB
                          spk = "PA" if par == 0 else "PB"
                          PSBt, psbkt = psb[par], psbk[par]
                          nsub = min(2, KS - 2 * pr)
                          for s2 in range(nsub):
                              ks = 2 * pr + s2
                              lt = KQ[:, hl * T + kc * CH + ks * 128:
                                      hl * T + kc * CH + ks * 128 + 128]
                              rh_ = QN[:, hl * T + j * CH: hl * T + (j + 1) * CH]
                              bnk = spk + str(s2)
                              MM(SP_[:, s2 * CH:(s2 + 1) * CH], lt, rh_,
                                 True, not b, r=[kqk(kc), qnk(j)], w=[bnk])
                              if b:
                                  ltr = V["KRB"][0:64, kc * CH + ks * 128:
                                                 kc * CH + ks * 128 + 128]
                                  rhr = V["QRB"][0:64, hl * T + j * CH:
                                                 hl * T + (j + 1) * CH]
                                  MM(SP_[:, s2 * CH:(s2 + 1) * CH], ltr, rhr,
                                     False, True, r=[krk(kc), qrk(j)], w=[bnk])
                              if kc == j:
                                  sl = SP_[:, s2 * CH + ks * 128:
                                           s2 * CH + ks * 128 + 128]
                                  TT(sl, sl, V["TRI"][:], ALU.add,
                                     r=[bnk, "TRI"], w=[bnk])
                          bcol = (32 if b else 0) + hl * 16 + cidx(j)
                          rk = [spk + "0"] + ([spk + "1"] if nsub == 2 else [])
                          ACTV(PSBt[:, 0:nsub * CH], SP_[:, 0:nsub * CH], AF.Exp,
                               bias=V["BIAS"][:, bcol:bcol + 1], scale=scale,
                               r=rk + ["BIAS"], w=[psbkt])
                          if kc == j:
                              for s2 in range(nsub):
                                  ks = 2 * pr + s2
                                  if ks > 0:
                                      MEMSET(PSBt[:, s2 * CH: s2 * CH + ks * 128],
                                             0.0, r=[psbkt], w=[psbkt])
                          if pending is not None:
                              emit_pv(pending)
                          pending = (PSBt, psbkt, pr, kc, nsub)
                  if pending is not None:
                      emit_pv(pending)
                  RECIP(V["ROWA"][0:1, 0:CH], PD[0:1, 0:CH], r=["PD"], w=["ROWA"])
                  MM(PD[:, 0:CH], V["ONESR"][:], V["ROWA"][0:1, 0:CH], True, True,
                     r=["ROWA", "ONESR"], w=["PD"])
                  ACTV(V["YS"][:, 0:CH], PC[:, yb * 512: yb * 512 + CH], AF.Copy,
                       r=[ybk], w=["YS"])
                  TT(V["YB16"][:, 0:CH], V["YS"][:, 0:CH], PD[:, 0:CH], ALU.mult,
                     r=["YS", "PD"], w=["YB16"])
                  DMA("pool", GY[hl][:, j * CH:(j + 1) * CH],
                      V["YB16"][:, 0:CH], "gy" + ("b" if b else "a") + str(hl),
                      r=["YB16"], w=[gyk[hl]])
                  if on_hl is not None and j == NCH - 1:
                      on_hl(hl)

      def ya_coll(hl):
          if hl == 0:
              COLL(GYA_in0, GYA_out0, "cc_gya0", r=["GYA0"], w=["GYA_out0"])
          else:
              COLL(GYA_in1, GYA_out1, "cc_gya1", r=["GYA1"], w=["GYA_out1"])
      emit_attn(0, ya_coll)

      def yret_bodies(gyout, yin, hl):
          bodies = []
          for me in range(NCORE):
              pairs = []
              for s in range(NCORE):
                  for qc in range(2):
                      ch = me if qc == 0 else NCH - 1 - me
                      src = gyout[s * 128:(s + 1) * 128, ch * CH:(ch + 1) * CH]
                      d = yin[:, (s * 2 + hl) * NT + qc * CH:
                              (s * 2 + hl) * NT + (qc + 1) * CH]
                      pairs.append((d, src))
              bodies.append(pairs)
          return bodies

      for hl in range(2):
          gyo = GYA_out0 if hl == 0 else GYA_out1
          gyok = "GYA_out0" if hl == 0 else "GYA_out1"
          yk = "YIN" + str(hl)
          SWDMA(yret_bodies(gyo, V["YIN"], hl), "swya" + str(hl),
                r=[gyok], w=[yk])
          for s in range(NCORE):
              dc = 2 * s + hl
              xw = V["XW0"] if dc % 2 == 0 else V["XW1"]
              xk = "XW0" if dc % 2 == 0 else "XW1"
              DMA("sp", xw[:], XST[:, dc * NT:(dc + 1) * NT], "dm" + xk,
                  r=["XST"], w=[xk])
              TT(xw[:], xw[:],
                 V["YIN"][:, (s * 2 + hl) * NT:(s * 2 + hl + 1) * NT],
                 ALU.add, r=[xk, yk], w=[xk])
              DMA("sp", XST[:, dc * NT:(dc + 1) * NT], xw[:], "st" + xk,
                  r=[xk], w=["XST"])

    P.barrier()

    def emit_ffn(first):
        emit_ln(V["NF"], "NF")
        for half in range(NH):
            hs = half * 512
            for jc in range(JC):
                upin = V["UPIN0"] if jc % 2 == 0 else V["UPIN1"]
                upk = "UPIN0" if jc % 2 == 0 else "UPIN1"
                uws = V["UWSB0"] if jc % 2 == 0 else V["UWSB1"]
                uwk = "UWSB0" if jc % 2 == 0 else "UWSB1"
                if first and half == 0:
                    DMA("sp", upin[:], UPB[jc], "up" + upk, r=[], w=[upk])
                    SIGN(uws[:], upin[:], r=[upk], w=[uwk])
                    DMA("pool", UWS[jc], uws[:], "uwsw", r=[uwk], w=["UWS"])
                else:
                    DMA("sp", uws[:], UWS[jc], "up" + uwk, r=["UWS"], w=[uwk])
                PU, puk = (PD, "PD") if jc % 2 == 0 else (PA[:, 0:512], "PA0")
                for kc in range(DC):
                    MM(PU[:, 0:512] if jc % 2 == 0 else PU,
                       uws[:, kc * 128:(kc + 1) * 128],
                       V["NF"][:, kc * NT + hs: kc * NT + hs + 512],
                       kc == 0, kc == DC - 1, r=[uwk, "NF"], w=[puk])
                op("act", (lambda jc_, PU_: lambda e: e.activation(
                    V["HSB"][:, jc_ * 512:(jc_ + 1) * 512], PU_, AF.Relu,
                    scale=V["USC"][:, jc_:jc_ + 1]))(
                        jc, PU[:, 0:512] if jc % 2 == 0 else PU),
                   r=[puk, "USC"], w=["HSB"])
            for dg in range(8):
                for dci in range(2):
                    dc = dg * 2 + dci
                    dws = V["DWSB0"] if dci == 0 else V["DWSB1"]
                    dwk = "DWSB0" if dci == 0 else "DWSB1"
                    if first and half == 0:
                        for hh2 in range(2):
                            DMA("sp", V["DNIN"][:],
                                DNB[dc][:, hh2 * (HID // 2):(hh2 + 1) * (HID // 2)],
                                "dnin", r=[], w=["DNIN", "UPIN0", "UPIN1"])
                            SIGN(dws[:, hh2 * (HID // 2):(hh2 + 1) * (HID // 2)],
                                 V["DNIN"][:], r=["DNIN"], w=[dwk])
                        DMA("pool", DWS[dc], dws[:], "dwsw", r=[dwk], w=["DWS"])
                    else:
                        DMA("sp", dws[:], DWS[dc], "dn" + dwk, r=["DWS"], w=[dwk])
                    pbk = "PB" + str(dci)
                    for jc in range(JC):
                        MM(PB[:, dci * 512: dci * 512 + 512],
                           dws[:, jc * 128:(jc + 1) * 128],
                           V["HSB"][:, jc * 512:(jc + 1) * 512],
                           jc == 0, jc == JC - 1, r=[dwk, "HSB"], w=[pbk])
                for dci in range(2):
                    dc = dg * 2 + dci
                    pbk = "PB" + str(dci)
                    xw = V["XW0"] if dci == 0 else V["XW1"]
                    xk = "XW0" if dci == 0 else "XW1"
                    DMA("sp", xw[:, 0:512],
                        XST[:, dc * NT + hs: dc * NT + hs + 512],
                        "dm" + xk, r=["XST"], w=[xk])
                    op("dve", (lambda dc_, dci_, xw_: lambda e:
                        e.scalar_tensor_tensor(
                            xw_[:, 0:512], PB[:, dci_ * 512: dci_ * 512 + 512],
                            V["CSC"][:, dc_:dc_ + 1], xw_[:, 0:512],
                            ALU.mult, ALU.add))(dc, dci, xw),
                       r=[pbk, "CSC", xk], w=[xk])
                    DMA("sp", XST[:, dc * NT + hs: dc * NT + hs + 512],
                        xw[:, 0:512], "st" + xk, r=[xk], w=["XST"])

    if STAGE >= 2:
        emit_ffn(True)
    P.barrier()

    if STAGE >= 2.5:
      emit_ln(V["NP"], "NP")

      def proj_mm(blk_dram, m, out_psum, pk, nsrc, nsrck, slot_extra=""):
          wb = V["WBLK0"] if m % 2 == 0 else V["WBLK1"]
          wk = "WBLK0" if m % 2 == 0 else "WBLK1"
          wbb = V["WBB0"] if m % 2 == 0 else V["WBB1"]
          wbbk = "WBB0" if m % 2 == 0 else "WBB1"
          nb = blk_dram.shape[1]
          DMA("sp", wb[:, 0:nb], blk_dram, "pr" + wk + slot_extra,
              r=[], w=[wk])
          ACTV(wbb[:, 0:nb], wb[:, 0:nb], AF.Copy, r=[wk], w=[wbbk])
          kcn = nb // 128
          for hh in range(NH):
              for kc in range(kcn):
                  MM(out_psum[:, hh * 512: hh * 512 + 512],
                     wbb[:, kc * 128:(kc + 1) * 128],
                     nsrc[:, kc * NT + hh * 512: kc * NT + hh * 512 + 512],
                     kc == 0, kc == kcn - 1, r=[wbbk, nsrck], w=[pk + str(hh)])

      for m in range(24):
          PQ, pk = (PA, "PA") if m % 2 == 0 else (PB, "PB")
          proj_mm(WQ[m], m, PQ, pk, V["NP"], "NP")
          pks = [pk + str(hh) for hh in range(NH)]
          if m < 16:
              ACTV(V["STG"][:], PQ[:, 0:NT], AF.Copy, r=pks, w=["STG"])
              DMA("sp", GB_inA[m * 128:(m + 1) * 128, :], V["STG"][:, 0:CH],
                  "gqn", r=["STG"], w=["GB_inA"])
              DMA("sp", GB_inB[m * 128:(m + 1) * 128, :], V["STG"][:, CH:NT],
                  "gqnb", r=["STG"], w=["GB_inB"])
              for hh in range(NH):
                  sq = V["SQ0"] if hh == 0 else V["SQ1"]
                  sqk = "SQ0" if hh == 0 else "SQ1"
                  ACTV(sq[:], PQ[:, hh * 512: hh * 512 + 512], AF.Square,
                       r=[pk + str(hh)], w=[sqk])
                  rw, rwk = row(hh)
                  MM(rw, V["ONESC"][:], sq[:], True, True, r=[sqk, "ONESC"], w=[rwk])
              for qc in range(2):
                  hh = (qc * CH) // 512
                  c0 = qc * CH - hh * 512
                  rw, rwk = row(hh)
                  RED(V["QN2"][0:1, qc * 16 + m: qc * 16 + m + 1],
                      rw[0:1, c0:c0 + CH], ALU.max, r=[rwk], w=["QN2"])
          else:
              h1, h2 = 2 * (m - 16), 2 * (m - 16) + 1
              ACTV(V["RQ"][0:64, h1 * NT:(h1 + 1) * NT], PQ[0:64, 0:NT], AF.Copy,
                   r=pks, w=["RQ"])
              ACTV(V["RQ"][0:64, h2 * NT:(h2 + 1) * NT], PQ[64:128, 0:NT], AF.Copy,
                   r=pks, w=["RQ"])

      def rope_apply(src, srck, dst, dstk):
          TS(V["RH"][0:32, :], src[32:64, :], -1.0, ALU.mult, r=[srck], w=["RH"])
          CPY(V["RH"][32:64, :], src[0:32, :], r=[srck], w=["RH"])
          TT(V["RT1"][0:64, :], src, V["COS"][0:64, :], ALU.mult,
             r=[srck, "COS"], w=["RT1"])
          TT(V["RT2"][0:64, :], V["RH"][0:64, :], V["SIN"][0:64, :], ALU.mult,
             r=["RH", "SIN"], w=["RT2"])
          TT(dst, V["RT1"][0:64, :], V["RT2"][0:64, :], ALU.add,
             r=["RT1", "RT2"], w=[dstk])

      for h in range(H):
          rope_apply(V["RQ"][0:64, h * NT:(h + 1) * NT], "RQ",
                     V["RQ"][0:64, h * NT:(h + 1) * NT], "RQ")
          ACTV(V["RT1"][0:64, :], V["RQ"][0:64, h * NT:(h + 1) * NT], AF.Square,
               r=["RQ"], w=["RT1"])
          for hh in range(NH):
              rw, rwk = row(hh)
              MM(rw, V["ONESC"][0:64, :], V["RT1"][0:64, hh * 512: hh * 512 + 512],
                 True, True, r=["RT1", "ONESC"], w=[rwk])
          for qc in range(2):
              hh = (qc * CH) // 512
              c0 = qc * CH - hh * 512
              rw, rwk = row(hh)
              RED(V["ROWA"][0:1, 0:1], rw[0:1, c0:c0 + CH], ALU.max,
                  r=[rwk], w=["ROWA"])
              TT(V["QN2"][0:1, qc * 16 + h: qc * 16 + h + 1],
                 V["QN2"][0:1, qc * 16 + h: qc * 16 + h + 1],
                 V["ROWA"][0:1, 0:1], ALU.add, r=["QN2", "ROWA"], w=["QN2"])
      DMA("sp", GB_inA[2 * D: 2 * D + H * 64, :].rearrange(
              "(h p) t -> p h t", p=64),
          V["RQ"][0:64, :].rearrange("p (h t) -> p h t", h=H)[:, :, 0:CH],
          "gqr", r=["RQ"], w=["GB_inA"])
      DMA("sp", GB_inB[2 * D: 2 * D + H * 64, :].rearrange(
              "(h p) t -> p h t", p=64),
          V["RQ"][0:64, :].rearrange("p (h t) -> p h t", h=H)[:, :, CH:NT],
          "gqrb", r=["RQ"], w=["GB_inB"])

      for m in range(4):
          PQ, pk = (PA, "PA") if m % 2 == 0 else (PB, "PB")
          proj_mm(WKA[m], m, PQ, pk, V["NP"], "NP")
          pks = [pk + str(hh) for hh in range(NH)]
          ACTV(V["LATN"][:, m * NT:(m + 1) * NT], PQ[:, 0:NT], AF.Copy,
               r=pks, w=["LATN"])
          for hh in range(NH):
              sq = V["SQ0"] if hh == 0 else V["SQ1"]
              sqk = "SQ0" if hh == 0 else "SQ1"
              ACTV(sq[:], PQ[:, hh * 512: hh * 512 + 512], AF.Square,
                   r=[pk + str(hh)], w=[sqk])
              rw, rwk = row(hh)
              MM(rw, V["ONESC"][:], sq[:], m == 0, m == 3, r=[sqk, "ONESC"], w=[rwk])
      for hh in range(NH):
          rw, rwk = row(hh)
          dstrow = V["ROWA"] if hh == 0 else V["ROWB"]
          dk = "ROWA" if hh == 0 else "ROWB"
          TS(dstrow[:], rw, 1.0 / 512.0, ALU.mult, r=[rwk], w=[dk])
          ACTV(dstrow[:], dstrow[:], AF.Sqrt, bias=V["EPSB"][:],
               r=[dk, "EPSB"], w=[dk])
          RECIP(dstrow[:], dstrow[:], r=[dk], w=[dk])
      for hh in range(NH):
          dstrow = V["ROWA"] if hh == 0 else V["ROWB"]
          dk = "ROWA" if hh == 0 else "ROWB"
          MM(PC[:, hh * 512: hh * 512 + 512], V["ONESR"][:], dstrow[:], True, True,
             r=[dk, "ONESR"], w=["PC" + str(hh)])
      for m in range(4):
          for hh in range(NH):
              STT(V["LATN"][:, m * NT + hh * 512: m * NT + hh * 512 + 512],
                  V["LATN"][:, m * NT + hh * 512: m * NT + hh * 512 + 512],
                  V["KNW"][:, m:m + 1], PC[:, hh * 512: hh * 512 + 512],
                  ALU.mult, ALU.mult, r=["LATN", "KNW", "PC" + str(hh)], w=["LATN"])

      for hh in range(NH):
          for kc in range(DC):
              MM(PA[0:64, hh * 512: hh * 512 + 512],
                 V["WKARb"][:, kc * 64:(kc + 1) * 64],
                 V["NP"][:, kc * NT + hh * 512: kc * NT + hh * 512 + 512],
                 kc == 0, kc == DC - 1, r=["WKARb", "NP"], w=["PA" + str(hh)])
      ACTV(V["KRO"][0:64, :], PA[0:64, 0:NT], AF.Copy,
           r=["PA" + str(hh) for hh in range(NH)], w=["KRO"])
      rope_apply(V["KRO"][0:64, :], "KRO", V["KRO"][0:64, :], "KRO")
      DMA("sp", GB_inA[2 * D + H * 64: 2 * D + H * 64 + 64, :],
          V["KRO"][0:64, 0:CH], "gkr", r=["KRO"], w=["GB_inA"])
      DMA("sp", GB_inB[2 * D + H * 64: 2 * D + H * 64 + 64, :],
          V["KRO"][0:64, CH:NT], "gkrb", r=["KRO"], w=["GB_inB"])
      ACTV(V["RT1"][0:64, :], V["KRO"][0:64, :], AF.Square, r=["KRO"], w=["RT1"])
      for hh in range(NH):
          rw, rwk = row(hh)
          MM(rw, V["ONESC"][0:64, :], V["RT1"][0:64, hh * 512: hh * 512 + 512],
             True, True, r=["RT1", "ONESC"], w=[rwk])
          dst = V["ROWC"] if hh == 0 else V["ROWD"]
          dk = "ROWC" if hh == 0 else "ROWD"
          CPY(dst[:], rw, r=[rwk], w=[dk])

      emit_vstage_pending = []
      for m in range(32):
          PQ, pk = (PA, "PA") if m % 2 == 0 else (PB, "PB")
          proj_mm(WKB[m], m, PQ, pk, V["LATN"], "LATN")
          pks = [pk + str(hh) for hh in range(NH)]
          if m < 16:
              ACTV(V["STG"][:], PQ[:, 0:NT], AF.Copy, r=pks, w=["STG"])
              DMA("sp", GB_inA[D + m * 128: D + (m + 1) * 128, :],
                  V["STG"][:, 0:CH], "gkn", r=["STG"], w=["GB_inA"])
              DMA("sp", GB_inB[D + m * 128: D + (m + 1) * 128, :],
                  V["STG"][:, CH:NT], "gknb", r=["STG"], w=["GB_inB"])
              first_red = True
              for hh in range(NH):
                  sq = V["SQ0"] if hh == 0 else V["SQ1"]
                  sqk = "SQ0" if hh == 0 else "SQ1"
                  ACTV(sq[:], PQ[:, hh * 512: hh * 512 + 512], AF.Square,
                       r=[pk + str(hh)], w=[sqk])
                  rw, rwk = row(hh)
                  MM(rw, V["ONESC"][:], sq[:], True, True, r=[sqk, "ONESC"], w=[rwk])
                  ro = V["ROWC"] if hh == 0 else V["ROWD"]
                  rok = "ROWC" if hh == 0 else "ROWD"
                  sq2 = V["SQ0"] if hh == 0 else V["SQ1"]
                  TT(sq2[0:1, 0:512], rw, ro[:], ALU.add, r=[rwk, rok], w=[sqk])
                  RED(V["ROWB"][0:1, hh:hh + 1], sq2[0:1, 0:512], ALU.max,
                      r=[sqk], w=["ROWB"])
              if NH == 2:
                  TT(V["ROWB"][0:1, 0:1], V["ROWB"][0:1, 0:1], V["ROWB"][0:1, 1:2],
                     ALU.max, r=["ROWB"], w=["ROWB"])
              CPY(V["KMN"][0:1, m:m + 1], V["ROWB"][0:1, 0:1], r=["ROWB"], w=["KMN"])
          else:
              h = m - 16
              ACTV(V["STG"][:], PQ[:, 0:NT], AF.Copy, r=pks, w=["STG"])
              for i in range(NTT):
                  TRANS(PTB[:, 0:128], V["STG"][:, i * 128:(i + 1) * 128],
                        V["IDB"][:], r=["STG", "IDB"], w=["PTB"])
                  CPY(V["VSTG2"][:, (i * H + h) * 128: (i * H + h) * 128 + 128],
                      PTB[:, 0:128], r=["PTB"], w=["VSTG2"], eng="act")
      DMA("sp", GVB_inA.rearrange("(i p) c -> p i c", p=128),
          V["VSTG2"].rearrange("p (i c) -> p i c", i=NTT)[:, 0:NTT // 2, :],
          "gvb", r=["VSTG2"], w=["GVB_inA"])
      DMA("sp", GVB_inB.rearrange("(i p) c -> p i c", p=128),
          V["VSTG2"].rearrange("p (i c) -> p i c", i=NTT)[:, NTT // 2:NTT, :],
          "gvbb", r=["VSTG2"], w=["GVB_inB"])
      MEMSET(V["CSTG"][:], 0.0, r=[], w=["CSTG"])
      ACTV(V["CSTG"][0:1, 0:16], V["KMN"][0:1, 0:16], AF.Copy,
           r=["KMN"], w=["CSTG"])
      DMA("sp", GB_inA[NRB - 1: NRB, :], V["CSTG"][0:1, 0:CH], "gkm",
          r=["CSTG"], w=["GB_inA"])
      MEMSET(V["CSTG2"][:], 0.0, r=[], w=["CSTG2"])
      ACTV(V["CSTG2"][0:1, 0:32], V["QN2"][0:1, 0:32], AF.Copy,
           r=["QN2"], w=["CSTG2"])
      DMA("sp", GB_inA[NRB - 2: NRB - 1, :], V["CSTG2"][0:1, 0:CH], "gqn2",
          r=["CSTG2"], w=["GB_inA"])
      COLL(GB_inA, GB_outA, "cc_gba", r=["GB_inA"], w=["GB_outA"])
      COLL(GVB_inA, GVB_outA, "cc_gvba", r=["GVB_inA"], w=["GVB_outA"])
      COLL(GB_inB, GB_outB, "cc_gbb", r=["GB_inB"], w=["GB_outB"])
      COLL(GVB_inB, GVB_outB, "cc_gvbb", r=["GVB_inB"], w=["GVB_outB"])

      P.barrier()

      pass
    if STAGE >= 2.8:
      def qr_bodies_f(gout, rowblk, qc):
          out = []
          for me in range(NCORE):
              pairs = []
              for s in range(NCORE):
                  for hl in range(2):
                      j = s if qc == 0 else NCH - 1 - s
                      r0 = s * rowblk + 2 * D + (2 * me + hl) * 64
                      src = gout[r0: r0 + 64, :]
                      d = V["QRB"][0:64, hl * T + j * CH: hl * T + (j + 1) * CH]
                      pairs.append((d, src))
              out.append(pairs)
          return out
      for s in range(NCORE):
          r0 = s * NRB + 2 * D + H * 64
          DMA("pool", V["KRB"][0:64, s * CH:(s + 1) * CH],
              GB_outA[r0: r0 + 64, :], "krl", r=["GB_outA"], w=["KRB_A"])
          r0 = s * (NRB - 2) + 2 * D + H * 64
          j = NCH - 1 - s
          DMA("pool", V["KRB"][0:64, j * CH:(j + 1) * CH],
              GB_outB[r0: r0 + 64, :], "krlb", r=["GB_outB"], w=["KRB_B"])
      SWDMA(merge_bodies(switched_heads_bodies(V["QNB"], GB_outA, NRB, 0, 0),
                         switched_heads_bodies(V["KNB"], GB_outA, NRB, D, 0),
                         qr_bodies_f(GB_outA, NRB, 0),
                         switched_V_bodies(V["VBB"], GVB_outA, 0),
                         switched_C_bodies(GB_outA, NRB, NRB - 2, V["CROW"]),
                         switched_KM_bodies(GB_outA, NRB, NRB - 1)),
            "swb", r=["GB_outA", "GVB_outA"],
            w=["KNB_A", "QNB_A", "QRB_A", "VBB_A", "CROW", "KMROW"])
      SWDMA(merge_bodies(switched_heads_bodies(V["QNB"], GB_outB, NRB - 2, 0, 1),
                         switched_heads_bodies(V["KNB"], GB_outB, NRB - 2, D, 1),
                         qr_bodies_f(GB_outB, NRB - 2, 1),
                         switched_V_bodies(V["VBB"], GVB_outB, 1)),
            "swbb", r=["GB_outB", "GVB_outB"],
            w=["KNB_B", "QNB_B", "QRB_B", "VBB_B"])
      bias_prep_b()
      if cfg.get("dump"):
          DBG1 = nc.dram_tensor("dbg_bias", [128, 64], F32, kind="ExternalOutput").ap()
          DBG2 = nc.dram_tensor("dbg_crow", [1, 32], F32, kind="ExternalOutput").ap()
          DBG3 = nc.dram_tensor("dbg_qn2", [1, 32], F32, kind="ExternalOutput").ap()
          DBG4 = nc.dram_tensor("dbg_kmx", [1, 16], F32, kind="ExternalOutput").ap()
          DBG5 = nc.dram_tensor("dbg_kqs", [128, 2048], BF16, kind="ExternalOutput").ap()
          DBG6 = nc.dram_tensor("dbg_qns", [128, 2048], BF16, kind="ExternalOutput").ap()
          DBG7 = nc.dram_tensor("dbg_vbs", [128, 2048], BF16, kind="ExternalOutput").ap()
          DBG8 = nc.dram_tensor("dbg_qrs", [64, 2048], BF16, kind="ExternalOutput").ap()
          DBG9 = nc.dram_tensor("dbg_krs", [64, 2048], BF16, kind="ExternalOutput").ap()
          DMA("pool", DBG1, V["BIAS"][:], "dbgw", r=["BIAS"], w=["DBG"])
          DMA("pool", DBG2, V["CROW"][0:1, 0:32], "dbgw", r=["CROW"], w=["DBG"])
          DMA("pool", DBG3, V["QN2"][0:1, 0:32], "dbgw", r=["QN2"], w=["DBG"])
          DMA("pool", DBG4, V["KMX"][0:1, 0:16], "dbgw", r=["KMX"], w=["DBG"])
          DMA("pool", DBG5, V["KNB"][:, 0:2048], "dbgw", r=["KNB"], w=["DBG"])
          DMA("pool", DBG6, V["QNB"][:, 0:2048], "dbgw", r=["QNB"], w=["DBG"])
          DMA("pool", DBG7, V["VBB"][:, 0:2048], "dbgw", r=["VBB"], w=["DBG"])
          DMA("pool", DBG8, V["QRB"][0:64, 0:2048], "dbgw", r=["QRB"], w=["DBG"])
          DMA("pool", DBG9, V["KRB"][0:64, 0:2048], "dbgw", r=["KRB"], w=["DBG"])
          DBG10 = nc.dram_tensor("dbg_gkm", [NCORE, 16], F32, kind="ExternalOutput").ap()
          DBG11 = nc.dram_tensor("dbg_kmn", [1, 128], F32, kind="ExternalOutput").ap()
          DBG12 = nc.dram_tensor("dbg_rowcd", [1, 1024], F32, kind="ExternalOutput").ap()
          DMA("pool", DBG10, GKM_out, "dbgw", r=["GKM_out"], w=["DBG"])
          DMA("pool", DBG11, V["KMN"][0:1, 0:128], "dbgw", r=["KMN"], w=["DBG"])
          DMA("pool", DBG12[0:1, 0:512], V["ROWC"][0:1, 0:512], "dbgw", r=["ROWC"], w=["DBG"])
          DMA("pool", DBG12[0:1, 512:1024], V["ROWD"][0:1, 0:512], "dbgw", r=["ROWD"], w=["DBG"])
          DBG13 = nc.dram_tensor("dbg_kro", [64, NT], BF16, kind="ExternalOutput").ap()
          DBG14 = nc.dram_tensor("dbg_latn", [128, NT], BF16, kind="ExternalOutput").ap()
          DBG15 = nc.dram_tensor("dbg_np", [128, NT], BF16, kind="ExternalOutput").ap()
          DMA("pool", DBG13, V["KRO"][0:64, :], "dbgw", r=["KRO"], w=["DBG"])
          DMA("pool", DBG14, V["LATN"][:, 0:NT], "dbgw", r=["LATN"], w=["DBG"])
          DMA("pool", DBG15, V["NP"][:, 0:NT], "dbgw", r=["NP"], w=["DBG"])
          op("sp", lambda e: e.nop(), r=["DBG"], w=[])

      pass
    if STAGE >= 3:
      def yb_coll(hl):
          if hl == 0:
              COLL(GYB_in0, GYB_out0, "cc_gyb0", r=["GYB0"], w=["GYB_out0"])
          else:
              COLL(GYB_in1, GYB_out1, "cc_gyb1", r=["GYB1"], w=["GYB_out1"])
      emit_attn(1, yb_coll)

      P.barrier()

      def yfm_bodies(gyout, hl):
          bodies = []
          for me in range(NCORE):
              pairs = []
              for s in range(NCORE):
                  h = 2 * s + hl
                  for qc in range(2):
                      ch = me if qc == 0 else NCH - 1 - me
                      src = gyout[s * 128:(s + 1) * 128, ch * CH:(ch + 1) * CH]
                      d = V["YFM"][:, h * NT + qc * CH: h * NT + (qc + 1) * CH]
                      pairs.append((d, src))
              bodies.append(pairs)
          return bodies

      for hl in range(2):
          gyo = GYB_out0 if hl == 0 else GYB_out1
          gyok = "GYB_out0" if hl == 0 else "GYB_out1"
          SWDMA(yfm_bodies(gyo, hl), "swyb" + str(hl),
                r=[gyok], w=["YFM"])
      for dc in range(DC):
          wob = V["WOB0"] if dc % 2 == 0 else V["WOB1"]
          wbk = "WOB0" if dc % 2 == 0 else "WOB1"
          DMA("sp", wob[:], WOd[dc], "wo" + wbk, r=[], w=[wbk])
          ACTV(V["WOBB"][:], wob[:], AF.Copy, r=[wbk], w=["WOBB"])
          for hh in range(NH):
              for hv in range(H):
                  MM(PA[:, hh * 512: hh * 512 + 512],
                     V["WOBB"][:, hv * 128:(hv + 1) * 128],
                     V["YFM"][:, hv * NT + hh * 512: hv * NT + hh * 512 + 512],
                     hv == 0, hv == H - 1, r=["WOBB", "YFM"], w=["PA" + str(hh)])
          xw = V["XW0"] if dc % 2 == 0 else V["XW1"]
          xk = "XW0" if dc % 2 == 0 else "XW1"
          DMA("sp", xw[:], XST[:, dc * NT:(dc + 1) * NT], "dm" + xk,
              r=["XST"], w=[xk])
          op("dve", (lambda xw_: lambda e: e.tensor_add(
              xw_[:, 0:NT], xw_[:, 0:NT], PA[:, 0:NT]))(xw),
             r=[xk] + ["PA" + str(hh) for hh in range(NH)], w=[xk])
          DMA("sp", XST[:, dc * NT:(dc + 1) * NT], xw[:], "st" + xk,
              r=[xk], w=["XST"])

    P.barrier()

    if STAGE >= 4:
        emit_ffn(False)
    P.barrier()

    for dc in range(DC):
        xw = V["XW0"] if dc % 2 == 0 else V["XW1"]
        xk = "XW0" if dc % 2 == 0 else "XW1"
        DMA("sp", xw[:], XST[:, dc * NT:(dc + 1) * NT], "dm" + xk,
            r=["XST"], w=[xk])
        for i in range(NTT):
            os_ = V["OS0"] if i % 2 == 0 else V["OS1"]
            osk = "OS0" if i % 2 == 0 else "OS1"
            TRANS(PA[:, 0:128], xw[:, i * 128:(i + 1) * 128], V["IDF"][:],
                  r=[xk, "IDF"], w=["PA0"])
            CPY(os_[:], PA[:, 0:128], r=["PA0"], w=[osk])
            DMA("pool", OUT[i * 128:(i + 1) * 128, dc * 128:(dc + 1) * 128],
                os_[:], "outw", r=[osk], w=["OUTW"])
    op("sp", lambda e: e.nop(), r=["OUTW"], w=[])

    emit = P.finalize(nc, stack)

    with nc.Block() as block:
        def runner(items):
            def go(eng):
                for waits, fn, sem, inc, selfsem in items:
                    for s_, v_ in waits:
                        eng.wait_ge(s_, v_)
                    if selfsem is not None:
                        fn(eng, selfsem)
                    else:
                        inst = fn(eng)
                        if sem is not None:
                            inst.then_inc(sem, inc)
            return go
        block.tensor(runner(emit["pe"]))
        block.scalar(runner(emit["act"]))
        block.vector(runner(emit["dve"]))
        block.sync(runner(emit["sp"]))
        block.gpsimd(runner(emit["pool"]))

    stack.pop_all()
    return nc


def kernel(**inputs):
    cfg = make_cfg()
    maps = host_prep(cfg, inputs)
    nc = build(cfg)
    res = run_bass_kernel_spmd(nc, maps, core_ids=list(range(NCORE)))
    T, D, CH = cfg["T"], cfg["D"], cfg["CH"]
    out = np.zeros((T, D), np.float32)
    for me in range(NCORE):
        y = res.results[me]["out_own"]
        out[me * CH:(me + 1) * CH] = y[:CH]
        out[(NCH - 1 - me) * CH:(NCH - me) * CH] = y[CH:]
    return out.reshape(1, T, D)


# version 07
# speedup vs baseline: 1.3683x; 1.2422x over previous
import numpy as np
import ml_dtypes
from contextlib import ExitStack

import concourse.bass as bass
import concourse.mybir as mybir
from concourse.bass_utils import run_bass_kernel_spmd

F32 = mybir.dt.float32
BF16 = mybir.dt.bfloat16
AF = mybir.ActivationFunctionType
ALU = mybir.AluOpType
AX = mybir.AxisListType

NEGBIG = -1.0e30
NCORE = 8
NCH = 16


def make_cfg(T=8192, HID=8192):
    c = {}
    c["T"], c["D"], c["H"] = T, 2048, 16
    c["CH"] = T // NCH
    c["KS"] = c["CH"] // 128
    c["NT"] = 2 * c["CH"]
    c["NH"] = c["NT"] // 512
    c["DC"], c["HID"], c["JC"] = 16, HID, HID // 128
    c["LNEPS"], c["RMSEPS"] = 1e-5, 1e-6
    c["stage"] = 4
    assert c["NT"] % 512 == 0 and c["CH"] % 128 == 0
    return c


def gloc(j):
    return (j, 0) if j < NCORE else (NCH - 1 - j, 1)


def cidx(j):
    return j if j < NCORE else 8 + (NCH - 1 - j)



def prep_blocks(W):
    K, M = W.shape
    Kc, Mc = K // 128, M // 128
    return np.ascontiguousarray(
        W.reshape(Kc, 128, Mc, 128).transpose(2, 1, 0, 3).reshape(Mc, 128, Kc * 128))


def host_prep(cfg, inputs):
    T, D, H, CH = cfg["T"], cfg["D"], cfg["H"], cfg["CH"]
    JC, DC = cfg["JC"], cfg["DC"]
    x = np.asarray(inputs["L0_attn_a_x"], np.float32).reshape(T, D)
    xT = np.ascontiguousarray(x.T)

    wq = np.asarray(inputs["L0_attn_b_wq"], np.float32)
    nope_cols = np.concatenate([wq[:, 192 * h:192 * h + 128] for h in range(H)], 1)
    rope_cols = np.concatenate([wq[:, 192 * h + 128:192 * h + 192] for h in range(H)], 1)
    wq_blk = prep_blocks(np.concatenate([nope_cols, rope_cols], 1))

    wkva = np.asarray(inputs["L0_attn_b_wkv_a"], np.float32)
    wkva_main = prep_blocks(wkva[:, :512])
    wkva_rope = np.ascontiguousarray(
        wkva[:, 512:].reshape(16, 128, 64).transpose(1, 0, 2).reshape(128, 1024))

    wkvb = np.asarray(inputs["L0_attn_b_wkv_b"], np.float32)
    nope_c = np.concatenate([wkvb[:, 256 * h:256 * h + 128] for h in range(H)], 1)
    v_c = np.concatenate([wkvb[:, 256 * h + 128:256 * h + 256] for h in range(H)], 1)
    wkvb_blk = prep_blocks(np.concatenate([nope_c, v_c], 1))

    wo_blk = prep_blocks(np.asarray(inputs["L0_attn_b_wo"], np.float32))

    up_blk = prep_blocks(np.ascontiguousarray(
        np.asarray(inputs["L0_ffn_up_weight"], np.float32).T))
    down_blk = prep_blocks(np.ascontiguousarray(
        np.asarray(inputs["L0_ffn_down_weight"], np.float32).T))

    usc = np.ascontiguousarray(
        np.asarray(inputs["L0_ffn_up_scale"], np.float32).reshape(JC, 128).T)
    dsc = np.ascontiguousarray(
        np.asarray(inputs["L0_ffn_down_scale"], np.float32).reshape(DC, 128).T)
    knw = np.ascontiguousarray(
        np.asarray(inputs["L0_attn_b_kv_norm_w"], np.float32).reshape(4, 128).T)
    osc = np.asarray(inputs["L0_ffn_output_scale"], np.float32).reshape(1, 1)

    ones_col = np.ones((128, 1), np.float32)
    ones_row = np.ones((1, 128), np.float32)
    idf = np.eye(128, dtype=np.float32)
    idb = np.eye(128, dtype=ml_dtypes.bfloat16)
    tri = np.where(np.arange(128)[None, :] >= np.arange(128)[:, None],
                   0.0, NEGBIG).astype(np.float32)

    inv = 1.0 / (10000.0 ** (np.arange(0, 64, 2, dtype=np.float32) / 64.0))
    maps = []
    for me in range(NCORE):
        c0, c1 = me, NCH - 1 - me
        cols = np.concatenate([np.arange(c0 * CH, (c0 + 1) * CH),
                               np.arange(c1 * CH, (c1 + 1) * CH)])
        f = np.outer(inv, cols.astype(np.float32))
        emb = np.concatenate([f, f], 0)
        maps.append({
            "x0T_own": np.ascontiguousarray(xT[:, cols]),
            "wq_blk": wq_blk, "wkva_main": wkva_main, "wkva_rope": wkva_rope,
            "wkvb_blk": wkvb_blk, "wo_blk": wo_blk,
            "up_blk": up_blk, "down_blk": down_blk,
            "usc": usc, "dsc": dsc, "knw": knw, "osc": osc,
            "ones_col": ones_col, "ones_row": ones_row,
            "idf": idf, "idb": idb, "tri": tri,
            "cosq": np.cos(emb).astype(np.float32),
            "sinq": np.sin(emb).astype(np.float32),
        })
    return maps



ENGS = ("pe", "act", "dve", "sp", "pool")


class Op:
    __slots__ = ("eng", "fn", "reads", "writes", "sem_key", "inc", "idx",
                 "edges", "need_inc", "val", "clock", "self_inc")

    def __init__(self, eng, fn, reads, writes, sem_key, inc, idx, self_inc):
        self.eng, self.fn = eng, fn
        self.reads, self.writes = tuple(reads), tuple(writes)
        self.sem_key, self.inc, self.idx = sem_key, inc, idx
        self.self_inc = self_inc
        self.edges = []
        self.need_inc = False
        self.val = None
        self.clock = None


class Prog:
    def __init__(self):
        self.ops = []
        self.last_w = {}
        self.readers = {}
        self.pending = {e: [] for e in ENGS}

    def op(self, eng, fn, r=(), w=(), sem_key=None, inc=1, self_inc=False):
        o = Op(eng, fn, r, w, sem_key, inc, len(self.ops), self_inc)
        seen = {}
        for k in o.reads:
            p = self.last_w.get(k)
            if p is not None:
                seen[p.idx] = p
        for k in o.writes:
            p = self.last_w.get(k)
            if p is not None:
                seen[p.idx] = p
            for rd in self.readers.get(k, ()):
                seen[rd.idx] = rd
        for p in self.pending[eng]:
            seen[p.idx] = p
        self.pending[eng] = []
        seen.pop(o.idx, None)
        o.edges = list(seen.values())
        for k in o.writes:
            self.last_w[k] = o
            self.readers[k] = []
        for k in o.reads:
            self.readers.setdefault(k, []).append(o)
        self.ops.append(o)
        return o

    def barrier(self):
        lasts = {}
        for o in self.ops:
            key = ("g", o.sem_key) if o.sem_key is not None else ("eng", o.eng)
            lasts[key] = o
        alll = list(lasts.values())
        for e in ENGS:
            self.pending[e] = list(alll)

    def finalize(self, nc, stack):
        for o in self.ops:
            for p in o.edges:
                p.need_inc = True
            if o.sem_key is not None:
                o.need_inc = True
        sems = {}
        for e in ENGS:
            sems[("eng", e)] = stack.enter_context(nc.semaphore(f"s_{e}"))
        for o in self.ops:
            if o.sem_key is not None and ("g", o.sem_key) not in sems:
                sems[("g", o.sem_key)] = stack.enter_context(
                    nc.semaphore(f"s_{o.sem_key}"))
        counts = {}
        for o in self.ops:
            if not o.need_inc:
                continue
            key = ("g", o.sem_key) if o.sem_key is not None else ("eng", o.eng)
            counts[key] = counts.get(key, 0) + o.inc
            o.val = counts[key]
        eng_clock = {e: {} for e in ENGS}
        emit = {e: [] for e in ENGS}
        for o in self.ops:
            clk = eng_clock[o.eng]
            waits = []
            for p in sorted(o.edges, key=lambda q: q.idx):
                pkey = ("g", p.sem_key) if p.sem_key is not None else ("eng", p.eng)
                if clk.get(pkey, 0) < p.val:
                    waits.append((sems[pkey], p.val))
                    clk[pkey] = p.val
                if p.clock:
                    for k2, v2 in p.clock.items():
                        if clk.get(k2, 0) < v2:
                            clk[k2] = v2
            okey = ("g", o.sem_key) if o.sem_key is not None else ("eng", o.eng)
            if o.need_inc:
                c2 = dict(clk)
                c2[okey] = max(c2.get(okey, 0), o.val)
                o.clock = c2
            else:
                o.clock = dict(clk)
            sem = sems[okey] if (o.need_inc and not o.self_inc) else None
            selfsem = sems[okey] if (o.need_inc and o.self_inc) else None
            emit[o.eng].append((waits, o.fn, sem, o.inc, selfsem))
        return emit



def build(cfg):
    nc = bass.Bass("TRN2", target_bir_lowering=False)
    P = Prog()
    stack = ExitStack()
    T, D, H, CH, NT, NH = (cfg["T"], cfg["D"], cfg["H"], cfg["CH"], cfg["NT"],
                           cfg["NH"])
    KS, DC, JC, HID = cfg["KS"], cfg["DC"], cfg["JC"], cfg["HID"]
    TTn = T // 128
    NTT = NT // 128

    din = lambda n, s, dt=F32: nc.dram_tensor(n, list(s), dt, kind="ExternalInput").ap()
    dint = lambda n, s, dt: nc.dram_tensor(n, list(s), dt, kind="Internal").ap()

    x0T = din("x0T_own", [D, NT])
    WQ = din("wq_blk", [24, 128, D])
    WKA = din("wkva_main", [4, 128, D])
    WKAR = din("wkva_rope", [128, 1024])
    WKB = din("wkvb_blk", [32, 128, 512])
    WOd = din("wo_blk", [16, 128, D])
    UPB = din("up_blk", [JC, 128, D])
    DNB = din("down_blk", [16, 128, HID])
    cons_in = [("USC", din("usc", [128, JC])), ("DSC", din("dsc", [128, DC])),
               ("KNW", din("knw", [128, 4])), ("OSC", din("osc", [1, 1])),
               ("ONESC", din("ones_col", [128, 1])),
               ("ONESR", din("ones_row", [1, 128])),
               ("IDF", din("idf", [128, 128])),
               ("IDB", din("idb", [128, 128], BF16)),
               ("TRI", din("tri", [128, 128])),
               ("COS", din("cosq", [64, NT])), ("SIN", din("sinq", [64, NT])),
               ("WKARs", WKAR)]
    OUT = nc.dram_tensor("out_own", [NT, D], F32, kind="ExternalOutput").ap()

    XST = dint("XST", [128, DC * NT], F32)
    GA_inA = dint("GA_inA", [D + 1, CH], BF16)
    GA_inB = dint("GA_inB", [D, CH], BF16)
    GA_outA = dint("GA_outA", [NCORE * (D + 1), CH], BF16)
    GA_outB = dint("GA_outB", [NCORE * D, CH], BF16)
    GVA_inA = dint("GVA_inA", [CH, H * 128], BF16)
    GVA_inB = dint("GVA_inB", [CH, H * 128], BF16)
    GVA_outA = dint("GVA_outA", [NCORE * CH, H * 128], BF16)
    GVA_outB = dint("GVA_outB", [NCORE * CH, H * 128], BF16)
    GYA_in0 = dint("GYA_in0", [128, T], BF16)
    GYA_in1 = dint("GYA_in1", [128, T], BF16)
    GYA_out0 = dint("GYA_out0", [NCORE * 128, T], BF16)
    GYA_out1 = dint("GYA_out1", [NCORE * 128, T], BF16)
    NRB = 2 * D + H * 64 + 64 + 2
    GB_inA = dint("GB_inA", [NRB, CH], BF16)
    GB_inB = dint("GB_inB", [NRB - 2, CH], BF16)
    GB_outA = dint("GB_outA", [NCORE * NRB, CH], BF16)
    GB_outB = dint("GB_outB", [NCORE * (NRB - 2), CH], BF16)
    GVB_inA = dint("GVB_inA", [CH, H * 128], BF16)
    GVB_inB = dint("GVB_inB", [CH, H * 128], BF16)
    GVB_outA = dint("GVB_outA", [NCORE * CH, H * 128], BF16)
    GVB_outB = dint("GVB_outB", [NCORE * CH, H * 128], BF16)
    GYB_in0 = dint("GYB_in0", [128, T], BF16)
    GYB_in1 = dint("GYB_in1", [128, T], BF16)
    GYB_out0 = dint("GYB_out0", [NCORE * 128, T], BF16)
    GYB_out1 = dint("GYB_out1", [NCORE * 128, T], BF16)
    UWS = dint("UWS", [JC, 128, D], BF16)
    DWS = dint("DWS", [16, 128, HID], BF16)

    spec_pers = [
        ("ONESC", [128, 1], F32), ("ONESR", [1, 128], F32),
        ("IDF", [128, 128], F32), ("IDB", [128, 128], BF16),
        ("TRI", [128, 128], F32), ("USC", [128, JC], F32),
        ("DSC", [128, DC], F32), ("CSC", [128, DC], F32),
        ("KNW", [128, 4], F32), ("OSC", [1, 1], F32),
        ("COS", [64, NT], F32), ("SIN", [64, NT], F32),
        ("WKARs", [128, 1024], F32), ("WKARb", [128, 1024], BF16),
        ("BIAS", [128, 64], F32), ("CROW", [1, 32], BF16),
        ("CROWF", [1, 32], F32), ("QN2", [1, 32], F32),
        ("ROWA", [1, 512], F32), ("ROWB", [1, 512], F32),
        ("ROWC", [1, 512], F32), ("ROWD", [1, 512], F32),
        ("KMN", [1, 128], F32), ("KMX", [1, 16], F32), ("RL", [128, 1], F32),
        ("ROWB16", [1, 64], BF16), ("QROW", [1, 32], BF16), ("KMROW", [1, 16], BF16),
        ("CSTG", [1, NT], BF16), ("CSTG2", [1, NT], BF16),
        ("KH", [1, 2], F32),
        ("EPSA", [1, 1], F32), ("EPSB", [1, 1], F32),
        ("YS", [128, 512], F32), ("YB16", [128, 512], BF16),
        ("ONESB", [128, 1], BF16),
        ("XW0", [128, NT], F32), ("XW1", [128, NT], F32),
        ("YTA", [128, 128], F32), ("YTB", [128, 128], BF16),
        ("SQ0", [128, 512], F32), ("SQ1", [128, 512], F32),
    ]
    spec_a = [
        ("N", [128, DC * NT], BF16),
        ("VSTG", [128, NTT * H * 128], BF16),
        ("PSB0", [128, 2 * CH], BF16), ("PSB1", [128, 2 * CH], BF16),
    ]
    spec_a2 = [
        ("KQA", [128, 2 * T], BF16),
        ("VAA", [128, TTn * 256], BF16),
        ("YIN", [128, NCORE * 2 * NT], BF16),
        ("PSB0b", [128, 2 * CH], BF16), ("PSB1b", [128, 2 * CH], BF16),
    ]
    spec_f = [
        ("NF", [128, DC * NT], BF16),
        ("HSB", [128, JC * 512], BF16),
        ("UPIN0", [128, D], F32), ("UPIN1", [128, D], F32),
        ("UWSB0", [128, D], BF16), ("UWSB1", [128, D], BF16),
        ("DWSB0", [128, HID], BF16), ("DWSB1", [128, HID], BF16),
        ("DNIN", [128, HID // 2], F32, "UPIN0"),
    ]
    spec_p = [
        ("NP", [128, DC * NT], BF16),
        ("WBLK0", [128, D], F32), ("WBLK1", [128, D], F32),
        ("WBB0", [128, D], BF16), ("WBB1", [128, D], BF16),
        ("RQ", [64, H * NT], BF16),
        ("RH", [64, NT], F32), ("RT1", [64, NT], F32), ("RT2", [64, NT], F32),
        ("LATN", [128, 4 * NT], BF16),
        ("KRO", [64, NT], BF16),
        ("STG", [128, NT], BF16),
        ("VSTG2", [128, NTT * H * 128], BF16),
    ]
    spec_b = [
        ("KNB", [128, 2 * T], BF16),
        ("QNB", [128, 2 * T], BF16),
        ("QRB", [64, 2 * T], BF16),
        ("KRB", [64, T], BF16),
        ("VBB", [128, TTn * 256], BF16),
        ("PSB2", [128, 2 * CH], BF16), ("PSB3", [128, 2 * CH], BF16),
    ]
    spec_w = [
        ("YFM", [128, H * NT], BF16),
        ("WOB0", [128, D], F32), ("WOB1", [128, D], F32),
        ("WOBB", [128, D], BF16),
    ]
    spec_o = [("OS0", [128, 128], F32), ("OS1", [128, 128], F32)]

    def size_of(spec):
        off, offs = 0, {}
        for ent in spec:
            nm, sh, dt = ent[0], ent[1], ent[2]
            if len(ent) == 4:
                offs[nm] = offs[ent[3]]
                continue
            nb = int(np.prod(sh[1:])) * mybir.dt.size(dt)
            offs[nm] = off
            off += ((nb + 31) // 32) * 32
        return off, offs

    pers_sz, pers_off = size_of(spec_pers)
    arena_specs = [spec_a, spec_a2, spec_f, spec_p, spec_b, spec_w, spec_o]
    arena_sz = max(size_of(s)[0] for s in arena_specs)
    total = pers_sz + arena_sz

    backing = stack.enter_context(
        nc.sbuf_tensor("arena", [128, total], mybir.dt.uint8))
    base = nc.lookup_mloc(backing).addr

    V = {}

    def place(spec, offs, origin):
        for ent in spec:
            nm, sh, dt = ent[0], ent[1], ent[2]
            hnd = nc.virtual_sbuf_tensor("sb_" + nm, list(sh), dt)
            nc._tensor(hnd.name, list(hnd.shape), hnd.dtype, type="SB")
            mloc = nc.lookup_mloc(hnd)
            mloc.allocated = True
            mloc.addr = base + origin + offs[nm]
            V[nm] = hnd.ap()

    place(spec_pers, pers_off, 0)
    for sp_ in arena_specs:
        _, offs = size_of(sp_)
        place(sp_, offs, pers_sz)

    PA = stack.enter_context(nc.psum_tensor("PA", [128, 1024], F32)).ap()
    PB = stack.enter_context(nc.psum_tensor("PB", [128, 1024], F32)).ap()
    PC = stack.enter_context(nc.psum_tensor("PC", [128, 1024], F32)).ap()
    PD = stack.enter_context(nc.psum_tensor("PD", [128, 512], F32)).ap()
    PTB = stack.enter_context(nc.psum_tensor("PTB", [128, 512], BF16)).ap()

    op = P.op

    def MM(out, lhsT, rhs, start, stop, r, w):
        op("pe", (lambda o, l, rr, st, sp2: lambda e: e.matmul(
            o, l, rr, start=st, stop=sp2, skip_group_check=True))(
                out, lhsT, rhs, start, stop), r, w)

    def TRANS(out, in_, ident, r, w):
        op("pe", (lambda o, i_, id_: lambda e: e.transpose(o, i_, id_))(
            out, in_, ident), r, w)

    def ACTV(out, in_, func, r, w, bias=0.0, scale=1.0):
        op("act", (lambda o, i_, f, bb, s: lambda e: e.activation(
            o, i_, f, bias=bb, scale=s))(out, in_, func, bias, scale), r, w)

    def SIGN(out, in_, r, w):
        op("act", (lambda o, i_: lambda e: e.sign(o, i_))(out, in_), r, w)

    def TT(out, a, b, alu, r, w):
        op("dve", (lambda o, a_, b_, u: lambda e: e.tensor_tensor(o, a_, b_, u))(
            out, a, b, alu), r, w)

    def TS(out, a, s1, alu, r, w):
        op("dve", (lambda o, a_, s_, u: lambda e: e.tensor_single_scalar(
            o, a_, s_, u))(out, a, s1, alu), r, w)

    def TSM(out, a, s1, r, w):
        op("dve", (lambda o, a_, s_: lambda e: e.tensor_scalar_mul(o, a_, s_))(
            out, a, s1), r, w)

    def STT(out, in0, scal, in1, op0, op1, r, w):
        op("dve", (lambda o, i0, s_, i1, u0, u1: lambda e: e.scalar_tensor_tensor(
            o, i0, s_, i1, u0, u1))(out, in0, scal, in1, op0, op1), r, w)

    def RED(out, in_, alu, r, w):
        op("dve", (lambda o, i_, u: lambda e: e.tensor_reduce(o, i_, AX.X, u))(
            out, in_, alu), r, w)

    def RECIP(out, in_, r, w):
        op("dve", (lambda o, i_: lambda e: e.reciprocal(o, i_))(out, in_), r, w)

    def CPY(out, in_, r, w, eng="dve"):
        if eng == "act":
            op(eng, (lambda o, i_: lambda e: e.copy(o, i_))(out, in_), r, w)
        else:
            op(eng, (lambda o, i_: lambda e: e.tensor_copy(o, i_))(out, in_), r, w)

    def MEMSET(ap_, val, r, w):
        op("dve", (lambda a_, v_: lambda e: e.memset(a_, v_))(ap_, val), r, w)

    def DMA(eng, out, in_, slot, r, w):
        op(eng, (lambda o, i_: lambda e: e.dma_start(out=o, in_=i_))(out, in_),
           r, w, sem_key=slot, inc=16)

    def SWDMA(bodies, slot, r, w):
        n = len(bodies[0])
        assert all(len(bb) == n for bb in bodies)

        def fn(e, sem):
            if not hasattr(e, "_pid_cached"):
                e._pid_cached = e.compute_val(e.partition_id())
            idx = e._pid_cached
            for i in nc.Switch(engines=[e], index=[idx], n=NCORE):
                for (o_, i_) in bodies[i]:
                    e.dma_start(out=o_, in_=i_).then_inc(sem, 16)
        op("pool", fn, r, w, sem_key=slot, inc=16 * n, self_inc=True)

    def COLL(ins, outs, slot, r, w):
        op("pool", (lambda i_, o_: lambda e: e.collective_compute(
            "AllGather", ALU.bypass, replica_groups=[list(range(NCORE))],
            ins=[i_], outs=[o_]))(ins, outs), r, w, sem_key=slot, inc=1)

    def row(half):
        return (PD[0:1, 0:512], "PD") if half == 0 else (PC[0:1, 0:512], "PC0")

    for nm, src in cons_in:
        DMA("sp", V[nm][:], src, "kc_" + nm, r=[], w=[nm])
    MM(PD[:, 0:1], V["ONESR"][:], V["OSC"][:], True, True,
       r=["ONESR", "OSC"], w=["PD"])
    TSM(V["CSC"][:], V["DSC"][:], PD[:, 0:1], r=["DSC", "PD"], w=["CSC"])
    ACTV(V["WKARb"][:], V["WKARs"][:], AF.Copy, r=["WKARs"], w=["WKARb"])
    ACTV(V["ONESB"][:], V["ONESC"][:], AF.Copy, r=["ONESC"], w=["ONESB"])
    MEMSET(V["EPSA"][:], float(cfg["LNEPS"]), r=[], w=["EPSA"])
    MEMSET(V["EPSB"][:], float(cfg["RMSEPS"]), r=[], w=["EPSB"])
    DMA("sp", XST.rearrange("p (c t) -> p c t", c=DC),
        x0T.rearrange("(c p) t -> p c t", p=128), "x0", r=[], w=["XST"])

    def emit_ln(nbuf, nkey):
        for half in range(NH):
            hs = half * 512
            for kc in range(DC):
                xw = V["XW0"] if kc % 2 == 0 else V["XW1"]
                xk = "XW0" if kc % 2 == 0 else "XW1"
                DMA("sp", xw[:, 0:512], XST[:, kc * NT + hs: kc * NT + hs + 512],
                    "dm" + xk, r=["XST"], w=[xk])
                MM(PD[0:1, 0:512], V["ONESC"][:], xw[:, 0:512], kc == 0,
                   kc == DC - 1, r=[xk, "ONESC"], w=["PD"])
                sq = V["SQ0"] if kc % 2 == 0 else V["SQ1"]
                sqk = "SQ0" if kc % 2 == 0 else "SQ1"
                ACTV(sq[:], xw[:, 0:512], AF.Square, r=[xk], w=[sqk])
                MM(PC[0:1, 0:512], V["ONESC"][:], sq[:], kc == 0, kc == DC - 1,
                   r=[sqk, "ONESC"], w=["PC0"])
            TS(V["ROWA"][:], PD[0:1, 0:512], 1.0 / D, ALU.mult, r=["PD"], w=["ROWA"])
            TS(V["ROWB"][:], PC[0:1, 0:512], 1.0 / D, ALU.mult, r=["PC0"], w=["ROWB"])
            TT(V["ROWC"][:], V["ROWA"][:], V["ROWA"][:], ALU.mult,
               r=["ROWA"], w=["ROWC"])
            TT(V["ROWB"][:], V["ROWB"][:], V["ROWC"][:], ALU.subtract,
               r=["ROWB", "ROWC"], w=["ROWB"])
            ACTV(V["ROWB"][:], V["ROWB"][:], AF.Sqrt, bias=V["EPSA"][:],
                 r=["ROWB", "EPSA"], w=["ROWB"])
            RECIP(V["ROWB"][:], V["ROWB"][:], r=["ROWB"], w=["ROWB"])
            TT(V["ROWC"][:], V["ROWA"][:], V["ROWB"][:], ALU.mult,
               r=["ROWA", "ROWB"], w=["ROWC"])
            MM(PC[:, 0:512], V["ONESR"][:], V["ROWB"][:], True, True,
               r=["ROWB", "ONESR"], w=["PC0"])
            MM(PC[:, 512:1024], V["ONESR"][:], V["ROWC"][:], True, True,
               r=["ROWC", "ONESR"], w=["PC1"])
            for kc in range(DC):
                xw = V["XW0"] if kc % 2 == 0 else V["XW1"]
                xk = "XW0" if kc % 2 == 0 else "XW1"
                DMA("sp", xw[:, 0:512], XST[:, kc * NT + hs: kc * NT + hs + 512],
                    "dm" + xk, r=["XST"], w=[xk])
                sq = V["SQ0"] if kc % 2 == 0 else V["SQ1"]
                sqk = "SQ0" if kc % 2 == 0 else "SQ1"
                TT(sq[:], xw[:, 0:512], PC[:, 0:512], ALU.mult,
                   r=[xk, "PC0"], w=[sqk])
                TT(nbuf[:, kc * NT + hs: kc * NT + hs + 512], sq[:],
                   PC[:, 512:1024], ALU.subtract, r=[sqk, "PC1"], w=[nkey])

    def emit_vstage(vsrc, vsrck, vstg, vstgk, i0=0, i1=None):
        if i1 is None:
            i1 = NTT
        for h in range(H):
            for i in range(i0, i1):
                TRANS(PTB[:, 0:128],
                      vsrc[:, h * NT + i * 128: h * NT + (i + 1) * 128],
                      V["IDB"][:], r=[vsrck, "IDB"], w=["PTB"])
                CPY(vstg[:, (i * H + h) * 128: (i * H + h) * 128 + 128],
                    PTB[:, 0:128], r=["PTB"], w=[vstgk], eng="act")

    STAGE = cfg.get("stage", 4)

    if STAGE >= 1:
      emit_ln(V["N"], "N")
      for h in range(H):
          ACTV(V["SQ0"][:, 0:512], V["N"][:, h * NT: h * NT + 512], AF.Square,
               r=["N"], w=["SQ0"])
          MM(PD[0:1, 0:512], V["ONESC"][:], V["SQ0"][:], True, True,
             r=["SQ0", "ONESC"], w=["PD"])
          if NH == 2:
              ACTV(V["SQ1"][:, 0:512], V["N"][:, h * NT + 512: h * NT + 1024],
                   AF.Square, r=["N"], w=["SQ1"])
              MM(PC[0:1, 0:512], V["ONESC"][:], V["SQ1"][:], True, True,
                 r=["SQ1", "ONESC"], w=["PC0"])
          for qc in range(2):
              hh = (qc * CH) // 512
              c0 = qc * CH - hh * 512
              srcp, srck = (PD[0:1, c0:c0 + CH], "PD") if hh == 0 else                   (PC[0:1, c0:c0 + CH], "PC0")
              RED(V["ROWA"][0:1, 0:1], srcp, ALU.max, r=[srck], w=["ROWA"])
              ACTV(V["ROWA"][0:1, 0:1], V["ROWA"][0:1, 0:1], AF.Sqrt,
                   r=["ROWA"], w=["ROWA"])
              TS(V["ROWD"][0:1, qc * 16 + h: qc * 16 + h + 1],
                 V["ROWA"][0:1, 0:1], float(np.sqrt(D)), ALU.mult,
                 r=["ROWA"], w=["ROWD"])
      MEMSET(V["CSTG"][:], 0.0, r=[], w=["CSTG"])
      ACTV(V["CSTG"][0:1, 0:32], V["ROWD"][0:1, 0:32], AF.Copy,
           r=["ROWD"], w=["CSTG"])
      DMA("sp", GA_inA[D:D + 1, :], V["CSTG"][0:1, 0:CH], "cai",
          r=["CSTG"], w=["GA_inA"])
      DMA("sp", GA_inA[0:D, :].rearrange("(c p) t -> p c t", p=128),
          V["N"].rearrange("p (c t) -> p c t", c=DC)[:, :, 0:CH],
          "gai", r=["N"], w=["GA_inA"])
      DMA("sp", GA_inB.rearrange("(c p) t -> p c t", p=128),
          V["N"].rearrange("p (c t) -> p c t", c=DC)[:, :, CH:NT],
          "gaib", r=["N"], w=["GA_inB"])
      COLL(GA_inA, GA_outA, "cc_gaa", r=["GA_inA"], w=["GA_outA"])
      emit_vstage(V["N"], "N", V["VSTG"], "VSTG", 0, NTT // 2)
      DMA("sp", GVA_inA.rearrange("(i p) c -> p i c", p=128),
          V["VSTG"].rearrange("p (i c) -> p i c", i=NTT)[:, 0:NTT // 2, :],
          "gvia", r=["VSTG"], w=["GVA_inA"])
      COLL(GVA_inA, GVA_outA, "cc_gvaa", r=["GVA_inA"], w=["GVA_outA"])
      COLL(GA_inB, GA_outB, "cc_gab", r=["GA_inB"], w=["GA_outB"])
      emit_vstage(V["N"], "N", V["VSTG"], "VSTGB", NTT // 2, NTT)
      DMA("sp", GVA_inB.rearrange("(i p) c -> p i c", p=128),
          V["VSTG"].rearrange("p (i c) -> p i c", i=NTT)[:, NTT // 2:NTT, :],
          "gvib", r=["VSTGB"], w=["GVA_inB"])
      COLL(GVA_inB, GVA_outB, "cc_gvab", r=["GVA_inB"], w=["GVA_outB"])

      P.barrier()

      def switched_heads_bodies(dst, gout, rowblk, rowoff, qc):
          bodies = []
          for me in range(NCORE):
              pairs = []
              for s in range(NCORE):
                  for hl in range(2):
                      h = 2 * me + hl
                      j = s if qc == 0 else NCH - 1 - s
                      src = gout[s * rowblk + rowoff + h * 128:
                                 s * rowblk + rowoff + (h + 1) * 128, :]
                      d = dst[:, hl * T + j * CH: hl * T + (j + 1) * CH]
                      pairs.append((d, src))
              bodies.append(pairs)
          return bodies

      def switched_V_bodies(dst, gout, qc):
          bodies = []
          for me in range(NCORE):
              pairs = []
              for s in range(NCORE):
                  j = s if qc == 0 else NCH - 1 - s
                  src = gout[s * CH: (s + 1) * CH,
                             me * 256:(me + 1) * 256].rearrange(
                                 "(k p) c -> p k c", p=128)
                  d = dst[:, j * KS * 256:(j + 1) * KS * 256].rearrange(
                      "p (k c) -> p k c", k=KS)
                  pairs.append((d, src))
              bodies.append(pairs)
          return bodies

      def switched_C_bodies(gout, rowblk, rowoff, dsttile):
          bodies = []
          for me in range(NCORE):
              rows = gout.rearrange("(r x) t -> r x t", x=rowblk)[:, rowoff, 0:32]
              src = rows.rearrange("r (q h) -> q r h", q=2)[
                  :, :, 2 * me: 2 * me + 2].unsqueeze(0)
              dstc = dsttile[0:1, 0:32].rearrange(
                  "a (q r h) -> a q r h", q=2, r=NCORE)
              bodies.append([(dstc, src)])
          return bodies

      def switched_KM_bodies(gout, rowblk, rowoff):
          bodies = []
          for me in range(NCORE):
              rows = gout.rearrange("(r x) t -> r x t", x=rowblk)[
                  :, rowoff, 2 * me: 2 * me + 2].unsqueeze(0)
              dstc = V["KMROW"][0:1, 0:16].rearrange("a (r h) -> a r h", r=NCORE)
              bodies.append([(dstc, rows)])
          return bodies

      def merge_bodies(*lists):
          out = []
          for me in range(NCORE):
              pairs = []
              for L in lists:
                  pairs.extend(L[me])
              out.append(pairs)
          return out

      def bias_prep(attn):
          scale = float(192 ** -0.5) if attn else float(128 ** -0.5)
          ACTV(V["CROWF"][:], V["CROW"][0:1, 0:32], AF.Copy,
               r=["CROW"], w=["CROWF"])
          for hl in range(2):
              for j in range(NCH):
                  r_, qc = gloc(j)
                  src = V["CROWF"][0:1, 16 * qc + 2 * r_ + hl:
                                   16 * qc + 2 * r_ + hl + 1]
                  MM(PD[:, 0:1], V["ONESR"][:], src, True, True,
                     r=["CROWF", "ONESR"], w=["PD"])
                  col = attn * 32 + hl * 16 + cidx(j)
                  ACTV(V["BIAS"][:, col:col + 1], PD[:, 0:1], AF.Copy,
                       scale=-scale, r=["PD"], w=["BIAS"])

      SWDMA(merge_bodies(switched_heads_bodies(V["KQA"], GA_outA, D + 1, 0, 0),
                         switched_V_bodies(V["VAA"], GVA_outA, 0),
                         switched_C_bodies(GA_outA, D + 1, D, V["CROW"])),
            "swa", r=["GA_outA", "GVA_outA"], w=["KQA_A", "VAA_A", "CROW"])
      SWDMA(merge_bodies(switched_heads_bodies(V["KQA"], GA_outB, D, 0, 1),
                         switched_V_bodies(V["VAA"], GVA_outB, 1)),
            "swab", r=["GA_outB", "GVA_outB"], w=["KQA_B", "VAA_B"])
      bias_prep(0)

      def bias_prep_b():
          scale = float(192 ** -0.5)
          ACTV(V["CROWF"][:], V["CROW"][0:1, 0:32], AF.Copy,
               r=["CROW"], w=["CROWF"])
          ACTV(V["KMX"][0:1, 0:16], V["KMROW"][0:1, 0:16], AF.Copy,
               r=["KMROW"], w=["KMX"])
          for hl in range(2):
              RED(V["KH"][0:1, hl:hl + 1],
                  V["KMX"].rearrange("a (r h) -> a h r", h=2)[0:1, hl, 0:NCORE],
                  ALU.max, r=["KMX"], w=["KH"])
          for hl in range(2):
              for j in range(NCH):
                  r_, qc = gloc(j)
                  TT(V["ROWA"][0:1, 0:1],
                     V["CROWF"][0:1, 16 * qc + 2 * r_ + hl:
                                16 * qc + 2 * r_ + hl + 1],
                     V["KH"][0:1, hl:hl + 1], ALU.mult,
                     r=["CROWF", "KH"], w=["ROWA"])
                  ACTV(V["ROWA"][0:1, 0:1], V["ROWA"][0:1, 0:1], AF.Sqrt,
                       r=["ROWA"], w=["ROWA"])
                  MM(PD[:, 0:1], V["ONESR"][:], V["ROWA"][0:1, 0:1], True, True,
                     r=["ROWA", "ONESR"], w=["PD"])
                  col = 32 + hl * 16 + cidx(j)
                  ACTV(V["BIAS"][:, col:col + 1], PD[:, 0:1], AF.Copy,
                       scale=-scale, r=["PD"], w=["BIAS"])

      def emit_attn(b, on_hl=None):
          KQ = V["KNB"] if b else V["KQA"]
          QN = V["QNB"] if b else V["KQA"]
          VA = V["VBB"] if b else V["VAA"]
          def kqk(c):
              base = "KNB" if b else "KQA"
              return base + ("_A" if c < NCORE else "_B")
          def qnk(c):
              base = "QNB" if b else "KQA"
              return base + ("_A" if c < NCORE else "_B")
          def vak(c):
              base = "VBB" if b else "VAA"
              return base + ("_A" if c < NCORE else "_B")
          def krk(c):
              return "KRB_A" if c < NCORE else "KRB_B"
          def qrk(c):
              return "QRB_A" if c < NCORE else "QRB_B"
          GY = (GYB_in0, GYB_in1) if b else (GYA_in0, GYA_in1)
          gyk = ("GYB0", "GYB1") if b else ("GYA0", "GYA1")
          scale = float(192 ** -0.5) if b else float(128 ** -0.5)
          psb = [V["PSB2"], V["PSB3"]] if b else [V["PSB0b"], V["PSB1b"]]
          psbk = ["PSB2", "PSB3"] if b else ["PSB0b", "PSB1b"]
          prc = [0]
          for hl in range(2):
              for j in range(NCH):
                  yb = j % 2
                  ybk = "PC" + str(yb)
                  def emit_pv(st):
                      (PSBt_, psbkt_, pr_, kc_, nsub_) = st
                      for s2 in range(nsub_):
                          ks = 2 * pr_ + s2
                          first = (kc_ == 0 and ks == 0)
                          last = (kc_ == j and ks == KS - 1)
                          lv = VA[:, (kc_ * KS + ks) * 256 + hl * 128:
                                  (kc_ * KS + ks) * 256 + hl * 128 + 128]
                          rp = PSBt_[:, s2 * CH: s2 * CH + CH]
                          MM(PC[:, yb * 512: yb * 512 + CH], lv, rp,
                             first, last, r=[psbkt_, vak(kc_)], w=[ybk])
                      for s2 in range(nsub_):
                          ks = 2 * pr_ + s2
                          first = (kc_ == 0 and ks == 0)
                          last = (kc_ == j and ks == KS - 1)
                          rp = PSBt_[:, s2 * CH: s2 * CH + CH]
                          MM(PD[0:1, 0:CH], V["ONESB"][:], rp, first, last,
                             r=[psbkt_, "ONESB"], w=["PD"])
                  pending = None
                  for kc in range(j + 1):
                      for pr in range((KS + 1) // 2):
                          par = prc[0] % 2
                          prc[0] += 1
                          SP_ = PA if par == 0 else PB
                          spk = "PA" if par == 0 else "PB"
                          PSBt, psbkt = psb[par], psbk[par]
                          nsub = min(2, KS - 2 * pr)
                          for s2 in range(nsub):
                              ks = 2 * pr + s2
                              lt = KQ[:, hl * T + kc * CH + ks * 128:
                                      hl * T + kc * CH + ks * 128 + 128]
                              rh_ = QN[:, hl * T + j * CH: hl * T + (j + 1) * CH]
                              bnk = spk + str(s2)
                              MM(SP_[:, s2 * CH:(s2 + 1) * CH], lt, rh_,
                                 True, not b, r=[kqk(kc), qnk(j)], w=[bnk])
                              if b:
                                  ltr = V["KRB"][0:64, kc * CH + ks * 128:
                                                 kc * CH + ks * 128 + 128]
                                  rhr = V["QRB"][0:64, hl * T + j * CH:
                                                 hl * T + (j + 1) * CH]
                                  MM(SP_[:, s2 * CH:(s2 + 1) * CH], ltr, rhr,
                                     False, True, r=[krk(kc), qrk(j)], w=[bnk])
                              if kc == j:
                                  sl = SP_[:, s2 * CH + ks * 128:
                                           s2 * CH + ks * 128 + 128]
                                  TT(sl, sl, V["TRI"][:], ALU.add,
                                     r=[bnk, "TRI"], w=[bnk])
                          bcol = (32 if b else 0) + hl * 16 + cidx(j)
                          rk = [spk + "0"] + ([spk + "1"] if nsub == 2 else [])
                          ACTV(PSBt[:, 0:nsub * CH], SP_[:, 0:nsub * CH], AF.Exp,
                               bias=V["BIAS"][:, bcol:bcol + 1], scale=scale,
                               r=rk + ["BIAS"], w=[psbkt])
                          if kc == j:
                              for s2 in range(nsub):
                                  ks = 2 * pr + s2
                                  if ks > 0:
                                      MEMSET(PSBt[:, s2 * CH: s2 * CH + ks * 128],
                                             0.0, r=[psbkt], w=[psbkt])
                          if pending is not None:
                              emit_pv(pending)
                          pending = (PSBt, psbkt, pr, kc, nsub)
                  if pending is not None:
                      emit_pv(pending)
                  RECIP(V["ROWA"][0:1, 0:CH], PD[0:1, 0:CH], r=["PD"], w=["ROWA"])
                  MM(PD[:, 0:CH], V["ONESR"][:], V["ROWA"][0:1, 0:CH], True, True,
                     r=["ROWA", "ONESR"], w=["PD"])
                  ACTV(V["YS"][:, 0:CH], PC[:, yb * 512: yb * 512 + CH], AF.Copy,
                       r=[ybk], w=["YS"])
                  TT(V["YB16"][:, 0:CH], V["YS"][:, 0:CH], PD[:, 0:CH], ALU.mult,
                     r=["YS", "PD"], w=["YB16"])
                  DMA("pool", GY[hl][:, j * CH:(j + 1) * CH],
                      V["YB16"][:, 0:CH], "gy" + ("b" if b else "a") + str(hl),
                      r=["YB16"], w=[gyk[hl]])
                  if on_hl is not None and j == NCH - 1:
                      on_hl(hl)

      def ya_coll(hl):
          if hl == 0:
              COLL(GYA_in0, GYA_out0, "cc_gya0", r=["GYA0"], w=["GYA_out0"])
          else:
              COLL(GYA_in1, GYA_out1, "cc_gya1", r=["GYA1"], w=["GYA_out1"])
      emit_attn(0, ya_coll)

      def yret_bodies(gyout, yin, hl):
          bodies = []
          for me in range(NCORE):
              pairs = []
              for s in range(NCORE):
                  for qc in range(2):
                      ch = me if qc == 0 else NCH - 1 - me
                      src = gyout[s * 128:(s + 1) * 128, ch * CH:(ch + 1) * CH]
                      d = yin[:, (s * 2 + hl) * NT + qc * CH:
                              (s * 2 + hl) * NT + (qc + 1) * CH]
                      pairs.append((d, src))
              bodies.append(pairs)
          return bodies

      for hl in range(2):
          gyo = GYA_out0 if hl == 0 else GYA_out1
          gyok = "GYA_out0" if hl == 0 else "GYA_out1"
          yk = "YIN" + str(hl)
          SWDMA(yret_bodies(gyo, V["YIN"], hl), "swya" + str(hl),
                r=[gyok], w=[yk])
          for s in range(NCORE):
              dc = 2 * s + hl
              xw = V["XW0"] if dc % 2 == 0 else V["XW1"]
              xk = "XW0" if dc % 2 == 0 else "XW1"
              DMA("sp", xw[:], XST[:, dc * NT:(dc + 1) * NT], "dm" + xk,
                  r=["XST"], w=[xk])
              TT(xw[:], xw[:],
                 V["YIN"][:, (s * 2 + hl) * NT:(s * 2 + hl + 1) * NT],
                 ALU.add, r=[xk, yk], w=[xk])
              DMA("sp", XST[:, dc * NT:(dc + 1) * NT], xw[:], "st" + xk,
                  r=[xk], w=["XST"])

    P.barrier()

    def emit_ffn(first):
        emit_ln(V["NF"], "NF")
        JH = JC // 2
        for ps in range(2):
            for jl in range(JH):
                jc = ps * JH + jl
                upin = V["UPIN0"] if jl % 2 == 0 else V["UPIN1"]
                upk = "UPIN0" if jl % 2 == 0 else "UPIN1"
                uws = V["UWSB0"] if jl % 2 == 0 else V["UWSB1"]
                uwk = "UWSB0" if jl % 2 == 0 else "UWSB1"
                if first:
                    DMA("sp", upin[:], UPB[jc], "up" + upk, r=[], w=[upk])
                    SIGN(uws[:], upin[:], r=[upk], w=[uwk])
                    DMA("pool", UWS[jc], uws[:], "uwsw", r=[uwk], w=["UWS"])
                else:
                    DMA("sp", uws[:], UWS[jc], "up" + uwk, r=["UWS"], w=[uwk])
                PU, puk = (PA, "PA") if jl % 2 == 0 else (PC, "PC")
                for kc in range(DC):
                    for hh in range(NH):
                        MM(PU[:, hh * 512: hh * 512 + 512],
                           uws[:, kc * 128:(kc + 1) * 128],
                           V["NF"][:, kc * NT + hh * 512: kc * NT + hh * 512 + 512],
                           kc == 0, kc == DC - 1,
                           r=[uwk, "NF"], w=[puk + str(hh)])
                for hh in range(NH):
                    op("act", (lambda jl_, hh_, PU_, jc_: lambda e: e.activation(
                        V["HSB"][:, jl_ * NH * 512 + hh_ * 512:
                                 jl_ * NH * 512 + hh_ * 512 + 512],
                        PU_[:, hh_ * 512: hh_ * 512 + 512], AF.Relu,
                        scale=V["USC"][:, jc_:jc_ + 1]))(jl, hh, PU, jc),
                       r=[puk + str(hh), "USC"], w=["HSB"])
            for dc in range(DC):
                dws = V["DWSB0"] if dc % 2 == 0 else V["DWSB1"]
                dwk = "DWSB0" if dc % 2 == 0 else "DWSB1"
                if first:
                    DMA("sp", V["DNIN"][:],
                        DNB[dc][:, ps * (HID // 2):(ps + 1) * (HID // 2)],
                        "dnin", r=[], w=["DNIN", "UPIN0", "UPIN1"])
                    SIGN(dws[:, 0:HID // 2], V["DNIN"][:], r=["DNIN"], w=[dwk])
                    DMA("pool", DWS[dc][:, ps * (HID // 2):(ps + 1) * (HID // 2)],
                        dws[:, 0:HID // 2], "dwsw", r=[dwk], w=["DWS"])
                else:
                    DMA("sp", dws[:, 0:HID // 2],
                        DWS[dc][:, ps * (HID // 2):(ps + 1) * (HID // 2)],
                        "dn" + dwk, r=["DWS"], w=[dwk])
                PDO, pdk = (PB, "PB") if dc % 2 == 0 else (PA, "PA")
                for jl in range(JH):
                    for hh in range(NH):
                        MM(PDO[:, hh * 512: hh * 512 + 512],
                           dws[:, jl * 128:(jl + 1) * 128],
                           V["HSB"][:, jl * NH * 512 + hh * 512:
                                    jl * NH * 512 + hh * 512 + 512],
                           jl == 0, jl == JH - 1,
                           r=[dwk, "HSB"], w=[pdk + str(hh)])
                for hh in range(NH):
                    hs = hh * 512
                    xw = V["XW0"] if dc % 2 == 0 else V["XW1"]
                    xk = "XW0" if dc % 2 == 0 else "XW1"
                    DMA("sp", xw[:, 0:512],
                        XST[:, dc * NT + hs: dc * NT + hs + 512],
                        "dm" + xk, r=["XST"], w=[xk])
                    op("dve", (lambda dc_, hh_, xw_, PDO_: lambda e:
                        e.scalar_tensor_tensor(
                            xw_[:, 0:512], PDO_[:, hh_ * 512: hh_ * 512 + 512],
                            V["CSC"][:, dc_:dc_ + 1], xw_[:, 0:512],
                            ALU.mult, ALU.add))(dc, hh, xw, PDO),
                       r=[pdk + str(hh), "CSC", xk], w=[xk])
                    DMA("sp", XST[:, dc * NT + hs: dc * NT + hs + 512],
                        xw[:, 0:512], "st" + xk, r=[xk], w=["XST"])

    if STAGE >= 2:
        emit_ffn(True)
    P.barrier()

    if STAGE >= 2.5:
      emit_ln(V["NP"], "NP")

      def proj_mm(blk_dram, m, out_psum, pk, nsrc, nsrck, slot_extra=""):
          wb = V["WBLK0"] if m % 2 == 0 else V["WBLK1"]
          wk = "WBLK0" if m % 2 == 0 else "WBLK1"
          wbb = V["WBB0"] if m % 2 == 0 else V["WBB1"]
          wbbk = "WBB0" if m % 2 == 0 else "WBB1"
          nb = blk_dram.shape[1]
          DMA("sp", wb[:, 0:nb], blk_dram, "pr" + wk + slot_extra,
              r=[], w=[wk])
          ACTV(wbb[:, 0:nb], wb[:, 0:nb], AF.Copy, r=[wk], w=[wbbk])
          kcn = nb // 128
          for kc in range(kcn):
              for hh in range(NH):
                  MM(out_psum[:, hh * 512: hh * 512 + 512],
                     wbb[:, kc * 128:(kc + 1) * 128],
                     nsrc[:, kc * NT + hh * 512: kc * NT + hh * 512 + 512],
                     kc == 0, kc == kcn - 1, r=[wbbk, nsrck], w=[pk + str(hh)])

      for m in range(24):
          PQ, pk = (PA, "PA") if m % 2 == 0 else (PB, "PB")
          proj_mm(WQ[m], m, PQ, pk, V["NP"], "NP")
          pks = [pk + str(hh) for hh in range(NH)]
          if m < 16:
              ACTV(V["STG"][:], PQ[:, 0:NT], AF.Copy, r=pks, w=["STG"])
              DMA("sp", GB_inA[m * 128:(m + 1) * 128, :], V["STG"][:, 0:CH],
                  "gqn", r=["STG"], w=["GB_inA"])
              DMA("sp", GB_inB[m * 128:(m + 1) * 128, :], V["STG"][:, CH:NT],
                  "gqnb", r=["STG"], w=["GB_inB"])
              for hh in range(NH):
                  sq = V["SQ0"] if hh == 0 else V["SQ1"]
                  sqk = "SQ0" if hh == 0 else "SQ1"
                  ACTV(sq[:], PQ[:, hh * 512: hh * 512 + 512], AF.Square,
                       r=[pk + str(hh)], w=[sqk])
                  rw, rwk = row(hh)
                  MM(rw, V["ONESC"][:], sq[:], True, True, r=[sqk, "ONESC"], w=[rwk])
              for qc in range(2):
                  hh = (qc * CH) // 512
                  c0 = qc * CH - hh * 512
                  rw, rwk = row(hh)
                  RED(V["QN2"][0:1, qc * 16 + m: qc * 16 + m + 1],
                      rw[0:1, c0:c0 + CH], ALU.max, r=[rwk], w=["QN2"])
          else:
              h1, h2 = 2 * (m - 16), 2 * (m - 16) + 1
              ACTV(V["RQ"][0:64, h1 * NT:(h1 + 1) * NT], PQ[0:64, 0:NT], AF.Copy,
                   r=pks, w=["RQ"])
              ACTV(V["RQ"][0:64, h2 * NT:(h2 + 1) * NT], PQ[64:128, 0:NT], AF.Copy,
                   r=pks, w=["RQ"])

      def rope_apply(src, srck, dst, dstk):
          TS(V["RH"][0:32, :], src[32:64, :], -1.0, ALU.mult, r=[srck], w=["RH"])
          CPY(V["RH"][32:64, :], src[0:32, :], r=[srck], w=["RH"])
          TT(V["RT1"][0:64, :], src, V["COS"][0:64, :], ALU.mult,
             r=[srck, "COS"], w=["RT1"])
          TT(V["RT2"][0:64, :], V["RH"][0:64, :], V["SIN"][0:64, :], ALU.mult,
             r=["RH", "SIN"], w=["RT2"])
          TT(dst, V["RT1"][0:64, :], V["RT2"][0:64, :], ALU.add,
             r=["RT1", "RT2"], w=[dstk])

      for h in range(H):
          rope_apply(V["RQ"][0:64, h * NT:(h + 1) * NT], "RQ",
                     V["RQ"][0:64, h * NT:(h + 1) * NT], "RQ")
          ACTV(V["RT1"][0:64, :], V["RQ"][0:64, h * NT:(h + 1) * NT], AF.Square,
               r=["RQ"], w=["RT1"])
          for hh in range(NH):
              rw, rwk = row(hh)
              MM(rw, V["ONESC"][0:64, :], V["RT1"][0:64, hh * 512: hh * 512 + 512],
                 True, True, r=["RT1", "ONESC"], w=[rwk])
          for qc in range(2):
              hh = (qc * CH) // 512
              c0 = qc * CH - hh * 512
              rw, rwk = row(hh)
              RED(V["ROWA"][0:1, 0:1], rw[0:1, c0:c0 + CH], ALU.max,
                  r=[rwk], w=["ROWA"])
              TT(V["QN2"][0:1, qc * 16 + h: qc * 16 + h + 1],
                 V["QN2"][0:1, qc * 16 + h: qc * 16 + h + 1],
                 V["ROWA"][0:1, 0:1], ALU.add, r=["QN2", "ROWA"], w=["QN2"])
      DMA("sp", GB_inA[2 * D: 2 * D + H * 64, :].rearrange(
              "(h p) t -> p h t", p=64),
          V["RQ"][0:64, :].rearrange("p (h t) -> p h t", h=H)[:, :, 0:CH],
          "gqr", r=["RQ"], w=["GB_inA"])
      DMA("sp", GB_inB[2 * D: 2 * D + H * 64, :].rearrange(
              "(h p) t -> p h t", p=64),
          V["RQ"][0:64, :].rearrange("p (h t) -> p h t", h=H)[:, :, CH:NT],
          "gqrb", r=["RQ"], w=["GB_inB"])

      for m in range(4):
          PQ, pk = (PA, "PA") if m % 2 == 0 else (PB, "PB")
          proj_mm(WKA[m], m, PQ, pk, V["NP"], "NP")
          pks = [pk + str(hh) for hh in range(NH)]
          ACTV(V["LATN"][:, m * NT:(m + 1) * NT], PQ[:, 0:NT], AF.Copy,
               r=pks, w=["LATN"])
          for hh in range(NH):
              sq = V["SQ0"] if hh == 0 else V["SQ1"]
              sqk = "SQ0" if hh == 0 else "SQ1"
              ACTV(sq[:], PQ[:, hh * 512: hh * 512 + 512], AF.Square,
                   r=[pk + str(hh)], w=[sqk])
              rw, rwk = row(hh)
              MM(rw, V["ONESC"][:], sq[:], m == 0, m == 3, r=[sqk, "ONESC"], w=[rwk])
      for hh in range(NH):
          rw, rwk = row(hh)
          dstrow = V["ROWA"] if hh == 0 else V["ROWB"]
          dk = "ROWA" if hh == 0 else "ROWB"
          TS(dstrow[:], rw, 1.0 / 512.0, ALU.mult, r=[rwk], w=[dk])
          ACTV(dstrow[:], dstrow[:], AF.Sqrt, bias=V["EPSB"][:],
               r=[dk, "EPSB"], w=[dk])
          RECIP(dstrow[:], dstrow[:], r=[dk], w=[dk])
      for hh in range(NH):
          dstrow = V["ROWA"] if hh == 0 else V["ROWB"]
          dk = "ROWA" if hh == 0 else "ROWB"
          MM(PC[:, hh * 512: hh * 512 + 512], V["ONESR"][:], dstrow[:], True, True,
             r=[dk, "ONESR"], w=["PC" + str(hh)])
      for m in range(4):
          for hh in range(NH):
              STT(V["LATN"][:, m * NT + hh * 512: m * NT + hh * 512 + 512],
                  V["LATN"][:, m * NT + hh * 512: m * NT + hh * 512 + 512],
                  V["KNW"][:, m:m + 1], PC[:, hh * 512: hh * 512 + 512],
                  ALU.mult, ALU.mult, r=["LATN", "KNW", "PC" + str(hh)], w=["LATN"])

      for kc in range(DC):
          for hh in range(NH):
              MM(PA[0:64, hh * 512: hh * 512 + 512],
                 V["WKARb"][:, kc * 64:(kc + 1) * 64],
                 V["NP"][:, kc * NT + hh * 512: kc * NT + hh * 512 + 512],
                 kc == 0, kc == DC - 1, r=["WKARb", "NP"], w=["PA" + str(hh)])
      ACTV(V["KRO"][0:64, :], PA[0:64, 0:NT], AF.Copy,
           r=["PA" + str(hh) for hh in range(NH)], w=["KRO"])
      rope_apply(V["KRO"][0:64, :], "KRO", V["KRO"][0:64, :], "KRO")
      DMA("sp", GB_inA[2 * D + H * 64: 2 * D + H * 64 + 64, :],
          V["KRO"][0:64, 0:CH], "gkr", r=["KRO"], w=["GB_inA"])
      DMA("sp", GB_inB[2 * D + H * 64: 2 * D + H * 64 + 64, :],
          V["KRO"][0:64, CH:NT], "gkrb", r=["KRO"], w=["GB_inB"])
      ACTV(V["RT1"][0:64, :], V["KRO"][0:64, :], AF.Square, r=["KRO"], w=["RT1"])
      for hh in range(NH):
          rw, rwk = row(hh)
          MM(rw, V["ONESC"][0:64, :], V["RT1"][0:64, hh * 512: hh * 512 + 512],
             True, True, r=["RT1", "ONESC"], w=[rwk])
          dst = V["ROWC"] if hh == 0 else V["ROWD"]
          dk = "ROWC" if hh == 0 else "ROWD"
          CPY(dst[:], rw, r=[rwk], w=[dk])

      emit_vstage_pending = []
      for m in range(32):
          PQ, pk = (PA, "PA") if m % 2 == 0 else (PB, "PB")
          proj_mm(WKB[m], m, PQ, pk, V["LATN"], "LATN")
          pks = [pk + str(hh) for hh in range(NH)]
          if m < 16:
              ACTV(V["STG"][:], PQ[:, 0:NT], AF.Copy, r=pks, w=["STG"])
              DMA("sp", GB_inA[D + m * 128: D + (m + 1) * 128, :],
                  V["STG"][:, 0:CH], "gkn", r=["STG"], w=["GB_inA"])
              DMA("sp", GB_inB[D + m * 128: D + (m + 1) * 128, :],
                  V["STG"][:, CH:NT], "gknb", r=["STG"], w=["GB_inB"])
              first_red = True
              for hh in range(NH):
                  sq = V["SQ0"] if hh == 0 else V["SQ1"]
                  sqk = "SQ0" if hh == 0 else "SQ1"
                  ACTV(sq[:], PQ[:, hh * 512: hh * 512 + 512], AF.Square,
                       r=[pk + str(hh)], w=[sqk])
                  rw, rwk = row(hh)
                  MM(rw, V["ONESC"][:], sq[:], True, True, r=[sqk, "ONESC"], w=[rwk])
                  ro = V["ROWC"] if hh == 0 else V["ROWD"]
                  rok = "ROWC" if hh == 0 else "ROWD"
                  sq2 = V["SQ0"] if hh == 0 else V["SQ1"]
                  TT(sq2[0:1, 0:512], rw, ro[:], ALU.add, r=[rwk, rok], w=[sqk])
                  RED(V["ROWB"][0:1, hh:hh + 1], sq2[0:1, 0:512], ALU.max,
                      r=[sqk], w=["ROWB"])
              if NH == 2:
                  TT(V["ROWB"][0:1, 0:1], V["ROWB"][0:1, 0:1], V["ROWB"][0:1, 1:2],
                     ALU.max, r=["ROWB"], w=["ROWB"])
              CPY(V["KMN"][0:1, m:m + 1], V["ROWB"][0:1, 0:1], r=["ROWB"], w=["KMN"])
          else:
              h = m - 16
              ACTV(V["STG"][:], PQ[:, 0:NT], AF.Copy, r=pks, w=["STG"])
              for i in range(NTT):
                  TRANS(PTB[:, 0:128], V["STG"][:, i * 128:(i + 1) * 128],
                        V["IDB"][:], r=["STG", "IDB"], w=["PTB"])
                  CPY(V["VSTG2"][:, (i * H + h) * 128: (i * H + h) * 128 + 128],
                      PTB[:, 0:128], r=["PTB"], w=["VSTG2"], eng="act")
      DMA("sp", GVB_inA.rearrange("(i p) c -> p i c", p=128),
          V["VSTG2"].rearrange("p (i c) -> p i c", i=NTT)[:, 0:NTT // 2, :],
          "gvb", r=["VSTG2"], w=["GVB_inA"])
      DMA("sp", GVB_inB.rearrange("(i p) c -> p i c", p=128),
          V["VSTG2"].rearrange("p (i c) -> p i c", i=NTT)[:, NTT // 2:NTT, :],
          "gvbb", r=["VSTG2"], w=["GVB_inB"])
      MEMSET(V["CSTG"][:], 0.0, r=[], w=["CSTG"])
      ACTV(V["CSTG"][0:1, 0:16], V["KMN"][0:1, 0:16], AF.Copy,
           r=["KMN"], w=["CSTG"])
      DMA("sp", GB_inA[NRB - 1: NRB, :], V["CSTG"][0:1, 0:CH], "gkm",
          r=["CSTG"], w=["GB_inA"])
      MEMSET(V["CSTG2"][:], 0.0, r=[], w=["CSTG2"])
      ACTV(V["CSTG2"][0:1, 0:32], V["QN2"][0:1, 0:32], AF.Copy,
           r=["QN2"], w=["CSTG2"])
      DMA("sp", GB_inA[NRB - 2: NRB - 1, :], V["CSTG2"][0:1, 0:CH], "gqn2",
          r=["CSTG2"], w=["GB_inA"])
      COLL(GB_inA, GB_outA, "cc_gba", r=["GB_inA"], w=["GB_outA"])
      COLL(GVB_inA, GVB_outA, "cc_gvba", r=["GVB_inA"], w=["GVB_outA"])
      COLL(GB_inB, GB_outB, "cc_gbb", r=["GB_inB"], w=["GB_outB"])
      COLL(GVB_inB, GVB_outB, "cc_gvbb", r=["GVB_inB"], w=["GVB_outB"])

      P.barrier()

      pass
    if STAGE >= 2.8:
      def qr_bodies_f(gout, rowblk, qc):
          out = []
          for me in range(NCORE):
              pairs = []
              for s in range(NCORE):
                  for hl in range(2):
                      j = s if qc == 0 else NCH - 1 - s
                      r0 = s * rowblk + 2 * D + (2 * me + hl) * 64
                      src = gout[r0: r0 + 64, :]
                      d = V["QRB"][0:64, hl * T + j * CH: hl * T + (j + 1) * CH]
                      pairs.append((d, src))
              out.append(pairs)
          return out
      for s in range(NCORE):
          r0 = s * NRB + 2 * D + H * 64
          DMA("pool", V["KRB"][0:64, s * CH:(s + 1) * CH],
              GB_outA[r0: r0 + 64, :], "krl", r=["GB_outA"], w=["KRB_A"])
          r0 = s * (NRB - 2) + 2 * D + H * 64
          j = NCH - 1 - s
          DMA("pool", V["KRB"][0:64, j * CH:(j + 1) * CH],
              GB_outB[r0: r0 + 64, :], "krlb", r=["GB_outB"], w=["KRB_B"])
      SWDMA(merge_bodies(switched_heads_bodies(V["QNB"], GB_outA, NRB, 0, 0),
                         switched_heads_bodies(V["KNB"], GB_outA, NRB, D, 0),
                         qr_bodies_f(GB_outA, NRB, 0),
                         switched_V_bodies(V["VBB"], GVB_outA, 0),
                         switched_C_bodies(GB_outA, NRB, NRB - 2, V["CROW"]),
                         switched_KM_bodies(GB_outA, NRB, NRB - 1)),
            "swb", r=["GB_outA", "GVB_outA"],
            w=["KNB_A", "QNB_A", "QRB_A", "VBB_A", "CROW", "KMROW"])
      SWDMA(merge_bodies(switched_heads_bodies(V["QNB"], GB_outB, NRB - 2, 0, 1),
                         switched_heads_bodies(V["KNB"], GB_outB, NRB - 2, D, 1),
                         qr_bodies_f(GB_outB, NRB - 2, 1),
                         switched_V_bodies(V["VBB"], GVB_outB, 1)),
            "swbb", r=["GB_outB", "GVB_outB"],
            w=["KNB_B", "QNB_B", "QRB_B", "VBB_B"])
      bias_prep_b()
      if cfg.get("dump"):
          DBG1 = nc.dram_tensor("dbg_bias", [128, 64], F32, kind="ExternalOutput").ap()
          DBG2 = nc.dram_tensor("dbg_crow", [1, 32], F32, kind="ExternalOutput").ap()
          DBG3 = nc.dram_tensor("dbg_qn2", [1, 32], F32, kind="ExternalOutput").ap()
          DBG4 = nc.dram_tensor("dbg_kmx", [1, 16], F32, kind="ExternalOutput").ap()
          DBG5 = nc.dram_tensor("dbg_kqs", [128, 2048], BF16, kind="ExternalOutput").ap()
          DBG6 = nc.dram_tensor("dbg_qns", [128, 2048], BF16, kind="ExternalOutput").ap()
          DBG7 = nc.dram_tensor("dbg_vbs", [128, 2048], BF16, kind="ExternalOutput").ap()
          DBG8 = nc.dram_tensor("dbg_qrs", [64, 2048], BF16, kind="ExternalOutput").ap()
          DBG9 = nc.dram_tensor("dbg_krs", [64, 2048], BF16, kind="ExternalOutput").ap()
          DMA("pool", DBG1, V["BIAS"][:], "dbgw", r=["BIAS"], w=["DBG"])
          DMA("pool", DBG2, V["CROW"][0:1, 0:32], "dbgw", r=["CROW"], w=["DBG"])
          DMA("pool", DBG3, V["QN2"][0:1, 0:32], "dbgw", r=["QN2"], w=["DBG"])
          DMA("pool", DBG4, V["KMX"][0:1, 0:16], "dbgw", r=["KMX"], w=["DBG"])
          DMA("pool", DBG5, V["KNB"][:, 0:2048], "dbgw", r=["KNB"], w=["DBG"])
          DMA("pool", DBG6, V["QNB"][:, 0:2048], "dbgw", r=["QNB"], w=["DBG"])
          DMA("pool", DBG7, V["VBB"][:, 0:2048], "dbgw", r=["VBB"], w=["DBG"])
          DMA("pool", DBG8, V["QRB"][0:64, 0:2048], "dbgw", r=["QRB"], w=["DBG"])
          DMA("pool", DBG9, V["KRB"][0:64, 0:2048], "dbgw", r=["KRB"], w=["DBG"])
          DBG10 = nc.dram_tensor("dbg_gkm", [NCORE, 16], F32, kind="ExternalOutput").ap()
          DBG11 = nc.dram_tensor("dbg_kmn", [1, 128], F32, kind="ExternalOutput").ap()
          DBG12 = nc.dram_tensor("dbg_rowcd", [1, 1024], F32, kind="ExternalOutput").ap()
          DMA("pool", DBG10, GKM_out, "dbgw", r=["GKM_out"], w=["DBG"])
          DMA("pool", DBG11, V["KMN"][0:1, 0:128], "dbgw", r=["KMN"], w=["DBG"])
          DMA("pool", DBG12[0:1, 0:512], V["ROWC"][0:1, 0:512], "dbgw", r=["ROWC"], w=["DBG"])
          DMA("pool", DBG12[0:1, 512:1024], V["ROWD"][0:1, 0:512], "dbgw", r=["ROWD"], w=["DBG"])
          DBG13 = nc.dram_tensor("dbg_kro", [64, NT], BF16, kind="ExternalOutput").ap()
          DBG14 = nc.dram_tensor("dbg_latn", [128, NT], BF16, kind="ExternalOutput").ap()
          DBG15 = nc.dram_tensor("dbg_np", [128, NT], BF16, kind="ExternalOutput").ap()
          DMA("pool", DBG13, V["KRO"][0:64, :], "dbgw", r=["KRO"], w=["DBG"])
          DMA("pool", DBG14, V["LATN"][:, 0:NT], "dbgw", r=["LATN"], w=["DBG"])
          DMA("pool", DBG15, V["NP"][:, 0:NT], "dbgw", r=["NP"], w=["DBG"])
          op("sp", lambda e: e.nop(), r=["DBG"], w=[])

      pass
    if STAGE >= 3:
      def yb_coll(hl):
          if hl == 0:
              COLL(GYB_in0, GYB_out0, "cc_gyb0", r=["GYB0"], w=["GYB_out0"])
          else:
              COLL(GYB_in1, GYB_out1, "cc_gyb1", r=["GYB1"], w=["GYB_out1"])
      emit_attn(1, yb_coll)

      P.barrier()

      def yfm_bodies(gyout, hl):
          bodies = []
          for me in range(NCORE):
              pairs = []
              for s in range(NCORE):
                  h = 2 * s + hl
                  for qc in range(2):
                      ch = me if qc == 0 else NCH - 1 - me
                      src = gyout[s * 128:(s + 1) * 128, ch * CH:(ch + 1) * CH]
                      d = V["YFM"][:, h * NT + qc * CH: h * NT + (qc + 1) * CH]
                      pairs.append((d, src))
              bodies.append(pairs)
          return bodies

      for hl in range(2):
          gyo = GYB_out0 if hl == 0 else GYB_out1
          gyok = "GYB_out0" if hl == 0 else "GYB_out1"
          SWDMA(yfm_bodies(gyo, hl), "swyb" + str(hl),
                r=[gyok], w=["YFM"])
      for dc in range(DC):
          wob = V["WOB0"] if dc % 2 == 0 else V["WOB1"]
          wbk = "WOB0" if dc % 2 == 0 else "WOB1"
          DMA("sp", wob[:], WOd[dc], "wo" + wbk, r=[], w=[wbk])
          ACTV(V["WOBB"][:], wob[:], AF.Copy, r=[wbk], w=["WOBB"])
          for hv in range(H):
              for hh in range(NH):
                  MM(PA[:, hh * 512: hh * 512 + 512],
                     V["WOBB"][:, hv * 128:(hv + 1) * 128],
                     V["YFM"][:, hv * NT + hh * 512: hv * NT + hh * 512 + 512],
                     hv == 0, hv == H - 1, r=["WOBB", "YFM"], w=["PA" + str(hh)])
          xw = V["XW0"] if dc % 2 == 0 else V["XW1"]
          xk = "XW0" if dc % 2 == 0 else "XW1"
          DMA("sp", xw[:], XST[:, dc * NT:(dc + 1) * NT], "dm" + xk,
              r=["XST"], w=[xk])
          op("dve", (lambda xw_: lambda e: e.tensor_add(
              xw_[:, 0:NT], xw_[:, 0:NT], PA[:, 0:NT]))(xw),
             r=[xk] + ["PA" + str(hh) for hh in range(NH)], w=[xk])
          DMA("sp", XST[:, dc * NT:(dc + 1) * NT], xw[:], "st" + xk,
              r=[xk], w=["XST"])

    P.barrier()

    if STAGE >= 4:
        emit_ffn(False)
    P.barrier()

    for dc in range(DC):
        xw = V["XW0"] if dc % 2 == 0 else V["XW1"]
        xk = "XW0" if dc % 2 == 0 else "XW1"
        DMA("sp", xw[:], XST[:, dc * NT:(dc + 1) * NT], "dm" + xk,
            r=["XST"], w=[xk])
        for i in range(NTT):
            os_ = V["OS0"] if i % 2 == 0 else V["OS1"]
            osk = "OS0" if i % 2 == 0 else "OS1"
            TRANS(PA[:, 0:128], xw[:, i * 128:(i + 1) * 128], V["IDF"][:],
                  r=[xk, "IDF"], w=["PA0"])
            CPY(os_[:], PA[:, 0:128], r=["PA0"], w=[osk])
            DMA("pool", OUT[i * 128:(i + 1) * 128, dc * 128:(dc + 1) * 128],
                os_[:], "outw", r=[osk], w=["OUTW"])
    op("sp", lambda e: e.nop(), r=["OUTW"], w=[])

    emit = P.finalize(nc, stack)

    with nc.Block() as block:
        def runner(items):
            def go(eng):
                for waits, fn, sem, inc, selfsem in items:
                    for s_, v_ in waits:
                        eng.wait_ge(s_, v_)
                    if selfsem is not None:
                        fn(eng, selfsem)
                    else:
                        inst = fn(eng)
                        if sem is not None:
                            inst.then_inc(sem, inc)
            return go
        block.tensor(runner(emit["pe"]))
        block.scalar(runner(emit["act"]))
        block.vector(runner(emit["dve"]))
        block.sync(runner(emit["sp"]))
        block.gpsimd(runner(emit["pool"]))

    stack.pop_all()
    return nc


def kernel(**inputs):
    cfg = make_cfg()
    maps = host_prep(cfg, inputs)
    nc = build(cfg)
    res = run_bass_kernel_spmd(nc, maps, core_ids=list(range(NCORE)))
    T, D, CH = cfg["T"], cfg["D"], cfg["CH"]
    out = np.zeros((T, D), np.float32)
    for me in range(NCORE):
        y = res.results[me]["out_own"]
        out[me * CH:(me + 1) * CH] = y[:CH]
        out[(NCH - 1 - me) * CH:(NCH - me) * CH] = y[CH:]
    return out.reshape(1, T, D)


# version 08
# speedup vs baseline: 1.4269x; 1.0428x over previous
import numpy as np
import ml_dtypes
from contextlib import ExitStack

import concourse.bass as bass
import concourse.mybir as mybir
from concourse.bass_utils import run_bass_kernel_spmd

F32 = mybir.dt.float32
BF16 = mybir.dt.bfloat16
AF = mybir.ActivationFunctionType
ALU = mybir.AluOpType
AX = mybir.AxisListType

NEGBIG = -1.0e30
NCORE = 8
NCH = 16


def make_cfg(T=8192, HID=8192):
    c = {}
    c["T"], c["D"], c["H"] = T, 2048, 16
    c["CH"] = T // NCH
    c["KS"] = c["CH"] // 128
    c["NT"] = 2 * c["CH"]
    c["NH"] = c["NT"] // 512
    c["DC"], c["HID"], c["JC"] = 16, HID, HID // 128
    c["LNEPS"], c["RMSEPS"] = 1e-5, 1e-6
    c["stage"] = 4
    assert c["NT"] % 512 == 0 and c["CH"] % 128 == 0
    return c


def gloc(j):
    return (j, 0) if j < NCORE else (NCH - 1 - j, 1)


def cidx(j):
    return j if j < NCORE else 8 + (NCH - 1 - j)



def prep_blocks(W):
    K, M = W.shape
    Kc, Mc = K // 128, M // 128
    return np.ascontiguousarray(
        W.reshape(Kc, 128, Mc, 128).transpose(2, 1, 0, 3).reshape(Mc, 128, Kc * 128))


def host_prep(cfg, inputs):
    T, D, H, CH = cfg["T"], cfg["D"], cfg["H"], cfg["CH"]
    JC, DC = cfg["JC"], cfg["DC"]
    x = np.asarray(inputs["L0_attn_a_x"], np.float32).reshape(T, D)
    xT = np.ascontiguousarray(x.T)

    wq = np.asarray(inputs["L0_attn_b_wq"], np.float32)
    nope_cols = np.concatenate([wq[:, 192 * h:192 * h + 128] for h in range(H)], 1)
    rope_cols = np.concatenate([wq[:, 192 * h + 128:192 * h + 192] for h in range(H)], 1)
    wq_blk = prep_blocks(np.concatenate([nope_cols, rope_cols], 1))

    wkva = np.asarray(inputs["L0_attn_b_wkv_a"], np.float32)
    wkva_main = prep_blocks(wkva[:, :512])
    wkva_rope = np.ascontiguousarray(
        wkva[:, 512:].reshape(16, 128, 64).transpose(1, 0, 2).reshape(128, 1024))

    wkvb = np.asarray(inputs["L0_attn_b_wkv_b"], np.float32)
    nope_c = np.concatenate([wkvb[:, 256 * h:256 * h + 128] for h in range(H)], 1)
    v_c = np.concatenate([wkvb[:, 256 * h + 128:256 * h + 256] for h in range(H)], 1)
    wkvb_blk = prep_blocks(np.concatenate([nope_c, v_c], 1))

    wo_blk = prep_blocks(np.asarray(inputs["L0_attn_b_wo"], np.float32))

    up_blk = prep_blocks(np.ascontiguousarray(
        np.asarray(inputs["L0_ffn_up_weight"], np.float32).T))
    down_blk = prep_blocks(np.ascontiguousarray(
        np.asarray(inputs["L0_ffn_down_weight"], np.float32).T))

    usc = np.ascontiguousarray(
        np.asarray(inputs["L0_ffn_up_scale"], np.float32).reshape(JC, 128).T)
    dsc = np.ascontiguousarray(
        np.asarray(inputs["L0_ffn_down_scale"], np.float32).reshape(DC, 128).T)
    knw = np.ascontiguousarray(
        np.asarray(inputs["L0_attn_b_kv_norm_w"], np.float32).reshape(4, 128).T)
    osc = np.asarray(inputs["L0_ffn_output_scale"], np.float32).reshape(1, 1)

    ones_col = np.ones((128, 1), np.float32)
    ones_row = np.ones((1, 128), np.float32)
    idf = np.eye(128, dtype=np.float32)
    idb = np.eye(128, dtype=ml_dtypes.bfloat16)
    tri = np.where(np.arange(128)[None, :] >= np.arange(128)[:, None],
                   0.0, NEGBIG).astype(np.float32)

    inv = 1.0 / (10000.0 ** (np.arange(0, 64, 2, dtype=np.float32) / 64.0))
    maps = []
    for me in range(NCORE):
        c0, c1 = me, NCH - 1 - me
        cols = np.concatenate([np.arange(c0 * CH, (c0 + 1) * CH),
                               np.arange(c1 * CH, (c1 + 1) * CH)])
        f = np.outer(inv, cols.astype(np.float32))
        emb = np.concatenate([f, f], 0)
        maps.append({
            "x0T_own": np.ascontiguousarray(xT[:, cols]),
            "wq_blk": wq_blk, "wkva_main": wkva_main, "wkva_rope": wkva_rope,
            "wkvb_blk": wkvb_blk, "wo_blk": wo_blk,
            "up_blk": up_blk, "down_blk": down_blk,
            "usc": usc, "dsc": dsc, "knw": knw, "osc": osc,
            "ones_col": ones_col, "ones_row": ones_row,
            "idf": idf, "idb": idb, "tri": tri,
            "cosq": np.cos(emb).astype(np.float32),
            "sinq": np.sin(emb).astype(np.float32),
        })
    return maps



ENGS = ("pe", "act", "dve", "sp", "pool")


class Op:
    __slots__ = ("eng", "fn", "reads", "writes", "sem_key", "inc", "idx",
                 "edges", "need_inc", "val", "clock", "self_inc")

    def __init__(self, eng, fn, reads, writes, sem_key, inc, idx, self_inc):
        self.eng, self.fn = eng, fn
        self.reads, self.writes = tuple(reads), tuple(writes)
        self.sem_key, self.inc, self.idx = sem_key, inc, idx
        self.self_inc = self_inc
        self.edges = []
        self.need_inc = False
        self.val = None
        self.clock = None


class Prog:
    def __init__(self):
        self.ops = []
        self.last_w = {}
        self.readers = {}
        self.pending = {e: [] for e in ENGS}

    def op(self, eng, fn, r=(), w=(), sem_key=None, inc=1, self_inc=False):
        o = Op(eng, fn, r, w, sem_key, inc, len(self.ops), self_inc)
        seen = {}
        for k in o.reads:
            p = self.last_w.get(k)
            if p is not None:
                seen[p.idx] = p
        for k in o.writes:
            p = self.last_w.get(k)
            if p is not None:
                seen[p.idx] = p
            for rd in self.readers.get(k, ()):
                seen[rd.idx] = rd
        for p in self.pending[eng]:
            seen[p.idx] = p
        self.pending[eng] = []
        seen.pop(o.idx, None)
        o.edges = list(seen.values())
        for k in o.writes:
            self.last_w[k] = o
            self.readers[k] = []
        for k in o.reads:
            self.readers.setdefault(k, []).append(o)
        self.ops.append(o)
        return o

    def barrier(self):
        lasts = {}
        for o in self.ops:
            key = ("g", o.sem_key) if o.sem_key is not None else ("eng", o.eng)
            lasts[key] = o
        alll = list(lasts.values())
        for e in ENGS:
            self.pending[e] = list(alll)

    def finalize(self, nc, stack):
        for o in self.ops:
            for p in o.edges:
                p.need_inc = True
            if o.sem_key is not None:
                o.need_inc = True
        sems = {}
        for e in ENGS:
            sems[("eng", e)] = stack.enter_context(nc.semaphore(f"s_{e}"))
        for o in self.ops:
            if o.sem_key is not None and ("g", o.sem_key) not in sems:
                sems[("g", o.sem_key)] = stack.enter_context(
                    nc.semaphore(f"s_{o.sem_key}"))
        counts = {}
        for o in self.ops:
            if not o.need_inc:
                continue
            key = ("g", o.sem_key) if o.sem_key is not None else ("eng", o.eng)
            counts[key] = counts.get(key, 0) + o.inc
            o.val = counts[key]
        eng_clock = {e: {} for e in ENGS}
        emit = {e: [] for e in ENGS}
        for o in self.ops:
            clk = eng_clock[o.eng]
            waits = []
            for p in sorted(o.edges, key=lambda q: q.idx):
                pkey = ("g", p.sem_key) if p.sem_key is not None else ("eng", p.eng)
                if clk.get(pkey, 0) < p.val:
                    waits.append((sems[pkey], p.val))
                    clk[pkey] = p.val
                if p.clock:
                    for k2, v2 in p.clock.items():
                        if clk.get(k2, 0) < v2:
                            clk[k2] = v2
            okey = ("g", o.sem_key) if o.sem_key is not None else ("eng", o.eng)
            if o.need_inc:
                c2 = dict(clk)
                c2[okey] = max(c2.get(okey, 0), o.val)
                o.clock = c2
            else:
                o.clock = dict(clk)
            sem = sems[okey] if (o.need_inc and not o.self_inc) else None
            selfsem = sems[okey] if (o.need_inc and o.self_inc) else None
            emit[o.eng].append((waits, o.fn, sem, o.inc, selfsem))
        return emit



def build(cfg):
    nc = bass.Bass("TRN2", target_bir_lowering=False)
    P = Prog()
    stack = ExitStack()
    T, D, H, CH, NT, NH = (cfg["T"], cfg["D"], cfg["H"], cfg["CH"], cfg["NT"],
                           cfg["NH"])
    KS, DC, JC, HID = cfg["KS"], cfg["DC"], cfg["JC"], cfg["HID"]
    TTn = T // 128
    NTT = NT // 128

    din = lambda n, s, dt=F32: nc.dram_tensor(n, list(s), dt, kind="ExternalInput").ap()
    dint = lambda n, s, dt: nc.dram_tensor(n, list(s), dt, kind="Internal").ap()

    x0T = din("x0T_own", [D, NT])
    WQ = din("wq_blk", [24, 128, D])
    WKA = din("wkva_main", [4, 128, D])
    WKAR = din("wkva_rope", [128, 1024])
    WKB = din("wkvb_blk", [32, 128, 512])
    WOd = din("wo_blk", [16, 128, D])
    UPB = din("up_blk", [JC, 128, D])
    DNB = din("down_blk", [16, 128, HID])
    cons_in = [("USC", din("usc", [128, JC])), ("DSC", din("dsc", [128, DC])),
               ("KNW", din("knw", [128, 4])), ("OSC", din("osc", [1, 1])),
               ("ONESC", din("ones_col", [128, 1])),
               ("ONESR", din("ones_row", [1, 128])),
               ("IDF", din("idf", [128, 128])),
               ("IDB", din("idb", [128, 128], BF16)),
               ("TRI", din("tri", [128, 128])),
               ("COS", din("cosq", [64, NT])), ("SIN", din("sinq", [64, NT])),
               ("WKARs", WKAR)]
    OUT = nc.dram_tensor("out_own", [NT, D], F32, kind="ExternalOutput").ap()

    XST = dint("XST", [128, DC * NT], F32)
    GA_inA = dint("GA_inA", [D + 1, CH], BF16)
    GA_inB = dint("GA_inB", [D, CH], BF16)
    GA_outA = dint("GA_outA", [NCORE * (D + 1), CH], BF16)
    GA_outB = dint("GA_outB", [NCORE * D, CH], BF16)
    GVA_inA = dint("GVA_inA", [CH, H * 128], BF16)
    GVA_inB = dint("GVA_inB", [CH, H * 128], BF16)
    GVA_outA = dint("GVA_outA", [NCORE * CH, H * 128], BF16)
    GVA_outB = dint("GVA_outB", [NCORE * CH, H * 128], BF16)
    GYA_in0 = dint("GYA_in0", [128, T], BF16)
    GYA_in1 = dint("GYA_in1", [128, T], BF16)
    GYA_out0 = dint("GYA_out0", [NCORE * 128, T], BF16)
    GYA_out1 = dint("GYA_out1", [NCORE * 128, T], BF16)
    NRB = 2 * D + H * 64 + 64 + 2
    GB_inA = dint("GB_inA", [NRB, CH], BF16)
    GB_inB = dint("GB_inB", [NRB - 2, CH], BF16)
    GB_outA = dint("GB_outA", [NCORE * NRB, CH], BF16)
    GB_outB = dint("GB_outB", [NCORE * (NRB - 2), CH], BF16)
    GVB_inA = dint("GVB_inA", [CH, H * 128], BF16)
    GVB_inB = dint("GVB_inB", [CH, H * 128], BF16)
    GVB_outA = dint("GVB_outA", [NCORE * CH, H * 128], BF16)
    GVB_outB = dint("GVB_outB", [NCORE * CH, H * 128], BF16)
    GYB_in0 = dint("GYB_in0", [128, T], BF16)
    GYB_in1 = dint("GYB_in1", [128, T], BF16)
    GYB_out0 = dint("GYB_out0", [NCORE * 128, T], BF16)
    GYB_out1 = dint("GYB_out1", [NCORE * 128, T], BF16)
    UWS = dint("UWS", [JC, 128, D], BF16)
    DWS = dint("DWS", [16, 128, HID], BF16)

    spec_pers = [
        ("ONESC", [128, 1], F32), ("ONESR", [1, 128], F32),
        ("IDF", [128, 128], F32), ("IDB", [128, 128], BF16),
        ("TRI", [128, 128], F32), ("USC", [128, JC], F32),
        ("DSC", [128, DC], F32), ("CSC", [128, DC], F32),
        ("KNW", [128, 4], F32), ("OSC", [1, 1], F32),
        ("COS", [64, NT], F32), ("SIN", [64, NT], F32),
        ("WKARs", [128, 1024], F32), ("WKARb", [128, 1024], BF16),
        ("BIAS", [128, 64], F32), ("CROW", [1, 32], BF16),
        ("CROWF", [1, 32], F32), ("QN2", [1, 32], F32),
        ("ROWA", [1, 512], F32), ("ROWB", [1, 512], F32),
        ("ROWC", [1, 512], F32), ("ROWD", [1, 512], F32),
        ("KMN", [1, 128], F32), ("KMX", [1, 16], F32), ("RL", [128, 1], F32),
        ("ROWB16", [1, 64], BF16), ("QROW", [1, 32], BF16), ("KMROW", [1, 16], BF16),
        ("CSTG", [1, NT], BF16), ("CSTG2", [1, NT], BF16),
        ("KH", [1, 2], F32),
        ("EPSA", [1, 1], F32), ("EPSB", [1, 1], F32),
        ("YS", [128, 512], F32), ("YB16", [128, 512], BF16),
        ("SUMP", [128, 512], F32),
        ("ONESB", [128, 1], BF16),
        ("XW0", [128, NT], F32), ("XW1", [128, NT], F32),
        ("YTA", [128, 128], F32), ("YTB", [128, 128], BF16),
        ("SQ0", [128, 512], F32), ("SQ1", [128, 512], F32),
    ]
    spec_a = [
        ("N", [128, DC * NT], BF16),
        ("VSTG", [128, NTT * H * 128], BF16),
        ("PSB0", [128, 2 * CH], BF16), ("PSB1", [128, 2 * CH], BF16),
    ]
    spec_a2 = [
        ("KQA", [128, 2 * T], BF16),
        ("VAA", [128, TTn * 256], BF16),
        ("YIN", [128, NCORE * 2 * NT], BF16),
        ("PSB0b", [128, 2 * CH], BF16), ("PSB1b", [128, 2 * CH], BF16),
        ("UPP0", [128, D], F32), ("UPP1", [128, D], F32),
        ("UPS", [128, D], BF16),
        ("DNP0", [128, HID // 2], F32), ("DNP1", [128, HID // 2], F32),
        ("DNS", [128, HID // 2], BF16),
    ]
    spec_f = [
        ("NF", [128, DC * NT], BF16),
        ("HSB", [128, JC * 512], BF16),
        ("UWSB0", [128, D], BF16), ("UWSB1", [128, D], BF16),
        ("DWSB0", [128, HID], BF16), ("DWSB1", [128, HID], BF16),
    ]
    spec_p = [
        ("NP", [128, DC * NT], BF16),
        ("WBLK0", [128, D], F32), ("WBLK1", [128, D], F32),
        ("WBB0", [128, D], BF16), ("WBB1", [128, D], BF16),
        ("RQ", [64, H * NT], BF16),
        ("RH", [64, NT], F32), ("RT1", [64, NT], F32), ("RT2", [64, NT], F32),
        ("LATN", [128, 4 * NT], BF16),
        ("KRO", [64, NT], BF16),
        ("STG", [128, NT], BF16),
        ("VSTG2", [128, NTT * H * 128], BF16),
    ]
    spec_b = [
        ("KNB", [128, 2 * T], BF16),
        ("QNB", [128, 2 * T], BF16),
        ("QRB", [64, 2 * T], BF16),
        ("KRB", [64, T], BF16),
        ("VBB", [128, TTn * 256], BF16),
        ("PSB2", [128, 2 * CH], BF16), ("PSB3", [128, 2 * CH], BF16),
    ]
    spec_w = [
        ("YFM", [128, H * NT], BF16),
        ("WOB0", [128, D], F32), ("WOB1", [128, D], F32),
        ("WOBB", [128, D], BF16),
    ]
    spec_o = [("OS0", [128, 128], F32), ("OS1", [128, 128], F32)]

    def size_of(spec):
        off, offs = 0, {}
        for ent in spec:
            nm, sh, dt = ent[0], ent[1], ent[2]
            if len(ent) == 4:
                offs[nm] = offs[ent[3]]
                continue
            nb = int(np.prod(sh[1:])) * mybir.dt.size(dt)
            offs[nm] = off
            off += ((nb + 31) // 32) * 32
        return off, offs

    pers_sz, pers_off = size_of(spec_pers)
    arena_specs = [spec_a, spec_a2, spec_f, spec_p, spec_b, spec_w, spec_o]
    arena_sz = max(size_of(s)[0] for s in arena_specs)
    total = pers_sz + arena_sz

    backing = stack.enter_context(
        nc.sbuf_tensor("arena", [128, total], mybir.dt.uint8))
    base = nc.lookup_mloc(backing).addr

    V = {}

    def place(spec, offs, origin):
        for ent in spec:
            nm, sh, dt = ent[0], ent[1], ent[2]
            hnd = nc.virtual_sbuf_tensor("sb_" + nm, list(sh), dt)
            nc._tensor(hnd.name, list(hnd.shape), hnd.dtype, type="SB")
            mloc = nc.lookup_mloc(hnd)
            mloc.allocated = True
            mloc.addr = base + origin + offs[nm]
            V[nm] = hnd.ap()

    place(spec_pers, pers_off, 0)
    for sp_ in arena_specs:
        _, offs = size_of(sp_)
        place(sp_, offs, pers_sz)

    PA = stack.enter_context(nc.psum_tensor("PA", [128, 1024], F32)).ap()
    PB = stack.enter_context(nc.psum_tensor("PB", [128, 1024], F32)).ap()
    PC = stack.enter_context(nc.psum_tensor("PC", [128, 1024], F32)).ap()
    PD = stack.enter_context(nc.psum_tensor("PD", [128, 512], F32)).ap()
    PTB = stack.enter_context(nc.psum_tensor("PTB", [128, 512], BF16)).ap()

    op = P.op

    def MM(out, lhsT, rhs, start, stop, r, w):
        op("pe", (lambda o, l, rr, st, sp2: lambda e: e.matmul(
            o, l, rr, start=st, stop=sp2, skip_group_check=True))(
                out, lhsT, rhs, start, stop), r, w)

    def TRANS(out, in_, ident, r, w):
        op("pe", (lambda o, i_, id_: lambda e: e.transpose(o, i_, id_))(
            out, in_, ident), r, w)

    def ACTV(out, in_, func, r, w, bias=0.0, scale=1.0):
        op("act", (lambda o, i_, f, bb, s: lambda e: e.activation(
            o, i_, f, bias=bb, scale=s))(out, in_, func, bias, scale), r, w)

    def SIGN(out, in_, r, w):
        op("act", (lambda o, i_: lambda e: e.sign(o, i_))(out, in_), r, w)

    def TT(out, a, b, alu, r, w):
        op("dve", (lambda o, a_, b_, u: lambda e: e.tensor_tensor(o, a_, b_, u))(
            out, a, b, alu), r, w)

    def TS(out, a, s1, alu, r, w):
        op("dve", (lambda o, a_, s_, u: lambda e: e.tensor_single_scalar(
            o, a_, s_, u))(out, a, s1, alu), r, w)

    def TSM(out, a, s1, r, w):
        op("dve", (lambda o, a_, s_: lambda e: e.tensor_scalar_mul(o, a_, s_))(
            out, a, s1), r, w)

    def STT(out, in0, scal, in1, op0, op1, r, w):
        op("dve", (lambda o, i0, s_, i1, u0, u1: lambda e: e.scalar_tensor_tensor(
            o, i0, s_, i1, u0, u1))(out, in0, scal, in1, op0, op1), r, w)

    def RED(out, in_, alu, r, w):
        op("dve", (lambda o, i_, u: lambda e: e.tensor_reduce(o, i_, AX.X, u))(
            out, in_, alu), r, w)

    def RECIP(out, in_, r, w):
        op("dve", (lambda o, i_: lambda e: e.reciprocal(o, i_))(out, in_), r, w)

    def CPY(out, in_, r, w, eng="dve"):
        if eng == "act":
            op(eng, (lambda o, i_: lambda e: e.copy(o, i_))(out, in_), r, w)
        else:
            op(eng, (lambda o, i_: lambda e: e.tensor_copy(o, i_))(out, in_), r, w)

    def MEMSET(ap_, val, r, w):
        op("dve", (lambda a_, v_: lambda e: e.memset(a_, v_))(ap_, val), r, w)

    def DMA(eng, out, in_, slot, r, w):
        op(eng, (lambda o, i_: lambda e: e.dma_start(out=o, in_=i_))(out, in_),
           r, w, sem_key=slot, inc=16)

    def SWDMA(bodies, slot, r, w):
        n = len(bodies[0])
        assert all(len(bb) == n for bb in bodies)

        def fn(e, sem):
            if not hasattr(e, "_pid_cached"):
                e._pid_cached = e.compute_val(e.partition_id())
            idx = e._pid_cached
            for i in nc.Switch(engines=[e], index=[idx], n=NCORE):
                for (o_, i_) in bodies[i]:
                    e.dma_start(out=o_, in_=i_).then_inc(sem, 16)
        op("pool", fn, r, w, sem_key=slot, inc=16 * n, self_inc=True)

    def COLL(ins, outs, slot, r, w):
        op("pool", (lambda i_, o_: lambda e: e.collective_compute(
            "AllGather", ALU.bypass, replica_groups=[list(range(NCORE))],
            ins=[i_], outs=[o_]))(ins, outs), r, w, sem_key=slot, inc=1)

    def row(half):
        return (PD[0:1, 0:512], "PD") if half == 0 else (PC[0:1, 0:512], "PC0")

    for nm, src in cons_in:
        DMA("sp", V[nm][:], src, "kc_" + nm, r=[], w=[nm])
    MM(PD[:, 0:1], V["ONESR"][:], V["OSC"][:], True, True,
       r=["ONESR", "OSC"], w=["PD"])
    TSM(V["CSC"][:], V["DSC"][:], PD[:, 0:1], r=["DSC", "PD"], w=["CSC"])
    ACTV(V["WKARb"][:], V["WKARs"][:], AF.Copy, r=["WKARs"], w=["WKARb"])
    ACTV(V["ONESB"][:], V["ONESC"][:], AF.Copy, r=["ONESC"], w=["ONESB"])
    MEMSET(V["EPSA"][:], float(cfg["LNEPS"]), r=[], w=["EPSA"])
    MEMSET(V["EPSB"][:], float(cfg["RMSEPS"]), r=[], w=["EPSB"])
    DMA("sp", XST.rearrange("p (c t) -> p c t", c=DC),
        x0T.rearrange("(c p) t -> p c t", p=128), "x0", r=[], w=["XST"])

    def emit_ln(nbuf, nkey):
        for half in range(NH):
            hs = half * 512
            for kc in range(DC):
                xw = V["XW0"] if kc % 2 == 0 else V["XW1"]
                xk = "XW0" if kc % 2 == 0 else "XW1"
                DMA("sp", xw[:, 0:512], XST[:, kc * NT + hs: kc * NT + hs + 512],
                    "dm" + xk, r=["XST"], w=[xk])
                MM(PD[0:1, 0:512], V["ONESC"][:], xw[:, 0:512], kc == 0,
                   kc == DC - 1, r=[xk, "ONESC"], w=["PD"])
                sq = V["SQ0"] if kc % 2 == 0 else V["SQ1"]
                sqk = "SQ0" if kc % 2 == 0 else "SQ1"
                ACTV(sq[:], xw[:, 0:512], AF.Square, r=[xk], w=[sqk])
                MM(PC[0:1, 0:512], V["ONESC"][:], sq[:], kc == 0, kc == DC - 1,
                   r=[sqk, "ONESC"], w=["PC0"])
            TS(V["ROWA"][:], PD[0:1, 0:512], 1.0 / D, ALU.mult, r=["PD"], w=["ROWA"])
            TS(V["ROWB"][:], PC[0:1, 0:512], 1.0 / D, ALU.mult, r=["PC0"], w=["ROWB"])
            TT(V["ROWC"][:], V["ROWA"][:], V["ROWA"][:], ALU.mult,
               r=["ROWA"], w=["ROWC"])
            TT(V["ROWB"][:], V["ROWB"][:], V["ROWC"][:], ALU.subtract,
               r=["ROWB", "ROWC"], w=["ROWB"])
            ACTV(V["ROWB"][:], V["ROWB"][:], AF.Sqrt, bias=V["EPSA"][:],
                 r=["ROWB", "EPSA"], w=["ROWB"])
            RECIP(V["ROWB"][:], V["ROWB"][:], r=["ROWB"], w=["ROWB"])
            TT(V["ROWC"][:], V["ROWA"][:], V["ROWB"][:], ALU.mult,
               r=["ROWA", "ROWB"], w=["ROWC"])
            MM(PC[:, 0:512], V["ONESR"][:], V["ROWB"][:], True, True,
               r=["ROWB", "ONESR"], w=["PC0"])
            MM(PC[:, 512:1024], V["ONESR"][:], V["ROWC"][:], True, True,
               r=["ROWC", "ONESR"], w=["PC1"])
            for kc in range(DC):
                xw = V["XW0"] if kc % 2 == 0 else V["XW1"]
                xk = "XW0" if kc % 2 == 0 else "XW1"
                DMA("sp", xw[:, 0:512], XST[:, kc * NT + hs: kc * NT + hs + 512],
                    "dm" + xk, r=["XST"], w=[xk])
                sq = V["SQ0"] if kc % 2 == 0 else V["SQ1"]
                sqk = "SQ0" if kc % 2 == 0 else "SQ1"
                TT(sq[:], xw[:, 0:512], PC[:, 0:512], ALU.mult,
                   r=[xk, "PC0"], w=[sqk])
                TT(nbuf[:, kc * NT + hs: kc * NT + hs + 512], sq[:],
                   PC[:, 512:1024], ALU.subtract, r=[sqk, "PC1"], w=[nkey])

    def emit_vstage(vsrc, vsrck, vstg, vstgk, i0=0, i1=None):
        if i1 is None:
            i1 = NTT
        for h in range(H):
            for i in range(i0, i1):
                TRANS(PTB[:, 0:128],
                      vsrc[:, h * NT + i * 128: h * NT + (i + 1) * 128],
                      V["IDB"][:], r=[vsrck, "IDB"], w=["PTB"])
                CPY(vstg[:, (i * H + h) * 128: (i * H + h) * 128 + 128],
                    PTB[:, 0:128], r=["PTB"], w=[vstgk], eng="act")

    STAGE = cfg.get("stage", 4)

    if STAGE >= 1:
      emit_ln(V["N"], "N")
      for h in range(H):
          ACTV(V["SQ0"][:, 0:512], V["N"][:, h * NT: h * NT + 512], AF.Square,
               r=["N"], w=["SQ0"])
          MM(PD[0:1, 0:512], V["ONESC"][:], V["SQ0"][:], True, True,
             r=["SQ0", "ONESC"], w=["PD"])
          if NH == 2:
              ACTV(V["SQ1"][:, 0:512], V["N"][:, h * NT + 512: h * NT + 1024],
                   AF.Square, r=["N"], w=["SQ1"])
              MM(PC[0:1, 0:512], V["ONESC"][:], V["SQ1"][:], True, True,
                 r=["SQ1", "ONESC"], w=["PC0"])
          for qc in range(2):
              hh = (qc * CH) // 512
              c0 = qc * CH - hh * 512
              srcp, srck = (PD[0:1, c0:c0 + CH], "PD") if hh == 0 else                   (PC[0:1, c0:c0 + CH], "PC0")
              RED(V["ROWA"][0:1, 0:1], srcp, ALU.max, r=[srck], w=["ROWA"])
              ACTV(V["ROWA"][0:1, 0:1], V["ROWA"][0:1, 0:1], AF.Sqrt,
                   r=["ROWA"], w=["ROWA"])
              TS(V["ROWD"][0:1, qc * 16 + h: qc * 16 + h + 1],
                 V["ROWA"][0:1, 0:1], float(np.sqrt(D)), ALU.mult,
                 r=["ROWA"], w=["ROWD"])
      MEMSET(V["CSTG"][:], 0.0, r=[], w=["CSTG"])
      ACTV(V["CSTG"][0:1, 0:32], V["ROWD"][0:1, 0:32], AF.Copy,
           r=["ROWD"], w=["CSTG"])
      DMA("sp", GA_inA[D:D + 1, :], V["CSTG"][0:1, 0:CH], "cai",
          r=["CSTG"], w=["GA_inA"])
      DMA("sp", GA_inA[0:D, :].rearrange("(c p) t -> p c t", p=128),
          V["N"].rearrange("p (c t) -> p c t", c=DC)[:, :, 0:CH],
          "gai", r=["N"], w=["GA_inA"])
      DMA("sp", GA_inB.rearrange("(c p) t -> p c t", p=128),
          V["N"].rearrange("p (c t) -> p c t", c=DC)[:, :, CH:NT],
          "gaib", r=["N"], w=["GA_inB"])
      COLL(GA_inA, GA_outA, "cc_gaa", r=["GA_inA"], w=["GA_outA"])
      emit_vstage(V["N"], "N", V["VSTG"], "VSTG", 0, NTT // 2)
      DMA("sp", GVA_inA.rearrange("(i p) c -> p i c", p=128),
          V["VSTG"].rearrange("p (i c) -> p i c", i=NTT)[:, 0:NTT // 2, :],
          "gvia", r=["VSTG"], w=["GVA_inA"])
      COLL(GVA_inA, GVA_outA, "cc_gvaa", r=["GVA_inA"], w=["GVA_outA"])
      COLL(GA_inB, GA_outB, "cc_gab", r=["GA_inB"], w=["GA_outB"])
      emit_vstage(V["N"], "N", V["VSTG"], "VSTGB", NTT // 2, NTT)
      DMA("sp", GVA_inB.rearrange("(i p) c -> p i c", p=128),
          V["VSTG"].rearrange("p (i c) -> p i c", i=NTT)[:, NTT // 2:NTT, :],
          "gvib", r=["VSTGB"], w=["GVA_inB"])
      COLL(GVA_inB, GVA_outB, "cc_gvab", r=["GVA_inB"], w=["GVA_outB"])

      P.barrier()

      def switched_heads_bodies(dst, gout, rowblk, rowoff, qc):
          bodies = []
          for me in range(NCORE):
              pairs = []
              for s in range(NCORE):
                  for hl in range(2):
                      h = 2 * me + hl
                      j = s if qc == 0 else NCH - 1 - s
                      src = gout[s * rowblk + rowoff + h * 128:
                                 s * rowblk + rowoff + (h + 1) * 128, :]
                      d = dst[:, hl * T + j * CH: hl * T + (j + 1) * CH]
                      pairs.append((d, src))
              bodies.append(pairs)
          return bodies

      def switched_V_bodies(dst, gout, qc):
          bodies = []
          for me in range(NCORE):
              pairs = []
              for s in range(NCORE):
                  j = s if qc == 0 else NCH - 1 - s
                  src = gout[s * CH: (s + 1) * CH,
                             me * 256:(me + 1) * 256].rearrange(
                                 "(k p) c -> p k c", p=128)
                  d = dst[:, j * KS * 256:(j + 1) * KS * 256].rearrange(
                      "p (k c) -> p k c", k=KS)
                  pairs.append((d, src))
              bodies.append(pairs)
          return bodies

      def switched_C_bodies(gout, rowblk, rowoff, dsttile):
          bodies = []
          for me in range(NCORE):
              rows = gout.rearrange("(r x) t -> r x t", x=rowblk)[:, rowoff, 0:32]
              src = rows.rearrange("r (q h) -> q r h", q=2)[
                  :, :, 2 * me: 2 * me + 2].unsqueeze(0)
              dstc = dsttile[0:1, 0:32].rearrange(
                  "a (q r h) -> a q r h", q=2, r=NCORE)
              bodies.append([(dstc, src)])
          return bodies

      def switched_KM_bodies(gout, rowblk, rowoff):
          bodies = []
          for me in range(NCORE):
              rows = gout.rearrange("(r x) t -> r x t", x=rowblk)[
                  :, rowoff, 2 * me: 2 * me + 2].unsqueeze(0)
              dstc = V["KMROW"][0:1, 0:16].rearrange("a (r h) -> a r h", r=NCORE)
              bodies.append([(dstc, rows)])
          return bodies

      def merge_bodies(*lists):
          out = []
          for me in range(NCORE):
              pairs = []
              for L in lists:
                  pairs.extend(L[me])
              out.append(pairs)
          return out

      def bias_prep(attn):
          scale = float(192 ** -0.5) if attn else float(128 ** -0.5)
          ACTV(V["CROWF"][:], V["CROW"][0:1, 0:32], AF.Copy,
               r=["CROW"], w=["CROWF"])
          for hl in range(2):
              for j in range(NCH):
                  r_, qc = gloc(j)
                  src = V["CROWF"][0:1, 16 * qc + 2 * r_ + hl:
                                   16 * qc + 2 * r_ + hl + 1]
                  MM(PD[:, 0:1], V["ONESR"][:], src, True, True,
                     r=["CROWF", "ONESR"], w=["PD"])
                  col = attn * 32 + hl * 16 + cidx(j)
                  ACTV(V["BIAS"][:, col:col + 1], PD[:, 0:1], AF.Copy,
                       scale=-scale, r=["PD"], w=["BIAS"])

      SWDMA(merge_bodies(switched_heads_bodies(V["KQA"], GA_outA, D + 1, 0, 0),
                         switched_V_bodies(V["VAA"], GVA_outA, 0),
                         switched_C_bodies(GA_outA, D + 1, D, V["CROW"])),
            "swa", r=["GA_outA", "GVA_outA"], w=["KQA_A", "VAA_A", "CROW"])
      SWDMA(merge_bodies(switched_heads_bodies(V["KQA"], GA_outB, D, 0, 1),
                         switched_V_bodies(V["VAA"], GVA_outB, 1)),
            "swab", r=["GA_outB", "GVA_outB"], w=["KQA_B", "VAA_B"])
      bias_prep(0)

      fillers = []

      def mk_up_filler(jc):
          def f():
              upp = V["UPP0"] if jc % 2 == 0 else V["UPP1"]
              upk = "UPP0" if jc % 2 == 0 else "UPP1"
              DMA("sp", upp[:], UPB[jc], "pw" + upk, r=[], w=[upk])
              SIGN(V["UPS"][:], upp[:], r=[upk], w=["UPS"])
              DMA("pool", UWS[jc], V["UPS"][:], "uwsw", r=["UPS"], w=["UWS"])
          return f

      def mk_dn_filler(dc, ps):
          def f():
              dnp = V["DNP0"] if (dc + ps) % 2 == 0 else V["DNP1"]
              dnk = "DNP0" if (dc + ps) % 2 == 0 else "DNP1"
              DMA("sp", dnp[:], DNB[dc][:, ps * (HID // 2):(ps + 1) * (HID // 2)],
                  "pw" + dnk, r=[], w=[dnk])
              SIGN(V["DNS"][:], dnp[:], r=[dnk], w=["DNS"])
              DMA("pool", DWS[dc][:, ps * (HID // 2):(ps + 1) * (HID // 2)],
                  V["DNS"][:], "dwsw", r=["DNS"], w=["DWS"])
          return f

      for jc in range(JC):
          fillers.append(mk_up_filler(jc))
      for dc in range(DC):
          for ps2 in range(2):
              fillers.append(mk_dn_filler(dc, ps2))

      def bias_prep_b():
          scale = float(192 ** -0.5)
          ACTV(V["CROWF"][:], V["CROW"][0:1, 0:32], AF.Copy,
               r=["CROW"], w=["CROWF"])
          ACTV(V["KMX"][0:1, 0:16], V["KMROW"][0:1, 0:16], AF.Copy,
               r=["KMROW"], w=["KMX"])
          for hl in range(2):
              RED(V["KH"][0:1, hl:hl + 1],
                  V["KMX"].rearrange("a (r h) -> a h r", h=2)[0:1, hl, 0:NCORE],
                  ALU.max, r=["KMX"], w=["KH"])
          for hl in range(2):
              for j in range(NCH):
                  r_, qc = gloc(j)
                  TT(V["ROWA"][0:1, 0:1],
                     V["CROWF"][0:1, 16 * qc + 2 * r_ + hl:
                                16 * qc + 2 * r_ + hl + 1],
                     V["KH"][0:1, hl:hl + 1], ALU.mult,
                     r=["CROWF", "KH"], w=["ROWA"])
                  ACTV(V["ROWA"][0:1, 0:1], V["ROWA"][0:1, 0:1], AF.Sqrt,
                       r=["ROWA"], w=["ROWA"])
                  MM(PD[:, 0:1], V["ONESR"][:], V["ROWA"][0:1, 0:1], True, True,
                     r=["ROWA", "ONESR"], w=["PD"])
                  col = 32 + hl * 16 + cidx(j)
                  ACTV(V["BIAS"][:, col:col + 1], PD[:, 0:1], AF.Copy,
                       scale=-scale, r=["PD"], w=["BIAS"])

      def emit_attn(b, on_hl=None):
          KQ = V["KNB"] if b else V["KQA"]
          QN = V["QNB"] if b else V["KQA"]
          VA = V["VBB"] if b else V["VAA"]
          def kqk(c):
              base = "KNB" if b else "KQA"
              return base + ("_A" if c < NCORE else "_B")
          def qnk(c):
              base = "QNB" if b else "KQA"
              return base + ("_A" if c < NCORE else "_B")
          def vak(c):
              base = "VBB" if b else "VAA"
              return base + ("_A" if c < NCORE else "_B")
          def krk(c):
              return "KRB_A" if c < NCORE else "KRB_B"
          def qrk(c):
              return "QRB_A" if c < NCORE else "QRB_B"
          GY = (GYB_in0, GYB_in1) if b else (GYA_in0, GYA_in1)
          gyk = ("GYB0", "GYB1") if b else ("GYA0", "GYA1")
          scale = float(192 ** -0.5) if b else float(128 ** -0.5)
          psb = [V["PSB2"], V["PSB3"]] if b else [V["PSB0b"], V["PSB1b"]]
          psbk = ["PSB2", "PSB3"] if b else ["PSB0b", "PSB1b"]
          prc = [0]
          for hl in range(2):
              for j in range(NCH):
                  yb = j % 2
                  ybk = "PC" + str(yb)
                  def emit_pv(st):
                      (PSBt_, psbkt_, pr_, kc_, nsub_) = st
                      for s2 in range(nsub_):
                          ks = 2 * pr_ + s2
                          first = (kc_ == 0 and ks == 0)
                          last = (kc_ == j and ks == KS - 1)
                          lv = VA[:, (kc_ * KS + ks) * 256 + hl * 128:
                                  (kc_ * KS + ks) * 256 + hl * 128 + 128]
                          rp = PSBt_[:, s2 * CH: s2 * CH + CH]
                          MM(PC[:, yb * 512: yb * 512 + CH], lv, rp,
                             first, last, r=[psbkt_, vak(kc_)], w=[ybk])
                      for s2 in range(nsub_):
                          ks = 2 * pr_ + s2
                          first = (kc_ == 0 and ks == 0)
                          rp = PSBt_[:, s2 * CH: s2 * CH + CH]
                          if first:
                              CPY(V["SUMP"][:, 0:CH], rp, r=[psbkt_], w=["SUMP"])
                          else:
                              TT(V["SUMP"][:, 0:CH], V["SUMP"][:, 0:CH], rp,
                                 ALU.add, r=["SUMP", psbkt_], w=["SUMP"])
                  pending = None
                  for kc in range(j + 1):
                      for pr in range((KS + 1) // 2):
                          par = prc[0] % 2
                          prc[0] += 1
                          SP_ = PA if par == 0 else PB
                          spk = "PA" if par == 0 else "PB"
                          PSBt, psbkt = psb[par], psbk[par]
                          nsub = min(2, KS - 2 * pr)
                          for s2 in range(nsub):
                              ks = 2 * pr + s2
                              lt = KQ[:, hl * T + kc * CH + ks * 128:
                                      hl * T + kc * CH + ks * 128 + 128]
                              rh_ = QN[:, hl * T + j * CH: hl * T + (j + 1) * CH]
                              bnk = spk + str(s2)
                              MM(SP_[:, s2 * CH:(s2 + 1) * CH], lt, rh_,
                                 True, not b, r=[kqk(kc), qnk(j)], w=[bnk])
                              if b:
                                  ltr = V["KRB"][0:64, kc * CH + ks * 128:
                                                 kc * CH + ks * 128 + 128]
                                  rhr = V["QRB"][0:64, hl * T + j * CH:
                                                 hl * T + (j + 1) * CH]
                                  MM(SP_[:, s2 * CH:(s2 + 1) * CH], ltr, rhr,
                                     False, True, r=[krk(kc), qrk(j)], w=[bnk])
                              if kc == j:
                                  sl = SP_[:, s2 * CH + ks * 128:
                                           s2 * CH + ks * 128 + 128]
                                  TT(sl, sl, V["TRI"][:], ALU.add,
                                     r=[bnk, "TRI"], w=[bnk])
                          bcol = (32 if b else 0) + hl * 16 + cidx(j)
                          rk = [spk + "0"] + ([spk + "1"] if nsub == 2 else [])
                          ACTV(PSBt[:, 0:nsub * CH], SP_[:, 0:nsub * CH], AF.Exp,
                               bias=V["BIAS"][:, bcol:bcol + 1], scale=scale,
                               r=rk + ["BIAS"], w=[psbkt])
                          if kc == j:
                              for s2 in range(nsub):
                                  ks = 2 * pr + s2
                                  if ks > 0:
                                      MEMSET(PSBt[:, s2 * CH: s2 * CH + ks * 128],
                                             0.0, r=[psbkt], w=[psbkt])
                          if pending is not None:
                              emit_pv(pending)
                          pending = (PSBt, psbkt, pr, kc, nsub)
                  if pending is not None:
                      emit_pv(pending)
                  MM(PD[0:1, 0:CH], V["ONESC"][:], V["SUMP"][:, 0:CH],
                     True, True, r=["SUMP", "ONESC"], w=["PD"])
                  RECIP(V["ROWA"][0:1, 0:CH], PD[0:1, 0:CH], r=["PD"], w=["ROWA"])
                  MM(PD[:, 0:CH], V["ONESR"][:], V["ROWA"][0:1, 0:CH], True, True,
                     r=["ROWA", "ONESR"], w=["PD"])
                  ACTV(V["YS"][:, 0:CH], PC[:, yb * 512: yb * 512 + CH], AF.Copy,
                       r=[ybk], w=["YS"])
                  TT(V["YB16"][:, 0:CH], V["YS"][:, 0:CH], PD[:, 0:CH], ALU.mult,
                     r=["YS", "PD"], w=["YB16"])
                  DMA("pool", GY[hl][:, j * CH:(j + 1) * CH],
                      V["YB16"][:, 0:CH], "gy" + ("b" if b else "a") + str(hl),
                      r=["YB16"], w=[gyk[hl]])
                  if not b:
                      take = 2 if fillers else 0
                      for _ in range(take):
                          if fillers:
                              fillers.pop(0)()
                  if on_hl is not None and j == NCH - 1:
                      on_hl(hl)

      def flush_fillers():
          while fillers:
              fillers.pop(0)()

      def ya_coll(hl):
          if hl == 0:
              COLL(GYA_in0, GYA_out0, "cc_gya0", r=["GYA0"], w=["GYA_out0"])
          else:
              COLL(GYA_in1, GYA_out1, "cc_gya1", r=["GYA1"], w=["GYA_out1"])
      emit_attn(0, ya_coll)
      flush_fillers()

      def yret_bodies(gyout, yin, hl):
          bodies = []
          for me in range(NCORE):
              pairs = []
              for s in range(NCORE):
                  for qc in range(2):
                      ch = me if qc == 0 else NCH - 1 - me
                      src = gyout[s * 128:(s + 1) * 128, ch * CH:(ch + 1) * CH]
                      d = yin[:, (s * 2 + hl) * NT + qc * CH:
                              (s * 2 + hl) * NT + (qc + 1) * CH]
                      pairs.append((d, src))
              bodies.append(pairs)
          return bodies

      for hl in range(2):
          gyo = GYA_out0 if hl == 0 else GYA_out1
          gyok = "GYA_out0" if hl == 0 else "GYA_out1"
          yk = "YIN" + str(hl)
          SWDMA(yret_bodies(gyo, V["YIN"], hl), "swya" + str(hl),
                r=[gyok], w=[yk])
          for s in range(NCORE):
              dc = 2 * s + hl
              xw = V["XW0"] if dc % 2 == 0 else V["XW1"]
              xk = "XW0" if dc % 2 == 0 else "XW1"
              DMA("sp", xw[:], XST[:, dc * NT:(dc + 1) * NT], "dm" + xk,
                  r=["XST"], w=[xk])
              TT(xw[:], xw[:],
                 V["YIN"][:, (s * 2 + hl) * NT:(s * 2 + hl + 1) * NT],
                 ALU.add, r=[xk, yk], w=[xk])
              DMA("sp", XST[:, dc * NT:(dc + 1) * NT], xw[:], "st" + xk,
                  r=[xk], w=["XST"])

    P.barrier()

    def emit_ffn(first):
        emit_ln(V["NF"], "NF")
        JH = JC // 2
        for ps in range(2):
            for jl in range(JH):
                jc = ps * JH + jl
                uws = V["UWSB0"] if jl % 2 == 0 else V["UWSB1"]
                uwk = "UWSB0" if jl % 2 == 0 else "UWSB1"
                DMA("sp", uws[:], UWS[jc], "up" + uwk, r=["UWS"], w=[uwk])
                PU, puk = (PA, "PA") if jl % 2 == 0 else (PC, "PC")
                for kc in range(DC):
                    for hh in range(NH):
                        MM(PU[:, hh * 512: hh * 512 + 512],
                           uws[:, kc * 128:(kc + 1) * 128],
                           V["NF"][:, kc * NT + hh * 512: kc * NT + hh * 512 + 512],
                           kc == 0, kc == DC - 1,
                           r=[uwk, "NF"], w=[puk + str(hh)])
                for hh in range(NH):
                    op("act", (lambda jl_, hh_, PU_, jc_: lambda e: e.activation(
                        V["HSB"][:, jl_ * NH * 512 + hh_ * 512:
                                 jl_ * NH * 512 + hh_ * 512 + 512],
                        PU_[:, hh_ * 512: hh_ * 512 + 512], AF.Relu,
                        scale=V["USC"][:, jc_:jc_ + 1]))(jl, hh, PU, jc),
                       r=[puk + str(hh), "USC"], w=["HSB"])
            for dc in range(DC):
                dws = V["DWSB0"] if dc % 2 == 0 else V["DWSB1"]
                dwk = "DWSB0" if dc % 2 == 0 else "DWSB1"
                DMA("sp", dws[:, 0:HID // 2],
                    DWS[dc][:, ps * (HID // 2):(ps + 1) * (HID // 2)],
                    "dn" + dwk, r=["DWS"], w=[dwk])
                PDO, pdk = (PB, "PB") if dc % 2 == 0 else (PA, "PA")
                for jl in range(JH):
                    for hh in range(NH):
                        MM(PDO[:, hh * 512: hh * 512 + 512],
                           dws[:, jl * 128:(jl + 1) * 128],
                           V["HSB"][:, jl * NH * 512 + hh * 512:
                                    jl * NH * 512 + hh * 512 + 512],
                           jl == 0, jl == JH - 1,
                           r=[dwk, "HSB"], w=[pdk + str(hh)])
                for hh in range(NH):
                    hs = hh * 512
                    xw = V["XW0"] if dc % 2 == 0 else V["XW1"]
                    xk = "XW0" if dc % 2 == 0 else "XW1"
                    DMA("sp", xw[:, 0:512],
                        XST[:, dc * NT + hs: dc * NT + hs + 512],
                        "dm" + xk, r=["XST"], w=[xk])
                    op("dve", (lambda dc_, hh_, xw_, PDO_: lambda e:
                        e.scalar_tensor_tensor(
                            xw_[:, 0:512], PDO_[:, hh_ * 512: hh_ * 512 + 512],
                            V["CSC"][:, dc_:dc_ + 1], xw_[:, 0:512],
                            ALU.mult, ALU.add))(dc, hh, xw, PDO),
                       r=[pdk + str(hh), "CSC", xk], w=[xk])
                    DMA("sp", XST[:, dc * NT + hs: dc * NT + hs + 512],
                        xw[:, 0:512], "st" + xk, r=[xk], w=["XST"])

    if STAGE >= 2:
        emit_ffn(False)
    P.barrier()

    if STAGE >= 2.5:
      emit_ln(V["NP"], "NP")

      def proj_mm(blk_dram, m, out_psum, pk, nsrc, nsrck, slot_extra=""):
          wb = V["WBLK0"] if m % 2 == 0 else V["WBLK1"]
          wk = "WBLK0" if m % 2 == 0 else "WBLK1"
          wbb = V["WBB0"] if m % 2 == 0 else V["WBB1"]
          wbbk = "WBB0" if m % 2 == 0 else "WBB1"
          nb = blk_dram.shape[1]
          DMA("sp", wb[:, 0:nb], blk_dram, "pr" + wk + slot_extra,
              r=[], w=[wk])
          ACTV(wbb[:, 0:nb], wb[:, 0:nb], AF.Copy, r=[wk], w=[wbbk])
          kcn = nb // 128
          for kc in range(kcn):
              for hh in range(NH):
                  MM(out_psum[:, hh * 512: hh * 512 + 512],
                     wbb[:, kc * 128:(kc + 1) * 128],
                     nsrc[:, kc * NT + hh * 512: kc * NT + hh * 512 + 512],
                     kc == 0, kc == kcn - 1, r=[wbbk, nsrck], w=[pk + str(hh)])

      for m in range(24):
          PQ, pk = (PA, "PA") if m % 2 == 0 else (PB, "PB")
          proj_mm(WQ[m], m, PQ, pk, V["NP"], "NP")
          pks = [pk + str(hh) for hh in range(NH)]
          if m < 16:
              ACTV(V["STG"][:], PQ[:, 0:NT], AF.Copy, r=pks, w=["STG"])
              DMA("sp", GB_inA[m * 128:(m + 1) * 128, :], V["STG"][:, 0:CH],
                  "gqn", r=["STG"], w=["GB_inA"])
              DMA("sp", GB_inB[m * 128:(m + 1) * 128, :], V["STG"][:, CH:NT],
                  "gqnb", r=["STG"], w=["GB_inB"])
              for hh in range(NH):
                  sq = V["SQ0"] if hh == 0 else V["SQ1"]
                  sqk = "SQ0" if hh == 0 else "SQ1"
                  ACTV(sq[:], PQ[:, hh * 512: hh * 512 + 512], AF.Square,
                       r=[pk + str(hh)], w=[sqk])
                  rw, rwk = row(hh)
                  MM(rw, V["ONESC"][:], sq[:], True, True, r=[sqk, "ONESC"], w=[rwk])
              for qc in range(2):
                  hh = (qc * CH) // 512
                  c0 = qc * CH - hh * 512
                  rw, rwk = row(hh)
                  RED(V["QN2"][0:1, qc * 16 + m: qc * 16 + m + 1],
                      rw[0:1, c0:c0 + CH], ALU.max, r=[rwk], w=["QN2"])
          else:
              h1, h2 = 2 * (m - 16), 2 * (m - 16) + 1
              ACTV(V["RQ"][0:64, h1 * NT:(h1 + 1) * NT], PQ[0:64, 0:NT], AF.Copy,
                   r=pks, w=["RQ"])
              ACTV(V["RQ"][0:64, h2 * NT:(h2 + 1) * NT], PQ[64:128, 0:NT], AF.Copy,
                   r=pks, w=["RQ"])

      def rope_apply(src, srck, dst, dstk):
          TS(V["RH"][0:32, :], src[32:64, :], -1.0, ALU.mult, r=[srck], w=["RH"])
          CPY(V["RH"][32:64, :], src[0:32, :], r=[srck], w=["RH"])
          TT(V["RT1"][0:64, :], src, V["COS"][0:64, :], ALU.mult,
             r=[srck, "COS"], w=["RT1"])
          TT(V["RT2"][0:64, :], V["RH"][0:64, :], V["SIN"][0:64, :], ALU.mult,
             r=["RH", "SIN"], w=["RT2"])
          TT(dst, V["RT1"][0:64, :], V["RT2"][0:64, :], ALU.add,
             r=["RT1", "RT2"], w=[dstk])

      for h in range(H):
          rope_apply(V["RQ"][0:64, h * NT:(h + 1) * NT], "RQ",
                     V["RQ"][0:64, h * NT:(h + 1) * NT], "RQ")
          ACTV(V["RT1"][0:64, :], V["RQ"][0:64, h * NT:(h + 1) * NT], AF.Square,
               r=["RQ"], w=["RT1"])
          for hh in range(NH):
              rw, rwk = row(hh)
              MM(rw, V["ONESC"][0:64, :], V["RT1"][0:64, hh * 512: hh * 512 + 512],
                 True, True, r=["RT1", "ONESC"], w=[rwk])
          for qc in range(2):
              hh = (qc * CH) // 512
              c0 = qc * CH - hh * 512
              rw, rwk = row(hh)
              RED(V["ROWA"][0:1, 0:1], rw[0:1, c0:c0 + CH], ALU.max,
                  r=[rwk], w=["ROWA"])
              TT(V["QN2"][0:1, qc * 16 + h: qc * 16 + h + 1],
                 V["QN2"][0:1, qc * 16 + h: qc * 16 + h + 1],
                 V["ROWA"][0:1, 0:1], ALU.add, r=["QN2", "ROWA"], w=["QN2"])
      DMA("sp", GB_inA[2 * D: 2 * D + H * 64, :].rearrange(
              "(h p) t -> p h t", p=64),
          V["RQ"][0:64, :].rearrange("p (h t) -> p h t", h=H)[:, :, 0:CH],
          "gqr", r=["RQ"], w=["GB_inA"])
      DMA("sp", GB_inB[2 * D: 2 * D + H * 64, :].rearrange(
              "(h p) t -> p h t", p=64),
          V["RQ"][0:64, :].rearrange("p (h t) -> p h t", h=H)[:, :, CH:NT],
          "gqrb", r=["RQ"], w=["GB_inB"])

      for m in range(4):
          PQ, pk = (PA, "PA") if m % 2 == 0 else (PB, "PB")
          proj_mm(WKA[m], m, PQ, pk, V["NP"], "NP")
          pks = [pk + str(hh) for hh in range(NH)]
          ACTV(V["LATN"][:, m * NT:(m + 1) * NT], PQ[:, 0:NT], AF.Copy,
               r=pks, w=["LATN"])
          for hh in range(NH):
              sq = V["SQ0"] if hh == 0 else V["SQ1"]
              sqk = "SQ0" if hh == 0 else "SQ1"
              ACTV(sq[:], PQ[:, hh * 512: hh * 512 + 512], AF.Square,
                   r=[pk + str(hh)], w=[sqk])
              rw, rwk = row(hh)
              MM(rw, V["ONESC"][:], sq[:], m == 0, m == 3, r=[sqk, "ONESC"], w=[rwk])
      for hh in range(NH):
          rw, rwk = row(hh)
          dstrow = V["ROWA"] if hh == 0 else V["ROWB"]
          dk = "ROWA" if hh == 0 else "ROWB"
          TS(dstrow[:], rw, 1.0 / 512.0, ALU.mult, r=[rwk], w=[dk])
          ACTV(dstrow[:], dstrow[:], AF.Sqrt, bias=V["EPSB"][:],
               r=[dk, "EPSB"], w=[dk])
          RECIP(dstrow[:], dstrow[:], r=[dk], w=[dk])
      for hh in range(NH):
          dstrow = V["ROWA"] if hh == 0 else V["ROWB"]
          dk = "ROWA" if hh == 0 else "ROWB"
          MM(PC[:, hh * 512: hh * 512 + 512], V["ONESR"][:], dstrow[:], True, True,
             r=[dk, "ONESR"], w=["PC" + str(hh)])
      for m in range(4):
          for hh in range(NH):
              STT(V["LATN"][:, m * NT + hh * 512: m * NT + hh * 512 + 512],
                  V["LATN"][:, m * NT + hh * 512: m * NT + hh * 512 + 512],
                  V["KNW"][:, m:m + 1], PC[:, hh * 512: hh * 512 + 512],
                  ALU.mult, ALU.mult, r=["LATN", "KNW", "PC" + str(hh)], w=["LATN"])

      for kc in range(DC):
          for hh in range(NH):
              MM(PA[0:64, hh * 512: hh * 512 + 512],
                 V["WKARb"][:, kc * 64:(kc + 1) * 64],
                 V["NP"][:, kc * NT + hh * 512: kc * NT + hh * 512 + 512],
                 kc == 0, kc == DC - 1, r=["WKARb", "NP"], w=["PA" + str(hh)])
      ACTV(V["KRO"][0:64, :], PA[0:64, 0:NT], AF.Copy,
           r=["PA" + str(hh) for hh in range(NH)], w=["KRO"])
      rope_apply(V["KRO"][0:64, :], "KRO", V["KRO"][0:64, :], "KRO")
      DMA("sp", GB_inA[2 * D + H * 64: 2 * D + H * 64 + 64, :],
          V["KRO"][0:64, 0:CH], "gkr", r=["KRO"], w=["GB_inA"])
      DMA("sp", GB_inB[2 * D + H * 64: 2 * D + H * 64 + 64, :],
          V["KRO"][0:64, CH:NT], "gkrb", r=["KRO"], w=["GB_inB"])
      ACTV(V["RT1"][0:64, :], V["KRO"][0:64, :], AF.Square, r=["KRO"], w=["RT1"])
      for hh in range(NH):
          rw, rwk = row(hh)
          MM(rw, V["ONESC"][0:64, :], V["RT1"][0:64, hh * 512: hh * 512 + 512],
             True, True, r=["RT1", "ONESC"], w=[rwk])
          dst = V["ROWC"] if hh == 0 else V["ROWD"]
          dk = "ROWC" if hh == 0 else "ROWD"
          CPY(dst[:], rw, r=[rwk], w=[dk])

      emit_vstage_pending = []
      for m in range(32):
          PQ, pk = (PA, "PA") if m % 2 == 0 else (PB, "PB")
          proj_mm(WKB[m], m, PQ, pk, V["LATN"], "LATN")
          pks = [pk + str(hh) for hh in range(NH)]
          if m < 16:
              ACTV(V["STG"][:], PQ[:, 0:NT], AF.Copy, r=pks, w=["STG"])
              DMA("sp", GB_inA[D + m * 128: D + (m + 1) * 128, :],
                  V["STG"][:, 0:CH], "gkn", r=["STG"], w=["GB_inA"])
              DMA("sp", GB_inB[D + m * 128: D + (m + 1) * 128, :],
                  V["STG"][:, CH:NT], "gknb", r=["STG"], w=["GB_inB"])
              first_red = True
              for hh in range(NH):
                  sq = V["SQ0"] if hh == 0 else V["SQ1"]
                  sqk = "SQ0" if hh == 0 else "SQ1"
                  ACTV(sq[:], PQ[:, hh * 512: hh * 512 + 512], AF.Square,
                       r=[pk + str(hh)], w=[sqk])
                  rw, rwk = row(hh)
                  MM(rw, V["ONESC"][:], sq[:], True, True, r=[sqk, "ONESC"], w=[rwk])
                  ro = V["ROWC"] if hh == 0 else V["ROWD"]
                  rok = "ROWC" if hh == 0 else "ROWD"
                  sq2 = V["SQ0"] if hh == 0 else V["SQ1"]
                  TT(sq2[0:1, 0:512], rw, ro[:], ALU.add, r=[rwk, rok], w=[sqk])
                  RED(V["ROWB"][0:1, hh:hh + 1], sq2[0:1, 0:512], ALU.max,
                      r=[sqk], w=["ROWB"])
              if NH == 2:
                  TT(V["ROWB"][0:1, 0:1], V["ROWB"][0:1, 0:1], V["ROWB"][0:1, 1:2],
                     ALU.max, r=["ROWB"], w=["ROWB"])
              CPY(V["KMN"][0:1, m:m + 1], V["ROWB"][0:1, 0:1], r=["ROWB"], w=["KMN"])
          else:
              h = m - 16
              ACTV(V["STG"][:], PQ[:, 0:NT], AF.Copy, r=pks, w=["STG"])
              for i in range(NTT):
                  TRANS(PTB[:, 0:128], V["STG"][:, i * 128:(i + 1) * 128],
                        V["IDB"][:], r=["STG", "IDB"], w=["PTB"])
                  CPY(V["VSTG2"][:, (i * H + h) * 128: (i * H + h) * 128 + 128],
                      PTB[:, 0:128], r=["PTB"], w=["VSTG2"], eng="act")
      DMA("sp", GVB_inA.rearrange("(i p) c -> p i c", p=128),
          V["VSTG2"].rearrange("p (i c) -> p i c", i=NTT)[:, 0:NTT // 2, :],
          "gvb", r=["VSTG2"], w=["GVB_inA"])
      DMA("sp", GVB_inB.rearrange("(i p) c -> p i c", p=128),
          V["VSTG2"].rearrange("p (i c) -> p i c", i=NTT)[:, NTT // 2:NTT, :],
          "gvbb", r=["VSTG2"], w=["GVB_inB"])
      MEMSET(V["CSTG"][:], 0.0, r=[], w=["CSTG"])
      ACTV(V["CSTG"][0:1, 0:16], V["KMN"][0:1, 0:16], AF.Copy,
           r=["KMN"], w=["CSTG"])
      DMA("sp", GB_inA[NRB - 1: NRB, :], V["CSTG"][0:1, 0:CH], "gkm",
          r=["CSTG"], w=["GB_inA"])
      MEMSET(V["CSTG2"][:], 0.0, r=[], w=["CSTG2"])
      ACTV(V["CSTG2"][0:1, 0:32], V["QN2"][0:1, 0:32], AF.Copy,
           r=["QN2"], w=["CSTG2"])
      DMA("sp", GB_inA[NRB - 2: NRB - 1, :], V["CSTG2"][0:1, 0:CH], "gqn2",
          r=["CSTG2"], w=["GB_inA"])
      COLL(GB_inA, GB_outA, "cc_gba", r=["GB_inA"], w=["GB_outA"])
      COLL(GVB_inA, GVB_outA, "cc_gvba", r=["GVB_inA"], w=["GVB_outA"])
      COLL(GB_inB, GB_outB, "cc_gbb", r=["GB_inB"], w=["GB_outB"])
      COLL(GVB_inB, GVB_outB, "cc_gvbb", r=["GVB_inB"], w=["GVB_outB"])

      P.barrier()

      pass
    if STAGE >= 2.8:
      def qr_bodies_f(gout, rowblk, qc):
          out = []
          for me in range(NCORE):
              pairs = []
              for s in range(NCORE):
                  for hl in range(2):
                      j = s if qc == 0 else NCH - 1 - s
                      r0 = s * rowblk + 2 * D + (2 * me + hl) * 64
                      src = gout[r0: r0 + 64, :]
                      d = V["QRB"][0:64, hl * T + j * CH: hl * T + (j + 1) * CH]
                      pairs.append((d, src))
              out.append(pairs)
          return out
      for s in range(NCORE):
          r0 = s * NRB + 2 * D + H * 64
          DMA("pool", V["KRB"][0:64, s * CH:(s + 1) * CH],
              GB_outA[r0: r0 + 64, :], "krl", r=["GB_outA"], w=["KRB_A"])
          r0 = s * (NRB - 2) + 2 * D + H * 64
          j = NCH - 1 - s
          DMA("pool", V["KRB"][0:64, j * CH:(j + 1) * CH],
              GB_outB[r0: r0 + 64, :], "krlb", r=["GB_outB"], w=["KRB_B"])
      SWDMA(merge_bodies(switched_heads_bodies(V["QNB"], GB_outA, NRB, 0, 0),
                         switched_heads_bodies(V["KNB"], GB_outA, NRB, D, 0),
                         qr_bodies_f(GB_outA, NRB, 0),
                         switched_V_bodies(V["VBB"], GVB_outA, 0),
                         switched_C_bodies(GB_outA, NRB, NRB - 2, V["CROW"]),
                         switched_KM_bodies(GB_outA, NRB, NRB - 1)),
            "swb", r=["GB_outA", "GVB_outA"],
            w=["KNB_A", "QNB_A", "QRB_A", "VBB_A", "CROW", "KMROW"])
      SWDMA(merge_bodies(switched_heads_bodies(V["QNB"], GB_outB, NRB - 2, 0, 1),
                         switched_heads_bodies(V["KNB"], GB_outB, NRB - 2, D, 1),
                         qr_bodies_f(GB_outB, NRB - 2, 1),
                         switched_V_bodies(V["VBB"], GVB_outB, 1)),
            "swbb", r=["GB_outB", "GVB_outB"],
            w=["KNB_B", "QNB_B", "QRB_B", "VBB_B"])
      bias_prep_b()
      if cfg.get("dump"):
          DBG1 = nc.dram_tensor("dbg_bias", [128, 64], F32, kind="ExternalOutput").ap()
          DBG2 = nc.dram_tensor("dbg_crow", [1, 32], F32, kind="ExternalOutput").ap()
          DBG3 = nc.dram_tensor("dbg_qn2", [1, 32], F32, kind="ExternalOutput").ap()
          DBG4 = nc.dram_tensor("dbg_kmx", [1, 16], F32, kind="ExternalOutput").ap()
          DBG5 = nc.dram_tensor("dbg_kqs", [128, 2048], BF16, kind="ExternalOutput").ap()
          DBG6 = nc.dram_tensor("dbg_qns", [128, 2048], BF16, kind="ExternalOutput").ap()
          DBG7 = nc.dram_tensor("dbg_vbs", [128, 2048], BF16, kind="ExternalOutput").ap()
          DBG8 = nc.dram_tensor("dbg_qrs", [64, 2048], BF16, kind="ExternalOutput").ap()
          DBG9 = nc.dram_tensor("dbg_krs", [64, 2048], BF16, kind="ExternalOutput").ap()
          DMA("pool", DBG1, V["BIAS"][:], "dbgw", r=["BIAS"], w=["DBG"])
          DMA("pool", DBG2, V["CROW"][0:1, 0:32], "dbgw", r=["CROW"], w=["DBG"])
          DMA("pool", DBG3, V["QN2"][0:1, 0:32], "dbgw", r=["QN2"], w=["DBG"])
          DMA("pool", DBG4, V["KMX"][0:1, 0:16], "dbgw", r=["KMX"], w=["DBG"])
          DMA("pool", DBG5, V["KNB"][:, 0:2048], "dbgw", r=["KNB"], w=["DBG"])
          DMA("pool", DBG6, V["QNB"][:, 0:2048], "dbgw", r=["QNB"], w=["DBG"])
          DMA("pool", DBG7, V["VBB"][:, 0:2048], "dbgw", r=["VBB"], w=["DBG"])
          DMA("pool", DBG8, V["QRB"][0:64, 0:2048], "dbgw", r=["QRB"], w=["DBG"])
          DMA("pool", DBG9, V["KRB"][0:64, 0:2048], "dbgw", r=["KRB"], w=["DBG"])
          DBG10 = nc.dram_tensor("dbg_gkm", [NCORE, 16], F32, kind="ExternalOutput").ap()
          DBG11 = nc.dram_tensor("dbg_kmn", [1, 128], F32, kind="ExternalOutput").ap()
          DBG12 = nc.dram_tensor("dbg_rowcd", [1, 1024], F32, kind="ExternalOutput").ap()
          DMA("pool", DBG10, GKM_out, "dbgw", r=["GKM_out"], w=["DBG"])
          DMA("pool", DBG11, V["KMN"][0:1, 0:128], "dbgw", r=["KMN"], w=["DBG"])
          DMA("pool", DBG12[0:1, 0:512], V["ROWC"][0:1, 0:512], "dbgw", r=["ROWC"], w=["DBG"])
          DMA("pool", DBG12[0:1, 512:1024], V["ROWD"][0:1, 0:512], "dbgw", r=["ROWD"], w=["DBG"])
          DBG13 = nc.dram_tensor("dbg_kro", [64, NT], BF16, kind="ExternalOutput").ap()
          DBG14 = nc.dram_tensor("dbg_latn", [128, NT], BF16, kind="ExternalOutput").ap()
          DBG15 = nc.dram_tensor("dbg_np", [128, NT], BF16, kind="ExternalOutput").ap()
          DMA("pool", DBG13, V["KRO"][0:64, :], "dbgw", r=["KRO"], w=["DBG"])
          DMA("pool", DBG14, V["LATN"][:, 0:NT], "dbgw", r=["LATN"], w=["DBG"])
          DMA("pool", DBG15, V["NP"][:, 0:NT], "dbgw", r=["NP"], w=["DBG"])
          op("sp", lambda e: e.nop(), r=["DBG"], w=[])

      pass
    if STAGE >= 3:
      def yb_coll(hl):
          if hl == 0:
              COLL(GYB_in0, GYB_out0, "cc_gyb0", r=["GYB0"], w=["GYB_out0"])
          else:
              COLL(GYB_in1, GYB_out1, "cc_gyb1", r=["GYB1"], w=["GYB_out1"])
      emit_attn(1, yb_coll)

      P.barrier()

      def yfm_bodies(gyout, hl):
          bodies = []
          for me in range(NCORE):
              pairs = []
              for s in range(NCORE):
                  h = 2 * s + hl
                  for qc in range(2):
                      ch = me if qc == 0 else NCH - 1 - me
                      src = gyout[s * 128:(s + 1) * 128, ch * CH:(ch + 1) * CH]
                      d = V["YFM"][:, h * NT + qc * CH: h * NT + (qc + 1) * CH]
                      pairs.append((d, src))
              bodies.append(pairs)
          return bodies

      for hl in range(2):
          gyo = GYB_out0 if hl == 0 else GYB_out1
          gyok = "GYB_out0" if hl == 0 else "GYB_out1"
          SWDMA(yfm_bodies(gyo, hl), "swyb" + str(hl),
                r=[gyok], w=["YFM"])
      for dc in range(DC):
          wob = V["WOB0"] if dc % 2 == 0 else V["WOB1"]
          wbk = "WOB0" if dc % 2 == 0 else "WOB1"
          DMA("sp", wob[:], WOd[dc], "wo" + wbk, r=[], w=[wbk])
          ACTV(V["WOBB"][:], wob[:], AF.Copy, r=[wbk], w=["WOBB"])
          for hv in range(H):
              for hh in range(NH):
                  MM(PA[:, hh * 512: hh * 512 + 512],
                     V["WOBB"][:, hv * 128:(hv + 1) * 128],
                     V["YFM"][:, hv * NT + hh * 512: hv * NT + hh * 512 + 512],
                     hv == 0, hv == H - 1, r=["WOBB", "YFM"], w=["PA" + str(hh)])
          xw = V["XW0"] if dc % 2 == 0 else V["XW1"]
          xk = "XW0" if dc % 2 == 0 else "XW1"
          DMA("sp", xw[:], XST[:, dc * NT:(dc + 1) * NT], "dm" + xk,
              r=["XST"], w=[xk])
          op("dve", (lambda xw_: lambda e: e.tensor_add(
              xw_[:, 0:NT], xw_[:, 0:NT], PA[:, 0:NT]))(xw),
             r=[xk] + ["PA" + str(hh) for hh in range(NH)], w=[xk])
          DMA("sp", XST[:, dc * NT:(dc + 1) * NT], xw[:], "st" + xk,
              r=[xk], w=["XST"])

    P.barrier()

    if STAGE >= 4:
        emit_ffn(False)
    P.barrier()

    for dc in range(DC):
        xw = V["XW0"] if dc % 2 == 0 else V["XW1"]
        xk = "XW0" if dc % 2 == 0 else "XW1"
        DMA("sp", xw[:], XST[:, dc * NT:(dc + 1) * NT], "dm" + xk,
            r=["XST"], w=[xk])
        for i in range(NTT):
            os_ = V["OS0"] if i % 2 == 0 else V["OS1"]
            osk = "OS0" if i % 2 == 0 else "OS1"
            TRANS(PA[:, 0:128], xw[:, i * 128:(i + 1) * 128], V["IDF"][:],
                  r=[xk, "IDF"], w=["PA0"])
            CPY(os_[:], PA[:, 0:128], r=["PA0"], w=[osk])
            DMA("pool", OUT[i * 128:(i + 1) * 128, dc * 128:(dc + 1) * 128],
                os_[:], "outw", r=[osk], w=["OUTW"])
    op("sp", lambda e: e.nop(), r=["OUTW"], w=[])

    emit = P.finalize(nc, stack)

    with nc.Block() as block:
        def runner(items):
            def go(eng):
                for waits, fn, sem, inc, selfsem in items:
                    for s_, v_ in waits:
                        eng.wait_ge(s_, v_)
                    if selfsem is not None:
                        fn(eng, selfsem)
                    else:
                        inst = fn(eng)
                        if sem is not None:
                            inst.then_inc(sem, inc)
            return go
        block.tensor(runner(emit["pe"]))
        block.scalar(runner(emit["act"]))
        block.vector(runner(emit["dve"]))
        block.sync(runner(emit["sp"]))
        block.gpsimd(runner(emit["pool"]))

    stack.pop_all()
    return nc


def kernel(**inputs):
    cfg = make_cfg()
    maps = host_prep(cfg, inputs)
    nc = build(cfg)
    res = run_bass_kernel_spmd(nc, maps, core_ids=list(range(NCORE)))
    T, D, CH = cfg["T"], cfg["D"], cfg["CH"]
    out = np.zeros((T, D), np.float32)
    for me in range(NCORE):
        y = res.results[me]["out_own"]
        out[me * CH:(me + 1) * CH] = y[:CH]
        out[(NCH - 1 - me) * CH:(NCH - me) * CH] = y[CH:]
    return out.reshape(1, T, D)


# version 09 (final)
# speedup vs baseline: 1.4359x; 1.0063x over previous
import numpy as np
import ml_dtypes
from contextlib import ExitStack

import concourse.bass as bass
import concourse.mybir as mybir
from concourse.bass_utils import run_bass_kernel_spmd

F32 = mybir.dt.float32
BF16 = mybir.dt.bfloat16
AF = mybir.ActivationFunctionType
ALU = mybir.AluOpType
AX = mybir.AxisListType

NEGBIG = -1.0e30
NCORE = 8
NCH = 16


def make_cfg(T=8192, HID=8192):
    c = {}
    c["T"], c["D"], c["H"] = T, 2048, 16
    c["CH"] = T // NCH
    c["KS"] = c["CH"] // 128
    c["NT"] = 2 * c["CH"]
    c["NH"] = c["NT"] // 512
    c["DC"], c["HID"], c["JC"] = 16, HID, HID // 128
    c["LNEPS"], c["RMSEPS"] = 1e-5, 1e-6
    c["stage"] = 4
    assert c["NT"] % 512 == 0 and c["CH"] % 128 == 0
    return c


def gloc(j):
    return (j, 0) if j < NCORE else (NCH - 1 - j, 1)


def cidx(j):
    return j if j < NCORE else 8 + (NCH - 1 - j)



def prep_blocks(W):
    K, M = W.shape
    Kc, Mc = K // 128, M // 128
    return np.ascontiguousarray(
        W.reshape(Kc, 128, Mc, 128).transpose(2, 1, 0, 3).reshape(Mc, 128, Kc * 128))


def host_prep(cfg, inputs):
    T, D, H, CH = cfg["T"], cfg["D"], cfg["H"], cfg["CH"]
    JC, DC = cfg["JC"], cfg["DC"]
    x = np.asarray(inputs["L0_attn_a_x"], np.float32).reshape(T, D)
    xT = np.ascontiguousarray(x.T)

    wq = np.asarray(inputs["L0_attn_b_wq"], np.float32)
    nope_cols = np.concatenate([wq[:, 192 * h:192 * h + 128] for h in range(H)], 1)
    rope_cols = np.concatenate([wq[:, 192 * h + 128:192 * h + 192] for h in range(H)], 1)
    wq_blk = prep_blocks(np.concatenate([nope_cols, rope_cols], 1))

    wkva = np.asarray(inputs["L0_attn_b_wkv_a"], np.float32)
    wkva_main = prep_blocks(wkva[:, :512])
    wkva_rope = np.ascontiguousarray(
        wkva[:, 512:].reshape(16, 128, 64).transpose(1, 0, 2).reshape(128, 1024))

    wkvb = np.asarray(inputs["L0_attn_b_wkv_b"], np.float32)
    nope_c = np.concatenate([wkvb[:, 256 * h:256 * h + 128] for h in range(H)], 1)
    v_c = np.concatenate([wkvb[:, 256 * h + 128:256 * h + 256] for h in range(H)], 1)
    wkvb_blk = prep_blocks(np.concatenate([nope_c, v_c], 1))

    wo_blk = prep_blocks(np.asarray(inputs["L0_attn_b_wo"], np.float32))

    up_blk = prep_blocks(np.ascontiguousarray(
        np.asarray(inputs["L0_ffn_up_weight"], np.float32).T))
    down_blk = prep_blocks(np.ascontiguousarray(
        np.asarray(inputs["L0_ffn_down_weight"], np.float32).T))

    usc = np.ascontiguousarray(
        np.asarray(inputs["L0_ffn_up_scale"], np.float32).reshape(JC, 128).T)
    dsc = np.ascontiguousarray(
        np.asarray(inputs["L0_ffn_down_scale"], np.float32).reshape(DC, 128).T)
    knw = np.ascontiguousarray(
        np.asarray(inputs["L0_attn_b_kv_norm_w"], np.float32).reshape(4, 128).T)
    osc = np.asarray(inputs["L0_ffn_output_scale"], np.float32).reshape(1, 1)

    ones_col = np.ones((128, 1), np.float32)
    ones_row = np.ones((1, 128), np.float32)
    idf = np.eye(128, dtype=np.float32)
    idb = np.eye(128, dtype=ml_dtypes.bfloat16)
    tri = np.where(np.arange(128)[None, :] >= np.arange(128)[:, None],
                   0.0, NEGBIG).astype(np.float32)

    inv = 1.0 / (10000.0 ** (np.arange(0, 64, 2, dtype=np.float32) / 64.0))
    maps = []
    for me in range(NCORE):
        c0, c1 = me, NCH - 1 - me
        cols = np.concatenate([np.arange(c0 * CH, (c0 + 1) * CH),
                               np.arange(c1 * CH, (c1 + 1) * CH)])
        f = np.outer(inv, cols.astype(np.float32))
        emb = np.concatenate([f, f], 0)
        maps.append({
            "x0T_own": np.ascontiguousarray(xT[:, cols]),
            "wq_blk": wq_blk, "wkva_main": wkva_main, "wkva_rope": wkva_rope,
            "wkvb_blk": wkvb_blk, "wo_blk": wo_blk,
            "up_blk": up_blk, "down_blk": down_blk,
            "usc": usc, "dsc": dsc, "knw": knw, "osc": osc,
            "ones_col": ones_col, "ones_row": ones_row,
            "idf": idf, "idb": idb, "tri": tri,
            "cosq": np.cos(emb).astype(np.float32),
            "sinq": np.sin(emb).astype(np.float32),
        })
    return maps



ENGS = ("pe", "act", "dve", "sp", "pool")


class Op:
    __slots__ = ("eng", "fn", "reads", "writes", "sem_key", "inc", "idx",
                 "edges", "need_inc", "val", "clock", "self_inc")

    def __init__(self, eng, fn, reads, writes, sem_key, inc, idx, self_inc):
        self.eng, self.fn = eng, fn
        self.reads, self.writes = tuple(reads), tuple(writes)
        self.sem_key, self.inc, self.idx = sem_key, inc, idx
        self.self_inc = self_inc
        self.edges = []
        self.need_inc = False
        self.val = None
        self.clock = None


class Prog:
    def __init__(self):
        self.ops = []
        self.last_w = {}
        self.readers = {}
        self.pending = {e: [] for e in ENGS}

    def op(self, eng, fn, r=(), w=(), sem_key=None, inc=1, self_inc=False):
        o = Op(eng, fn, r, w, sem_key, inc, len(self.ops), self_inc)
        seen = {}
        for k in o.reads:
            p = self.last_w.get(k)
            if p is not None:
                seen[p.idx] = p
        for k in o.writes:
            p = self.last_w.get(k)
            if p is not None:
                seen[p.idx] = p
            for rd in self.readers.get(k, ()):
                seen[rd.idx] = rd
        for p in self.pending[eng]:
            seen[p.idx] = p
        self.pending[eng] = []
        seen.pop(o.idx, None)
        o.edges = list(seen.values())
        for k in o.writes:
            self.last_w[k] = o
            self.readers[k] = []
        for k in o.reads:
            self.readers.setdefault(k, []).append(o)
        self.ops.append(o)
        return o

    def barrier(self):
        lasts = {}
        for o in self.ops:
            key = ("g", o.sem_key) if o.sem_key is not None else ("eng", o.eng)
            lasts[key] = o
        alll = list(lasts.values())
        for e in ENGS:
            self.pending[e] = list(alll)

    def finalize(self, nc, stack):
        for o in self.ops:
            for p in o.edges:
                p.need_inc = True
            if o.sem_key is not None:
                o.need_inc = True
        sems = {}
        for e in ENGS:
            sems[("eng", e)] = stack.enter_context(nc.semaphore(f"s_{e}"))
        for o in self.ops:
            if o.sem_key is not None and ("g", o.sem_key) not in sems:
                sems[("g", o.sem_key)] = stack.enter_context(
                    nc.semaphore(f"s_{o.sem_key}"))
        counts = {}
        for o in self.ops:
            if not o.need_inc:
                continue
            key = ("g", o.sem_key) if o.sem_key is not None else ("eng", o.eng)
            counts[key] = counts.get(key, 0) + o.inc
            o.val = counts[key]
        eng_clock = {e: {} for e in ENGS}
        emit = {e: [] for e in ENGS}
        for o in self.ops:
            clk = eng_clock[o.eng]
            waits = []
            for p in sorted(o.edges, key=lambda q: q.idx):
                pkey = ("g", p.sem_key) if p.sem_key is not None else ("eng", p.eng)
                if clk.get(pkey, 0) < p.val:
                    waits.append((sems[pkey], p.val))
                    clk[pkey] = p.val
                if p.clock:
                    for k2, v2 in p.clock.items():
                        if clk.get(k2, 0) < v2:
                            clk[k2] = v2
            okey = ("g", o.sem_key) if o.sem_key is not None else ("eng", o.eng)
            if o.need_inc:
                c2 = dict(clk)
                c2[okey] = max(c2.get(okey, 0), o.val)
                o.clock = c2
            else:
                o.clock = dict(clk)
            sem = sems[okey] if (o.need_inc and not o.self_inc) else None
            selfsem = sems[okey] if (o.need_inc and o.self_inc) else None
            emit[o.eng].append((waits, o.fn, sem, o.inc, selfsem))
        return emit



def build(cfg):
    nc = bass.Bass("TRN2", target_bir_lowering=False)
    P = Prog()
    stack = ExitStack()
    T, D, H, CH, NT, NH = (cfg["T"], cfg["D"], cfg["H"], cfg["CH"], cfg["NT"],
                           cfg["NH"])
    KS, DC, JC, HID = cfg["KS"], cfg["DC"], cfg["JC"], cfg["HID"]
    TTn = T // 128
    NTT = NT // 128

    din = lambda n, s, dt=F32: nc.dram_tensor(n, list(s), dt, kind="ExternalInput").ap()
    dint = lambda n, s, dt: nc.dram_tensor(n, list(s), dt, kind="Internal").ap()

    x0T = din("x0T_own", [D, NT])
    WQ = din("wq_blk", [24, 128, D])
    WKA = din("wkva_main", [4, 128, D])
    WKAR = din("wkva_rope", [128, 1024])
    WKB = din("wkvb_blk", [32, 128, 512])
    WOd = din("wo_blk", [16, 128, D])
    UPB = din("up_blk", [JC, 128, D])
    DNB = din("down_blk", [16, 128, HID])
    cons_in = [("USC", din("usc", [128, JC])), ("DSC", din("dsc", [128, DC])),
               ("KNW", din("knw", [128, 4])), ("OSC", din("osc", [1, 1])),
               ("ONESC", din("ones_col", [128, 1])),
               ("ONESR", din("ones_row", [1, 128])),
               ("IDF", din("idf", [128, 128])),
               ("IDB", din("idb", [128, 128], BF16)),
               ("TRI", din("tri", [128, 128])),
               ("COS", din("cosq", [64, NT])), ("SIN", din("sinq", [64, NT])),
               ("WKARs", WKAR)]
    OUT = nc.dram_tensor("out_own", [NT, D], F32, kind="ExternalOutput").ap()

    XST = dint("XST", [128, DC * NT], F32)
    GA_inA = dint("GA_inA", [D, CH], BF16)
    GCAs_in = dint("GCAs_in", [1, 32], BF16)
    GCAs_out = dint("GCAs_out", [NCORE, 32], BF16)
    GA_inB = dint("GA_inB", [D, CH], BF16)
    GA_outA = dint("GA_outA", [NCORE * D, CH], BF16)
    GA_outB = dint("GA_outB", [NCORE * D, CH], BF16)
    GVA_inA = dint("GVA_inA", [CH, H * 128], BF16)
    GVA_inB = dint("GVA_inB", [CH, H * 128], BF16)
    GVA_outA = dint("GVA_outA", [NCORE * CH, H * 128], BF16)
    GVA_outB = dint("GVA_outB", [NCORE * CH, H * 128], BF16)
    GYA_in0 = dint("GYA_in0", [128, T], BF16)
    GYA_in1 = dint("GYA_in1", [128, T], BF16)
    GYA_out0 = dint("GYA_out0", [NCORE * 128, T], BF16)
    GYA_out1 = dint("GYA_out1", [NCORE * 128, T], BF16)
    NRB = 2 * D + H * 64 + 64 + 2
    GB_inA = dint("GB_inA", [NRB, CH], BF16)
    GB_inB = dint("GB_inB", [NRB - 2, CH], BF16)
    GB_outA = dint("GB_outA", [NCORE * NRB, CH], BF16)
    GB_outB = dint("GB_outB", [NCORE * (NRB - 2), CH], BF16)
    GVB_inA = dint("GVB_inA", [CH, H * 128], BF16)
    GVB_inB = dint("GVB_inB", [CH, H * 128], BF16)
    GVB_outA = dint("GVB_outA", [NCORE * CH, H * 128], BF16)
    GVB_outB = dint("GVB_outB", [NCORE * CH, H * 128], BF16)
    GYB_in0 = dint("GYB_in0", [128, T], BF16)
    GYB_in1 = dint("GYB_in1", [128, T], BF16)
    GYB_out0 = dint("GYB_out0", [NCORE * 128, T], BF16)
    GYB_out1 = dint("GYB_out1", [NCORE * 128, T], BF16)
    UWS = dint("UWS", [JC, 128, D], BF16)
    DWS = dint("DWS", [16, 128, HID], BF16)

    spec_pers = [
        ("ONESC", [128, 1], F32), ("ONESR", [1, 128], F32),
        ("IDF", [128, 128], F32), ("IDB", [128, 128], BF16),
        ("TRI", [128, 128], F32), ("USC", [128, JC], F32),
        ("DSC", [128, DC], F32), ("CSC", [128, DC], F32),
        ("KNW", [128, 4], F32), ("OSC", [1, 1], F32),
        ("COS", [64, NT], F32), ("SIN", [64, NT], F32),
        ("WKARs", [128, 1024], F32), ("WKARb", [128, 1024], BF16),
        ("BIAS", [128, 64], F32), ("CROW", [1, 32], BF16),
        ("CROWF", [1, 32], F32), ("QN2", [1, 32], F32),
        ("ROWA", [1, 512], F32), ("ROWB", [1, 512], F32),
        ("ROWC", [1, 512], F32), ("ROWD", [1, 512], F32),
        ("KMN", [1, 128], F32), ("KMX", [1, 16], F32), ("RL", [128, 1], F32),
        ("ROWB16", [1, 64], BF16), ("QROW", [1, 32], BF16), ("KMROW", [1, 16], BF16),
        ("CSTG", [1, NT], BF16), ("CSTG2", [1, NT], BF16),
        ("KH", [1, 2], F32),
        ("EPSA", [1, 1], F32), ("EPSB", [1, 1], F32),
        ("YS", [128, 512], F32), ("YB16", [128, 512], BF16),
        ("SUMP", [128, 512], F32),
        ("ONESB", [128, 1], BF16),
        ("XW0", [128, NT], F32), ("XW1", [128, NT], F32),
        ("YTA", [128, 128], F32), ("YTB", [128, 128], BF16),
        ("SQ0", [128, 512], F32), ("SQ1", [128, 512], F32),
    ]
    spec_a = [
        ("N", [128, DC * NT], BF16),
        ("VSTG", [128, NTT * H * 128], BF16),
        ("PSB0", [128, 2 * CH], BF16), ("PSB1", [128, 2 * CH], BF16),
    ]
    spec_a2 = [
        ("KQA", [128, 2 * T], BF16),
        ("VAA", [128, TTn * 256], BF16),
        ("YIN", [128, NCORE * 2 * NT], BF16),
        ("PSB0b", [128, 2 * CH], BF16), ("PSB1b", [128, 2 * CH], BF16),
        ("UPP0", [128, D], F32), ("UPP1", [128, D], F32),
        ("UPS", [128, D], BF16),
        ("DNP0", [128, HID // 2], F32), ("DNP1", [128, HID // 2], F32),
        ("DNS", [128, HID // 2], BF16),
    ]
    spec_f = [
        ("NF", [128, DC * NT], BF16),
        ("HSB", [128, JC * 512], BF16),
        ("UWSB0", [128, D], BF16), ("UWSB1", [128, D], BF16),
        ("DWSB0", [128, HID], BF16), ("DWSB1", [128, HID], BF16),
    ]
    spec_p = [
        ("NP", [128, DC * NT], BF16),
        ("WBLK0", [128, D], F32), ("WBLK1", [128, D], F32),
        ("WBB0", [128, D], BF16), ("WBB1", [128, D], BF16),
        ("RQ", [64, H * NT], BF16),
        ("RH", [64, NT], F32), ("RT1", [64, NT], F32), ("RT2", [64, NT], F32),
        ("LATN", [128, 4 * NT], BF16),
        ("KRO", [64, NT], BF16),
        ("STG", [128, NT], BF16),
        ("VSTG2", [128, NTT * H * 128], BF16),
    ]
    spec_b = [
        ("KNB", [128, 2 * T], BF16),
        ("QNB", [128, 2 * T], BF16),
        ("QRB", [64, 2 * T], BF16),
        ("KRB", [64, T], BF16),
        ("VBB", [128, TTn * 256], BF16),
        ("PSB2", [128, 2 * CH], BF16), ("PSB3", [128, 2 * CH], BF16),
    ]
    spec_w = [
        ("YFM", [128, H * NT], BF16),
        ("WOB0", [128, D], F32), ("WOB1", [128, D], F32),
        ("WOBB", [128, D], BF16),
    ]
    spec_o = [("OS0", [128, 128], F32), ("OS1", [128, 128], F32)]

    def size_of(spec):
        off, offs = 0, {}
        for ent in spec:
            nm, sh, dt = ent[0], ent[1], ent[2]
            if len(ent) == 4:
                offs[nm] = offs[ent[3]]
                continue
            nb = int(np.prod(sh[1:])) * mybir.dt.size(dt)
            offs[nm] = off
            off += ((nb + 31) // 32) * 32
        return off, offs

    pers_sz, pers_off = size_of(spec_pers)
    arena_specs = [spec_a, spec_a2, spec_f, spec_p, spec_b, spec_w, spec_o]
    arena_sz = max(size_of(s)[0] for s in arena_specs)
    total = pers_sz + arena_sz

    backing = stack.enter_context(
        nc.sbuf_tensor("arena", [128, total], mybir.dt.uint8))
    base = nc.lookup_mloc(backing).addr

    V = {}

    def place(spec, offs, origin):
        for ent in spec:
            nm, sh, dt = ent[0], ent[1], ent[2]
            hnd = nc.virtual_sbuf_tensor("sb_" + nm, list(sh), dt)
            nc._tensor(hnd.name, list(hnd.shape), hnd.dtype, type="SB")
            mloc = nc.lookup_mloc(hnd)
            mloc.allocated = True
            mloc.addr = base + origin + offs[nm]
            V[nm] = hnd.ap()

    place(spec_pers, pers_off, 0)
    for sp_ in arena_specs:
        _, offs = size_of(sp_)
        place(sp_, offs, pers_sz)

    PA = stack.enter_context(nc.psum_tensor("PA", [128, 1024], F32)).ap()
    PB = stack.enter_context(nc.psum_tensor("PB", [128, 1024], F32)).ap()
    PC = stack.enter_context(nc.psum_tensor("PC", [128, 1024], F32)).ap()
    PD = stack.enter_context(nc.psum_tensor("PD", [128, 512], F32)).ap()
    PTB = stack.enter_context(nc.psum_tensor("PTB", [128, 512], BF16)).ap()

    op = P.op

    def MM(out, lhsT, rhs, start, stop, r, w):
        op("pe", (lambda o, l, rr, st, sp2: lambda e: e.matmul(
            o, l, rr, start=st, stop=sp2, skip_group_check=True))(
                out, lhsT, rhs, start, stop), r, w)

    def TRANS(out, in_, ident, r, w):
        op("pe", (lambda o, i_, id_: lambda e: e.transpose(o, i_, id_))(
            out, in_, ident), r, w)

    def ACTV(out, in_, func, r, w, bias=0.0, scale=1.0):
        op("act", (lambda o, i_, f, bb, s: lambda e: e.activation(
            o, i_, f, bias=bb, scale=s))(out, in_, func, bias, scale), r, w)

    def SIGN(out, in_, r, w):
        op("act", (lambda o, i_: lambda e: e.sign(o, i_))(out, in_), r, w)

    def TT(out, a, b, alu, r, w):
        op("dve", (lambda o, a_, b_, u: lambda e: e.tensor_tensor(o, a_, b_, u))(
            out, a, b, alu), r, w)

    def TS(out, a, s1, alu, r, w):
        op("dve", (lambda o, a_, s_, u: lambda e: e.tensor_single_scalar(
            o, a_, s_, u))(out, a, s1, alu), r, w)

    def TSM(out, a, s1, r, w):
        op("dve", (lambda o, a_, s_: lambda e: e.tensor_scalar_mul(o, a_, s_))(
            out, a, s1), r, w)

    def STT(out, in0, scal, in1, op0, op1, r, w):
        op("dve", (lambda o, i0, s_, i1, u0, u1: lambda e: e.scalar_tensor_tensor(
            o, i0, s_, i1, u0, u1))(out, in0, scal, in1, op0, op1), r, w)

    def RED(out, in_, alu, r, w):
        op("dve", (lambda o, i_, u: lambda e: e.tensor_reduce(o, i_, AX.X, u))(
            out, in_, alu), r, w)

    def RECIP(out, in_, r, w):
        op("dve", (lambda o, i_: lambda e: e.reciprocal(o, i_))(out, in_), r, w)

    def CPY(out, in_, r, w, eng="dve"):
        if eng == "act":
            op(eng, (lambda o, i_: lambda e: e.copy(o, i_))(out, in_), r, w)
        else:
            op(eng, (lambda o, i_: lambda e: e.tensor_copy(o, i_))(out, in_), r, w)

    def MEMSET(ap_, val, r, w):
        op("dve", (lambda a_, v_: lambda e: e.memset(a_, v_))(ap_, val), r, w)

    def DMA(eng, out, in_, slot, r, w):
        op(eng, (lambda o, i_: lambda e: e.dma_start(out=o, in_=i_))(out, in_),
           r, w, sem_key=slot, inc=16)

    def SWDMA(bodies, slot, r, w):
        n = len(bodies[0])
        assert all(len(bb) == n for bb in bodies)

        def fn(e, sem):
            if not hasattr(e, "_pid_cached"):
                e._pid_cached = e.compute_val(e.partition_id())
            idx = e._pid_cached
            for i in nc.Switch(engines=[e], index=[idx], n=NCORE):
                for (o_, i_) in bodies[i]:
                    e.dma_start(out=o_, in_=i_).then_inc(sem, 16)
        op("pool", fn, r, w, sem_key=slot, inc=16 * n, self_inc=True)

    def COLL(ins, outs, slot, r, w):
        op("pool", (lambda i_, o_: lambda e: e.collective_compute(
            "AllGather", ALU.bypass, replica_groups=[list(range(NCORE))],
            ins=[i_], outs=[o_]))(ins, outs), r, w, sem_key=slot, inc=1)

    def row(half):
        return (PD[0:1, 0:512], "PD") if half == 0 else (PC[0:1, 0:512], "PC0")

    for nm, src in cons_in:
        DMA("sp", V[nm][:], src, "kc_" + nm, r=[], w=[nm])
    MM(PD[:, 0:1], V["ONESR"][:], V["OSC"][:], True, True,
       r=["ONESR", "OSC"], w=["PD"])
    TSM(V["CSC"][:], V["DSC"][:], PD[:, 0:1], r=["DSC", "PD"], w=["CSC"])
    ACTV(V["WKARb"][:], V["WKARs"][:], AF.Copy, r=["WKARs"], w=["WKARb"])
    ACTV(V["ONESB"][:], V["ONESC"][:], AF.Copy, r=["ONESC"], w=["ONESB"])
    MEMSET(V["EPSA"][:], float(cfg["LNEPS"]), r=[], w=["EPSA"])
    MEMSET(V["EPSB"][:], float(cfg["RMSEPS"]), r=[], w=["EPSB"])
    DMA("sp", XST.rearrange("p (c t) -> p c t", c=DC),
        x0T.rearrange("(c p) t -> p c t", p=128), "x0", r=[], w=["XST"])

    def emit_ln(nbuf, nkey):
        for half in range(NH):
            hs = half * 512
            for kc in range(DC):
                xw = V["XW0"] if kc % 2 == 0 else V["XW1"]
                xk = "XW0" if kc % 2 == 0 else "XW1"
                DMA("sp", xw[:, 0:512], XST[:, kc * NT + hs: kc * NT + hs + 512],
                    "dm" + xk, r=["XST"], w=[xk])
                MM(PD[0:1, 0:512], V["ONESC"][:], xw[:, 0:512], kc == 0,
                   kc == DC - 1, r=[xk, "ONESC"], w=["PD"])
                sq = V["SQ0"] if kc % 2 == 0 else V["SQ1"]
                sqk = "SQ0" if kc % 2 == 0 else "SQ1"
                ACTV(sq[:], xw[:, 0:512], AF.Square, r=[xk], w=[sqk])
                MM(PC[0:1, 0:512], V["ONESC"][:], sq[:], kc == 0, kc == DC - 1,
                   r=[sqk, "ONESC"], w=["PC0"])
            TS(V["ROWA"][:], PD[0:1, 0:512], 1.0 / D, ALU.mult, r=["PD"], w=["ROWA"])
            TS(V["ROWB"][:], PC[0:1, 0:512], 1.0 / D, ALU.mult, r=["PC0"], w=["ROWB"])
            TT(V["ROWC"][:], V["ROWA"][:], V["ROWA"][:], ALU.mult,
               r=["ROWA"], w=["ROWC"])
            TT(V["ROWB"][:], V["ROWB"][:], V["ROWC"][:], ALU.subtract,
               r=["ROWB", "ROWC"], w=["ROWB"])
            ACTV(V["ROWB"][:], V["ROWB"][:], AF.Sqrt, bias=V["EPSA"][:],
                 r=["ROWB", "EPSA"], w=["ROWB"])
            RECIP(V["ROWB"][:], V["ROWB"][:], r=["ROWB"], w=["ROWB"])
            TT(V["ROWC"][:], V["ROWA"][:], V["ROWB"][:], ALU.mult,
               r=["ROWA", "ROWB"], w=["ROWC"])
            MM(PC[:, 0:512], V["ONESR"][:], V["ROWB"][:], True, True,
               r=["ROWB", "ONESR"], w=["PC0"])
            MM(PC[:, 512:1024], V["ONESR"][:], V["ROWC"][:], True, True,
               r=["ROWC", "ONESR"], w=["PC1"])
            for kc in range(DC):
                xw = V["XW0"] if kc % 2 == 0 else V["XW1"]
                xk = "XW0" if kc % 2 == 0 else "XW1"
                DMA("sp", xw[:, 0:512], XST[:, kc * NT + hs: kc * NT + hs + 512],
                    "dm" + xk, r=["XST"], w=[xk])
                sq = V["SQ0"] if kc % 2 == 0 else V["SQ1"]
                sqk = "SQ0" if kc % 2 == 0 else "SQ1"
                TT(sq[:], xw[:, 0:512], PC[:, 0:512], ALU.mult,
                   r=[xk, "PC0"], w=[sqk])
                TT(nbuf[:, kc * NT + hs: kc * NT + hs + 512], sq[:],
                   PC[:, 512:1024], ALU.subtract, r=[sqk, "PC1"], w=[nkey])

    def emit_vstage(vsrc, vsrck, vstg, vstgk, i0=0, i1=None):
        if i1 is None:
            i1 = NTT
        for h in range(H):
            for i in range(i0, i1):
                TRANS(PTB[:, 0:128],
                      vsrc[:, h * NT + i * 128: h * NT + (i + 1) * 128],
                      V["IDB"][:], r=[vsrck, "IDB"], w=["PTB"])
                CPY(vstg[:, (i * H + h) * 128: (i * H + h) * 128 + 128],
                    PTB[:, 0:128], r=["PTB"], w=[vstgk], eng="act")

    STAGE = cfg.get("stage", 4)

    if STAGE >= 1:
      emit_ln(V["N"], "N")
      for h in range(H):
          ACTV(V["SQ0"][:, 0:512], V["N"][:, h * NT: h * NT + 512], AF.Square,
               r=["N"], w=["SQ0"])
          MM(PD[0:1, 0:512], V["ONESC"][:], V["SQ0"][:], True, True,
             r=["SQ0", "ONESC"], w=["PD"])
          if NH == 2:
              ACTV(V["SQ1"][:, 0:512], V["N"][:, h * NT + 512: h * NT + 1024],
                   AF.Square, r=["N"], w=["SQ1"])
              MM(PC[0:1, 0:512], V["ONESC"][:], V["SQ1"][:], True, True,
                 r=["SQ1", "ONESC"], w=["PC0"])
          for qc in range(2):
              hh = (qc * CH) // 512
              c0 = qc * CH - hh * 512
              srcp, srck = (PD[0:1, c0:c0 + CH], "PD") if hh == 0 else                   (PC[0:1, c0:c0 + CH], "PC0")
              RED(V["ROWA"][0:1, 0:1], srcp, ALU.max, r=[srck], w=["ROWA"])
              ACTV(V["ROWA"][0:1, 0:1], V["ROWA"][0:1, 0:1], AF.Sqrt,
                   r=["ROWA"], w=["ROWA"])
              TS(V["ROWD"][0:1, qc * 16 + h: qc * 16 + h + 1],
                 V["ROWA"][0:1, 0:1], float(np.sqrt(D)), ALU.mult,
                 r=["ROWA"], w=["ROWD"])
      ACTV(V["CSTG"][0:1, 0:32], V["ROWD"][0:1, 0:32], AF.Copy,
           r=["ROWD"], w=["CSTG"])
      DMA("sp", GCAs_in, V["CSTG"][0:1, 0:32], "cai",
          r=["CSTG"], w=["GCAs_in"])
      DMA("sp", GA_inA[0:D, :].rearrange("(c p) t -> p c t", p=128),
          V["N"].rearrange("p (c t) -> p c t", c=DC)[:, :, 0:CH],
          "gai", r=["N"], w=["GA_inA"])
      DMA("sp", GA_inB.rearrange("(c p) t -> p c t", p=128),
          V["N"].rearrange("p (c t) -> p c t", c=DC)[:, :, CH:NT],
          "gaib", r=["N"], w=["GA_inB"])
      COLL(GCAs_in, GCAs_out, "cc_gcas", r=["GCAs_in"], w=["GCAs_out"])
      COLL(GA_inA, GA_outA, "cc_gaa", r=["GA_inA"], w=["GA_outA"])
      emit_vstage(V["N"], "N", V["VSTG"], "VSTG", 0, NTT // 2)
      DMA("sp", GVA_inA.rearrange("(i p) c -> p i c", p=128),
          V["VSTG"].rearrange("p (i c) -> p i c", i=NTT)[:, 0:NTT // 2, :],
          "gvia", r=["VSTG"], w=["GVA_inA"])
      COLL(GVA_inA, GVA_outA, "cc_gvaa", r=["GVA_inA"], w=["GVA_outA"])
      COLL(GA_inB, GA_outB, "cc_gab", r=["GA_inB"], w=["GA_outB"])
      emit_vstage(V["N"], "N", V["VSTG"], "VSTGB", NTT // 2, NTT)
      DMA("sp", GVA_inB.rearrange("(i p) c -> p i c", p=128),
          V["VSTG"].rearrange("p (i c) -> p i c", i=NTT)[:, NTT // 2:NTT, :],
          "gvib", r=["VSTGB"], w=["GVA_inB"])
      COLL(GVA_inB, GVA_outB, "cc_gvab", r=["GVA_inB"], w=["GVA_outB"])

      P.barrier()

      def switched_heads_bodies(dst, gout, rowblk, rowoff, qc):
          bodies = []
          for me in range(NCORE):
              pairs = []
              for s in range(NCORE):
                  for hl in range(2):
                      h = 2 * me + hl
                      j = s if qc == 0 else NCH - 1 - s
                      src = gout[s * rowblk + rowoff + h * 128:
                                 s * rowblk + rowoff + (h + 1) * 128, :]
                      d = dst[:, hl * T + j * CH: hl * T + (j + 1) * CH]
                      pairs.append((d, src))
              bodies.append(pairs)
          return bodies

      def switched_V_bodies(dst, gout, qc):
          bodies = []
          for me in range(NCORE):
              pairs = []
              for s in range(NCORE):
                  j = s if qc == 0 else NCH - 1 - s
                  src = gout[s * CH: (s + 1) * CH,
                             me * 256:(me + 1) * 256].rearrange(
                                 "(k p) c -> p k c", p=128)
                  d = dst[:, j * KS * 256:(j + 1) * KS * 256].rearrange(
                      "p (k c) -> p k c", k=KS)
                  pairs.append((d, src))
              bodies.append(pairs)
          return bodies

      def switched_C_bodies(gout, rowblk, rowoff, dsttile):
          bodies = []
          for me in range(NCORE):
              rows = gout.rearrange("(r x) t -> r x t", x=rowblk)[:, rowoff, 0:32]
              src = rows.rearrange("r (q h) -> q r h", q=2)[
                  :, :, 2 * me: 2 * me + 2].unsqueeze(0)
              dstc = dsttile[0:1, 0:32].rearrange(
                  "a (q r h) -> a q r h", q=2, r=NCORE)
              bodies.append([(dstc, src)])
          return bodies

      def switched_KM_bodies(gout, rowblk, rowoff):
          bodies = []
          for me in range(NCORE):
              rows = gout.rearrange("(r x) t -> r x t", x=rowblk)[
                  :, rowoff, 2 * me: 2 * me + 2].unsqueeze(0)
              dstc = V["KMROW"][0:1, 0:16].rearrange("a (r h) -> a r h", r=NCORE)
              bodies.append([(dstc, rows)])
          return bodies

      def merge_bodies(*lists):
          out = []
          for me in range(NCORE):
              pairs = []
              for L in lists:
                  pairs.extend(L[me])
              out.append(pairs)
          return out

      def bias_prep(attn):
          scale = float(192 ** -0.5) if attn else float(128 ** -0.5)
          ACTV(V["CROWF"][:], V["CROW"][0:1, 0:32], AF.Copy,
               r=["CROW"], w=["CROWF"])
          for hl in range(2):
              for j in range(NCH):
                  r_, qc = gloc(j)
                  src = V["CROWF"][0:1, 16 * qc + 2 * r_ + hl:
                                   16 * qc + 2 * r_ + hl + 1]
                  MM(PD[:, 0:1], V["ONESR"][:], src, True, True,
                     r=["CROWF", "ONESR"], w=["PD"])
                  col = attn * 32 + hl * 16 + cidx(j)
                  ACTV(V["BIAS"][:, col:col + 1], PD[:, 0:1], AF.Copy,
                       scale=-scale, r=["PD"], w=["BIAS"])

      SWDMA(merge_bodies(switched_heads_bodies(V["KQA"], GA_outA, D, 0, 0),
                         switched_V_bodies(V["VAA"], GVA_outA, 0),
                         switched_C_bodies(GCAs_out, 1, 0, V["CROW"])),
            "swa", r=["GA_outA", "GVA_outA", "GCAs_out"],
            w=["KQA_A", "VAA_A", "CROW"])
      SWDMA(merge_bodies(switched_heads_bodies(V["KQA"], GA_outB, D, 0, 1),
                         switched_V_bodies(V["VAA"], GVA_outB, 1)),
            "swab", r=["GA_outB", "GVA_outB"], w=["KQA_B", "VAA_B"])
      bias_prep(0)

      fillers = []

      def mk_up_filler(jc):
          def f():
              upp = V["UPP0"] if jc % 2 == 0 else V["UPP1"]
              upk = "UPP0" if jc % 2 == 0 else "UPP1"
              DMA("sp", upp[:], UPB[jc], "pw" + upk, r=[], w=[upk])
              SIGN(V["UPS"][:], upp[:], r=[upk], w=["UPS"])
              DMA("pool", UWS[jc], V["UPS"][:], "uwsw", r=["UPS"], w=["UWS"])
          return f

      def mk_dn_filler(dc, ps):
          def f():
              dnp = V["DNP0"] if (dc + ps) % 2 == 0 else V["DNP1"]
              dnk = "DNP0" if (dc + ps) % 2 == 0 else "DNP1"
              DMA("sp", dnp[:], DNB[dc][:, ps * (HID // 2):(ps + 1) * (HID // 2)],
                  "pw" + dnk, r=[], w=[dnk])
              SIGN(V["DNS"][:], dnp[:], r=[dnk], w=["DNS"])
              DMA("pool", DWS[dc][:, ps * (HID // 2):(ps + 1) * (HID // 2)],
                  V["DNS"][:], "dwsw", r=["DNS"], w=["DWS"])
          return f

      for jc in range(JC):
          fillers.append(mk_up_filler(jc))
      for dc in range(DC):
          for ps2 in range(2):
              fillers.append(mk_dn_filler(dc, ps2))

      def bias_prep_b():
          scale = float(192 ** -0.5)
          ACTV(V["CROWF"][:], V["CROW"][0:1, 0:32], AF.Copy,
               r=["CROW"], w=["CROWF"])
          ACTV(V["KMX"][0:1, 0:16], V["KMROW"][0:1, 0:16], AF.Copy,
               r=["KMROW"], w=["KMX"])
          for hl in range(2):
              RED(V["KH"][0:1, hl:hl + 1],
                  V["KMX"].rearrange("a (r h) -> a h r", h=2)[0:1, hl, 0:NCORE],
                  ALU.max, r=["KMX"], w=["KH"])
          for hl in range(2):
              for j in range(NCH):
                  r_, qc = gloc(j)
                  TT(V["ROWA"][0:1, 0:1],
                     V["CROWF"][0:1, 16 * qc + 2 * r_ + hl:
                                16 * qc + 2 * r_ + hl + 1],
                     V["KH"][0:1, hl:hl + 1], ALU.mult,
                     r=["CROWF", "KH"], w=["ROWA"])
                  ACTV(V["ROWA"][0:1, 0:1], V["ROWA"][0:1, 0:1], AF.Sqrt,
                       r=["ROWA"], w=["ROWA"])
                  MM(PD[:, 0:1], V["ONESR"][:], V["ROWA"][0:1, 0:1], True, True,
                     r=["ROWA", "ONESR"], w=["PD"])
                  col = 32 + hl * 16 + cidx(j)
                  ACTV(V["BIAS"][:, col:col + 1], PD[:, 0:1], AF.Copy,
                       scale=-scale, r=["PD"], w=["BIAS"])

      def emit_attn(b, on_hl=None):
          KQ = V["KNB"] if b else V["KQA"]
          QN = V["QNB"] if b else V["KQA"]
          VA = V["VBB"] if b else V["VAA"]
          def kqk(c):
              base = "KNB" if b else "KQA"
              return base + ("_A" if c < NCORE else "_B")
          def qnk(c):
              base = "QNB" if b else "KQA"
              return base + ("_A" if c < NCORE else "_B")
          def vak(c):
              base = "VBB" if b else "VAA"
              return base + ("_A" if c < NCORE else "_B")
          def krk(c):
              return "KRB_A" if c < NCORE else "KRB_B"
          def qrk(c):
              return "QRB_A" if c < NCORE else "QRB_B"
          GY = (GYB_in0, GYB_in1) if b else (GYA_in0, GYA_in1)
          gyk = ("GYB0", "GYB1") if b else ("GYA0", "GYA1")
          scale = float(192 ** -0.5) if b else float(128 ** -0.5)
          psb = [V["PSB2"], V["PSB3"]] if b else [V["PSB0b"], V["PSB1b"]]
          psbk = ["PSB2", "PSB3"] if b else ["PSB0b", "PSB1b"]
          prc = [0]
          for hl in range(2):
              for j in range(NCH):
                  yb = j % 2
                  ybk = "PC" + str(yb)
                  def emit_pv(st):
                      (PSBt_, psbkt_, pr_, kc_, nsub_) = st
                      for s2 in range(nsub_):
                          ks = 2 * pr_ + s2
                          first = (kc_ == 0 and ks == 0)
                          last = (kc_ == j and ks == KS - 1)
                          lv = VA[:, (kc_ * KS + ks) * 256 + hl * 128:
                                  (kc_ * KS + ks) * 256 + hl * 128 + 128]
                          rp = PSBt_[:, s2 * CH: s2 * CH + CH]
                          MM(PC[:, yb * 512: yb * 512 + CH], lv, rp,
                             first, last, r=[psbkt_, vak(kc_)], w=[ybk])
                      for s2 in range(nsub_):
                          ks = 2 * pr_ + s2
                          first = (kc_ == 0 and ks == 0)
                          rp = PSBt_[:, s2 * CH: s2 * CH + CH]
                          if first:
                              CPY(V["SUMP"][:, 0:CH], rp, r=[psbkt_], w=["SUMP"])
                          else:
                              TT(V["SUMP"][:, 0:CH], V["SUMP"][:, 0:CH], rp,
                                 ALU.add, r=["SUMP", psbkt_], w=["SUMP"])
                  pending = None
                  for kc in range(j + 1):
                      for pr in range((KS + 1) // 2):
                          par = prc[0] % 2
                          prc[0] += 1
                          SP_ = PA if par == 0 else PB
                          spk = "PA" if par == 0 else "PB"
                          PSBt, psbkt = psb[par], psbk[par]
                          nsub = min(2, KS - 2 * pr)
                          for s2 in range(nsub):
                              ks = 2 * pr + s2
                              lt = KQ[:, hl * T + kc * CH + ks * 128:
                                      hl * T + kc * CH + ks * 128 + 128]
                              rh_ = QN[:, hl * T + j * CH: hl * T + (j + 1) * CH]
                              bnk = spk + str(s2)
                              MM(SP_[:, s2 * CH:(s2 + 1) * CH], lt, rh_,
                                 True, not b, r=[kqk(kc), qnk(j)], w=[bnk])
                              if b:
                                  ltr = V["KRB"][0:64, kc * CH + ks * 128:
                                                 kc * CH + ks * 128 + 128]
                                  rhr = V["QRB"][0:64, hl * T + j * CH:
                                                 hl * T + (j + 1) * CH]
                                  MM(SP_[:, s2 * CH:(s2 + 1) * CH], ltr, rhr,
                                     False, True, r=[krk(kc), qrk(j)], w=[bnk])
                              if kc == j:
                                  sl = SP_[:, s2 * CH + ks * 128:
                                           s2 * CH + ks * 128 + 128]
                                  TT(sl, sl, V["TRI"][:], ALU.add,
                                     r=[bnk, "TRI"], w=[bnk])
                          bcol = (32 if b else 0) + hl * 16 + cidx(j)
                          rk = [spk + "0"] + ([spk + "1"] if nsub == 2 else [])
                          ACTV(PSBt[:, 0:nsub * CH], SP_[:, 0:nsub * CH], AF.Exp,
                               bias=V["BIAS"][:, bcol:bcol + 1], scale=scale,
                               r=rk + ["BIAS"], w=[psbkt])
                          if kc == j:
                              for s2 in range(nsub):
                                  ks = 2 * pr + s2
                                  if ks > 0:
                                      MEMSET(PSBt[:, s2 * CH: s2 * CH + ks * 128],
                                             0.0, r=[psbkt], w=[psbkt])
                          if pending is not None:
                              emit_pv(pending)
                          pending = (PSBt, psbkt, pr, kc, nsub)
                  if pending is not None:
                      emit_pv(pending)
                  MM(PD[0:1, 0:CH], V["ONESC"][:], V["SUMP"][:, 0:CH],
                     True, True, r=["SUMP", "ONESC"], w=["PD"])
                  RECIP(V["ROWA"][0:1, 0:CH], PD[0:1, 0:CH], r=["PD"], w=["ROWA"])
                  MM(PD[:, 0:CH], V["ONESR"][:], V["ROWA"][0:1, 0:CH], True, True,
                     r=["ROWA", "ONESR"], w=["PD"])
                  ACTV(V["YS"][:, 0:CH], PC[:, yb * 512: yb * 512 + CH], AF.Copy,
                       r=[ybk], w=["YS"])
                  TT(V["YB16"][:, 0:CH], V["YS"][:, 0:CH], PD[:, 0:CH], ALU.mult,
                     r=["YS", "PD"], w=["YB16"])
                  DMA("pool", GY[hl][:, j * CH:(j + 1) * CH],
                      V["YB16"][:, 0:CH], "gy" + ("b" if b else "a") + str(hl),
                      r=["YB16"], w=[gyk[hl]])
                  if not b:
                      take = 3 if fillers else 0
                      for _ in range(take):
                          if fillers:
                              fillers.pop(0)()
                  if on_hl is not None and j == NCH - 1:
                      on_hl(hl)

      def flush_fillers():
          while fillers:
              fillers.pop(0)()

      def ya_coll(hl):
          if hl == 0:
              COLL(GYA_in0, GYA_out0, "cc_gya0", r=["GYA0"], w=["GYA_out0"])
          else:
              COLL(GYA_in1, GYA_out1, "cc_gya1", r=["GYA1"], w=["GYA_out1"])
      emit_attn(0, ya_coll)
      flush_fillers()

      def yret_bodies(gyout, yin, hl):
          bodies = []
          for me in range(NCORE):
              pairs = []
              for s in range(NCORE):
                  for qc in range(2):
                      ch = me if qc == 0 else NCH - 1 - me
                      src = gyout[s * 128:(s + 1) * 128, ch * CH:(ch + 1) * CH]
                      d = yin[:, (s * 2 + hl) * NT + qc * CH:
                              (s * 2 + hl) * NT + (qc + 1) * CH]
                      pairs.append((d, src))
              bodies.append(pairs)
          return bodies

      for hl in range(2):
          gyo = GYA_out0 if hl == 0 else GYA_out1
          gyok = "GYA_out0" if hl == 0 else "GYA_out1"
          yk = "YIN" + str(hl)
          SWDMA(yret_bodies(gyo, V["YIN"], hl), "swya" + str(hl),
                r=[gyok], w=[yk])
          for s in range(NCORE):
              dc = 2 * s + hl
              xw = V["XW0"] if dc % 2 == 0 else V["XW1"]
              xk = "XW0" if dc % 2 == 0 else "XW1"
              DMA("sp", xw[:], XST[:, dc * NT:(dc + 1) * NT], "dm" + xk,
                  r=["XST"], w=[xk])
              TT(xw[:], xw[:],
                 V["YIN"][:, (s * 2 + hl) * NT:(s * 2 + hl + 1) * NT],
                 ALU.add, r=[xk, yk], w=[xk])
              DMA("sp", XST[:, dc * NT:(dc + 1) * NT], xw[:], "st" + xk,
                  r=[xk], w=["XST"])

    P.barrier()

    def emit_ffn(first):
        emit_ln(V["NF"], "NF")
        JH = JC // 2
        for ps in range(2):
            for jl in range(JH):
                jc = ps * JH + jl
                uws = V["UWSB0"] if jl % 2 == 0 else V["UWSB1"]
                uwk = "UWSB0" if jl % 2 == 0 else "UWSB1"
                DMA("sp", uws[:], UWS[jc], "up" + uwk, r=["UWS"], w=[uwk])
                PU, puk = (PA, "PA") if jl % 2 == 0 else (PC, "PC")
                for kc in range(DC):
                    for hh in range(NH):
                        MM(PU[:, hh * 512: hh * 512 + 512],
                           uws[:, kc * 128:(kc + 1) * 128],
                           V["NF"][:, kc * NT + hh * 512: kc * NT + hh * 512 + 512],
                           kc == 0, kc == DC - 1,
                           r=[uwk, "NF"], w=[puk + str(hh)])
                for hh in range(NH):
                    op("act", (lambda jl_, hh_, PU_, jc_: lambda e: e.activation(
                        V["HSB"][:, jl_ * NH * 512 + hh_ * 512:
                                 jl_ * NH * 512 + hh_ * 512 + 512],
                        PU_[:, hh_ * 512: hh_ * 512 + 512], AF.Relu,
                        scale=V["USC"][:, jc_:jc_ + 1]))(jl, hh, PU, jc),
                       r=[puk + str(hh), "USC"], w=["HSB"])
            for dc in range(DC):
                dws = V["DWSB0"] if dc % 2 == 0 else V["DWSB1"]
                dwk = "DWSB0" if dc % 2 == 0 else "DWSB1"
                DMA("sp", dws[:, 0:HID // 2],
                    DWS[dc][:, ps * (HID // 2):(ps + 1) * (HID // 2)],
                    "dn" + dwk, r=["DWS"], w=[dwk])
                PDO, pdk = (PB, "PB") if dc % 2 == 0 else (PA, "PA")
                for jl in range(JH):
                    for hh in range(NH):
                        MM(PDO[:, hh * 512: hh * 512 + 512],
                           dws[:, jl * 128:(jl + 1) * 128],
                           V["HSB"][:, jl * NH * 512 + hh * 512:
                                    jl * NH * 512 + hh * 512 + 512],
                           jl == 0, jl == JH - 1,
                           r=[dwk, "HSB"], w=[pdk + str(hh)])
                for hh in range(NH):
                    hs = hh * 512
                    xw = V["XW0"] if dc % 2 == 0 else V["XW1"]
                    xk = "XW0" if dc % 2 == 0 else "XW1"
                    DMA("sp", xw[:, 0:512],
                        XST[:, dc * NT + hs: dc * NT + hs + 512],
                        "dm" + xk, r=["XST"], w=[xk])
                    op("dve", (lambda dc_, hh_, xw_, PDO_: lambda e:
                        e.scalar_tensor_tensor(
                            xw_[:, 0:512], PDO_[:, hh_ * 512: hh_ * 512 + 512],
                            V["CSC"][:, dc_:dc_ + 1], xw_[:, 0:512],
                            ALU.mult, ALU.add))(dc, hh, xw, PDO),
                       r=[pdk + str(hh), "CSC", xk], w=[xk])
                    DMA("sp", XST[:, dc * NT + hs: dc * NT + hs + 512],
                        xw[:, 0:512], "st" + xk, r=[xk], w=["XST"])

    if STAGE >= 2:
        emit_ffn(False)
    P.barrier()

    if STAGE >= 2.5:
      emit_ln(V["NP"], "NP")

      def proj_mm(blk_dram, m, out_psum, pk, nsrc, nsrck, slot_extra=""):
          wb = V["WBLK0"] if m % 2 == 0 else V["WBLK1"]
          wk = "WBLK0" if m % 2 == 0 else "WBLK1"
          wbb = V["WBB0"] if m % 2 == 0 else V["WBB1"]
          wbbk = "WBB0" if m % 2 == 0 else "WBB1"
          nb = blk_dram.shape[1]
          DMA("sp", wb[:, 0:nb], blk_dram, "pr" + wk + slot_extra,
              r=[], w=[wk])
          ACTV(wbb[:, 0:nb], wb[:, 0:nb], AF.Copy, r=[wk], w=[wbbk])
          kcn = nb // 128
          for kc in range(kcn):
              for hh in range(NH):
                  MM(out_psum[:, hh * 512: hh * 512 + 512],
                     wbb[:, kc * 128:(kc + 1) * 128],
                     nsrc[:, kc * NT + hh * 512: kc * NT + hh * 512 + 512],
                     kc == 0, kc == kcn - 1, r=[wbbk, nsrck], w=[pk + str(hh)])

      for m in range(24):
          PQ, pk = (PA, "PA") if m % 2 == 0 else (PB, "PB")
          proj_mm(WQ[m], m, PQ, pk, V["NP"], "NP")
          pks = [pk + str(hh) for hh in range(NH)]
          if m < 16:
              ACTV(V["STG"][:], PQ[:, 0:NT], AF.Copy, r=pks, w=["STG"])
              DMA("sp", GB_inA[m * 128:(m + 1) * 128, :], V["STG"][:, 0:CH],
                  "gqn", r=["STG"], w=["GB_inA"])
              DMA("sp", GB_inB[m * 128:(m + 1) * 128, :], V["STG"][:, CH:NT],
                  "gqnb", r=["STG"], w=["GB_inB"])
              for hh in range(NH):
                  sq = V["SQ0"] if hh == 0 else V["SQ1"]
                  sqk = "SQ0" if hh == 0 else "SQ1"
                  ACTV(sq[:], PQ[:, hh * 512: hh * 512 + 512], AF.Square,
                       r=[pk + str(hh)], w=[sqk])
                  rw, rwk = row(hh)
                  MM(rw, V["ONESC"][:], sq[:], True, True, r=[sqk, "ONESC"], w=[rwk])
              for qc in range(2):
                  hh = (qc * CH) // 512
                  c0 = qc * CH - hh * 512
                  rw, rwk = row(hh)
                  RED(V["QN2"][0:1, qc * 16 + m: qc * 16 + m + 1],
                      rw[0:1, c0:c0 + CH], ALU.max, r=[rwk], w=["QN2"])
          else:
              h1, h2 = 2 * (m - 16), 2 * (m - 16) + 1
              ACTV(V["RQ"][0:64, h1 * NT:(h1 + 1) * NT], PQ[0:64, 0:NT], AF.Copy,
                   r=pks, w=["RQ"])
              ACTV(V["RQ"][0:64, h2 * NT:(h2 + 1) * NT], PQ[64:128, 0:NT], AF.Copy,
                   r=pks, w=["RQ"])

      def rope_apply(src, srck, dst, dstk):
          TS(V["RH"][0:32, :], src[32:64, :], -1.0, ALU.mult, r=[srck], w=["RH"])
          CPY(V["RH"][32:64, :], src[0:32, :], r=[srck], w=["RH"])
          TT(V["RT1"][0:64, :], src, V["COS"][0:64, :], ALU.mult,
             r=[srck, "COS"], w=["RT1"])
          TT(V["RT2"][0:64, :], V["RH"][0:64, :], V["SIN"][0:64, :], ALU.mult,
             r=["RH", "SIN"], w=["RT2"])
          TT(dst, V["RT1"][0:64, :], V["RT2"][0:64, :], ALU.add,
             r=["RT1", "RT2"], w=[dstk])

      for h in range(H):
          rope_apply(V["RQ"][0:64, h * NT:(h + 1) * NT], "RQ",
                     V["RQ"][0:64, h * NT:(h + 1) * NT], "RQ")
          ACTV(V["RT1"][0:64, :], V["RQ"][0:64, h * NT:(h + 1) * NT], AF.Square,
               r=["RQ"], w=["RT1"])
          for hh in range(NH):
              rw, rwk = row(hh)
              MM(rw, V["ONESC"][0:64, :], V["RT1"][0:64, hh * 512: hh * 512 + 512],
                 True, True, r=["RT1", "ONESC"], w=[rwk])
          for qc in range(2):
              hh = (qc * CH) // 512
              c0 = qc * CH - hh * 512
              rw, rwk = row(hh)
              RED(V["ROWA"][0:1, 0:1], rw[0:1, c0:c0 + CH], ALU.max,
                  r=[rwk], w=["ROWA"])
              TT(V["QN2"][0:1, qc * 16 + h: qc * 16 + h + 1],
                 V["QN2"][0:1, qc * 16 + h: qc * 16 + h + 1],
                 V["ROWA"][0:1, 0:1], ALU.add, r=["QN2", "ROWA"], w=["QN2"])
      DMA("sp", GB_inA[2 * D: 2 * D + H * 64, :].rearrange(
              "(h p) t -> p h t", p=64),
          V["RQ"][0:64, :].rearrange("p (h t) -> p h t", h=H)[:, :, 0:CH],
          "gqr", r=["RQ"], w=["GB_inA"])
      DMA("sp", GB_inB[2 * D: 2 * D + H * 64, :].rearrange(
              "(h p) t -> p h t", p=64),
          V["RQ"][0:64, :].rearrange("p (h t) -> p h t", h=H)[:, :, CH:NT],
          "gqrb", r=["RQ"], w=["GB_inB"])

      for m in range(4):
          PQ, pk = (PA, "PA") if m % 2 == 0 else (PB, "PB")
          proj_mm(WKA[m], m, PQ, pk, V["NP"], "NP")
          pks = [pk + str(hh) for hh in range(NH)]
          ACTV(V["LATN"][:, m * NT:(m + 1) * NT], PQ[:, 0:NT], AF.Copy,
               r=pks, w=["LATN"])
          for hh in range(NH):
              sq = V["SQ0"] if hh == 0 else V["SQ1"]
              sqk = "SQ0" if hh == 0 else "SQ1"
              ACTV(sq[:], PQ[:, hh * 512: hh * 512 + 512], AF.Square,
                   r=[pk + str(hh)], w=[sqk])
              rw, rwk = row(hh)
              MM(rw, V["ONESC"][:], sq[:], m == 0, m == 3, r=[sqk, "ONESC"], w=[rwk])
      for hh in range(NH):
          rw, rwk = row(hh)
          dstrow = V["ROWA"] if hh == 0 else V["ROWB"]
          dk = "ROWA" if hh == 0 else "ROWB"
          TS(dstrow[:], rw, 1.0 / 512.0, ALU.mult, r=[rwk], w=[dk])
          ACTV(dstrow[:], dstrow[:], AF.Sqrt, bias=V["EPSB"][:],
               r=[dk, "EPSB"], w=[dk])
          RECIP(dstrow[:], dstrow[:], r=[dk], w=[dk])
      for hh in range(NH):
          dstrow = V["ROWA"] if hh == 0 else V["ROWB"]
          dk = "ROWA" if hh == 0 else "ROWB"
          MM(PC[:, hh * 512: hh * 512 + 512], V["ONESR"][:], dstrow[:], True, True,
             r=[dk, "ONESR"], w=["PC" + str(hh)])
      for m in range(4):
          for hh in range(NH):
              STT(V["LATN"][:, m * NT + hh * 512: m * NT + hh * 512 + 512],
                  V["LATN"][:, m * NT + hh * 512: m * NT + hh * 512 + 512],
                  V["KNW"][:, m:m + 1], PC[:, hh * 512: hh * 512 + 512],
                  ALU.mult, ALU.mult, r=["LATN", "KNW", "PC" + str(hh)], w=["LATN"])

      for kc in range(DC):
          for hh in range(NH):
              MM(PA[0:64, hh * 512: hh * 512 + 512],
                 V["WKARb"][:, kc * 64:(kc + 1) * 64],
                 V["NP"][:, kc * NT + hh * 512: kc * NT + hh * 512 + 512],
                 kc == 0, kc == DC - 1, r=["WKARb", "NP"], w=["PA" + str(hh)])
      ACTV(V["KRO"][0:64, :], PA[0:64, 0:NT], AF.Copy,
           r=["PA" + str(hh) for hh in range(NH)], w=["KRO"])
      rope_apply(V["KRO"][0:64, :], "KRO", V["KRO"][0:64, :], "KRO")
      DMA("sp", GB_inA[2 * D + H * 64: 2 * D + H * 64 + 64, :],
          V["KRO"][0:64, 0:CH], "gkr", r=["KRO"], w=["GB_inA"])
      DMA("sp", GB_inB[2 * D + H * 64: 2 * D + H * 64 + 64, :],
          V["KRO"][0:64, CH:NT], "gkrb", r=["KRO"], w=["GB_inB"])
      ACTV(V["RT1"][0:64, :], V["KRO"][0:64, :], AF.Square, r=["KRO"], w=["RT1"])
      for hh in range(NH):
          rw, rwk = row(hh)
          MM(rw, V["ONESC"][0:64, :], V["RT1"][0:64, hh * 512: hh * 512 + 512],
             True, True, r=["RT1", "ONESC"], w=[rwk])
          dst = V["ROWC"] if hh == 0 else V["ROWD"]
          dk = "ROWC" if hh == 0 else "ROWD"
          CPY(dst[:], rw, r=[rwk], w=[dk])

      emit_vstage_pending = []
      for m in range(32):
          PQ, pk = (PA, "PA") if m % 2 == 0 else (PB, "PB")
          proj_mm(WKB[m], m, PQ, pk, V["LATN"], "LATN")
          pks = [pk + str(hh) for hh in range(NH)]
          if m < 16:
              ACTV(V["STG"][:], PQ[:, 0:NT], AF.Copy, r=pks, w=["STG"])
              DMA("sp", GB_inA[D + m * 128: D + (m + 1) * 128, :],
                  V["STG"][:, 0:CH], "gkn", r=["STG"], w=["GB_inA"])
              DMA("sp", GB_inB[D + m * 128: D + (m + 1) * 128, :],
                  V["STG"][:, CH:NT], "gknb", r=["STG"], w=["GB_inB"])
              first_red = True
              for hh in range(NH):
                  sq = V["SQ0"] if hh == 0 else V["SQ1"]
                  sqk = "SQ0" if hh == 0 else "SQ1"
                  ACTV(sq[:], PQ[:, hh * 512: hh * 512 + 512], AF.Square,
                       r=[pk + str(hh)], w=[sqk])
                  rw, rwk = row(hh)
                  MM(rw, V["ONESC"][:], sq[:], True, True, r=[sqk, "ONESC"], w=[rwk])
                  ro = V["ROWC"] if hh == 0 else V["ROWD"]
                  rok = "ROWC" if hh == 0 else "ROWD"
                  sq2 = V["SQ0"] if hh == 0 else V["SQ1"]
                  TT(sq2[0:1, 0:512], rw, ro[:], ALU.add, r=[rwk, rok], w=[sqk])
                  RED(V["ROWB"][0:1, hh:hh + 1], sq2[0:1, 0:512], ALU.max,
                      r=[sqk], w=["ROWB"])
              if NH == 2:
                  TT(V["ROWB"][0:1, 0:1], V["ROWB"][0:1, 0:1], V["ROWB"][0:1, 1:2],
                     ALU.max, r=["ROWB"], w=["ROWB"])
              CPY(V["KMN"][0:1, m:m + 1], V["ROWB"][0:1, 0:1], r=["ROWB"], w=["KMN"])
          else:
              h = m - 16
              ACTV(V["STG"][:], PQ[:, 0:NT], AF.Copy, r=pks, w=["STG"])
              for i in range(NTT):
                  TRANS(PTB[:, 0:128], V["STG"][:, i * 128:(i + 1) * 128],
                        V["IDB"][:], r=["STG", "IDB"], w=["PTB"])
                  CPY(V["VSTG2"][:, (i * H + h) * 128: (i * H + h) * 128 + 128],
                      PTB[:, 0:128], r=["PTB"], w=["VSTG2"], eng="act")
      DMA("sp", GVB_inA.rearrange("(i p) c -> p i c", p=128),
          V["VSTG2"].rearrange("p (i c) -> p i c", i=NTT)[:, 0:NTT // 2, :],
          "gvb", r=["VSTG2"], w=["GVB_inA"])
      DMA("sp", GVB_inB.rearrange("(i p) c -> p i c", p=128),
          V["VSTG2"].rearrange("p (i c) -> p i c", i=NTT)[:, NTT // 2:NTT, :],
          "gvbb", r=["VSTG2"], w=["GVB_inB"])
      MEMSET(V["CSTG"][:], 0.0, r=[], w=["CSTG"])
      ACTV(V["CSTG"][0:1, 0:16], V["KMN"][0:1, 0:16], AF.Copy,
           r=["KMN"], w=["CSTG"])
      DMA("sp", GB_inA[NRB - 1: NRB, :], V["CSTG"][0:1, 0:CH], "gkm",
          r=["CSTG"], w=["GB_inA"])
      MEMSET(V["CSTG2"][:], 0.0, r=[], w=["CSTG2"])
      ACTV(V["CSTG2"][0:1, 0:32], V["QN2"][0:1, 0:32], AF.Copy,
           r=["QN2"], w=["CSTG2"])
      DMA("sp", GB_inA[NRB - 2: NRB - 1, :], V["CSTG2"][0:1, 0:CH], "gqn2",
          r=["CSTG2"], w=["GB_inA"])
      COLL(GB_inA, GB_outA, "cc_gba", r=["GB_inA"], w=["GB_outA"])
      COLL(GVB_inA, GVB_outA, "cc_gvba", r=["GVB_inA"], w=["GVB_outA"])
      COLL(GB_inB, GB_outB, "cc_gbb", r=["GB_inB"], w=["GB_outB"])
      COLL(GVB_inB, GVB_outB, "cc_gvbb", r=["GVB_inB"], w=["GVB_outB"])

      P.barrier()

      pass
    if STAGE >= 2.8:
      def qr_bodies_f(gout, rowblk, qc):
          out = []
          for me in range(NCORE):
              pairs = []
              for s in range(NCORE):
                  for hl in range(2):
                      j = s if qc == 0 else NCH - 1 - s
                      r0 = s * rowblk + 2 * D + (2 * me + hl) * 64
                      src = gout[r0: r0 + 64, :]
                      d = V["QRB"][0:64, hl * T + j * CH: hl * T + (j + 1) * CH]
                      pairs.append((d, src))
              out.append(pairs)
          return out
      for s in range(NCORE):
          r0 = s * NRB + 2 * D + H * 64
          DMA("pool", V["KRB"][0:64, s * CH:(s + 1) * CH],
              GB_outA[r0: r0 + 64, :], "krl", r=["GB_outA"], w=["KRB_A"])
          r0 = s * (NRB - 2) + 2 * D + H * 64
          j = NCH - 1 - s
          DMA("pool", V["KRB"][0:64, j * CH:(j + 1) * CH],
              GB_outB[r0: r0 + 64, :], "krlb", r=["GB_outB"], w=["KRB_B"])
      SWDMA(merge_bodies(switched_heads_bodies(V["QNB"], GB_outA, NRB, 0, 0),
                         switched_heads_bodies(V["KNB"], GB_outA, NRB, D, 0),
                         qr_bodies_f(GB_outA, NRB, 0),
                         switched_C_bodies(GB_outA, NRB, NRB - 2, V["CROW"]),
                         switched_KM_bodies(GB_outA, NRB, NRB - 1)),
            "swb", r=["GB_outA"],
            w=["KNB_A", "QNB_A", "QRB_A", "CROW", "KMROW"])
      SWDMA(switched_V_bodies(V["VBB"], GVB_outA, 0), "swbav",
            r=["GVB_outA"], w=["VBB_A"])
      SWDMA(merge_bodies(switched_heads_bodies(V["QNB"], GB_outB, NRB - 2, 0, 1),
                         switched_heads_bodies(V["KNB"], GB_outB, NRB - 2, D, 1),
                         qr_bodies_f(GB_outB, NRB - 2, 1)),
            "swbb", r=["GB_outB"],
            w=["KNB_B", "QNB_B", "QRB_B"])
      SWDMA(switched_V_bodies(V["VBB"], GVB_outB, 1), "swbbv",
            r=["GVB_outB"], w=["VBB_B"])
      bias_prep_b()
      if cfg.get("dump"):
          DBG1 = nc.dram_tensor("dbg_bias", [128, 64], F32, kind="ExternalOutput").ap()
          DBG2 = nc.dram_tensor("dbg_crow", [1, 32], F32, kind="ExternalOutput").ap()
          DBG3 = nc.dram_tensor("dbg_qn2", [1, 32], F32, kind="ExternalOutput").ap()
          DBG4 = nc.dram_tensor("dbg_kmx", [1, 16], F32, kind="ExternalOutput").ap()
          DBG5 = nc.dram_tensor("dbg_kqs", [128, 2048], BF16, kind="ExternalOutput").ap()
          DBG6 = nc.dram_tensor("dbg_qns", [128, 2048], BF16, kind="ExternalOutput").ap()
          DBG7 = nc.dram_tensor("dbg_vbs", [128, 2048], BF16, kind="ExternalOutput").ap()
          DBG8 = nc.dram_tensor("dbg_qrs", [64, 2048], BF16, kind="ExternalOutput").ap()
          DBG9 = nc.dram_tensor("dbg_krs", [64, 2048], BF16, kind="ExternalOutput").ap()
          DMA("pool", DBG1, V["BIAS"][:], "dbgw", r=["BIAS"], w=["DBG"])
          DMA("pool", DBG2, V["CROW"][0:1, 0:32], "dbgw", r=["CROW"], w=["DBG"])
          DMA("pool", DBG3, V["QN2"][0:1, 0:32], "dbgw", r=["QN2"], w=["DBG"])
          DMA("pool", DBG4, V["KMX"][0:1, 0:16], "dbgw", r=["KMX"], w=["DBG"])
          DMA("pool", DBG5, V["KNB"][:, 0:2048], "dbgw", r=["KNB"], w=["DBG"])
          DMA("pool", DBG6, V["QNB"][:, 0:2048], "dbgw", r=["QNB"], w=["DBG"])
          DMA("pool", DBG7, V["VBB"][:, 0:2048], "dbgw", r=["VBB"], w=["DBG"])
          DMA("pool", DBG8, V["QRB"][0:64, 0:2048], "dbgw", r=["QRB"], w=["DBG"])
          DMA("pool", DBG9, V["KRB"][0:64, 0:2048], "dbgw", r=["KRB"], w=["DBG"])
          DBG10 = nc.dram_tensor("dbg_gkm", [NCORE, 16], F32, kind="ExternalOutput").ap()
          DBG11 = nc.dram_tensor("dbg_kmn", [1, 128], F32, kind="ExternalOutput").ap()
          DBG12 = nc.dram_tensor("dbg_rowcd", [1, 1024], F32, kind="ExternalOutput").ap()
          DMA("pool", DBG10, GKM_out, "dbgw", r=["GKM_out"], w=["DBG"])
          DMA("pool", DBG11, V["KMN"][0:1, 0:128], "dbgw", r=["KMN"], w=["DBG"])
          DMA("pool", DBG12[0:1, 0:512], V["ROWC"][0:1, 0:512], "dbgw", r=["ROWC"], w=["DBG"])
          DMA("pool", DBG12[0:1, 512:1024], V["ROWD"][0:1, 0:512], "dbgw", r=["ROWD"], w=["DBG"])
          DBG13 = nc.dram_tensor("dbg_kro", [64, NT], BF16, kind="ExternalOutput").ap()
          DBG14 = nc.dram_tensor("dbg_latn", [128, NT], BF16, kind="ExternalOutput").ap()
          DBG15 = nc.dram_tensor("dbg_np", [128, NT], BF16, kind="ExternalOutput").ap()
          DMA("pool", DBG13, V["KRO"][0:64, :], "dbgw", r=["KRO"], w=["DBG"])
          DMA("pool", DBG14, V["LATN"][:, 0:NT], "dbgw", r=["LATN"], w=["DBG"])
          DMA("pool", DBG15, V["NP"][:, 0:NT], "dbgw", r=["NP"], w=["DBG"])
          op("sp", lambda e: e.nop(), r=["DBG"], w=[])

      pass
    if STAGE >= 3:
      def yb_coll(hl):
          if hl == 0:
              COLL(GYB_in0, GYB_out0, "cc_gyb0", r=["GYB0"], w=["GYB_out0"])
          else:
              COLL(GYB_in1, GYB_out1, "cc_gyb1", r=["GYB1"], w=["GYB_out1"])
      emit_attn(1, yb_coll)

      P.barrier()

      def yfm_bodies(gyout, hl):
          bodies = []
          for me in range(NCORE):
              pairs = []
              for s in range(NCORE):
                  h = 2 * s + hl
                  for qc in range(2):
                      ch = me if qc == 0 else NCH - 1 - me
                      src = gyout[s * 128:(s + 1) * 128, ch * CH:(ch + 1) * CH]
                      d = V["YFM"][:, h * NT + qc * CH: h * NT + (qc + 1) * CH]
                      pairs.append((d, src))
              bodies.append(pairs)
          return bodies

      for hl in range(2):
          gyo = GYB_out0 if hl == 0 else GYB_out1
          gyok = "GYB_out0" if hl == 0 else "GYB_out1"
          SWDMA(yfm_bodies(gyo, hl), "swyb" + str(hl),
                r=[gyok], w=["YFM"])
      for dc in range(DC):
          wob = V["WOB0"] if dc % 2 == 0 else V["WOB1"]
          wbk = "WOB0" if dc % 2 == 0 else "WOB1"
          DMA("sp", wob[:], WOd[dc], "wo" + wbk, r=[], w=[wbk])
          ACTV(V["WOBB"][:], wob[:], AF.Copy, r=[wbk], w=["WOBB"])
          for hv in range(H):
              for hh in range(NH):
                  MM(PA[:, hh * 512: hh * 512 + 512],
                     V["WOBB"][:, hv * 128:(hv + 1) * 128],
                     V["YFM"][:, hv * NT + hh * 512: hv * NT + hh * 512 + 512],
                     hv == 0, hv == H - 1, r=["WOBB", "YFM"], w=["PA" + str(hh)])
          xw = V["XW0"] if dc % 2 == 0 else V["XW1"]
          xk = "XW0" if dc % 2 == 0 else "XW1"
          DMA("sp", xw[:], XST[:, dc * NT:(dc + 1) * NT], "dm" + xk,
              r=["XST"], w=[xk])
          op("dve", (lambda xw_: lambda e: e.tensor_add(
              xw_[:, 0:NT], xw_[:, 0:NT], PA[:, 0:NT]))(xw),
             r=[xk] + ["PA" + str(hh) for hh in range(NH)], w=[xk])
          DMA("sp", XST[:, dc * NT:(dc + 1) * NT], xw[:], "st" + xk,
              r=[xk], w=["XST"])

    P.barrier()

    if STAGE >= 4:
        emit_ffn(False)
    P.barrier()

    for dc in range(DC):
        xw = V["XW0"] if dc % 2 == 0 else V["XW1"]
        xk = "XW0" if dc % 2 == 0 else "XW1"
        DMA("sp", xw[:], XST[:, dc * NT:(dc + 1) * NT], "dm" + xk,
            r=["XST"], w=[xk])
        for i in range(NTT):
            os_ = V["OS0"] if i % 2 == 0 else V["OS1"]
            osk = "OS0" if i % 2 == 0 else "OS1"
            TRANS(PA[:, 0:128], xw[:, i * 128:(i + 1) * 128], V["IDF"][:],
                  r=[xk, "IDF"], w=["PA0"])
            CPY(os_[:], PA[:, 0:128], r=["PA0"], w=[osk])
            DMA("pool", OUT[i * 128:(i + 1) * 128, dc * 128:(dc + 1) * 128],
                os_[:], "outw", r=[osk], w=["OUTW"])
    op("sp", lambda e: e.nop(), r=["OUTW"], w=[])

    emit = P.finalize(nc, stack)

    with nc.Block() as block:
        def runner(items):
            def go(eng):
                for waits, fn, sem, inc, selfsem in items:
                    for s_, v_ in waits:
                        eng.wait_ge(s_, v_)
                    if selfsem is not None:
                        fn(eng, selfsem)
                    else:
                        inst = fn(eng)
                        if sem is not None:
                            inst.then_inc(sem, inc)
            return go
        block.tensor(runner(emit["pe"]))
        block.scalar(runner(emit["act"]))
        block.vector(runner(emit["dve"]))
        block.sync(runner(emit["sp"]))
        block.gpsimd(runner(emit["pool"]))

    stack.pop_all()
    return nc


def kernel(**inputs):
    cfg = make_cfg()
    maps = host_prep(cfg, inputs)
    nc = build(cfg)
    res = run_bass_kernel_spmd(nc, maps, core_ids=list(range(NCORE)))
    T, D, CH = cfg["T"], cfg["D"], cfg["CH"]
    out = np.zeros((T, D), np.float32)
    for me in range(NCORE):
        y = res.results[me]["out_own"]
        out[me * CH:(me + 1) * CH] = y[:CH]
        out[(NCH - 1 - me) * CH:(NCH - me) * CH] = y[CH:]
    return out.reshape(1, T, D)
